# version 08
# baseline (speedup 1.0000x reference)
import numpy as np
import ml_dtypes
import concourse.bass as bass
import concourse.mybir as mybir

F32 = mybir.dt.float32
BF16 = mybir.dt.bfloat16
AF = mybir.ActivationFunctionType
ALU = mybir.AluOpType
AX = mybir.AxisListType
bf = ml_dtypes.bfloat16

ENGINES = ("tensor", "vector", "scalar", "gpsimd", "sync")

S = 4096
D = 1024
NT = 32
NH = 16
DH = 64
NSEL = 409
NCMP = 512
HME = 2048
NE = 8
NSH = 2


class Buf:
    __slots__ = ("last_writer", "readers")

    def __init__(self):
        self.last_writer = None
        self.readers = []


class Prog:

    NSTREAMS = {"sync": 8, "gpsimd": 8, "scalar": 4, "vector": 2, "tensor": 2}

    def __init__(self):
        self.ops = []
        self._bar = []
        self._stream_counts = {}

    def emit(self, engine, fn, reads=(), writes=(), extra=(), dma=False):
        tok = len(self.ops)
        waits = set(self._bar)
        for b in reads:
            if b.last_writer is not None:
                waits.add(b.last_writer)
        for b in writes:
            if b.last_writer is not None:
                waits.add(b.last_writer)
            for r in b.readers:
                waits.add(r)
        for t in extra:
            if t is not None:
                waits.add(t)
        waits.discard(tok)
        skey = seq = None
        if dma:
            n = self._stream_counts.get(engine, 0)
            self._stream_counts[engine] = n + 1
            skey = (engine, n % self.NSTREAMS[engine])
            seq = n // self.NSTREAMS[engine] + 1
        self.ops.append([engine, fn, dma, waits, skey, seq])
        for b in reads:
            b.readers.append(tok)
        for b in writes:
            b.last_writer = tok
            b.readers = []
        return tok

    def barrier(self):
        last = {}
        for i, op in enumerate(self.ops):
            last[op[0]] = i
        self._bar = list(last.values())

    def finalize(self, nc, block, final_extra=()):
        ops = self.ops
        n = len(ops)
        esem = {e: nc.alloc_semaphore(f"sem_{e}") for e in ENGINES}
        ssem = {}
        for (eng, k) in {op[4] for op in ops if op[2]}:
            ssem[(eng, k)] = nc.alloc_semaphore(f"dma_{eng}_{k}")
        needed = [False] * n
        for _, _, _, waits, _, _ in ops:
            for t in waits:
                needed[t] = True
        for t in final_extra:
            needed[t] = True
        cum = [None] * n
        run = {e: 0 for e in ENGINES}
        for i, (eng, _, dma, _, skey, seq) in enumerate(ops):
            if dma:
                cum[i] = (ssem[skey], 16 * seq)
            else:
                if needed[i]:
                    run[eng] += 1
                cum[i] = (esem[eng], run[eng])

        by_engine = {e: [] for e in ENGINES}
        for i, op in enumerate(ops):
            by_engine[op[0]].append(i)

        def body(eng_name):
            def runner(eng):
                water = {}
                def do_wait(sem, v):
                    if water.get(id(sem), 0) < v:
                        eng.wait_ge(sem, v)
                        water[id(sem)] = v
                for i in by_engine[eng_name]:
                    _, fn, dma, waits, skey, seq = ops[i]
                    per = {}
                    for t in waits:
                        if ops[t][0] == "tensor" and eng_name == "tensor" and not ops[t][2]:
                            continue
                        sem, v = cum[t]
                        if v > per.get(id(sem), (None, 0))[1]:
                            per[id(sem)] = (sem, v)
                    for sem, v in per.values():
                        do_wait(sem, v)
                    if dma and seq > 1:
                        do_wait(ssem[skey], 16 * (seq - 1))
                    inst = fn(eng)
                    if dma:
                        inst.then_inc(ssem[skey], 16)
                    elif needed[i]:
                        inst.then_inc(esem[eng_name], 1)
                if eng_name == "sync":
                    per = {}
                    for t in final_extra:
                        sem, v = cum[t]
                        if v > per.get(id(sem), (None, 0))[1]:
                            per[id(sem)] = (sem, v)
                    for sem, v in per.values():
                        do_wait(sem, v)
            return runner

        for e in ENGINES:
            getattr(block, e)(body(e))


def host_consts():
    c = {}
    c["c_ident"] = np.eye(128).astype(bf)
    c["c_identf"] = np.eye(128).astype(np.float32)
    c["c_uinc"] = np.triu(np.ones((128, 128))).astype(bf)
    c["c_ppool"] = (np.arange(128)[:, None] // 8 == np.arange(16)[None, :]).astype(bf)
    c["c_l32"] = (np.arange(32)[:, None] < np.arange(32)[None, :]).astype(bf)
    c["c_ones_col"] = np.ones((128, 1)).astype(bf)
    c["c_ones_row_f"] = np.ones((1, 128)).astype(np.float32)
    c["c_onesf"] = np.ones((65, 128)).astype(np.float32)
    pb = np.zeros((128, 4), np.float32)
    pb[409 - 3 * 128:, 3] = -60.0
    c["c_padbias"] = pb
    c["c_ones_row_b"] = np.ones((1, 128)).astype(bf)
    return c


class KB:
    def __init__(self, nc, prog):
        self.nc = nc
        self.p = prog
        self.dram = {}
        self.dbuf = {}
        self.uid = 0

    def din(self, name, shape, dtype=F32):
        self.dram[name] = self.nc.dram_tensor(name, list(shape), dtype, kind="ExternalInput").ap()
        self.dbuf[name] = Buf()
        return self.dram[name]

    def dout(self, name, shape, dtype=F32):
        self.dram[name] = self.nc.dram_tensor(name, list(shape), dtype, kind="ExternalOutput").ap()
        self.dbuf[name] = Buf()
        return self.dram[name]

    def dint(self, name, shape, dtype=F32):
        self.dram[name] = self.nc.dram_tensor(name, list(shape), dtype, kind="Internal").ap()
        self.dbuf[name] = Buf()
        return self.dram[name]

    def sb(self, name, shape, dtype):
        return self.nc.alloc_sbuf_tensor(name, list(shape), dtype)

    def setup(self, consts):
        nc = self.nc
        self.pool1 = self.sb("pool1", [128, 32768], BF16)
        self.pool2 = self.sb("pool2", [128, 32768], BF16)
        self.pool3 = self.sb("pool3", [128, 16384], F32)
        self.b_pool1 = Buf()
        self.b_pool2 = Buf()
        self.b_pool3 = Buf()
        self.P0 = nc.alloc_psum_tensor("P0", [128, 1024], F32)
        self.P1 = nc.alloc_psum_tensor("P1", [128, 1024], F32)
        self.P2 = nc.alloc_psum_tensor("P2", [128, 1024], F32)
        self.PS = nc.alloc_psum_tensor("PS", [128, 512], F32)
        self.PT = nc.alloc_psum_tensor("PT", [128, 1024], BF16)
        self.bP0, self.bP1, self.bP2, self.bPS, self.bPT = Buf(), Buf(), Buf(), Buf(), Buf()
        self.ident = self.sb("ident", [128, 128], BF16)
        self.identf = self.sb("identf", [128, 128], F32)
        self.uinc = self.sb("uinc", [128, 128], BF16)
        self.ppool = self.sb("ppool", [128, 16], BF16)
        self.l32 = self.sb("l32", [32, 32], BF16)
        self.ones_col = self.sb("ones_col", [128, 1], BF16)
        self.ones_row_f = self.sb("ones_row_f", [1, 128], F32)
        self.onesf = self.sb("onesf", [65, 128], F32)
        self.ones_row_b = self.sb("ones_row_b", [1, 128], BF16)
        self.b_const = Buf()
        p = self.p
        for nm in ("c_ident", "c_uinc", "c_ppool", "c_l32", "c_ones_col",
                   "c_ones_row_f", "c_ones_row_b", "c_identf", "c_onesf"):
            dst = {"c_ident": self.ident, "c_uinc": self.uinc, "c_ppool": self.ppool,
                   "c_l32": self.l32, "c_ones_col": self.ones_col,
                   "c_ones_row_f": self.ones_row_f, "c_ones_row_b": self.ones_row_b,
                   "c_identf": self.identf, "c_onesf": self.onesf}[nm]
            src = self.dram[nm]
            p.emit("sync", (lambda d, s: lambda e: e.dma_start(out=d[:], in_=s))(dst, src),
                   writes=[self.b_const], dma=True)
        self.bnc = self.dint("bnc_scr", [128, 1024], F32)
        self.epsln = self.sb("epsln", [128, 1], F32)
        self.p.emit("vector", lambda e: e.memset(self.epsln[:], 1e-5), writes=[self.b_const])
        self.dummy_f = self.P0
        self.dummy_f2 = self.P0
        self.wstg = self.sb("wstg", [128, 2, 1024], F32)
        self.b_wstg = [Buf(), Buf()]
        self.w1b = self.sb("w1b", [128, 2, 1024], BF16)
        self.b_w1 = [Buf(), Buf()]
        self.gwork = self.sb("gwork", [128, 48], F32)
        self.b_gw = Buf()
        self.lnsm = self.sb("lnsm", [128, 8], F32)
        self.b_lnsm = Buf()
        self.xtile = self.wstg
        self.b_xt = self.b_wstg
        self.xntile = self.sb("xntile", [128, 2, 1024], BF16)
        self.b_xn = [Buf(), Buf()]
        self.eps9 = self.sb("eps9", [128, 1], F32)
        self.p.emit("vector", lambda e: e.memset(self.eps9[:], 1e-9), writes=[self.b_const])
        self.pcstg = self.wstg
        self.b_pcstg = self.b_wstg
        self.pcb = self.pool2[:, 24576:25600]
        self.b_pcb = Buf()
        self._pc_q = None
        self._pc_drained = False

    def init_precast(self, w):
        if getattr(self, "_precast_init", False):
            return
        self._precast_init = True
        self.W1C = self.dint("W1C", [NE + NSH, 16, 128, 1024], BF16)
        self.W2C = self.dint("W2C", [NE + NSH, 16, 128, 1024], BF16)
        self.b_wc = [Buf() for _ in range(NE + NSH)]
        q = []
        for ei in range(NE + NSH):
            shared = ei >= NE
            w1src = w["sw1"][ei - NE] if shared else w["w1"][ei]
            w2src = w["sw2"][ei - NE] if shared else w["w2"][ei]
            for hs in range(16):
                q.append((ei, hs, 0, w1src))
                q.append((ei, hs, 1, w2src))
        self._pc_q = q
        self._pc_i = 0

    def pump_precast(self, n):
        if self._pc_q is None:
            return
        p = self.p
        for _ in range(n):
            if self._pc_i >= len(self._pc_q):
                self._pc_q = None
                return
            ei, hs, kind, wsrc = self._pc_q[self._pc_i]
            self._pc_i += 1
            sl = self._pc_i % 2
            if kind == 0:
                p.emit("sync", (lambda hs_=hs, sl_=sl, ws=wsrc: lambda e: e.dma_start(
                    out=self.pcstg[:, sl_, :].rearrange("p (c n) -> p c n", c=8),
                    in_=ws[:, hs_ * 128:(hs_ + 1) * 128].rearrange("(c p) n -> p c n", p=128)))(),
                    writes=[self.b_pcstg[sl]], dma=True)
            else:
                p.emit("sync", (lambda hs_=hs, sl_=sl, ws=wsrc: lambda e: e.dma_start(
                    out=self.pcstg[:, sl_, :],
                    in_=ws[hs_ * 128:(hs_ + 1) * 128, :]))(),
                    writes=[self.b_pcstg[sl]], dma=True)
            p.emit("gpsimd", (lambda sl_=sl: lambda e: e.tensor_copy(
                self.pcb[:], self.pcstg[:, sl_, :]))(),
                reads=[self.b_pcstg[sl]], writes=[self.b_pcb])
            dstT = self.W1C if kind == 0 else self.W2C
            p.emit("scalar", (lambda ei_=ei, hs_=hs, d_=dstT: lambda e: e.dma_start(
                out=d_[ei_][hs_], in_=self.pcb[:]))(),
                reads=[self.b_pcb], writes=[self.b_wc[ei]], dma=True)

    def flush_precast(self):
        while self._pc_q is not None:
            self.pump_precast(8)
        self.drain_precast()

    def drain_precast(self):
        if self._pc_drained or self._pc_q is not None or not getattr(self, "_precast_init", False):
            return
        self._pc_drained = True
        self.p.emit("scalar", lambda e: e.copy(self.pcb[0:1, 0:8], self.pcb[0:1, 0:8]),
                    reads=[self.b_pcb], writes=[self.b_pcb])


    def u(self):
        self.uid += 1
        return self.uid

    def ln_tile(self, xt, bxt, out_ap, bout, smalls, bsm, dummy=None, bdummy=None):
        p = self.p
        if dummy is None:
            dummy, bdummy = self.P0, self.bP0
        ss, s_, m, a, var, rs = (smalls[:, i:i + 1] for i in range(6))
        p.emit("vector", (lambda d_=dummy: lambda e: e.scalar_tensor_tensor(
            d_[:], xt, 1.0, xt, ALU.mult, ALU.mult, accum_out=ss))(),
            reads=[bxt], writes=[bdummy, bsm])
        p.emit("vector", (lambda d_=dummy: lambda e: e.tensor_scalar(
            d_[:], xt, 0.0, None, ALU.add, ALU.add, accum_out=s_))(),
            reads=[bxt], writes=[bdummy, bsm])
        p.emit("vector", lambda e: e.tensor_scalar(m, s_, 1.0 / D, None, ALU.mult),
               reads=[bsm], writes=[bsm])
        p.emit("vector", lambda e: e.tensor_scalar(a, ss, 1.0 / D, None, ALU.mult),
               reads=[bsm], writes=[bsm])
        p.emit("vector", lambda e: e.scalar_tensor_tensor(
            var, m, 1.0, m, ALU.mult, ALU.mult), reads=[bsm], writes=[bsm])
        p.emit("vector", lambda e: e.tensor_sub(var, a, var), reads=[bsm], writes=[bsm])
        p.emit("scalar", lambda e: e.activation(a, var, AF.Ln, bias=self.epsln[:]),
               reads=[bsm], writes=[bsm])
        p.emit("scalar", lambda e: e.activation(rs, a, AF.Exp, scale=-0.5),
               reads=[bsm], writes=[bsm])
        p.emit("vector", lambda e: e.tensor_scalar(
            out_ap, xt, m, rs, ALU.subtract, ALU.mult),
            reads=[bxt, bsm], writes=[bout])

    def transpose8(self, src_ap, bsrc, dst_view, bdst):
        p = self.p
        for c in range(8):
            p.emit("tensor", (lambda c_: lambda e: e.transpose(
                self.PT[:, c_ * 128:(c_ + 1) * 128], src_ap[:, c_ * 128:(c_ + 1) * 128],
                self.ident[:]))(c),
                reads=[bsrc, self.b_const], writes=[self.bPT])
        p.emit("vector", lambda e: e.tensor_copy(
            dst_view, self.PT[:].rearrange("p (c n) -> p c n", c=8)),
            reads=[self.bPT], writes=[bdst])

    def stage_ffn(self, src, bsrc, dst, bdst, w):
        p = self.p
        nc = self.nc
        self.init_precast(w)
        self.flush_precast()
        p3 = self.pool3[:].bitcast(BF16)
        out_acc = self.pool3[:, 0:8192].rearrange("p (j n) -> p j n", j=8)
        b_acc = self.b_pool3
        xnT = p3[:, 16384:24576].rearrange("p (c n) -> p c n", c=8)
        b_xnT = Buf()
        hT = self.pool2[:, 0:32768].rearrange("p (r n) -> p r n", r=2)
        hTv = [hT[:, 0, :].rearrange("p (h n) -> p h n", h=16),
               hT[:, 1, :].rearrange("p (h n) -> p h n", h=16)]
        b_hT = [Buf(), Buf()]
        w2b = self.pool1[:, 0:32768].rearrange("p (r n) -> p r n", r=2)
        w2v = [w2b[:, 0, :].rearrange("p (h n) -> p h n", h=16),
               w2b[:, 1, :].rearrange("p (h n) -> p h n", h=16)]
        b_w2 = [Buf(), Buf()]
        if not hasattr(self, "ffn_bufs"):
            self.rwb = self.sb("rwb", [128, 8, 8], BF16)
            self.rbrow = self.sb("rbrow", [1, 8], F32)
            self.bext = self.sb("bext", [16, 1024], BF16)
            self.b1col = self.sb("b1col", [128, 2, 16], F32)
            self.b1stg = self.sb("b1stg", [16, 2, 128], F32)
            self.combf = self.sb("combf", [128, 8, 8], F32)
            self.combx = self.sb("combx", [128, 16], BF16)
            self.combxT = self.sb("combxT", [16, 2, 128], BF16)
            self.ffn_bufs = True
            self.b_rw = Buf()
            self.b_bext = Buf()
            self.b_b1 = [Buf(), Buf()]
            self.b_comb = Buf()
            p.emit("vector", lambda e: e.memset(self.combx[:], 0.0), writes=[self.b_comb])
            p.emit("vector", lambda e: e.memset(self.combx[:, 8:10], 0.5), writes=[self.b_comb])
        p.emit("sync", lambda e: e.dma_start(
            out=self.wstg[:, 0, 0:64].rearrange("p (c e) -> p c e", c=8),
            in_=w["router_w"].rearrange("(c p) e -> p c e", p=128)),
               writes=[self.b_wstg[0]], dma=True)
        p.emit("vector", lambda e: e.tensor_copy(
            self.rwb[:].rearrange("p a b -> p (a b)"), self.wstg[:, 0, 0:64]),
               reads=[self.b_wstg[0]], writes=[self.b_rw])
        p.emit("sync", lambda e: e.dma_start(out=self.rbrow[:], in_=w["router_b"].unsqueeze(0)),
               writes=[self.b_rw], dma=True)
        p.emit("vector", lambda e: e.memset(self.bext[:], 0.0), writes=[self.b_bext])
        p.emit("sync", lambda e: e.dma_start(out=self.wstg[0:8, 1, :], in_=w["b2"]),
               writes=[self.b_wstg[1]], dma=True)
        p.emit("sync", lambda e: e.dma_start(out=self.wstg[8:10, 1, :], in_=w["sb2"]),
               writes=[self.b_wstg[1]], dma=True)
        p.emit("vector", lambda e: e.tensor_copy(self.bext[0:10, :], self.wstg[0:10, 1, :]),
               reads=[self.b_wstg[1]], writes=[self.b_bext])

        outs = []
        for qt in range(4):
            r0 = qt * 1024
            for j in range(8):
                xt = self.xtile[:, j % 2, :]
                bxt = self.b_xt[j % 2]
                p.emit("sync", (lambda j_=j, xt_=xt, r0_=r0: lambda e: e.dma_start(
                    out=xt_, in_=src[r0_ + j_ * 128: r0_ + (j_ + 1) * 128, :]))(),
                    reads=[bsrc], writes=[bxt], dma=True)
                xn = self.xntile[:, j % 2, :]
                bxn = self.b_xn[j % 2]
                self.ln_tile(xt, bxt, xn, bxn, self.lnsm, self.b_lnsm)
                self.transpose8(xn, bxn, xnT[:, :, j * 128:(j + 1) * 128], b_xnT)
                for c in range(8):
                    p.emit("tensor", (lambda c_=c, j_=j: lambda e: e.matmul(
                        self.PS[:, 0:8], xnT[:, c_, j_ * 128:(j_ + 1) * 128],
                        self.rwb[:, c_, :], start=(c_ == 0), stop=False))(),
                        reads=[b_xnT, self.b_rw], writes=[self.bPS])
                p.emit("tensor", lambda e: e.matmul(
                    self.PS[:, 0:8], self.ones_row_f[:], self.rbrow[:],
                    start=False, stop=True),
                    reads=[self.b_rw, self.b_const], writes=[self.bPS])
                g = self.gwork
                bg = self.b_gw
                gs_, m1, e1, g2, m2, e2, d12, p1, p2, t1 = (
                    g[:, 0:8], g[:, 8:9], g[:, 9:17], g[:, 17:25], g[:, 25:26],
                    g[:, 26:34], g[:, 34:35], g[:, 35:36], g[:, 36:37], g[:, 37:45])
                comb = self.combf[:, j, :]
                p.emit("vector", lambda e: e.tensor_copy(gs_, self.PS[:, 0:8]),
                       reads=[self.bPS], writes=[bg])
                p.emit("vector", lambda e: e.reduce_max(m1, gs_, AX.X), reads=[bg], writes=[bg])
                p.emit("vector", lambda e: e.tensor_scalar(e1, gs_, m1, None, ALU.is_ge),
                       reads=[bg], writes=[bg])
                p.emit("vector", lambda e: e.scalar_tensor_tensor(
                    g2, e1, -1e30, gs_, ALU.mult, ALU.add), reads=[bg], writes=[bg])
                p.emit("vector", lambda e: e.reduce_max(m2, g2, AX.X), reads=[bg], writes=[bg])
                p.emit("vector", lambda e: e.tensor_scalar(e2, g2, m2, None, ALU.is_ge),
                       reads=[bg], writes=[bg])
                p.emit("vector", lambda e: e.tensor_tensor(d12, m2, m1, ALU.subtract),
                       reads=[bg], writes=[bg])
                p.emit("scalar", lambda e: e.activation(p2, d12, AF.Exp),
                       reads=[bg], writes=[bg])
                p.emit("vector", lambda e: e.tensor_scalar(p1, p2, 1.0, None, ALU.add),
                       reads=[bg], writes=[bg])
                p.emit("vector", lambda e: e.reciprocal(p1, p1), reads=[bg], writes=[bg])
                p.emit("vector", lambda e: e.tensor_tensor(p2, p2, p1, ALU.mult),
                       reads=[bg], writes=[bg])
                p.emit("vector", (lambda e1_=e1, p1_=p1, t1_=t1: lambda e: e.tensor_scalar(
                    t1_, e1_, p1_, None, ALU.mult))(), reads=[bg], writes=[bg])
                p.emit("vector", (lambda comb_=comb: lambda e: e.scalar_tensor_tensor(
                    comb_, e2, p2, t1, ALU.mult, ALU.add))(),
                    reads=[bg], writes=[self.b_comb])
                p.emit("vector", (lambda comb_=comb: lambda e: e.tensor_copy(
                    self.combx[:, 0:8], comb_))(), reads=[self.b_comb], writes=[self.b_comb])
                p.emit("tensor", lambda e: e.transpose(
                    self.PT[0:16, 0:128], self.combx[:], self.ident[:]),
                    reads=[self.b_comb, self.b_const], writes=[self.bPT])
                p.emit("vector", (lambda j_=j: lambda e: e.tensor_copy(
                    self.combxT[:, j_ % 2, :], self.PT[0:16, 0:128]))(),
                    reads=[self.bPT], writes=[self.b_comb])
                for mv in range(2):
                    p.emit("tensor", (lambda j_=j, mv_=mv: lambda e: e.matmul(
                        self.P2[:, mv_ * 512:(mv_ + 1) * 512], self.combxT[:, j_ % 2, :],
                        self.bext[:, mv_ * 512:(mv_ + 1) * 512],
                        start=True, stop=True))(),
                        reads=[self.b_comb, self.b_bext], writes=[self.bP2])
                p.emit("vector", (lambda j_=j, xt_=xt: lambda e: e.scalar_tensor_tensor(
                    out_acc[:, j_, :], self.P2[:], 1.0, xt_, ALU.mult, ALU.add))(),
                    reads=[self.bP2, bxt], writes=[b_acc])
            for ei in range(NE + NSH):
                par = ei % 2
                shared = ei >= NE
                w1src = w["sw1"][ei - NE] if shared else w["w1"][ei]
                w2src = w["sw2"][ei - NE] if shared else w["w2"][ei]
                b1src = w["sb1"][ei - NE] if shared else w["b1"][ei]
                p.emit("sync", (lambda b1s=b1src, par_=par: lambda e: e.dma_start(
                    out=self.b1stg[:, par_, :], in_=b1s.rearrange("(s p) -> s p", p=128)))(),
                    writes=[self.b_b1[par]], dma=True)
                ptf = self.PT[:].bitcast(F32)
                p.emit("tensor", (lambda par_=par: lambda e: e.transpose(
                    ptf[:, 0:16], self.b1stg[:, par_, :], self.identf[0:16, 0:16]))(),
                    reads=[self.b_b1[par], self.b_const], writes=[self.bPT])
                p.emit("vector", (lambda par_=par: lambda e: e.tensor_copy(
                    self.b1col[:, par_, :], ptf[:, 0:16]))(),
                    reads=[self.bPT], writes=[self.b_b1[par]])
                for hs in range(16):
                    sl = hs % 2
                    p.emit("sync",
                           (lambda hs_=hs, sl_=sl, ei_=ei: lambda e: e.dma_start(
                               out=self.w1b[:, sl_, :], in_=self.W1C[ei_][hs_]))(),
                           reads=[self.b_wc[ei]], writes=[self.b_w1[sl]], dma=True)
                    pp = [self.P0, self.P1][hs % 2]
                    bpp = [self.bP0, self.bP1][hs % 2]
                    for mv in range(2):
                        for c in range(8):
                            p.emit("tensor", (lambda sl_=sl, c_=c, mv_=mv, pp_=pp: lambda e: e.matmul(
                                pp_[:, mv_ * 512:(mv_ + 1) * 512],
                                self.w1b[:, sl_, c_ * 128:(c_ + 1) * 128],
                                xnT[:, c_, mv_ * 512:(mv_ + 1) * 512],
                                start=(c_ == 0), stop=(c_ == 7)))(),
                                reads=[self.b_w1[sl], b_xnT], writes=[bpp])
                    p.emit("scalar", (lambda hs_=hs, pp_=pp, par_=par: lambda e: e.activation(
                        hTv[par_][:, hs_, :], pp_[:],
                        AF.Gelu, bias=self.b1col[:, par_, hs_:hs_ + 1]))(),
                        reads=[bpp, self.b_b1[par]], writes=[b_hT[par]])
                    hc = hs
                    p.emit("scalar",
                           (lambda hc_=hc, ei_=ei, par_=par: lambda e: e.dma_start(
                               out=w2v[par_][:, hc_, :], in_=self.W2C[ei_][hc_]))(),
                           reads=[self.b_wc[ei]], writes=[b_w2[par]], dma=True)
                for j in range(8):
                    yp = [self.P0, self.P1, self.P2][j % 3]
                    byp = [self.bP0, self.bP1, self.bP2][j % 3]
                    for mv in range(2):
                        for hc in range(16):
                            p.emit("tensor", (lambda j_=j, hc_=hc, mv_=mv, par_=par, yp_=yp: lambda e: e.matmul(
                                yp_[:, mv_ * 512:(mv_ + 1) * 512],
                                hTv[par_][:, hc_, j_ * 128:(j_ + 1) * 128],
                                w2v[par_][:, hc_, mv_ * 512:(mv_ + 1) * 512],
                                start=(hc_ == 0), stop=(hc_ == 15)))(),
                                reads=[b_hT[par], b_w2[par]], writes=[byp])
                    if shared:
                        p.emit("vector", (lambda j_=j, yp_=yp: lambda e: e.scalar_tensor_tensor(
                            out_acc[:, j_, :], yp_[:], 0.5, out_acc[:, j_, :],
                            ALU.mult, ALU.add))(),
                            reads=[byp, b_acc], writes=[b_acc])
                    else:
                        p.emit("vector", (lambda j_=j, ei_=ei, yp_=yp: lambda e: e.scalar_tensor_tensor(
                            out_acc[:, j_, :], yp_[:], self.combf[:, j_, ei_:ei_ + 1],
                            out_acc[:, j_, :], ALU.mult, ALU.add))(),
                            reads=[byp, b_acc, self.b_comb], writes=[b_acc])
            for j in range(8):
                outs.append(p.emit("sync", (lambda j_=j, r0_=r0: lambda e: e.dma_start(
                    out=dst[r0_ + j_ * 128: r0_ + (j_ + 1) * 128, :], in_=out_acc[:, j_, :]))(),
                    reads=[b_acc], writes=[bdst], dma=True))
        return outs

    def stage_attn_a(self, src, bsrc, dst, bdst, w):
        p = self.p
        NSEL_PAD = 512
        xnT = self.pool1[:, 0:32768].rearrange("p (c n) -> p c n", c=8)
        b_xnT = self.b_pool1
        p2b = self.pool2
        qTaug = [p2b[:, 0:4096], p2b[:, 4096:8192]]
        b_qTa = [Buf(), Buf()]
        kTaug = [p2b[:, 8192:12288], p2b[:, 12288:16384]]
        b_kTa = [Buf(), Buf()]
        kv_s = p2b[:, 16384:20480].rearrange("p (j n) -> p j n", j=32)
        b_kv = Buf()
        ering = p2b[:, 20480:24576].rearrange("p (r n) -> p r n", r=4)
        b_er = [Buf(), Buf(), Buf(), Buf()]
        p3 = self.pool3[:].bitcast(BF16)
        wsl = p3[:, 0:1024].rearrange("p (r n) -> p r n", r=2)
        b_wsl = [Buf(), Buf()]
        kcT = [p3[0:64, 1024 + i * 512: 1024 + (i + 1) * 512] for i in range(16)]
        b_kcT = [Buf() for _ in range(16)]
        vcn = p3[:, 9216:13376].rearrange("p (h n) -> p h n", h=16)
        b_vcn = Buf()
        rgcol = self.pool3[:, 6688:6752].rearrange("p (h n) -> p h n", h=16)
        b_rg = Buf()
        st = self.pool3[:, 6752:7008].rearrange("p (k n) -> p k n", k=8)
        eacc, macc, iacc, lnzt, ut, uhit, ulot, n2t = (st[:, i, :] for i in range(8))
        b_st = Buf()
        b_lnz = Buf()
        b_u = Buf()
        zp = self.gwork
        p.emit("vector", lambda e: e.memset(eacc, 0.0), writes=[b_st])
        p.emit("vector", lambda e: e.memset(macc, 0.0), writes=[b_st])
        p.emit("vector", lambda e: e.memset(iacc, 0.0), writes=[b_st])
        for sl in range(2):
            p.emit("vector", (lambda sl_=sl: lambda e: e.memset(kTaug[sl_][64:67, :], 0.0))(),
                   writes=[b_kTa[sl]])
            p.emit("vector", (lambda sl_=sl: lambda e: e.tensor_scalar(
                kTaug[sl_][64:67, :], kTaug[sl_][64:67, :], 1.0, None, ALU.add))(),
                reads=[b_kTa[sl]], writes=[b_kTa[sl]])
        p.emit("vector", lambda e: e.memset(
            vcn[:].rearrange("p h n -> p (h n)").rearrange("p (g o) -> p g o", o=65)[:, :, 64:65], 1.0),
            writes=[b_vcn])
        bncb = self.dint("bncb_scr", [96, 128], BF16)
        b_bncb = Buf()
        staKC = self.sb("staKC", [128, 132], BF16)
        staV = self.sb("staV", [128, 64], BF16)
        b_staK, b_staV = Buf(), Buf()
        p.emit("vector", lambda e: e.memset(staKC[:, 65:129], 0.0), writes=[b_staK])

        def load_wslice(name, h, slot, bslot):
            p.emit("sync", (lambda h_=h: lambda e: e.dma_start(
                out=self.wstg[:, 0, 0:512].rearrange("p (c n) -> p c n", c=8),
                in_=w[name][:, h_ * 64:(h_ + 1) * 64].rearrange("(c p) n -> p c n", p=128)))(),
                writes=[self.b_wstg[0]], dma=True)
            p.emit("vector", (lambda slot_=slot: lambda e: e.tensor_copy(
                slot_, self.wstg[:, 0, 0:512]))(),
                reads=[self.b_wstg[0]], writes=[bslot])

        for j in range(NT):
            xt = self.xtile[:, j % 2, :]
            bxt = self.b_xt[j % 2]
            p.emit("sync", (lambda j_=j, xt_=xt: lambda e: e.dma_start(
                out=xt_, in_=src[j_ * 128:(j_ + 1) * 128, :]))(),
                reads=[bsrc], writes=[bxt], dma=True)
            xn = self.xntile[:, j % 2, :]
            bxn = self.b_xn[j % 2]
            self.ln_tile(xt, bxt, xn, bxn, self.lnsm, self.b_lnsm)
            self.transpose8(xn, bxn, xnT[:, :, j * 128:(j + 1) * 128], b_xnT)

        for h in range(NH):
            sl = h % 2
            qT, bqT = qTaug[sl], b_qTa[sl]
            kT, bkT = kTaug[sl], b_kTa[sl]
            load_wslice("Wq", h, wsl[:, 0, :], b_wsl[0])
            wq8 = wsl[:, 0, :].rearrange("p (c n) -> p c n", c=8)
            for tb8 in range(8):
                ppj = [self.P0, self.P1][tb8 % 2]
                bppj = [self.bP0, self.bP1][tb8 % 2]
                for c in range(8):
                    p.emit("tensor", (lambda tb_=tb8, c_=c, ppj_=ppj: lambda e: e.matmul(
                        ppj_[0:64, 0:512], wq8[:, c_, :],
                        xnT[:, c_, tb_ * 512:(tb_ + 1) * 512],
                        start=(c_ == 0), stop=(c_ == 7)))(),
                        reads=[b_wsl[0], b_xnT], writes=[bppj])
                p.emit("scalar", (lambda tb_=tb8, qT_=qT, ppj_=ppj: lambda e: e.activation(
                    qT_[0:64, tb_ * 512:(tb_ + 1) * 512], ppj_[0:64, 0:512],
                    AF.Copy, scale=0.125))(),
                    reads=[bppj], writes=[bqT])
                if tb8 % 2 == 1:
                    self.pump_precast(2)
            kvw = self.w1b[:, 0, :].rearrange("p (c n) -> p c n", c=8)
            p.emit("sync", (lambda h_=h: lambda e: e.dma_start(
                out=self.wstg[:, 0, 0:512].rearrange("p (c n) -> p c n", c=8),
                in_=w["Wk"][:, h_ * 64:(h_ + 1) * 64].rearrange("(c p) n -> p c n", p=128)))(),
                writes=[self.b_wstg[0]], dma=True)
            p.emit("sync", (lambda h_=h: lambda e: e.dma_start(
                out=self.wstg[:, 1, 0:512].rearrange("p (c n) -> p c n", c=8),
                in_=w["Wv"][:, h_ * 64:(h_ + 1) * 64].rearrange("(c p) n -> p c n", p=128)))(),
                writes=[self.b_wstg[1]], dma=True)
            p.emit("vector", lambda e: e.tensor_copy(
                kvw[:, :, 0:64], self.wstg[:, 0, 0:512].rearrange("p (c n) -> p c n", c=8)),
                reads=[self.b_wstg[0]], writes=[self.b_w1[0]])
            p.emit("vector", lambda e: e.tensor_copy(
                kvw[:, :, 64:128], self.wstg[:, 1, 0:512].rearrange("p (c n) -> p c n", c=8)),
                reads=[self.b_wstg[1]], writes=[self.b_w1[0]])
            for tb8 in range(8):
                ppj = [self.P0, self.P1][tb8 % 2]
                bppj = [self.bP0, self.bP1][tb8 % 2]
                for c in range(8):
                    p.emit("tensor", (lambda tb_=tb8, c_=c, ppj_=ppj: lambda e: e.matmul(
                        ppj_[0:64, 0:512], kvw[:, c_, 0:64],
                        xnT[:, c_, tb_ * 512:(tb_ + 1) * 512],
                        start=(c_ == 0), stop=(c_ == 7)))(),
                        reads=[self.b_w1[0], b_xnT], writes=[bppj])
                p.emit("scalar", (lambda tb_=tb8, kT_=kT, ppj_=ppj: lambda e: e.activation(
                    kT_[0:64, tb_ * 512:(tb_ + 1) * 512], ppj_[0:64, 0:512], AF.Copy))(),
                    reads=[bppj], writes=[bkT])
                if tb8 % 2 == 1:
                    self.pump_precast(2)
            for j in range(NT):
                off = (j % 2) * 128
                for c in range(8):
                    p.emit("tensor", (lambda j_=j, c_=c, off_=off: lambda e: e.matmul(
                        self.P2[:, off_:off_ + 128], xnT[:, c_, j_ * 128:(j_ + 1) * 128],
                        kvw[:, c_, :], start=(c_ == 0), stop=(c_ == 7)))(),
                        reads=[self.b_w1[0], b_xnT], writes=[self.bP2])
                p.emit("vector", (lambda j_=j, off_=off: lambda e: e.tensor_copy(
                    kv_s[:, j_, :], self.P2[:, off_:off_ + 128]))(),
                    reads=[self.bP2], writes=[b_kv])
                if j % 4 == 3:
                    self.pump_precast(1)
            zarr = self.pool3[:, 7040:7168]
            zarr = self.pool3[:, 8096:8224]
            warr = self.pool3[:, 8224:8352]
            marr = self.pool3[:, 8352:8480]
            b_zarr, b_warr, b_marr = Buf(), Buf(), Buf()
            for j in range(NT):
                for cb in range(4):
                    pp = [self.P0, self.P1][cb % 2]
                    bpp = [self.bP0, self.bP1][cb % 2]
                    for mv in range(2):
                        p.emit("tensor", (lambda j_=j, cb_=cb, mv_=mv, pp_=pp, qT_=qT, kT_=kT: lambda e: e.matmul(
                            pp_[:, mv_ * 512:(mv_ + 1) * 512],
                            qT_[0:64, j_ * 128:(j_ + 1) * 128],
                            kT_[0:64, (cb_ * 2 + mv_) * 512:(cb_ * 2 + mv_ + 1) * 512],
                            start=True, stop=True))(),
                            reads=[bqT, bkT], writes=[bpp])
                    er = ering[:, cb, :]
                    ber = b_er[cb]
                    p.emit("scalar", (lambda j_=j, cb_=cb, pp_=pp, er_=er: lambda e: e.activation(
                        er_, pp_[:], AF.Exp, accum_out=zarr[:, j_ * 4 + cb_:j_ * 4 + cb_ + 1]))(),
                        reads=[bpp], writes=[ber, b_zarr])
                    p.emit("vector", (lambda j_=j, cb_=cb, er_=er: lambda e: e.tensor_scalar(
                        er_, er_, 0.0, None, ALU.add, ALU.max,
                        accum_out=marr[:, j_ * 4 + cb_:j_ * 4 + cb_ + 1]))(),
                        reads=[ber], writes=[ber, b_marr])
                    p.emit("vector", (lambda j_=j, cb_=cb, pp_=pp, er_=er: lambda e: e.scalar_tensor_tensor(
                        er_, er_, 1.0, pp_[:], ALU.mult, ALU.mult,
                        accum_out=warr[:, j_ * 4 + cb_:j_ * 4 + cb_ + 1]))(),
                        reads=[ber, bpp], writes=[ber, b_warr])
            zt = self.pool3[:, 8480:8512]
            wt = self.pool3[:, 8512:8544]
            rzt = self.pool3[:, 8544:8576]
            b_zc = Buf()
            p.emit("vector", lambda e: e.tensor_reduce(
                zt, zarr.rearrange("p (j c) -> p j c", c=4), AX.X, ALU.add),
                reads=[b_zarr], writes=[b_zc])
            p.emit("vector", lambda e: e.tensor_reduce(
                wt, warr.rearrange("p (j c) -> p j c", c=4), AX.X, ALU.add),
                reads=[b_warr], writes=[b_zc])
            p.emit("vector", lambda e: e.tensor_reduce(
                rzt, marr.rearrange("p (j c) -> p j c", c=4), AX.X, ALU.max),
                reads=[b_marr], writes=[b_zc])
            p.emit("scalar", lambda e: e.activation(lnzt, zt, AF.Ln),
                   reads=[b_zc], writes=[b_lnz])
            p.emit("vector", lambda e: e.reciprocal(zt, zt), reads=[b_zc], writes=[b_zc])
            p.emit("vector", lambda e: e.tensor_tensor(rzt, rzt, zt, ALU.mult),
                   reads=[b_zc], writes=[b_zc])
            p.emit("vector", lambda e: e.tensor_add(macc, macc, rzt), reads=[b_zc], writes=[b_st])
            p.emit("vector", lambda e: e.tensor_tensor(wt, wt, zt, ALU.mult),
                   reads=[b_zc], writes=[b_zc])
            p.emit("vector", lambda e: e.tensor_sub(wt, lnzt, wt), reads=[b_zc, b_lnz], writes=[b_zc])
            p.emit("vector", lambda e: e.tensor_add(eacc, eacc, wt), reads=[b_zc], writes=[b_st])
            g = zp
            spl = ering[:, 0, 0:96].rearrange("p (v n) -> p v n", v=3)
            p.emit("vector", lambda e: e.tensor_scalar(g[:, 20:52] if False else lnzt, lnzt, -1.0, None, ALU.mult),
                   reads=[b_lnz], writes=[b_lnz])
            p.emit("vector", lambda e: e.tensor_copy(spl[:, 0, :], lnzt),
                   reads=[b_lnz], writes=[b_er[0]])
            tmp32 = self.pool3[:, 7008:7040]
            b_t32 = Buf()
            p.emit("vector", lambda e: e.tensor_sub(tmp32, lnzt, spl[:, 0, :]),
                   reads=[b_lnz, b_er[0]], writes=[b_t32])
            p.emit("vector", lambda e: e.tensor_copy(spl[:, 1, :], tmp32),
                   reads=[b_t32], writes=[b_er[0]])
            p.emit("vector", lambda e: e.tensor_sub(tmp32, tmp32, spl[:, 1, :]),
                   reads=[b_t32, b_er[0]], writes=[b_t32])
            p.emit("vector", lambda e: e.tensor_copy(spl[:, 2, :], tmp32),
                   reads=[b_t32], writes=[b_er[0]])
            p.emit("tensor", lambda e: e.transpose(
                self.PT[0:96, 0:128], spl.rearrange("p v n -> p (v n)"), self.ident[:]),
                reads=[b_er[0], self.b_const], writes=[self.bPT])
            p.emit("vector", lambda e: e.tensor_copy(ering[0:96, 1, 0:128], self.PT[0:96, 0:128]),
                   reads=[self.bPT], writes=[b_er[1]])
            p.emit("sync", lambda e: e.dma_start(out=self.dram["bncb_scr"], in_=ering[0:96, 1, 0:128]),
                   reads=[b_er[1]], writes=[b_bncb], dma=True)
            for v3 in range(3):
                p.emit("sync", (lambda v_=v3, qT_=qT: lambda e: e.dma_start(
                    out=qT_[64 + v_:65 + v_, :],
                    in_=self.dram["bncb_scr"].rearrange("a b -> (a b)").unsqueeze(0)[:, v_ * 4096:(v_ + 1) * 4096]))(),
                    reads=[b_bncb], writes=[bqT], dma=True)
            iar = self.pool3[:, 8576:8704]
            b_iar = Buf()
            swdum = p3[:, 23744:25792].rearrange("p (r n) -> p r n", r=2)
            b_swd = [Buf(), Buf()]
            for c32 in range(NT):
                for qb2 in range(4):
                    pp = [self.P0, self.P1][qb2 % 2]
                    bpp = [self.bP0, self.bP1][qb2 % 2]
                    for mv in range(2):
                        p.emit("tensor", (lambda c_=c32, q_=qb2, mv_=mv, pp_=pp, qT_=qT, kT_=kT: lambda e: e.matmul(
                            pp_[:, mv_ * 512:(mv_ + 1) * 512],
                            kT_[0:67, c_ * 128:(c_ + 1) * 128],
                            qT_[0:67, (q_ * 2 + mv_) * 512:(q_ * 2 + mv_ + 1) * 512],
                            start=True, stop=True))(),
                            reads=[bqT, bkT], writes=[bpp])
                    p.emit("scalar", (lambda c_=c32, q_=qb2, pp_=pp: lambda e: e.activation(
                        swdum[:, q_ % 2, :], pp_[:], AF.Exp,
                        accum_out=iar[:, c_ * 4 + q_:c_ * 4 + q_ + 1]))(),
                        reads=[bpp], writes=[b_swd[qb2 % 2], b_iar])
            p.emit("vector", lambda e: e.tensor_reduce(
                self.pool3[:, 8704:8736], iar.rearrange("p (j c) -> p j c", c=4), AX.X, ALU.add),
                reads=[b_iar], writes=[b_iar])
            p.emit("vector", lambda e: e.tensor_add(iacc, iacc, self.pool3[:, 8704:8736]),
                   reads=[b_iar], writes=[b_st])
            p.emit("vector", lambda e: e.tensor_tensor(
                ering[:, 0:2, :].rearrange("p r n -> p (r n)").rearrange("p (j n) -> p j n", n=64),
                kv_s[:][:, :, 0:64], kv_s[:][:, :, 0:64], ALU.mult),
                reads=[b_kv], writes=[b_er[0], b_er[1]])
            p.emit("vector", lambda e: e.tensor_reduce(
                n2t, ering[:, 0:2, :].rearrange("p r n -> p (r n)").rearrange("p (j n) -> p j n", n=64),
                AX.X, ALU.add),
                reads=[b_er[0], b_er[1]], writes=[b_st])
            p.emit("scalar", lambda e: e.activation(ut, n2t, AF.Ln), reads=[b_st], writes=[b_u])
            p.emit("scalar", lambda e: e.activation(ut, ut, AF.Exp, scale=0.5), reads=[b_u], writes=[b_u])
            p.emit("vector", lambda e: e.reduce_max(zp[:, 30:31], ut, AX.X),
                   reads=[b_u], writes=[self.b_gw])
            p.emit("vector", lambda e: e.tensor_copy(staV[:, 0:1], zp[:, 30:31]),
                   reads=[self.b_gw], writes=[b_staV])
            p.emit("tensor", lambda e: e.transpose(
                self.PT[0:1, 0:128], staV[:, 0:1], self.ident[:]),
                reads=[b_staV, self.b_const], writes=[self.bPT])
            p.emit("vector", lambda e: e.tensor_copy(ering[0:1, 0, 0:128], self.PT[0:1, 0:128]),
                   reads=[self.bPT], writes=[b_er[0]])
            p.emit("vector", lambda e: e.reduce_max(zp[0:1, 31:32], ering[0:1, 0, 0:128], AX.X),
                   reads=[b_er[0]], writes=[self.b_gw])
            p.emit("vector", lambda e: e.tensor_scalar(zp[0:1, 32:33], zp[0:1, 31:32], -1.0, None, ALU.mult),
                   reads=[self.b_gw], writes=[self.b_gw])
            p.emit("tensor", lambda e: e.matmul(
                self.P2[:, 0:1], self.onesf[0:1, :], zp[0:1, 32:33], start=True, stop=True),
                reads=[self.b_gw, self.b_const], writes=[self.bP2])
            p.emit("vector", lambda e: e.tensor_copy(zp[:, 33:34], self.P2[:, 0:1]),
                   reads=[self.bP2], writes=[self.b_gw])
            p.emit("vector", lambda e: e.tensor_scalar(ut, ut, zp[:, 33:34], None, ALU.add),
                   reads=[self.b_gw, b_u], writes=[b_u])
            p.emit("scalar", lambda e: e.activation(ut, ut, AF.Exp, accum_out=zp[:, 34:35]),
                   reads=[b_u], writes=[b_u, self.b_gw])
            p.emit("vector", lambda e: e.tensor_copy(staV[:, 1:2], zp[:, 34:35]),
                   reads=[self.b_gw], writes=[b_staV])
            p.emit("tensor", lambda e: e.matmul(
                self.P2[0:1, 1:2], self.ones_col[:], staV[:, 1:2], start=True, stop=True),
                reads=[b_staV, self.b_const], writes=[self.bP2])
            p.emit("vector", lambda e: e.tensor_copy(zp[0:1, 35:36], self.P2[0:1, 1:2]),
                   reads=[self.bP2], writes=[self.b_gw])
            p.emit("vector", lambda e: e.tensor_scalar(zp[0:1, 35:36], zp[0:1, 35:36], 1e-8, None, ALU.mult),
                   reads=[self.b_gw], writes=[self.b_gw])
            p.emit("tensor", lambda e: e.matmul(
                self.P2[:, 2:3], self.onesf[0:1, :], zp[0:1, 35:36], start=True, stop=True),
                reads=[self.b_gw, self.b_const], writes=[self.bP2])
            p.emit("vector", lambda e: e.tensor_copy(zp[:, 36:37], self.P2[:, 2:3]),
                   reads=[self.bP2], writes=[self.b_gw])
            p.emit("vector", lambda e: e.tensor_copy(uhit, ut), reads=[b_u], writes=[b_u])
            p.emit("vector", lambda e: e.tensor_sub(ulot, ut, uhit), reads=[b_u], writes=[b_u])
            staKall = p3[:, 17472:21696].rearrange("p (j n) -> p j n", j=32)
            vwall = p3[:, 21696:23744].rearrange("p (j n) -> p j n", j=32)
            if h == 0:
                p.emit("vector", lambda e: e.memset(
                    staKall[:].rearrange("p j n -> p (j n)").rearrange("p (j o) -> p j o", o=132)[:, :, 65:129], 0.0),
                    writes=[b_staK])
            p.emit("vector", lambda e: e.tensor_tensor(
                staKall[:, :, 0:64],
                kv_s[:].rearrange("p j n -> p j n")[:, :, 0:64],
                ut.unsqueeze(2).broadcast_to([128, 32, 64]), ALU.mult),
                reads=[b_kv, b_u], writes=[b_staK])
            p.emit("vector", lambda e: e.tensor_copy(staKall[:, :, 64:65], uhit.unsqueeze(2)),
                   reads=[b_u], writes=[b_staK])
            p.emit("vector", lambda e: e.tensor_copy(staKall[:, :, 129:130], ulot.unsqueeze(2)),
                   reads=[b_u], writes=[b_staK])
            p.emit("vector", lambda e: e.tensor_tensor(
                vwall[:], kv_s[:][:, :, 64:128],
                ut.unsqueeze(2).broadcast_to([128, 32, 64]), ALU.mult),
                reads=[b_kv, b_u], writes=[b_staV])
            for j in range(NT):
                p.emit("tensor", (lambda j_=j: lambda e: e.matmul(
                    self.PS[0:65, j_ * 16:(j_ + 1) * 16], staKall[:, j_, 0:65],
                    self.ppool[:], start=True, stop=False))(),
                    reads=[b_staK, self.b_const], writes=[self.bPS])
                p.emit("tensor", (lambda j_=j: lambda e: e.matmul(
                    self.PS[0:65, j_ * 16:(j_ + 1) * 16], staKall[:, j_, 65:130],
                    self.ppool[:], start=False, stop=True))(),
                    reads=[b_staK, self.b_const], writes=[self.bPS])
                p.emit("tensor", (lambda j_=j: lambda e: e.matmul(
                    self.P2[0:64, 512 + j_ * 16: 512 + (j_ + 1) * 16], vwall[:, j_, :],
                    self.ppool[:], start=True, stop=True))(),
                    reads=[b_staV, self.b_const], writes=[self.bP2])
            p.emit("vector", (lambda h_=h: lambda e: e.tensor_copy(
                kcT[h_], self.PS[0:64, :]))(),
                reads=[self.bPS], writes=[b_kcT[h]])
            gsr = self.pool3[:, 7040:7552]
            b_gsr = Buf()
            p.emit("vector", lambda e: e.tensor_copy(gsr[64:65, :], self.PS[64:65, :]),
                   reads=[self.bPS], writes=[b_gsr])
            p.emit("vector", lambda e: e.tensor_scalar(
                gsr[64:65, :], gsr[64:65, :], zp[64:65, 36:37], None, ALU.add),
                reads=[self.b_gw, b_gsr], writes=[b_gsr])
            p.emit("vector", lambda e: e.reciprocal(gsr[64:65, :], gsr[64:65, :]),
                   reads=[b_gsr], writes=[b_gsr])
            p.emit("sync", lambda e: e.dma_start(
                out=self.dram["bnc_scr"][0:1, 0:512], in_=gsr[64:65, :]),
                reads=[b_gsr], writes=[self.dbuf["bnc_scr"]], dma=True)
            p.emit("sync", lambda e: e.dma_start(
                out=gsr[0:4, 0:128],
                in_=self.dram["bnc_scr"][0:1, 0:512].rearrange("o (a b) -> (o a) b", a=4)),
                reads=[self.dbuf["bnc_scr"]], writes=[b_gsr], dma=True)
            p.emit("tensor", lambda e: e.transpose(
                self.PT[:].bitcast(F32)[:, 0:4], gsr[0:4, 0:128], self.identf[0:4, 0:4]),
                reads=[b_gsr, self.b_const], writes=[self.bPT])
            p.emit("vector", (lambda h_=h: lambda e: e.tensor_copy(
                rgcol[:, h_, :], self.PT[:].bitcast(F32)[:, 0:4]))(),
                reads=[self.bPT], writes=[b_rg])
            vct = ering[0:64, 2, 0:512]
            p.emit("vector", lambda e: e.tensor_copy(vct, self.P2[0:64, 512:1024]),
                   reads=[self.bP2], writes=[b_er[2]])
            for cc in range(4):
                p.emit("tensor", (lambda cc_=cc: lambda e: e.transpose(
                    self.PT[:, cc_ * 64:(cc_ + 1) * 64], vct[:, cc_ * 128:(cc_ + 1) * 128],
                    self.ident[0:64, 0:64]))(),
                    reads=[b_er[2], self.b_const], writes=[self.bPT])
                p.emit("vector", (lambda cc_=cc, h_=h: lambda e: e.tensor_scalar(
                    vcn[:, h_, cc_ * 65: cc_ * 65 + 64], self.PT[:, cc_ * 64:(cc_ + 1) * 64],
                    rgcol[:, h_, cc_:cc_ + 1], None, ALU.mult))(),
                    reads=[self.bPT, b_rg], writes=[b_vcn])

        imp = eacc
        p.emit("vector", lambda e: e.tensor_scalar(imp, eacc, -0.4 / 16, None, ALU.mult),
               reads=[b_st], writes=[b_st])
        p.emit("vector", lambda e: e.scalar_tensor_tensor(imp, macc, 0.3 / 16, imp, ALU.mult, ALU.add),
               reads=[b_st], writes=[b_st])
        p.emit("vector", lambda e: e.scalar_tensor_tensor(imp, iacc, 0.3 / 16, imp, ALU.mult, ALU.add),
               reads=[b_st], writes=[b_st])
        g = self.gwork
        sV = self.sb("srch", [128, 8], F32)
        sVb = self.sb("srchb", [128, 2], BF16)
        b_sv = Buf()
        LO, HI, MID, CNT, PRD, T1, T2, MIDC = (sV[:, i:i + 1] for i in range(8))
        p.emit("vector", lambda e: e.reduce_max(g[:, 0:1], imp, AX.X), reads=[b_st], writes=[self.b_gw])
        p.emit("vector", lambda e: e.tensor_reduce(g[:, 1:2], imp, AX.X, ALU.min), reads=[b_st], writes=[self.b_gw])
        p.emit("vector", lambda e: e.tensor_copy(sVb[:, 0:2], g[:, 0:2]), reads=[self.b_gw], writes=[b_sv])
        p.emit("tensor", lambda e: e.transpose(self.PT[0:2, 0:128], sVb[:, 0:2], self.ident[:]),
               reads=[b_sv, self.b_const], writes=[self.bPT])
        p.emit("vector", lambda e: e.tensor_copy(ering[0:2, 0, 0:128], self.PT[0:2, 0:128]),
               reads=[self.bPT], writes=[b_er[0]])
        p.emit("vector", lambda e: e.reduce_max(g[0:1, 2:3], ering[0:1, 0, 0:128], AX.X),
               reads=[b_er[0]], writes=[self.b_gw])
        p.emit("vector", lambda e: e.tensor_copy(sVb[:, 0:1], g[:, 1:2]), reads=[self.b_gw], writes=[b_sv])
        p.emit("tensor", lambda e: e.transpose(self.PT[0:1, 128:256], sVb[:, 0:1], self.ident[:]),
               reads=[b_sv, self.b_const], writes=[self.bPT])
        p.emit("vector", lambda e: e.tensor_copy(ering[0:1, 0, 128:256], self.PT[0:1, 128:256]),
               reads=[self.bPT], writes=[b_er[0]])
        p.emit("vector", lambda e: e.tensor_reduce(g[0:1, 5:6], ering[0:1, 0, 128:256], AX.X, ALU.min),
               reads=[b_er[0]], writes=[self.b_gw])
        p.emit("vector", lambda e: e.tensor_scalar(LO[0:1, :], g[0:1, 5:6], -1.0, None, ALU.add),
               reads=[self.b_gw], writes=[b_sv])
        p.emit("vector", lambda e: e.tensor_scalar(HI[0:1, :], g[0:1, 2:3], 1.0, None, ALU.add),
               reads=[self.b_gw], writes=[b_sv])
        for it in range(40):
            p.emit("vector", lambda e: e.tensor_tensor(MID[0:1, :], LO[0:1, :], HI[0:1, :], ALU.add),
                   reads=[b_sv], writes=[b_sv])
            p.emit("vector", lambda e: e.tensor_scalar(MID[0:1, :], MID[0:1, :], 0.5, None, ALU.mult),
                   reads=[b_sv], writes=[b_sv])
            p.emit("tensor", lambda e: e.matmul(self.P2[:, 3:4], self.onesf[0:1, :], MID[0:1, :],
                                                start=True, stop=True),
                   reads=[b_sv, self.b_const], writes=[self.bP2])
            p.emit("vector", lambda e: e.tensor_copy(MIDC[:], self.P2[:, 3:4]),
                   reads=[self.bP2], writes=[b_sv])
            p.emit("vector", lambda e: e.tensor_scalar(
                ering[:, 3, 0:32], imp, MIDC[:], None, ALU.is_ge, ALU.add, accum_out=CNT[:]),
                reads=[b_st, b_sv], writes=[b_er[3], b_sv])
            p.emit("vector", lambda e: e.tensor_copy(sVb[:, 1:2], CNT[:]), reads=[b_sv], writes=[b_sv])
            p.emit("tensor", lambda e: e.matmul(self.P2[0:1, 4:5], self.ones_col[:], sVb[:, 1:2],
                                                start=True, stop=True),
                   reads=[b_sv, self.b_const], writes=[self.bP2])
            p.emit("vector", lambda e: e.tensor_copy(T1[0:1, :], self.P2[0:1, 4:5]),
                   reads=[self.bP2], writes=[b_sv])
            p.emit("vector", lambda e: e.tensor_scalar(PRD[0:1, :], T1[0:1, :], float(NSEL), None, ALU.is_ge),
                   reads=[b_sv], writes=[b_sv])
            p.emit("vector", lambda e: e.tensor_sub(T1[0:1, :], MID[0:1, :], LO[0:1, :]),
                   reads=[b_sv], writes=[b_sv])
            p.emit("vector", lambda e: e.tensor_tensor(T1[0:1, :], T1[0:1, :], PRD[0:1, :], ALU.mult),
                   reads=[b_sv], writes=[b_sv])
            p.emit("vector", lambda e: e.tensor_add(LO[0:1, :], LO[0:1, :], T1[0:1, :]),
                   reads=[b_sv], writes=[b_sv])
            p.emit("vector", lambda e: e.tensor_sub(T2[0:1, :], MID[0:1, :], HI[0:1, :]),
                   reads=[b_sv], writes=[b_sv])
            p.emit("vector", lambda e: e.tensor_scalar(T1[0:1, :], PRD[0:1, :], 1.0, -1.0, ALU.subtract, ALU.mult),
                   reads=[b_sv], writes=[b_sv])
            p.emit("vector", lambda e: e.tensor_tensor(T2[0:1, :], T2[0:1, :], T1[0:1, :], ALU.mult),
                   reads=[b_sv], writes=[b_sv])
            p.emit("vector", lambda e: e.tensor_add(HI[0:1, :], HI[0:1, :], T2[0:1, :]),
                   reads=[b_sv], writes=[b_sv])
        p.emit("tensor", lambda e: e.matmul(self.P2[:, 5:6], self.onesf[0:1, :], LO[0:1, :],
                                            start=True, stop=True),
               reads=[b_sv, self.b_const], writes=[self.bP2])
        p.emit("vector", lambda e: e.tensor_copy(MIDC[:], self.P2[:, 5:6]), reads=[self.bP2], writes=[b_sv])
        mask = macc
        p.emit("vector", lambda e: e.tensor_scalar(mask, imp, MIDC[:], None, ALU.is_ge),
               reads=[b_st, b_sv], writes=[b_st])
        maskb = ering[:, 3, 0:32]
        p.emit("vector", lambda e: e.tensor_copy(maskb, mask), reads=[b_st], writes=[b_er[3]])
        p.emit("tensor", lambda e: e.matmul(self.P2[:, 32:64], self.uinc[:], maskb,
                                            start=True, stop=False),
               reads=[b_er[3], self.b_const], writes=[self.bP2])
        p.emit("tensor", lambda e: e.matmul(self.P2[0:1, 576:608], self.ones_col[:], maskb,
                                            start=True, stop=True),
               reads=[b_er[3], self.b_const], writes=[self.bP2])
        p.emit("vector", lambda e: e.tensor_copy(ering[0:1, 3, 32:64], self.P2[0:1, 576:608]),
               reads=[self.bP2], writes=[b_er[3]])
        p.emit("tensor", lambda e: e.transpose(self.PT[0:32, 0:1], ering[0:1, 3, 32:64],
                                               self.ident[0:1, 0:1]),
               reads=[b_er[3], self.b_const], writes=[self.bPT])
        p.emit("vector", lambda e: e.tensor_copy(ering[0:32, 3, 64:65], self.PT[0:32, 0:1]),
               reads=[self.bPT], writes=[b_er[3]])
        p.emit("tensor", lambda e: e.matmul(self.P2[0:1, 608:640], ering[0:32, 3, 64:65], self.l32[:],
                                            start=True, stop=True),
               reads=[b_er[3], self.b_const], writes=[self.bP2])
        p.emit("vector", lambda e: e.tensor_copy(self.pool3[0:1, 7552:7584], self.P2[0:1, 608:640]),
               reads=[self.bP2], writes=[b_er[3]])
        p.emit("tensor", lambda e: e.matmul(self.P2[:, 32:64], self.onesf[0:1, :],
                                            self.pool3[0:1, 7552:7584], start=False, stop=True),
               reads=[b_er[3], self.b_const], writes=[self.bP2])
        posm1 = iacc
        p.emit("vector", lambda e: e.tensor_scalar(posm1, self.P2[:, 32:64], -1.0, None, ALU.add),
               reads=[self.bP2], writes=[b_st])

        self.drain_precast()
        p.barrier()
        outN = self.pool2[:, 0:32768].rearrange("p (c n) -> p c n", c=8)
        b_outN = Buf()
        iota512 = self.pool3[:, 7584:8096]
        b_iota = Buf()
        selj = p3[:, 16192:16704]
        b_selj = Buf()
        xnselT = p3[:, 17216:21312].rearrange("p (c n) -> p c n", c=8)
        b_xsel = Buf()
        qT2 = p3[:, 21312:25408]
        b_qT2 = Buf()
        kselT = p3[0:64, 25408:25920]
        b_ksel = Buf()
        vsel = p3[:, 25920:26180].rearrange("p (s n) -> p s n", s=4)
        b_vsel = Buf()
        esel = p3[:, 26180:28228]
        b_esel = Buf()
        ec = p3[:, 28228:30276]
        b_ec = Buf()
        zrow = self.pool3[:, 15168:15680]
        b_zrow = Buf()
        replS = self.pool3[:, 15680:16192]
        b_repl = Buf()
        padcol = self.sb("padcol", [128, 4], F32)
        b_pad = Buf()
        p.emit("sync", lambda e: e.dma_start(out=padcol[:], in_=self.dram["c_padbias"]),
               writes=[b_pad], dma=True)
        p.emit("gpsimd", lambda e: e.iota(iota512, pattern=[[1, 512]], base=0,
                                          channel_multiplier=0,
                                          allow_small_or_imprecise_dtypes=True),
               writes=[b_iota])
        p.emit("vector", lambda e: e.memset(
            vsel[:].rearrange("p s n -> p (s n)").rearrange("p (g o) -> p g o", o=65)[:, :, 64:65], 1.0),
            writes=[b_vsel])
        for pas in range(2):
            for j in range(NT):
                xt = self.xtile[:, j % 2, :]
                bxt = self.b_xt[j % 2]
                p.emit("sync", (lambda j_=j, xt_=xt: lambda e: e.dma_start(
                    out=xt_, in_=src[j_ * 128:(j_ + 1) * 128, :]))(),
                    reads=[bsrc], writes=[bxt], dma=True)
                xn = self.xntile[:, j % 2, :]
                bxn = self.b_xn[j % 2]
                self.ln_tile(xt, bxt, xn, bxn, self.lnsm, self.b_lnsm,
                             dummy=self.P2, bdummy=self.bP2)
                p.emit("vector", (lambda j_=j: lambda e: e.tensor_scalar(
                    selj, iota512, posm1[:, j_:j_ + 1], None, ALU.is_equal))(),
                    reads=[b_iota, b_st], writes=[b_selj])
                p.emit("vector", (lambda j_=j: lambda e: e.tensor_scalar(
                    selj, selj, mask[:, j_:j_ + 1], None, ALU.mult))(),
                    reads=[b_st, b_selj], writes=[b_selj])
                for ds in range(4):
                    dslice = pas * 4 + ds
                    pp = [self.P0, self.P1][ds // 2]
                    bpp = [self.bP0, self.bP1][ds // 2]
                    p.emit("tensor", (lambda j_=j, ds_=ds, dsl_=dslice, pp_=pp, bxn_=bxn, xn_=xn: lambda e: e.matmul(
                        pp_[:, (ds_ % 2) * 512:(ds_ % 2 + 1) * 512],
                        xn_[:, dsl_ * 128:(dsl_ + 1) * 128], selj,
                        start=(j_ == 0), stop=(j_ == NT - 1)))(),
                        reads=[bxn, b_selj], writes=[bpp])
            for ds in range(4):
                dslice = pas * 4 + ds
                pp = [self.P0, self.P1][ds // 2]
                bpp = [self.bP0, self.bP1][ds // 2]
                p.emit("vector", (lambda ds_=ds, dsl_=dslice, pp_=pp: lambda e: e.tensor_copy(
                    xnselT[:, dsl_, :], pp_[:, (ds_ % 2) * 512:(ds_ % 2 + 1) * 512]))(),
                    reads=[bpp], writes=[b_xsel])
        for h in range(NH):
            load_wslice("Wq", h, wsl[:, 0, :], b_wsl[0])
            wq8 = wsl[:, 0, :].rearrange("p (c n) -> p c n", c=8)
            for tb8 in range(8):
                for c in range(8):
                    p.emit("tensor", (lambda tb_=tb8, c_=c: lambda e: e.matmul(
                        self.PS[0:64, :], wq8[:, c_, :],
                        xnT[:, c_, tb_ * 512:(tb_ + 1) * 512],
                        start=(c_ == 0), stop=(c_ == 7)))(),
                        reads=[b_wsl[0], b_xnT], writes=[self.bPS])
                p.emit("scalar", (lambda tb_=tb8: lambda e: e.activation(
                    qT2[0:64, tb_ * 512:(tb_ + 1) * 512], self.PS[0:64, :],
                    AF.Copy, scale=0.125))(),
                    reads=[self.bPS], writes=[b_qT2])
            load_wslice("Wk", h, wsl[:, 1, :], b_wsl[1])
            wk8 = wsl[:, 1, :].rearrange("p (c n) -> p c n", c=8)
            for c in range(8):
                p.emit("tensor", (lambda c_=c: lambda e: e.matmul(
                    self.PS[0:64, :], wk8[:, c_, :], xnselT[:, c_, :],
                    start=(c_ == 0), stop=(c_ == 7)))(),
                    reads=[b_wsl[1], b_xsel], writes=[self.bPS])
            p.emit("vector", lambda e: e.tensor_copy(kselT, self.PS[0:64, :]),
                   reads=[self.bPS], writes=[b_ksel])
            load_wslice("Wv", h, wsl[:, 0, :], b_wsl[0])
            wv8 = wsl[:, 0, :].rearrange("p (c n) -> p c n", c=8)
            for s4 in range(4):
                for c in range(8):
                    p.emit("tensor", (lambda s_=s4, c_=c: lambda e: e.matmul(
                        self.P2[:, s_ * 64:(s_ + 1) * 64],
                        xnselT[:, c_, s_ * 128:(s_ + 1) * 128], wv8[:, c_, :],
                        start=(c_ == 0), stop=(c_ == 7)))(),
                        reads=[b_wsl[0], b_xsel], writes=[self.bP2])
                p.emit("vector", (lambda s_=s4: lambda e: e.tensor_copy(
                    vsel[:, s_, 0:64], self.P2[:, s_ * 64:(s_ + 1) * 64]))(),
                    reads=[self.bP2], writes=[b_vsel])
            for qb in range(8):
                for s4 in range(4):
                    pp = [self.P0, self.P1][s4 // 2]
                    bpp = [self.bP0, self.bP1][s4 // 2]
                    p.emit("tensor", (lambda s_=s4, qb_=qb, pp_=pp: lambda e: e.matmul(
                        pp_[:, (s_ % 2) * 512:(s_ % 2 + 1) * 512],
                        kselT[:, s_ * 128:(s_ + 1) * 128],
                        qT2[0:64, qb_ * 512:(qb_ + 1) * 512],
                        start=True, stop=True))(),
                        reads=[b_ksel, b_qT2], writes=[bpp])
                for s4 in range(4):
                    pp = [self.P0, self.P1][s4 // 2]
                    bpp = [self.bP0, self.bP1][s4 // 2]
                    p.emit("scalar", (lambda s_=s4, pp_=pp: lambda e: e.activation(
                        esel[:, s_ * 512:(s_ + 1) * 512], pp_[:, (s_ % 2) * 512:(s_ % 2 + 1) * 512],
                        AF.Exp, bias=padcol[:, s_:s_ + 1]))(),
                        reads=[bpp, b_pad], writes=[b_esel])
                for s4 in range(4):
                    pp = [self.P0, self.P1][s4 // 2]
                    bpp = [self.bP0, self.bP1][s4 // 2]
                    p.emit("tensor", (lambda s_=s4, qb_=qb, h_=h, pp_=pp: lambda e: e.matmul(
                        pp_[:, (s_ % 2) * 512:(s_ % 2 + 1) * 512],
                        kcT[h_][:, s_ * 128:(s_ + 1) * 128],
                        qT2[0:64, qb_ * 512:(qb_ + 1) * 512],
                        start=True, stop=True))(),
                        reads=[b_kcT[h], b_qT2], writes=[bpp])
                for s4 in range(4):
                    pp = [self.P0, self.P1][s4 // 2]
                    bpp = [self.bP0, self.bP1][s4 // 2]
                    p.emit("scalar", (lambda s_=s4, h_=h, pp_=pp: lambda e: e.activation(
                        ec[:, s_ * 512:(s_ + 1) * 512], pp_[:, (s_ % 2) * 512:(s_ % 2 + 1) * 512],
                        AF.Exp, scale=rgcol[:, h_, s_:s_ + 1]))(),
                        reads=[bpp, b_rg], writes=[b_ec])
                for s4 in range(4):
                    p.emit("tensor", (lambda s_=s4: lambda e: e.matmul(
                        self.PS[0:65, :], vsel[:, s_, :], esel[:, s_ * 512:(s_ + 1) * 512],
                        start=(s_ == 0), stop=False))(),
                        reads=[b_vsel, b_esel], writes=[self.bPS])
                for s4 in range(4):
                    p.emit("tensor", (lambda s_=s4, h_=h: lambda e: e.matmul(
                        self.PS[0:65, :], vcn[:, h_, s_ * 65:(s_ + 1) * 65],
                        ec[:, s_ * 512:(s_ + 1) * 512],
                        start=False, stop=(s_ == 3)))(),
                        reads=[b_vcn, b_ec], writes=[self.bPS])
                p.emit("vector", lambda e: e.tensor_copy(zrow[64:65, :], self.PS[64:65, :]),
                       reads=[self.bPS], writes=[b_zrow])
                p.emit("vector", lambda e: e.reciprocal(zrow[64:65, :], zrow[64:65, :]),
                       reads=[b_zrow], writes=[b_zrow])
                p.emit("tensor", lambda e: e.matmul(
                    self.P2[0:64, 0:512], self.onesf[64:65, 0:64], zrow[64:65, :],
                    start=True, stop=True),
                    reads=[b_zrow, self.b_const], writes=[self.bP2])
                p.emit("vector", lambda e: e.tensor_copy(replS[0:64, :], self.P2[0:64, 0:512]),
                       reads=[self.bP2], writes=[b_repl])
                p.emit("vector", (lambda h_=h, qb_=qb: lambda e: e.tensor_tensor(
                    outN[(h_ % 2) * 64:(h_ % 2) * 64 + 64, h_ // 2, qb_ * 512:(qb_ + 1) * 512],
                    self.PS[0:64, :], replS[0:64, :], ALU.mult))(),
                    reads=[self.bPS, b_repl], writes=[b_outN])
        p.barrier()
        Wo = p3[:, 17216:25408].rearrange("p (c n) -> p c n", c=8)
        b_Wo = Buf()
        for c in range(8):
            sl = c % 2
            p.emit("sync", (lambda c_=c, sl_=sl: lambda e: e.dma_start(
                out=self.wstg[:, sl_, :], in_=w["Wo"][c_ * 128:(c_ + 1) * 128, :]))(),
                writes=[self.b_wstg[sl]], dma=True)
            p.emit("vector", (lambda c_=c, sl_=sl: lambda e: e.tensor_copy(
                Wo[:, c_, :], self.wstg[:, sl_, :]))(),
                reads=[self.b_wstg[sl]], writes=[b_Wo])
        outtile = self.pool3[:, 13090:15138].rearrange("p (r n) -> p r n", r=2)
        b_ot = [Buf(), Buf()]
        outs = []
        for j in range(NT):
            pp = [self.P2, self.P0][j % 2]
            bpp = [self.bP2, self.bP0][j % 2]
            for mv in range(2):
                for c in range(8):
                    p.emit("tensor", (lambda j_=j, c_=c, mv_=mv, pp_=pp: lambda e: e.matmul(
                        pp_[:, mv_ * 512:(mv_ + 1) * 512],
                        outN[:, c_, j_ * 128:(j_ + 1) * 128],
                        Wo[:, c_, mv_ * 512:(mv_ + 1) * 512],
                        start=(c_ == 0), stop=(c_ == 7)))(),
                        reads=[b_outN, b_Wo], writes=[bpp])
            xt = self.xtile[:, j % 2, :]
            bxt = self.b_xt[j % 2]
            p.emit("sync", (lambda j_=j, xt_=xt: lambda e: e.dma_start(
                out=xt_, in_=src[j_ * 128:(j_ + 1) * 128, :]))(),
                reads=[bsrc], writes=[bxt], dma=True)
            p.emit("vector", (lambda j_=j, pp_=pp, xt_=xt: lambda e: e.scalar_tensor_tensor(
                outtile[:, j_ % 2, :], pp_[:], 1.0, xt_, ALU.mult, ALU.add))(),
                reads=[bpp, bxt], writes=[b_ot[j % 2]])
            outs.append(p.emit("sync", (lambda j_=j: lambda e: e.dma_start(
                out=dst[j_ * 128:(j_ + 1) * 128, :], in_=outtile[:, j_ % 2, :]))(),
                reads=[b_ot[j % 2]], writes=[bdst], dma=True))
        return outs

    def stage_attn_b(self, src, bsrc, dst, bdst, w):
        p = self.p
        xn2T = self.pool1[:, 0:32768].rearrange("p (c n) -> p c n", c=8)
        b_xn2T = self.b_pool1
        xo_all = self.pool2[:, 0:32768].rearrange("p (j n) -> p j n", j=32)
        b_xo = self.b_pool2
        p3 = self.pool3[:].bitcast(BF16)
        qkT = p3[:, 0:4096]
        b_qkT = Buf()
        vpair = p3[:, 4096:8192].rearrange("p (j n) -> p j n", j=32)
        b_vp = Buf()
        KEp = p3[:, 8192:9216].rearrange("p (j n) -> p j n", j=32)
        b_KE = Buf()
        qhp = p3[:, 9216:10240].rearrange("p (j n) -> p j n", j=32)
        b_qh = Buf()
        OFF = [self.pool3[0:1, 5120 + i * 1040: 5120 + (i + 1) * 1040] for i in range(2)]
        b_OFF = [Buf(), Buf()]
        wring = p3[:, 14592:20736].rearrange("p (r n) -> p r n", r=2)
        b_wr = [Buf(), Buf()]
        ktil = p3[0:16, 20736:24832]
        b_ktil = Buf()
        ktex = p3[0:16, 24832:28928]
        b_ktex = Buf()
        ab = p3[:, 28928:29968]
        b_ab = Buf()
        xp = p3[:, 29968:30992]
        b_xp = Buf()
        kvsb = p3[:, 31040:32064]
        b_kvsb = Buf()
        xored = self.pool3[:, 15520:15584]
        b_xor = Buf()
        codeT = p3[:, 30992:31024].rearrange("p (h n) -> p h n", h=2)
        b_code = Buf()
        kmx = self.gwork[0:16, 40:42]
        for j in range(NT):
            xt = self.xtile[:, j % 2, :]
            bxt = self.b_xt[j % 2]
            p.emit("sync", (lambda j_=j, xt_=xt: lambda e: e.dma_start(
                out=xt_, in_=src[j_ * 128:(j_ + 1) * 128, :]))(),
                reads=[bsrc], writes=[bxt], dma=True)
            xn = self.xntile[:, j % 2, :]
            bxn = self.b_xn[j % 2]
            self.ln_tile(xt, bxt, xn, bxn, self.lnsm, self.b_lnsm)
            self.transpose8(xn, bxn, xn2T[:, :, j * 128:(j + 1) * 128], b_xn2T)
        for pr in range(NH // 2):
            wr = wring[:, pr % 2, :].rearrange("p (b n) -> p b n", b=3)
            bwr = b_wr[pr % 2]
            for b3 in range(3):
                sl = b3 % 2
                p.emit("sync", (lambda b3_=b3, sl_=sl: lambda e: e.dma_start(
                    out=self.wstg[:, sl_, :],
                    in_=w["w_qkv"][b3_ * 1024 + pr * 128: b3_ * 1024 + (pr + 1) * 128, :]))(),
                    writes=[self.b_wstg[sl]], dma=True)
                p.emit("vector", (lambda sl_=sl: lambda e: e.tensor_copy(
                    self.w1b[:, sl_, :], self.wstg[:, sl_, :]))(),
                    reads=[self.b_wstg[sl]], writes=[self.b_w1[sl]])
                for c in range(8):
                    p.emit("tensor", (lambda c_=c, sl_=sl: lambda e: e.transpose(
                        self.PT[:, c_ * 128:(c_ + 1) * 128],
                        self.w1b[:, sl_, c_ * 128:(c_ + 1) * 128], self.ident[:]))(),
                        reads=[self.b_w1[sl], self.b_const], writes=[self.bPT])
                p.emit("vector", (lambda b3_=b3: lambda e: e.tensor_copy(
                    wr[:, b3_, :].rearrange("p (c n) -> p c n", c=8).transpose([0, 2, 1]).rearrange("p n c -> p (n c)") if False else
                    wr[:, b3_, :],
                    self.PT[:]))(),
                    reads=[self.bPT], writes=[bwr])
            for hip in range(2):
                h = pr * 2 + hip
                p.emit("sync", (lambda h_=h: lambda e: e.dma_start(
                    out=self.wstg[0:16, 0, 0:64], in_=w["fc_code"][0, h_]))(),
                    writes=[self.b_wstg[0]], dma=True)
                p.emit("vector", lambda e: e.tensor_copy(
                    self.w1b[0:16, 0, 0:64], self.wstg[0:16, 0, 0:64]),
                    reads=[self.b_wstg[0]], writes=[self.b_w1[0]])
                p.emit("tensor", lambda e: e.transpose(
                    self.PT[0:64, 0:16], self.w1b[0:16, 0, 0:64], self.ident[0:16, 0:16]),
                    reads=[self.b_w1[0], self.b_const], writes=[self.bPT])
                p.emit("tensor", lambda e: e.transpose(
                    self.PT[64:128, 16:32], self.w1b[0:16, 0, 0:64], self.ident[0:16, 0:16]),
                    reads=[self.b_w1[0], self.b_const], writes=[self.bPT])
                p.emit("scalar", (lambda hip_=hip: lambda e: e.activation(
                    codeT[0:64, hip_, :], self.PT[0:64, 0:16], AF.Copy, scale=0.125))(),
                    reads=[self.bPT], writes=[b_code])
                p.emit("scalar", (lambda hip_=hip: lambda e: e.activation(
                    codeT[64:128, hip_, :], self.PT[64:128, 16:32], AF.Copy, scale=0.125))(),
                    reads=[self.bPT], writes=[b_code])
                for half64 in range(2):
                    for tb8 in range(8):
                        for c in range(8):
                            p.emit("tensor", (lambda half_=half64, tb_=tb8, c_=c, hip_=hip: lambda e: e.matmul(
                                self.PS[0:64, :],
                                wr[:, half_, c_ * 128 + hip_ * 64: c_ * 128 + hip_ * 64 + 64],
                                xn2T[:, c_, tb_ * 512:(tb_ + 1) * 512],
                                start=(c_ == 0), stop=(c_ == 7)))(),
                                reads=[bwr, b_xn2T], writes=[self.bPS])
                        p.emit("vector", (lambda half_=half64, tb_=tb8: lambda e: e.tensor_copy(
                            qkT[half_ * 64:(half_ + 1) * 64, tb_ * 512:(tb_ + 1) * 512],
                            self.PS[0:64, :]))(),
                            reads=[self.bPS], writes=[b_qkT])
                for j in range(NT):
                    for c in range(8):
                        p.emit("tensor", (lambda j_=j, c_=c, hip_=hip: lambda e: e.matmul(
                            self.PS[:, 0:64],
                            xn2T[:, c_, j_ * 128:(j_ + 1) * 128],
                            wr[:, 2, c_ * 128 + hip_ * 64: c_ * 128 + hip_ * 64 + 64],
                            start=(c_ == 0), stop=(c_ == 7)))(),
                            reads=[bwr, b_xn2T], writes=[self.bPS])
                    p.emit("vector", (lambda j_=j, hip_=hip: lambda e: e.tensor_copy(
                        vpair[:, j_, hip_ * 64:(hip_ + 1) * 64], self.PS[:, 0:64]))(),
                        reads=[self.bPS], writes=[b_vp])
                for tb8 in range(8):
                    p.emit("tensor", (lambda tb_=tb8, hip_=hip: lambda e: e.matmul(
                        self.PS[0:16, :], codeT[64:128, hip_, :],
                        qkT[64:128, tb_ * 512:(tb_ + 1) * 512],
                        start=True, stop=True))(),
                        reads=[b_code, b_qkT], writes=[self.bPS])
                    p.emit("vector", (lambda tb_=tb8: lambda e: e.tensor_copy(
                        ktil[:, tb_ * 512:(tb_ + 1) * 512], self.PS[0:16, :]))(),
                        reads=[self.bPS], writes=[b_ktil])
                p.emit("vector", lambda e: e.reduce_max(kmx[:, 0:1], ktil, AX.X),
                       reads=[b_ktil], writes=[self.b_gw])
                p.emit("vector", lambda e: e.tensor_scalar(kmx[:, 1:2], kmx[:, 0:1], -1.0, None, ALU.mult),
                       reads=[self.b_gw], writes=[self.b_gw])
                p.emit("scalar", lambda e: e.activation(ktex, ktil, AF.Exp, bias=kmx[0:16, 1:2]),
                       reads=[b_ktil, self.b_gw], writes=[b_ktex])
                for half16 in range(2):
                    for jj in range(16):
                        j = half16 * 16 + jj
                        p.emit("tensor", (lambda j_=j, jj_=jj: lambda e: e.transpose(
                            self.PT[:, jj_ * 16:(jj_ + 1) * 16],
                            ktex[:, j_ * 128:(j_ + 1) * 128], self.ident[0:16, 0:16]))(),
                            reads=[b_ktex, self.b_const], writes=[self.bPT])
                    p.emit("vector", (lambda half_=half16, hip_=hip: lambda e: e.tensor_copy(
                        KEp[:, half_ * 16:(half_ + 1) * 16, hip_ * 16:(hip_ + 1) * 16],
                        self.PT[:, 0:256].rearrange("p (j n) -> p j n", j=16)))(),
                        reads=[self.bPT], writes=[b_KE])
                for j in range(NT):
                    p.emit("tensor", (lambda j_=j, hip_=hip: lambda e: e.matmul(
                        self.PS[:, j_ * 16:(j_ + 1) * 16], qkT[0:64, j_ * 128:(j_ + 1) * 128],
                        codeT[0:64, hip_, :], start=True, stop=True))(),
                        reads=[b_qkT, b_code], writes=[self.bPS])
                qsm = self.pool3[:, 8096:8608]
                qmx = self.pool3[:, 8608:8640]
                b_qs = Buf()
                p.emit("vector", lambda e: e.tensor_reduce(
                    qmx, self.PS[:].rearrange("p (j n) -> p j n", n=16), AX.X, ALU.max),
                    reads=[self.bPS], writes=[b_qs])
                p.emit("vector", lambda e: e.tensor_tensor(
                    qsm.rearrange("p (j n) -> p j n", n=16),
                    self.PS[:].rearrange("p (j n) -> p j n", n=16),
                    qmx.unsqueeze(2).broadcast_to([128, 32, 16]), ALU.subtract),
                    reads=[self.bPS, b_qs], writes=[b_qs])
                p.emit("scalar", lambda e: e.activation(qsm, qsm, AF.Exp),
                       reads=[b_qs], writes=[b_qs])
                p.emit("vector", lambda e: e.tensor_reduce(
                    qmx, qsm.rearrange("p (j n) -> p j n", n=16), AX.X, ALU.add),
                    reads=[b_qs], writes=[b_qs])
                p.emit("vector", lambda e: e.reciprocal(qmx, qmx), reads=[b_qs], writes=[b_qs])
                p.emit("vector", (lambda hip_=hip: lambda e: e.tensor_tensor(
                    qhp[:, :, hip_ * 16:(hip_ + 1) * 16],
                    qsm.rearrange("p (j n) -> p j n", n=16),
                    qmx.unsqueeze(2).broadcast_to([128, 32, 16]), ALU.mult))(),
                    reads=[b_qs], writes=[b_qh])
            for hip in range(2):
                p.emit("vector", (lambda hip_=hip: lambda e: e.memset(OFF[hip_][:], 0.0))(),
                       writes=[b_OFF[hip]])
            for tb in range(NT):
                for hip in range(2):
                    h = pr * 2 + hip
                    PKV = [self.P0, self.P1][(tb * 2 + hip) % 2]
                    bPKV = [self.bP0, self.bP1][(tb * 2 + hip) % 2]
                    p.emit("vector", (lambda tb_=tb, hip_=hip: lambda e: e.tensor_tensor(
                        ab[:, 0:1024].rearrange("p (c d) -> p c d", c=16),
                        KEp[:, tb_, hip_ * 16:(hip_ + 1) * 16].unsqueeze(2).broadcast_to([128, 16, 64]),
                        vpair[:, tb_, hip_ * 64:(hip_ + 1) * 64].unsqueeze(1).broadcast_to([128, 16, 64]),
                        ALU.mult))(),
                        reads=[b_KE, b_vp], writes=[b_ab])
                    p.emit("vector", (lambda tb_=tb, hip_=hip: lambda e: e.tensor_copy(
                        ab[:, 1024:1040], KEp[:, tb_, hip_ * 16:(hip_ + 1) * 16]))(),
                        reads=[b_KE], writes=[b_ab])
                    for mv in range(2):
                        p.emit("tensor", (lambda mv_=mv, PKV_=PKV: lambda e: e.matmul(
                            PKV_[:, mv_ * 512:(mv_ + 1) * 512], self.uinc[:],
                            ab[:, mv_ * 512:(mv_ + 1) * 512], start=True, stop=False))(),
                            reads=[self.b_const, b_ab], writes=[bPKV])
                    p.emit("tensor", lambda e: e.matmul(
                        self.PS[:, 0:16], self.uinc[:], ab[:, 1024:1040],
                        start=True, stop=False),
                        reads=[self.b_const, b_ab], writes=[self.bPS])
                    for mv in range(2):
                        p.emit("tensor", (lambda mv_=mv: lambda e: e.matmul(
                            self.P2[0:1, mv_ * 512:(mv_ + 1) * 512], self.ones_col[:],
                            ab[:, mv_ * 512:(mv_ + 1) * 512], start=True, stop=True))(),
                            reads=[self.b_const, b_ab], writes=[self.bP2])
                    p.emit("tensor", lambda e: e.matmul(
                        self.P2[0:1, 1008:1024], self.ones_col[:], ab[:, 1024:1040],
                        start=True, stop=True),
                        reads=[self.b_const, b_ab], writes=[self.bP2])
                    for mv in range(2):
                        p.emit("tensor", (lambda mv_=mv, hip_=hip, PKV_=PKV: lambda e: e.matmul(
                            PKV_[:, mv_ * 512:(mv_ + 1) * 512], self.ones_row_f[0:1, :],
                            OFF[hip_][:, mv_ * 512:(mv_ + 1) * 512],
                            start=False, stop=True))(),
                            reads=[self.b_const, b_OFF[hip]], writes=[bPKV])
                    p.emit("tensor", (lambda hip_=hip: lambda e: e.matmul(
                        self.PS[:, 0:16], self.ones_row_f[0:1, :],
                        OFF[hip_][:, 1024:1040], start=False, stop=True))(),
                        reads=[self.b_const, b_OFF[hip]], writes=[self.bPS])
                    p.emit("vector", (lambda hip_=hip: lambda e: e.tensor_tensor(
                        OFF[hip_][:, 0:1024], OFF[hip_][:, 0:1024], self.P2[0:1, 0:1024],
                        ALU.add))(),
                        reads=[self.bP2, b_OFF[hip]], writes=[b_OFF[hip]])
                    p.emit("vector", (lambda hip_=hip: lambda e: e.tensor_tensor(
                        OFF[hip_][:, 1024:1040], OFF[hip_][:, 1024:1040], self.P2[0:1, 1008:1024],
                        ALU.add))(),
                        reads=[self.bP2, b_OFF[hip]], writes=[b_OFF[hip]])
                    g = self.gwork
                    p.emit("vector", lambda e: e.tensor_scalar(
                        g[:, 20:36], self.PS[:, 0:16], self.eps9[:], None, ALU.add),
                        reads=[self.bPS, self.b_const], writes=[self.b_gw])
                    p.emit("vector", lambda e: e.reciprocal(g[:, 20:36], g[:, 20:36]),
                           reads=[self.b_gw], writes=[self.b_gw])
                    p.emit("vector", (lambda tb_=tb, hip_=hip: lambda e: e.scalar_tensor_tensor(
                        g[:, 20:36], qhp[:, tb_, hip_ * 16:(hip_ + 1) * 16], 0.125,
                        g[:, 20:36], ALU.mult, ALU.mult))(),
                        reads=[b_qh, self.b_gw], writes=[self.b_gw])
                    kvb = ab
                    p.emit("scalar", (lambda PKV_=PKV: lambda e: e.activation(
                        kvsb, PKV_[:], AF.Copy))(),
                        reads=[bPKV], writes=[b_kvsb])
                    p.emit("vector", lambda e: e.tensor_tensor(
                        xp[:].rearrange("p (c d) -> p c d", c=16),
                        g[:, 20:36].unsqueeze(2).broadcast_to([128, 16, 64]),
                        kvsb.rearrange("p (c d) -> p c d", c=16), ALU.mult),
                        reads=[self.b_gw, b_kvsb], writes=[b_xp])
                    p.emit("vector", lambda e: e.tensor_reduce(
                        xored, xp[:].rearrange("p (c d) -> p d c", c=16), AX.X, ALU.add),
                        reads=[b_xp], writes=[b_xor])
                    p.emit("vector", (lambda tb_=tb, h_=h: lambda e: e.tensor_copy(
                        xo_all[:, tb_, h_ * 64:(h_ + 1) * 64], xored))(),
                        reads=[b_xor], writes=[b_xo])
        p.barrier()
        woT = p3[:, 0:8192].rearrange("p (c n) -> p c n", c=8)
        b_woT = Buf()
        for r in range(8):
            sl = r % 2
            p.emit("sync", (lambda r_=r, sl_=sl: lambda e: e.dma_start(
                out=self.wstg[:, sl_, :], in_=w["w_out"][r_ * 128:(r_ + 1) * 128, :]))(),
                writes=[self.b_wstg[sl]], dma=True)
            p.emit("vector", (lambda sl_=sl: lambda e: e.tensor_copy(
                self.w1b[:, sl_, :], self.wstg[:, sl_, :]))(),
                reads=[self.b_wstg[sl]], writes=[self.b_w1[sl]])
            for c in range(8):
                p.emit("tensor", (lambda c_=c, sl_=sl: lambda e: e.transpose(
                    self.PT[:, c_ * 128:(c_ + 1) * 128],
                    self.w1b[:, sl_, c_ * 128:(c_ + 1) * 128], self.ident[:]))(),
                    reads=[self.b_w1[sl], self.b_const], writes=[self.bPT])
            p.emit("vector", (lambda r_=r: lambda e: e.tensor_copy(
                woT[:, :, r_ * 128:(r_ + 1) * 128],
                self.PT[:].rearrange("p (c n) -> p c n", c=8)))(),
                reads=[self.bPT], writes=[b_woT])
        xoT = xn2T
        b_xoT = b_xn2T
        for j in range(NT):
            for c in range(8):
                p.emit("tensor", (lambda j_=j, c_=c: lambda e: e.transpose(
                    self.PT[:, c_ * 128:(c_ + 1) * 128],
                    xo_all[:, j_, c_ * 128:(c_ + 1) * 128], self.ident[:]))(),
                    reads=[b_xo, self.b_const], writes=[self.bPT])
            p.emit("vector", (lambda j_=j: lambda e: e.tensor_copy(
                xoT[:, :, j_ * 128:(j_ + 1) * 128],
                self.PT[:].rearrange("p (c n) -> p c n", c=8)))(),
                reads=[self.bPT], writes=[b_xoT])
        outtile = self.pool3[:, 10240:12288].rearrange("p (r n) -> p r n", r=2)
        b_ot = [Buf(), Buf()]
        outs = []
        for j in range(NT):
            pp = [self.P2, self.P0][j % 2]
            bpp = [self.bP2, self.bP0][j % 2]
            for mv in range(2):
                for c in range(8):
                    p.emit("tensor", (lambda j_=j, c_=c, mv_=mv, pp_=pp: lambda e: e.matmul(
                        pp_[:, mv_ * 512:(mv_ + 1) * 512],
                        xoT[:, c_, j_ * 128:(j_ + 1) * 128],
                        woT[:, c_, mv_ * 512:(mv_ + 1) * 512],
                        start=(c_ == 0), stop=(c_ == 7)))(),
                        reads=[b_xoT, b_woT], writes=[bpp])
            xt = self.xtile[:, j % 2, :]
            bxt = self.b_xt[j % 2]
            p.emit("sync", (lambda j_=j, xt_=xt: lambda e: e.dma_start(
                out=xt_, in_=src[j_ * 128:(j_ + 1) * 128, :]))(),
                reads=[bsrc], writes=[bxt], dma=True)
            p.emit("vector", (lambda j_=j, pp_=pp, xt_=xt: lambda e: e.scalar_tensor_tensor(
                outtile[:, j_ % 2, :], pp_[:], 1.0, xt_, ALU.mult, ALU.add))(),
                reads=[bpp, bxt], writes=[b_ot[j % 2]])
            outs.append(p.emit("sync", (lambda j_=j: lambda e: e.dma_start(
                out=dst[j_ * 128:(j_ + 1) * 128, :], in_=outtile[:, j_ % 2, :]))(),
                reads=[b_ot[j % 2]], writes=[bdst], dma=True))
        return outs


def build_attn_b_only():
    nc = bass.Bass("TRN2", target_bir_lowering=False, dynamic_dma_scratch_size=8192)
    p = Prog()
    kb = KB(nc, p)
    src = kb.din("xsrc", [S, D])
    dst = kb.dout("out", [S, D])
    w = {
        "w_qkv": kb.din("w_qkv", [3 * D, D]),
        "w_out": kb.din("w_out", [D, D]),
        "fc_code": kb.din("fc_code", [1, NH, 16, DH]),
    }
    for nm, arr in host_consts().items():
        kb.din(nm, arr.shape, BF16 if arr.dtype == bf else F32)
    with nc.Block() as block:
        kb.setup(host_consts())
        outs = kb.stage_attn_b(src, kb.dbuf["xsrc"], dst, kb.dbuf["out"], w)
        p.finalize(nc, block, final_extra=outs)
    return nc


def build_ffn_only():
    nc = bass.Bass("TRN2", target_bir_lowering=False, dynamic_dma_scratch_size=8192)
    p = Prog()
    kb = KB(nc, p)
    src = kb.din("xsrc", [S, D])
    dst = kb.dout("out", [S, D])
    w = {
        "router_w": kb.din("router_w", [D, NE]),
        "router_b": kb.din("router_b", [NE]),
        "w1": kb.din("w1", [NE, D, HME]),
        "b1": kb.din("b1", [NE, HME]),
        "w2": kb.din("w2", [NE, HME, D]),
        "b2": kb.din("b2", [NE, D]),
        "sw1": kb.din("sw1", [NSH, D, HME]),
        "sb1": kb.din("sb1", [NSH, HME]),
        "sw2": kb.din("sw2", [NSH, HME, D]),
        "sb2": kb.din("sb2", [NSH, D]),
    }
    for nm, arr in host_consts().items():
        kb.din(nm, arr.shape, BF16 if arr.dtype == bf else F32)
    with nc.Block() as block:
        kb.setup(host_consts())
        outs = kb.stage_ffn(src, kb.dbuf["xsrc"], dst, kb.dbuf["out"], w)
        p.finalize(nc, block, final_extra=outs)
    return nc


def build_attn_a_only():
    nc = bass.Bass("TRN2", target_bir_lowering=False, dynamic_dma_scratch_size=8192)
    p = Prog()
    kb = KB(nc, p)
    src_ = kb.din("xsrc", [S, D])
    dst = kb.dout("out", [S, D])
    w = {"Wq": kb.din("Wq", [D, D]), "Wk": kb.din("Wk", [D, D]),
         "Wv": kb.din("Wv", [D, D]), "Wo": kb.din("Wo", [D, D])}
    for nm, arr in host_consts().items():
        kb.din(nm, arr.shape, BF16 if arr.dtype == bf else F32)
    with nc.Block() as block:
        kb.setup(host_consts())
        outs = kb.stage_attn_a(src_, kb.dbuf["xsrc"], dst, kb.dbuf["out"], w)
        p.finalize(nc, block, final_extra=outs)
    return nc


def build_full():
    nc = bass.Bass("TRN2", target_bir_lowering=False, dynamic_dma_scratch_size=8192)
    p = Prog()
    kb = KB(nc, p)
    x0 = kb.din("x0", [S, D])
    out = kb.dout("out", [S, D])
    wa = {"Wq": kb.din("L0_attn_a_Wq", [D, D]), "Wk": kb.din("L0_attn_a_Wk", [D, D]),
          "Wv": kb.din("L0_attn_a_Wv", [D, D]), "Wo": kb.din("L0_attn_a_Wo", [D, D])}
    wb = {"w_qkv": kb.din("L0_attn_b_w_qkv", [3 * D, D]),
          "w_out": kb.din("L0_attn_b_w_out", [D, D]),
          "fc_code": kb.din("L0_attn_b_fc_code", [1, NH, 16, DH])}
    wf = {"router_w": kb.din("L0_ffn_router_w", [D, NE]),
          "router_b": kb.din("L0_ffn_router_b", [NE]),
          "w1": kb.din("L0_ffn_w1", [NE, D, HME]), "b1": kb.din("L0_ffn_b1", [NE, HME]),
          "w2": kb.din("L0_ffn_w2", [NE, HME, D]), "b2": kb.din("L0_ffn_b2", [NE, D]),
          "sw1": kb.din("L0_ffn_sw1", [NSH, D, HME]), "sb1": kb.din("L0_ffn_sb1", [NSH, HME]),
          "sw2": kb.din("L0_ffn_sw2", [NSH, HME, D]), "sb2": kb.din("L0_ffn_sb2", [NSH, D])}
    for nm, arr in host_consts().items():
        kb.din(nm, arr.shape, BF16 if arr.dtype == bf else F32)
    X1 = kb.dint("X1", [S, D])
    X2 = kb.dint("X2", [S, D])
    X3 = kb.dint("X3", [S, D])
    with nc.Block() as block:
        kb.setup(host_consts())
        kb.init_precast(wf)
        kb.stage_attn_a(x0, kb.dbuf["x0"], X1, kb.dbuf["X1"], wa)
        p.barrier()
        kb.stage_ffn(X1, kb.dbuf["X1"], X2, kb.dbuf["X2"], wf)
        p.barrier()
        kb.stage_attn_b(X2, kb.dbuf["X2"], X3, kb.dbuf["X3"], wb)
        p.barrier()
        outs = kb.stage_ffn(X3, kb.dbuf["X3"], out, kb.dbuf["out"], wf)
        p.finalize(nc, block, final_extra=outs)
    return nc


_NC_CACHE = {}


def kernel(**inputs):
    import numpy as np
    if "nc" not in _NC_CACHE:
        _NC_CACHE["nc"] = build_full()
    nc = _NC_CACHE["nc"]
    from concourse.bass_utils import run_bass_kernel_spmd
    consts = host_consts()
    x = np.ascontiguousarray(inputs["L0_attn_a_x"])
    base = {k: np.ascontiguousarray(v) for k, v in inputs.items() if k != "L0_attn_a_x"}
    base.update(consts)
    in_maps = []
    for i in range(8):
        m = dict(base)
        m["x0"] = np.ascontiguousarray(x[i])
        in_maps.append(m)
    res = run_bass_kernel_spmd(nc, in_maps, core_ids=list(range(8)))
    return np.stack([r["out"] for r in res.results], axis=0)


# version 09
# speedup vs baseline: 1.1047x; 1.1047x over previous
import numpy as np
import ml_dtypes
import concourse.bass as bass
import concourse.mybir as mybir

F32 = mybir.dt.float32
BF16 = mybir.dt.bfloat16
AF = mybir.ActivationFunctionType
ALU = mybir.AluOpType
AX = mybir.AxisListType
bf = ml_dtypes.bfloat16

ENGINES = ("tensor", "vector", "scalar", "gpsimd", "sync")

S = 4096
D = 1024
NT = 32
NH = 16
DH = 64
NSEL = 409
NCMP = 512
HME = 2048
NE = 8
NSH = 2


class Buf:
    __slots__ = ("last_writer", "readers")

    def __init__(self):
        self.last_writer = None
        self.readers = []


class Prog:

    NSTREAMS = {"sync": 8, "gpsimd": 8, "scalar": 4, "vector": 2, "tensor": 2}

    def __init__(self):
        self.ops = []
        self._bar = []
        self._stream_counts = {}

    def emit(self, engine, fn, reads=(), writes=(), extra=(), dma=False):
        tok = len(self.ops)
        waits = set(self._bar)
        for b in reads:
            if b.last_writer is not None:
                waits.add(b.last_writer)
        for b in writes:
            if b.last_writer is not None:
                waits.add(b.last_writer)
            for r in b.readers:
                waits.add(r)
        for t in extra:
            if t is not None:
                waits.add(t)
        waits.discard(tok)
        skey = seq = None
        if dma:
            n = self._stream_counts.get(engine, 0)
            self._stream_counts[engine] = n + 1
            skey = (engine, n % self.NSTREAMS[engine])
            seq = n // self.NSTREAMS[engine] + 1
        self.ops.append([engine, fn, dma, waits, skey, seq])
        for b in reads:
            b.readers.append(tok)
        for b in writes:
            b.last_writer = tok
            b.readers = []
        return tok

    def barrier(self):
        last = {}
        for i, op in enumerate(self.ops):
            last[op[0]] = i
        self._bar = list(last.values())

    def finalize(self, nc, block, final_extra=()):
        ops = self.ops
        n = len(ops)
        esem = {e: nc.alloc_semaphore(f"sem_{e}") for e in ENGINES}
        ssem = {}
        for (eng, k) in {op[4] for op in ops if op[2]}:
            ssem[(eng, k)] = nc.alloc_semaphore(f"dma_{eng}_{k}")
        needed = [False] * n
        for _, _, _, waits, _, _ in ops:
            for t in waits:
                needed[t] = True
        for t in final_extra:
            needed[t] = True
        cum = [None] * n
        run = {e: 0 for e in ENGINES}
        for i, (eng, _, dma, _, skey, seq) in enumerate(ops):
            if dma:
                cum[i] = (ssem[skey], 16 * seq)
            else:
                if needed[i]:
                    run[eng] += 1
                cum[i] = (esem[eng], run[eng])

        by_engine = {e: [] for e in ENGINES}
        for i, op in enumerate(ops):
            by_engine[op[0]].append(i)

        def body(eng_name):
            def runner(eng):
                water = {}
                def do_wait(sem, v):
                    if water.get(id(sem), 0) < v:
                        eng.wait_ge(sem, v)
                        water[id(sem)] = v
                for i in by_engine[eng_name]:
                    _, fn, dma, waits, skey, seq = ops[i]
                    per = {}
                    for t in waits:
                        if ops[t][0] == "tensor" and eng_name == "tensor" and not ops[t][2]:
                            continue
                        sem, v = cum[t]
                        if v > per.get(id(sem), (None, 0))[1]:
                            per[id(sem)] = (sem, v)
                    for sem, v in per.values():
                        do_wait(sem, v)
                    if dma and seq > 1:
                        do_wait(ssem[skey], 16 * (seq - 1))
                    inst = fn(eng)
                    if dma:
                        inst.then_inc(ssem[skey], 16)
                    elif needed[i]:
                        inst.then_inc(esem[eng_name], 1)
                if eng_name == "sync":
                    per = {}
                    for t in final_extra:
                        sem, v = cum[t]
                        if v > per.get(id(sem), (None, 0))[1]:
                            per[id(sem)] = (sem, v)
                    for sem, v in per.values():
                        do_wait(sem, v)
            return runner

        for e in ENGINES:
            getattr(block, e)(body(e))


def host_consts():
    c = {}
    c["c_ident"] = np.eye(128).astype(bf)
    c["c_identf"] = np.eye(128).astype(np.float32)
    c["c_uinc"] = np.triu(np.ones((128, 128))).astype(bf)
    c["c_ppool"] = (np.arange(128)[:, None] // 8 == np.arange(16)[None, :]).astype(bf)
    c["c_l32"] = (np.arange(32)[:, None] < np.arange(32)[None, :]).astype(bf)
    c["c_ones_col"] = np.ones((128, 1)).astype(bf)
    c["c_ones_row_f"] = np.ones((1, 128)).astype(np.float32)
    c["c_onesf"] = np.ones((65, 128)).astype(np.float32)
    pb = np.zeros((128, 4), np.float32)
    pb[409 - 3 * 128:, 3] = -60.0
    c["c_padbias"] = pb
    c["c_ones_row_b"] = np.ones((1, 128)).astype(bf)
    return c


class KB:
    def __init__(self, nc, prog):
        self.nc = nc
        self.p = prog
        self.dram = {}
        self.dbuf = {}
        self.uid = 0

    def din(self, name, shape, dtype=F32):
        self.dram[name] = self.nc.dram_tensor(name, list(shape), dtype, kind="ExternalInput").ap()
        self.dbuf[name] = Buf()
        return self.dram[name]

    def dout(self, name, shape, dtype=F32):
        self.dram[name] = self.nc.dram_tensor(name, list(shape), dtype, kind="ExternalOutput").ap()
        self.dbuf[name] = Buf()
        return self.dram[name]

    def dint(self, name, shape, dtype=F32):
        self.dram[name] = self.nc.dram_tensor(name, list(shape), dtype, kind="Internal").ap()
        self.dbuf[name] = Buf()
        return self.dram[name]

    def sb(self, name, shape, dtype):
        return self.nc.alloc_sbuf_tensor(name, list(shape), dtype)

    def setup(self, consts):
        nc = self.nc
        self.pool1 = self.sb("pool1", [128, 32768], BF16)
        self.pool2 = self.sb("pool2", [128, 32768], BF16)
        self.pool3 = self.sb("pool3", [128, 16384], F32)
        self.b_pool1 = Buf()
        self.b_pool2 = Buf()
        self.b_pool3 = Buf()
        self.P0 = nc.alloc_psum_tensor("P0", [128, 1024], F32)
        self.P1 = nc.alloc_psum_tensor("P1", [128, 1024], F32)
        self.P2 = nc.alloc_psum_tensor("P2", [128, 1024], F32)
        self.PS = nc.alloc_psum_tensor("PS", [128, 512], F32)
        self.PT = nc.alloc_psum_tensor("PT", [128, 1024], BF16)
        self.bP0, self.bP1, self.bP2, self.bPS, self.bPT = Buf(), Buf(), Buf(), Buf(), Buf()
        self.ident = self.sb("ident", [128, 128], BF16)
        self.identf = self.sb("identf", [128, 128], F32)
        self.uinc = self.sb("uinc", [128, 128], BF16)
        self.ppool = self.sb("ppool", [128, 16], BF16)
        self.l32 = self.sb("l32", [32, 32], BF16)
        self.ones_col = self.sb("ones_col", [128, 1], BF16)
        self.ones_row_f = self.sb("ones_row_f", [1, 128], F32)
        self.onesf = self.sb("onesf", [65, 128], F32)
        self.ones_row_b = self.sb("ones_row_b", [1, 128], BF16)
        self.b_const = Buf()
        p = self.p
        for nm in ("c_ident", "c_uinc", "c_ppool", "c_l32", "c_ones_col",
                   "c_ones_row_f", "c_ones_row_b", "c_identf", "c_onesf"):
            dst = {"c_ident": self.ident, "c_uinc": self.uinc, "c_ppool": self.ppool,
                   "c_l32": self.l32, "c_ones_col": self.ones_col,
                   "c_ones_row_f": self.ones_row_f, "c_ones_row_b": self.ones_row_b,
                   "c_identf": self.identf, "c_onesf": self.onesf}[nm]
            src = self.dram[nm]
            p.emit("sync", (lambda d, s: lambda e: e.dma_start(out=d[:], in_=s))(dst, src),
                   writes=[self.b_const], dma=True)
        self.bnc = self.dint("bnc_scr", [128, 1024], F32)
        self.epsln = self.sb("epsln", [128, 1], F32)
        self.p.emit("vector", lambda e: e.memset(self.epsln[:], 1e-5), writes=[self.b_const])
        self.dummy_f = self.P0
        self.dummy_f2 = self.P0
        self.wstg = self.sb("wstg", [128, 2, 1024], F32)
        self.b_wstg = [Buf(), Buf()]
        self.w1b = self.sb("w1b", [128, 2, 1024], BF16)
        self.b_w1 = [Buf(), Buf()]
        self.gwork = self.sb("gwork", [128, 48], F32)
        self.b_gw = Buf()
        self.lnsm = self.sb("lnsm", [128, 8], F32)
        self.b_lnsm = Buf()
        self.xtile = self.wstg
        self.b_xt = self.b_wstg
        self.xntile = self.sb("xntile", [128, 2, 1024], BF16)
        self.b_xn = [Buf(), Buf()]
        self.eps9 = self.sb("eps9", [128, 1], F32)
        self.p.emit("vector", lambda e: e.memset(self.eps9[:], 1e-9), writes=[self.b_const])
        self.pcstg = self.wstg
        self.b_pcstg = self.b_wstg
        self.pcb = self.pool2[:, 24576:25600]
        self.b_pcb = Buf()
        self._pc_q = None
        self._pc_drained = False

    def init_precast(self, w):
        if getattr(self, "_precast_init", False):
            return
        self._precast_init = True
        self.W1C = self.dint("W1C", [NE + NSH, 16, 128, 1024], BF16)
        self.W2C = self.dint("W2C", [NE + NSH, 16, 128, 1024], BF16)
        self.b_wc = [Buf() for _ in range(NE + NSH)]
        q = []
        for ei in range(NE + NSH):
            shared = ei >= NE
            w1src = w["sw1"][ei - NE] if shared else w["w1"][ei]
            w2src = w["sw2"][ei - NE] if shared else w["w2"][ei]
            for hs in range(16):
                q.append((ei, hs, 0, w1src))
                q.append((ei, hs, 1, w2src))
        self._pc_q = q
        self._pc_i = 0

    def pump_precast(self, n):
        if self._pc_q is None:
            return
        p = self.p
        for _ in range(n):
            if self._pc_i >= len(self._pc_q):
                self._pc_q = None
                return
            ei, hs, kind, wsrc = self._pc_q[self._pc_i]
            self._pc_i += 1
            sl = self._pc_i % 2
            if kind == 0:
                p.emit("sync", (lambda hs_=hs, sl_=sl, ws=wsrc: lambda e: e.dma_start(
                    out=self.pcstg[:, sl_, :].rearrange("p (c n) -> p c n", c=8),
                    in_=ws[:, hs_ * 128:(hs_ + 1) * 128].rearrange("(c p) n -> p c n", p=128)))(),
                    writes=[self.b_pcstg[sl]], dma=True)
            else:
                p.emit("sync", (lambda hs_=hs, sl_=sl, ws=wsrc: lambda e: e.dma_start(
                    out=self.pcstg[:, sl_, :],
                    in_=ws[hs_ * 128:(hs_ + 1) * 128, :]))(),
                    writes=[self.b_pcstg[sl]], dma=True)
            p.emit("scalar", (lambda sl_=sl: lambda e: e.activation(
                self.pcb[:], self.pcstg[:, sl_, :], AF.Copy))(),
                reads=[self.b_pcstg[sl]], writes=[self.b_pcb])
            dstT = self.W1C if kind == 0 else self.W2C
            p.emit("scalar", (lambda ei_=ei, hs_=hs, d_=dstT: lambda e: e.dma_start(
                out=d_[ei_][hs_], in_=self.pcb[:]))(),
                reads=[self.b_pcb], writes=[self.b_wc[ei]], dma=True)

    def flush_precast(self):
        while self._pc_q is not None:
            self.pump_precast(8)
        self.drain_precast()

    def drain_precast(self):
        if self._pc_drained or self._pc_q is not None or not getattr(self, "_precast_init", False):
            return
        self._pc_drained = True
        self.p.emit("scalar", lambda e: e.copy(self.pcb[0:1, 0:8], self.pcb[0:1, 0:8]),
                    reads=[self.b_pcb], writes=[self.b_pcb])


    def u(self):
        self.uid += 1
        return self.uid

    def ln_tile(self, xt, bxt, out_ap, bout, smalls, bsm, dummy=None, bdummy=None):
        p = self.p
        if dummy is None:
            dummy, bdummy = self.P0, self.bP0
        ss, s_, m, a, var, rs = (smalls[:, i:i + 1] for i in range(6))
        p.emit("vector", (lambda d_=dummy: lambda e: e.scalar_tensor_tensor(
            d_[:], xt, 1.0, xt, ALU.mult, ALU.mult, accum_out=ss))(),
            reads=[bxt], writes=[bdummy, bsm])
        p.emit("vector", (lambda d_=dummy: lambda e: e.tensor_scalar(
            d_[:], xt, 0.0, None, ALU.add, ALU.add, accum_out=s_))(),
            reads=[bxt], writes=[bdummy, bsm])
        p.emit("vector", lambda e: e.tensor_scalar(m, s_, 1.0 / D, None, ALU.mult),
               reads=[bsm], writes=[bsm])
        p.emit("vector", lambda e: e.tensor_scalar(a, ss, 1.0 / D, None, ALU.mult),
               reads=[bsm], writes=[bsm])
        p.emit("vector", lambda e: e.scalar_tensor_tensor(
            var, m, 1.0, m, ALU.mult, ALU.mult), reads=[bsm], writes=[bsm])
        p.emit("vector", lambda e: e.tensor_sub(var, a, var), reads=[bsm], writes=[bsm])
        p.emit("scalar", lambda e: e.activation(a, var, AF.Ln, bias=self.epsln[:]),
               reads=[bsm], writes=[bsm])
        p.emit("scalar", lambda e: e.activation(rs, a, AF.Exp, scale=-0.5),
               reads=[bsm], writes=[bsm])
        p.emit("vector", lambda e: e.tensor_scalar(
            out_ap, xt, m, rs, ALU.subtract, ALU.mult),
            reads=[bxt, bsm], writes=[bout])

    def transpose8(self, src_ap, bsrc, dst_view, bdst):
        p = self.p
        for c in range(8):
            p.emit("tensor", (lambda c_: lambda e: e.transpose(
                self.PT[:, c_ * 128:(c_ + 1) * 128], src_ap[:, c_ * 128:(c_ + 1) * 128],
                self.ident[:]))(c),
                reads=[bsrc, self.b_const], writes=[self.bPT])
        p.emit("vector", lambda e: e.tensor_copy(
            dst_view, self.PT[:].rearrange("p (c n) -> p c n", c=8)),
            reads=[self.bPT], writes=[bdst])

    def stage_ffn(self, src, bsrc, dst, bdst, w):
        p = self.p
        nc = self.nc
        self.init_precast(w)
        self.flush_precast()
        p3 = self.pool3[:].bitcast(BF16)
        out_acc = self.pool3[:, 0:8192].rearrange("p (j n) -> p j n", j=8)
        b_acc = self.b_pool3
        xnT = p3[:, 16384:24576].rearrange("p (c n) -> p c n", c=8)
        b_xnT = Buf()
        hT = self.pool2[:, 0:32768].rearrange("p (r n) -> p r n", r=2)
        hTv = [hT[:, 0, :].rearrange("p (h n) -> p h n", h=16),
               hT[:, 1, :].rearrange("p (h n) -> p h n", h=16)]
        b_hT = [Buf(), Buf()]
        w2b = self.pool1[:, 0:32768].rearrange("p (r n) -> p r n", r=2)
        w2v = [w2b[:, 0, :].rearrange("p (h n) -> p h n", h=16),
               w2b[:, 1, :].rearrange("p (h n) -> p h n", h=16)]
        b_w2 = [Buf(), Buf()]
        if not hasattr(self, "ffn_bufs"):
            self.rwb = self.sb("rwb", [128, 8, 8], BF16)
            self.rbrow = self.sb("rbrow", [1, 8], F32)
            self.bext = self.sb("bext", [16, 1024], BF16)
            self.b1col = self.sb("b1col", [128, 2, 16], F32)
            self.b1stg = self.sb("b1stg", [16, 2, 128], F32)
            self.combf = self.sb("combf", [128, 8, 8], F32)
            self.combx = self.sb("combx", [128, 16], BF16)
            self.combxT = self.sb("combxT", [16, 2, 128], BF16)
            self.ffn_bufs = True
            self.b_rw = Buf()
            self.b_bext = Buf()
            self.b_b1 = [Buf(), Buf()]
            self.b_comb = Buf()
            p.emit("vector", lambda e: e.memset(self.combx[:], 0.0), writes=[self.b_comb])
            p.emit("vector", lambda e: e.memset(self.combx[:, 8:10], 0.5), writes=[self.b_comb])
        p.emit("sync", lambda e: e.dma_start(
            out=self.wstg[:, 0, 0:64].rearrange("p (c e) -> p c e", c=8),
            in_=w["router_w"].rearrange("(c p) e -> p c e", p=128)),
               writes=[self.b_wstg[0]], dma=True)
        p.emit("vector", lambda e: e.tensor_copy(
            self.rwb[:].rearrange("p a b -> p (a b)"), self.wstg[:, 0, 0:64]),
               reads=[self.b_wstg[0]], writes=[self.b_rw])
        p.emit("sync", lambda e: e.dma_start(out=self.rbrow[:], in_=w["router_b"].unsqueeze(0)),
               writes=[self.b_rw], dma=True)
        p.emit("vector", lambda e: e.memset(self.bext[:], 0.0), writes=[self.b_bext])
        p.emit("sync", lambda e: e.dma_start(out=self.wstg[0:8, 1, :], in_=w["b2"]),
               writes=[self.b_wstg[1]], dma=True)
        p.emit("sync", lambda e: e.dma_start(out=self.wstg[8:10, 1, :], in_=w["sb2"]),
               writes=[self.b_wstg[1]], dma=True)
        p.emit("vector", lambda e: e.tensor_copy(self.bext[0:10, :], self.wstg[0:10, 1, :]),
               reads=[self.b_wstg[1]], writes=[self.b_bext])

        outs = []
        for qt in range(4):
            r0 = qt * 1024
            for j in range(8):
                xt = self.xtile[:, j % 2, :]
                bxt = self.b_xt[j % 2]
                p.emit("sync", (lambda j_=j, xt_=xt, r0_=r0: lambda e: e.dma_start(
                    out=xt_, in_=src[r0_ + j_ * 128: r0_ + (j_ + 1) * 128, :]))(),
                    reads=[bsrc], writes=[bxt], dma=True)
                xn = self.xntile[:, j % 2, :]
                bxn = self.b_xn[j % 2]
                self.ln_tile(xt, bxt, xn, bxn, self.lnsm, self.b_lnsm)
                self.transpose8(xn, bxn, xnT[:, :, j * 128:(j + 1) * 128], b_xnT)
                for c in range(8):
                    p.emit("tensor", (lambda c_=c, j_=j: lambda e: e.matmul(
                        self.PS[:, 0:8], xnT[:, c_, j_ * 128:(j_ + 1) * 128],
                        self.rwb[:, c_, :], start=(c_ == 0), stop=False))(),
                        reads=[b_xnT, self.b_rw], writes=[self.bPS])
                p.emit("tensor", lambda e: e.matmul(
                    self.PS[:, 0:8], self.ones_row_f[:], self.rbrow[:],
                    start=False, stop=True),
                    reads=[self.b_rw, self.b_const], writes=[self.bPS])
                g = self.gwork
                bg = self.b_gw
                gs_, m1, e1, g2, m2, e2, d12, p1, p2, t1 = (
                    g[:, 0:8], g[:, 8:9], g[:, 9:17], g[:, 17:25], g[:, 25:26],
                    g[:, 26:34], g[:, 34:35], g[:, 35:36], g[:, 36:37], g[:, 37:45])
                comb = self.combf[:, j, :]
                p.emit("vector", lambda e: e.tensor_copy(gs_, self.PS[:, 0:8]),
                       reads=[self.bPS], writes=[bg])
                p.emit("vector", lambda e: e.reduce_max(m1, gs_, AX.X), reads=[bg], writes=[bg])
                p.emit("vector", lambda e: e.tensor_scalar(e1, gs_, m1, None, ALU.is_ge),
                       reads=[bg], writes=[bg])
                p.emit("vector", lambda e: e.scalar_tensor_tensor(
                    g2, e1, -1e30, gs_, ALU.mult, ALU.add), reads=[bg], writes=[bg])
                p.emit("vector", lambda e: e.reduce_max(m2, g2, AX.X), reads=[bg], writes=[bg])
                p.emit("vector", lambda e: e.tensor_scalar(e2, g2, m2, None, ALU.is_ge),
                       reads=[bg], writes=[bg])
                p.emit("vector", lambda e: e.tensor_tensor(d12, m2, m1, ALU.subtract),
                       reads=[bg], writes=[bg])
                p.emit("scalar", lambda e: e.activation(p2, d12, AF.Exp),
                       reads=[bg], writes=[bg])
                p.emit("vector", lambda e: e.tensor_scalar(p1, p2, 1.0, None, ALU.add),
                       reads=[bg], writes=[bg])
                p.emit("vector", lambda e: e.reciprocal(p1, p1), reads=[bg], writes=[bg])
                p.emit("vector", lambda e: e.tensor_tensor(p2, p2, p1, ALU.mult),
                       reads=[bg], writes=[bg])
                p.emit("vector", (lambda e1_=e1, p1_=p1, t1_=t1: lambda e: e.tensor_scalar(
                    t1_, e1_, p1_, None, ALU.mult))(), reads=[bg], writes=[bg])
                p.emit("vector", (lambda comb_=comb: lambda e: e.scalar_tensor_tensor(
                    comb_, e2, p2, t1, ALU.mult, ALU.add))(),
                    reads=[bg], writes=[self.b_comb])
                p.emit("vector", (lambda comb_=comb: lambda e: e.tensor_copy(
                    self.combx[:, 0:8], comb_))(), reads=[self.b_comb], writes=[self.b_comb])
                p.emit("tensor", lambda e: e.transpose(
                    self.PT[0:16, 0:128], self.combx[:], self.ident[:]),
                    reads=[self.b_comb, self.b_const], writes=[self.bPT])
                p.emit("vector", (lambda j_=j: lambda e: e.tensor_copy(
                    self.combxT[:, j_ % 2, :], self.PT[0:16, 0:128]))(),
                    reads=[self.bPT], writes=[self.b_comb])
                for mv in range(2):
                    p.emit("tensor", (lambda j_=j, mv_=mv: lambda e: e.matmul(
                        self.P2[:, mv_ * 512:(mv_ + 1) * 512], self.combxT[:, j_ % 2, :],
                        self.bext[:, mv_ * 512:(mv_ + 1) * 512],
                        start=True, stop=True))(),
                        reads=[self.b_comb, self.b_bext], writes=[self.bP2])
                p.emit("vector", (lambda j_=j, xt_=xt: lambda e: e.scalar_tensor_tensor(
                    out_acc[:, j_, :], self.P2[:], 1.0, xt_, ALU.mult, ALU.add))(),
                    reads=[self.bP2, bxt], writes=[b_acc])
            for ei in range(NE + NSH):
                par = ei % 2
                shared = ei >= NE
                w1src = w["sw1"][ei - NE] if shared else w["w1"][ei]
                w2src = w["sw2"][ei - NE] if shared else w["w2"][ei]
                b1src = w["sb1"][ei - NE] if shared else w["b1"][ei]
                p.emit("sync", (lambda b1s=b1src, par_=par: lambda e: e.dma_start(
                    out=self.b1stg[:, par_, :], in_=b1s.rearrange("(s p) -> s p", p=128)))(),
                    writes=[self.b_b1[par]], dma=True)
                ptf = self.PT[:].bitcast(F32)
                p.emit("tensor", (lambda par_=par: lambda e: e.transpose(
                    ptf[:, 0:16], self.b1stg[:, par_, :], self.identf[0:16, 0:16]))(),
                    reads=[self.b_b1[par], self.b_const], writes=[self.bPT])
                p.emit("vector", (lambda par_=par: lambda e: e.tensor_copy(
                    self.b1col[:, par_, :], ptf[:, 0:16]))(),
                    reads=[self.bPT], writes=[self.b_b1[par]])
                for hs in range(16):
                    sl = hs % 2
                    p.emit("sync",
                           (lambda hs_=hs, sl_=sl, ei_=ei: lambda e: e.dma_start(
                               out=self.w1b[:, sl_, :], in_=self.W1C[ei_][hs_]))(),
                           reads=[self.b_wc[ei]], writes=[self.b_w1[sl]], dma=True)
                    pp = [self.P0, self.P1][hs % 2]
                    bpp = [self.bP0, self.bP1][hs % 2]
                    for mv in range(2):
                        for c in range(8):
                            p.emit("tensor", (lambda sl_=sl, c_=c, mv_=mv, pp_=pp: lambda e: e.matmul(
                                pp_[:, mv_ * 512:(mv_ + 1) * 512],
                                self.w1b[:, sl_, c_ * 128:(c_ + 1) * 128],
                                xnT[:, c_, mv_ * 512:(mv_ + 1) * 512],
                                start=(c_ == 0), stop=(c_ == 7)))(),
                                reads=[self.b_w1[sl], b_xnT], writes=[bpp])
                    p.emit("scalar", (lambda hs_=hs, pp_=pp, par_=par: lambda e: e.activation(
                        hTv[par_][:, hs_, :], pp_[:],
                        AF.Gelu, bias=self.b1col[:, par_, hs_:hs_ + 1]))(),
                        reads=[bpp, self.b_b1[par]], writes=[b_hT[par]])
                    hc = hs
                    p.emit("scalar",
                           (lambda hc_=hc, ei_=ei, par_=par: lambda e: e.dma_start(
                               out=w2v[par_][:, hc_, :], in_=self.W2C[ei_][hc_]))(),
                           reads=[self.b_wc[ei]], writes=[b_w2[par]], dma=True)
                for j in range(8):
                    yp = [self.P0, self.P1, self.P2][j % 3]
                    byp = [self.bP0, self.bP1, self.bP2][j % 3]
                    for mv in range(2):
                        for hc in range(16):
                            p.emit("tensor", (lambda j_=j, hc_=hc, mv_=mv, par_=par, yp_=yp: lambda e: e.matmul(
                                yp_[:, mv_ * 512:(mv_ + 1) * 512],
                                hTv[par_][:, hc_, j_ * 128:(j_ + 1) * 128],
                                w2v[par_][:, hc_, mv_ * 512:(mv_ + 1) * 512],
                                start=(hc_ == 0), stop=(hc_ == 15)))(),
                                reads=[b_hT[par], b_w2[par]], writes=[byp])
                    if shared:
                        p.emit("vector", (lambda j_=j, yp_=yp: lambda e: e.scalar_tensor_tensor(
                            out_acc[:, j_, :], yp_[:], 0.5, out_acc[:, j_, :],
                            ALU.mult, ALU.add))(),
                            reads=[byp, b_acc], writes=[b_acc])
                    else:
                        p.emit("vector", (lambda j_=j, ei_=ei, yp_=yp: lambda e: e.scalar_tensor_tensor(
                            out_acc[:, j_, :], yp_[:], self.combf[:, j_, ei_:ei_ + 1],
                            out_acc[:, j_, :], ALU.mult, ALU.add))(),
                            reads=[byp, b_acc, self.b_comb], writes=[b_acc])
            for j in range(8):
                outs.append(p.emit("sync", (lambda j_=j, r0_=r0: lambda e: e.dma_start(
                    out=dst[r0_ + j_ * 128: r0_ + (j_ + 1) * 128, :], in_=out_acc[:, j_, :]))(),
                    reads=[b_acc], writes=[bdst], dma=True))
        return outs

    def stage_attn_a(self, src, bsrc, dst, bdst, w):
        p = self.p
        NSEL_PAD = 512
        xnT = self.pool1[:, 0:32768].rearrange("p (c n) -> p c n", c=8)
        b_xnT = self.b_pool1
        p2b = self.pool2
        qTaug = [p2b[:, 0:4096], p2b[:, 4096:8192]]
        b_qTa = [Buf(), Buf()]
        kTaug = [p2b[:, 8192:12288], p2b[:, 12288:16384]]
        b_kTa = [Buf(), Buf()]
        kv_s = p2b[:, 16384:20480].rearrange("p (j n) -> p j n", j=32)
        b_kv = Buf()
        ering = p2b[:, 20480:24576].rearrange("p (r n) -> p r n", r=4)
        b_er = [Buf(), Buf(), Buf(), Buf()]
        p3 = self.pool3[:].bitcast(BF16)
        wsl = p3[:, 0:1024].rearrange("p (r n) -> p r n", r=2)
        b_wsl = [Buf(), Buf()]
        kcT = [p3[0:64, 1024 + i * 512: 1024 + (i + 1) * 512] for i in range(16)]
        b_kcT = [Buf() for _ in range(16)]
        vcn = p3[:, 9216:13376].rearrange("p (h n) -> p h n", h=16)
        b_vcn = Buf()
        rgcol = self.pool3[:, 6688:6752].rearrange("p (h n) -> p h n", h=16)
        b_rg = Buf()
        st = self.pool3[:, 6752:7008].rearrange("p (k n) -> p k n", k=8)
        eacc, macc, iacc, lnzt, ut, uhit, ulot, n2t = (st[:, i, :] for i in range(8))
        b_st = Buf()
        b_lnz = Buf()
        b_u = Buf()
        zp = self.gwork
        p.emit("vector", lambda e: e.memset(eacc, 0.0), writes=[b_st])
        p.emit("vector", lambda e: e.memset(macc, 0.0), writes=[b_st])
        p.emit("vector", lambda e: e.memset(iacc, 0.0), writes=[b_st])
        for sl in range(2):
            p.emit("vector", (lambda sl_=sl: lambda e: e.memset(kTaug[sl_][64:67, :], 0.0))(),
                   writes=[b_kTa[sl]])
            p.emit("vector", (lambda sl_=sl: lambda e: e.tensor_scalar(
                kTaug[sl_][64:67, :], kTaug[sl_][64:67, :], 1.0, None, ALU.add))(),
                reads=[b_kTa[sl]], writes=[b_kTa[sl]])
        p.emit("vector", lambda e: e.memset(
            vcn[:].rearrange("p h n -> p (h n)").rearrange("p (g o) -> p g o", o=65)[:, :, 64:65], 1.0),
            writes=[b_vcn])
        bncb = self.dint("bncb_scr", [96, 128], BF16)
        b_bncb = Buf()
        staKC = self.sb("staKC", [128, 132], BF16)
        staV = self.sb("staV", [128, 64], BF16)
        b_staK, b_staV = Buf(), Buf()
        p.emit("vector", lambda e: e.memset(staKC[:, 65:129], 0.0), writes=[b_staK])

        def load_wslice(name, h, slot, bslot):
            p.emit("sync", (lambda h_=h: lambda e: e.dma_start(
                out=self.wstg[:, 0, 0:512].rearrange("p (c n) -> p c n", c=8),
                in_=w[name][:, h_ * 64:(h_ + 1) * 64].rearrange("(c p) n -> p c n", p=128)))(),
                writes=[self.b_wstg[0]], dma=True)
            p.emit("vector", (lambda slot_=slot: lambda e: e.tensor_copy(
                slot_, self.wstg[:, 0, 0:512]))(),
                reads=[self.b_wstg[0]], writes=[bslot])

        for j in range(NT):
            xt = self.xtile[:, j % 2, :]
            bxt = self.b_xt[j % 2]
            p.emit("sync", (lambda j_=j, xt_=xt: lambda e: e.dma_start(
                out=xt_, in_=src[j_ * 128:(j_ + 1) * 128, :]))(),
                reads=[bsrc], writes=[bxt], dma=True)
            xn = self.xntile[:, j % 2, :]
            bxn = self.b_xn[j % 2]
            self.ln_tile(xt, bxt, xn, bxn, self.lnsm, self.b_lnsm)
            self.transpose8(xn, bxn, xnT[:, :, j * 128:(j + 1) * 128], b_xnT)

        for h in range(NH):
            sl = h % 2
            qT, bqT = qTaug[sl], b_qTa[sl]
            kT, bkT = kTaug[sl], b_kTa[sl]
            load_wslice("Wq", h, wsl[:, 0, :], b_wsl[0])
            wq8 = wsl[:, 0, :].rearrange("p (c n) -> p c n", c=8)
            for tb8 in range(8):
                ppj = [self.P0, self.P1][tb8 % 2]
                bppj = [self.bP0, self.bP1][tb8 % 2]
                for c in range(8):
                    p.emit("tensor", (lambda tb_=tb8, c_=c, ppj_=ppj: lambda e: e.matmul(
                        ppj_[0:64, 0:512], wq8[:, c_, :],
                        xnT[:, c_, tb_ * 512:(tb_ + 1) * 512],
                        start=(c_ == 0), stop=(c_ == 7)))(),
                        reads=[b_wsl[0], b_xnT], writes=[bppj])
                p.emit("scalar", (lambda tb_=tb8, qT_=qT, ppj_=ppj: lambda e: e.activation(
                    qT_[0:64, tb_ * 512:(tb_ + 1) * 512], ppj_[0:64, 0:512],
                    AF.Copy, scale=0.125))(),
                    reads=[bppj], writes=[bqT])
                if tb8 % 2 == 1:
                    self.pump_precast(2)
            kvw = self.w1b[:, 0, :].rearrange("p (c n) -> p c n", c=8)
            p.emit("sync", (lambda h_=h: lambda e: e.dma_start(
                out=self.wstg[:, 0, 0:512].rearrange("p (c n) -> p c n", c=8),
                in_=w["Wk"][:, h_ * 64:(h_ + 1) * 64].rearrange("(c p) n -> p c n", p=128)))(),
                writes=[self.b_wstg[0]], dma=True)
            p.emit("sync", (lambda h_=h: lambda e: e.dma_start(
                out=self.wstg[:, 1, 0:512].rearrange("p (c n) -> p c n", c=8),
                in_=w["Wv"][:, h_ * 64:(h_ + 1) * 64].rearrange("(c p) n -> p c n", p=128)))(),
                writes=[self.b_wstg[1]], dma=True)
            p.emit("vector", lambda e: e.tensor_copy(
                kvw[:, :, 0:64], self.wstg[:, 0, 0:512].rearrange("p (c n) -> p c n", c=8)),
                reads=[self.b_wstg[0]], writes=[self.b_w1[0]])
            p.emit("vector", lambda e: e.tensor_copy(
                kvw[:, :, 64:128], self.wstg[:, 1, 0:512].rearrange("p (c n) -> p c n", c=8)),
                reads=[self.b_wstg[1]], writes=[self.b_w1[0]])
            for tb8 in range(8):
                ppj = [self.P0, self.P1][tb8 % 2]
                bppj = [self.bP0, self.bP1][tb8 % 2]
                for c in range(8):
                    p.emit("tensor", (lambda tb_=tb8, c_=c, ppj_=ppj: lambda e: e.matmul(
                        ppj_[0:64, 0:512], kvw[:, c_, 0:64],
                        xnT[:, c_, tb_ * 512:(tb_ + 1) * 512],
                        start=(c_ == 0), stop=(c_ == 7)))(),
                        reads=[self.b_w1[0], b_xnT], writes=[bppj])
                p.emit("scalar", (lambda tb_=tb8, kT_=kT, ppj_=ppj: lambda e: e.activation(
                    kT_[0:64, tb_ * 512:(tb_ + 1) * 512], ppj_[0:64, 0:512], AF.Copy))(),
                    reads=[bppj], writes=[bkT])
                if tb8 % 2 == 1:
                    self.pump_precast(2)
            for j in range(NT):
                off = (j % 2) * 128
                for c in range(8):
                    p.emit("tensor", (lambda j_=j, c_=c, off_=off: lambda e: e.matmul(
                        self.P2[:, off_:off_ + 128], xnT[:, c_, j_ * 128:(j_ + 1) * 128],
                        kvw[:, c_, :], start=(c_ == 0), stop=(c_ == 7)))(),
                        reads=[self.b_w1[0], b_xnT], writes=[self.bP2])
                p.emit("vector", (lambda j_=j, off_=off: lambda e: e.tensor_copy(
                    kv_s[:, j_, :], self.P2[:, off_:off_ + 128]))(),
                    reads=[self.bP2], writes=[b_kv])
                if j % 4 == 3:
                    self.pump_precast(1)
            zarr = self.pool3[:, 7040:7168]
            zarr = self.pool3[:, 8096:8224]
            warr = self.pool3[:, 8224:8352]
            marr = self.pool3[:, 8352:8480]
            b_zarr, b_warr, b_marr = Buf(), Buf(), Buf()
            for j in range(NT):
                for cb in range(4):
                    pp = [self.P0, self.P1][cb % 2]
                    bpp = [self.bP0, self.bP1][cb % 2]
                    for mv in range(2):
                        p.emit("tensor", (lambda j_=j, cb_=cb, mv_=mv, pp_=pp, qT_=qT, kT_=kT: lambda e: e.matmul(
                            pp_[:, mv_ * 512:(mv_ + 1) * 512],
                            qT_[0:64, j_ * 128:(j_ + 1) * 128],
                            kT_[0:64, (cb_ * 2 + mv_) * 512:(cb_ * 2 + mv_ + 1) * 512],
                            start=True, stop=True))(),
                            reads=[bqT, bkT], writes=[bpp])
                    er = ering[:, cb, :]
                    ber = b_er[cb]
                    p.emit("scalar", (lambda j_=j, cb_=cb, pp_=pp, er_=er: lambda e: e.activation(
                        er_, pp_[:], AF.Exp, accum_out=zarr[:, j_ * 4 + cb_:j_ * 4 + cb_ + 1]))(),
                        reads=[bpp], writes=[ber, b_zarr])
                    p.emit("vector", (lambda j_=j, cb_=cb, er_=er: lambda e: e.tensor_scalar(
                        er_, er_, 0.0, None, ALU.add, ALU.max,
                        accum_out=marr[:, j_ * 4 + cb_:j_ * 4 + cb_ + 1]))(),
                        reads=[ber], writes=[ber, b_marr])
                    p.emit("vector", (lambda j_=j, cb_=cb, pp_=pp, er_=er: lambda e: e.scalar_tensor_tensor(
                        er_, er_, 1.0, pp_[:], ALU.mult, ALU.mult,
                        accum_out=warr[:, j_ * 4 + cb_:j_ * 4 + cb_ + 1]))(),
                        reads=[ber, bpp], writes=[ber, b_warr])
            zt = self.pool3[:, 8480:8512]
            wt = self.pool3[:, 8512:8544]
            rzt = self.pool3[:, 8544:8576]
            b_zc = Buf()
            p.emit("vector", lambda e: e.tensor_reduce(
                zt, zarr.rearrange("p (j c) -> p j c", c=4), AX.X, ALU.add),
                reads=[b_zarr], writes=[b_zc])
            p.emit("vector", lambda e: e.tensor_reduce(
                wt, warr.rearrange("p (j c) -> p j c", c=4), AX.X, ALU.add),
                reads=[b_warr], writes=[b_zc])
            p.emit("vector", lambda e: e.tensor_reduce(
                rzt, marr.rearrange("p (j c) -> p j c", c=4), AX.X, ALU.max),
                reads=[b_marr], writes=[b_zc])
            p.emit("scalar", lambda e: e.activation(lnzt, zt, AF.Ln),
                   reads=[b_zc], writes=[b_lnz])
            p.emit("vector", lambda e: e.reciprocal(zt, zt), reads=[b_zc], writes=[b_zc])
            p.emit("vector", lambda e: e.tensor_tensor(rzt, rzt, zt, ALU.mult),
                   reads=[b_zc], writes=[b_zc])
            p.emit("vector", lambda e: e.tensor_add(macc, macc, rzt), reads=[b_zc], writes=[b_st])
            p.emit("vector", lambda e: e.tensor_tensor(wt, wt, zt, ALU.mult),
                   reads=[b_zc], writes=[b_zc])
            p.emit("vector", lambda e: e.tensor_sub(wt, lnzt, wt), reads=[b_zc, b_lnz], writes=[b_zc])
            p.emit("vector", lambda e: e.tensor_add(eacc, eacc, wt), reads=[b_zc], writes=[b_st])
            g = zp
            spl = ering[:, 0, 0:96].rearrange("p (v n) -> p v n", v=3)
            p.emit("vector", lambda e: e.tensor_scalar(g[:, 20:52] if False else lnzt, lnzt, -1.0, None, ALU.mult),
                   reads=[b_lnz], writes=[b_lnz])
            p.emit("vector", lambda e: e.tensor_copy(spl[:, 0, :], lnzt),
                   reads=[b_lnz], writes=[b_er[0]])
            tmp32 = self.pool3[:, 7008:7040]
            b_t32 = Buf()
            p.emit("vector", lambda e: e.tensor_sub(tmp32, lnzt, spl[:, 0, :]),
                   reads=[b_lnz, b_er[0]], writes=[b_t32])
            p.emit("vector", lambda e: e.tensor_copy(spl[:, 1, :], tmp32),
                   reads=[b_t32], writes=[b_er[0]])
            p.emit("vector", lambda e: e.tensor_sub(tmp32, tmp32, spl[:, 1, :]),
                   reads=[b_t32, b_er[0]], writes=[b_t32])
            p.emit("vector", lambda e: e.tensor_copy(spl[:, 2, :], tmp32),
                   reads=[b_t32], writes=[b_er[0]])
            p.emit("tensor", lambda e: e.transpose(
                self.PT[0:96, 0:128], spl.rearrange("p v n -> p (v n)"), self.ident[:]),
                reads=[b_er[0], self.b_const], writes=[self.bPT])
            p.emit("vector", lambda e: e.tensor_copy(ering[0:96, 1, 0:128], self.PT[0:96, 0:128]),
                   reads=[self.bPT], writes=[b_er[1]])
            p.emit("sync", lambda e: e.dma_start(out=self.dram["bncb_scr"], in_=ering[0:96, 1, 0:128]),
                   reads=[b_er[1]], writes=[b_bncb], dma=True)
            for v3 in range(3):
                p.emit("sync", (lambda v_=v3, qT_=qT: lambda e: e.dma_start(
                    out=qT_[64 + v_:65 + v_, :],
                    in_=self.dram["bncb_scr"].rearrange("a b -> (a b)").unsqueeze(0)[:, v_ * 4096:(v_ + 1) * 4096]))(),
                    reads=[b_bncb], writes=[bqT], dma=True)
            iar = self.pool3[:, 8576:8704]
            b_iar = Buf()
            swdum = p3[:, 23744:25792].rearrange("p (r n) -> p r n", r=2)
            b_swd = [Buf(), Buf()]
            for c32 in range(NT):
                for qb2 in range(4):
                    pp = [self.P0, self.P1][qb2 % 2]
                    bpp = [self.bP0, self.bP1][qb2 % 2]
                    for mv in range(2):
                        p.emit("tensor", (lambda c_=c32, q_=qb2, mv_=mv, pp_=pp, qT_=qT, kT_=kT: lambda e: e.matmul(
                            pp_[:, mv_ * 512:(mv_ + 1) * 512],
                            kT_[0:67, c_ * 128:(c_ + 1) * 128],
                            qT_[0:67, (q_ * 2 + mv_) * 512:(q_ * 2 + mv_ + 1) * 512],
                            start=True, stop=True))(),
                            reads=[bqT, bkT], writes=[bpp])
                    p.emit("scalar", (lambda c_=c32, q_=qb2, pp_=pp: lambda e: e.activation(
                        swdum[:, q_ % 2, :], pp_[:], AF.Exp,
                        accum_out=iar[:, c_ * 4 + q_:c_ * 4 + q_ + 1]))(),
                        reads=[bpp], writes=[b_swd[qb2 % 2], b_iar])
            p.emit("vector", lambda e: e.tensor_reduce(
                self.pool3[:, 8704:8736], iar.rearrange("p (j c) -> p j c", c=4), AX.X, ALU.add),
                reads=[b_iar], writes=[b_iar])
            p.emit("vector", lambda e: e.tensor_add(iacc, iacc, self.pool3[:, 8704:8736]),
                   reads=[b_iar], writes=[b_st])
            p.emit("vector", lambda e: e.tensor_tensor(
                ering[:, 0:2, :].rearrange("p r n -> p (r n)").rearrange("p (j n) -> p j n", n=64),
                kv_s[:][:, :, 0:64], kv_s[:][:, :, 0:64], ALU.mult),
                reads=[b_kv], writes=[b_er[0], b_er[1]])
            p.emit("vector", lambda e: e.tensor_reduce(
                n2t, ering[:, 0:2, :].rearrange("p r n -> p (r n)").rearrange("p (j n) -> p j n", n=64),
                AX.X, ALU.add),
                reads=[b_er[0], b_er[1]], writes=[b_st])
            p.emit("scalar", lambda e: e.activation(ut, n2t, AF.Ln), reads=[b_st], writes=[b_u])
            p.emit("scalar", lambda e: e.activation(ut, ut, AF.Exp, scale=0.5), reads=[b_u], writes=[b_u])
            p.emit("vector", lambda e: e.reduce_max(zp[:, 30:31], ut, AX.X),
                   reads=[b_u], writes=[self.b_gw])
            p.emit("vector", lambda e: e.tensor_copy(staV[:, 0:1], zp[:, 30:31]),
                   reads=[self.b_gw], writes=[b_staV])
            p.emit("tensor", lambda e: e.transpose(
                self.PT[0:1, 0:128], staV[:, 0:1], self.ident[:]),
                reads=[b_staV, self.b_const], writes=[self.bPT])
            p.emit("vector", lambda e: e.tensor_copy(ering[0:1, 0, 0:128], self.PT[0:1, 0:128]),
                   reads=[self.bPT], writes=[b_er[0]])
            p.emit("vector", lambda e: e.reduce_max(zp[0:1, 31:32], ering[0:1, 0, 0:128], AX.X),
                   reads=[b_er[0]], writes=[self.b_gw])
            p.emit("vector", lambda e: e.tensor_scalar(zp[0:1, 32:33], zp[0:1, 31:32], -1.0, None, ALU.mult),
                   reads=[self.b_gw], writes=[self.b_gw])
            p.emit("tensor", lambda e: e.matmul(
                self.P2[:, 0:1], self.onesf[0:1, :], zp[0:1, 32:33], start=True, stop=True),
                reads=[self.b_gw, self.b_const], writes=[self.bP2])
            p.emit("vector", lambda e: e.tensor_copy(zp[:, 33:34], self.P2[:, 0:1]),
                   reads=[self.bP2], writes=[self.b_gw])
            p.emit("vector", lambda e: e.tensor_scalar(ut, ut, zp[:, 33:34], None, ALU.add),
                   reads=[self.b_gw, b_u], writes=[b_u])
            p.emit("scalar", lambda e: e.activation(ut, ut, AF.Exp, accum_out=zp[:, 34:35]),
                   reads=[b_u], writes=[b_u, self.b_gw])
            p.emit("vector", lambda e: e.tensor_copy(staV[:, 1:2], zp[:, 34:35]),
                   reads=[self.b_gw], writes=[b_staV])
            p.emit("tensor", lambda e: e.matmul(
                self.P2[0:1, 1:2], self.ones_col[:], staV[:, 1:2], start=True, stop=True),
                reads=[b_staV, self.b_const], writes=[self.bP2])
            p.emit("vector", lambda e: e.tensor_copy(zp[0:1, 35:36], self.P2[0:1, 1:2]),
                   reads=[self.bP2], writes=[self.b_gw])
            p.emit("vector", lambda e: e.tensor_scalar(zp[0:1, 35:36], zp[0:1, 35:36], 1e-8, None, ALU.mult),
                   reads=[self.b_gw], writes=[self.b_gw])
            p.emit("tensor", lambda e: e.matmul(
                self.P2[:, 2:3], self.onesf[0:1, :], zp[0:1, 35:36], start=True, stop=True),
                reads=[self.b_gw, self.b_const], writes=[self.bP2])
            p.emit("vector", lambda e: e.tensor_copy(zp[:, 36:37], self.P2[:, 2:3]),
                   reads=[self.bP2], writes=[self.b_gw])
            p.emit("vector", lambda e: e.tensor_copy(uhit, ut), reads=[b_u], writes=[b_u])
            p.emit("vector", lambda e: e.tensor_sub(ulot, ut, uhit), reads=[b_u], writes=[b_u])
            staKall = p3[:, 17472:21696].rearrange("p (j n) -> p j n", j=32)
            vwall = p3[:, 21696:23744].rearrange("p (j n) -> p j n", j=32)
            if h == 0:
                p.emit("vector", lambda e: e.memset(
                    staKall[:].rearrange("p j n -> p (j n)").rearrange("p (j o) -> p j o", o=132)[:, :, 65:129], 0.0),
                    writes=[b_staK])
            p.emit("vector", lambda e: e.tensor_tensor(
                staKall[:, :, 0:64],
                kv_s[:].rearrange("p j n -> p j n")[:, :, 0:64],
                ut.unsqueeze(2).broadcast_to([128, 32, 64]), ALU.mult),
                reads=[b_kv, b_u], writes=[b_staK])
            p.emit("vector", lambda e: e.tensor_copy(staKall[:, :, 64:65], uhit.unsqueeze(2)),
                   reads=[b_u], writes=[b_staK])
            p.emit("vector", lambda e: e.tensor_copy(staKall[:, :, 129:130], ulot.unsqueeze(2)),
                   reads=[b_u], writes=[b_staK])
            p.emit("vector", lambda e: e.tensor_tensor(
                vwall[:], kv_s[:][:, :, 64:128],
                ut.unsqueeze(2).broadcast_to([128, 32, 64]), ALU.mult),
                reads=[b_kv, b_u], writes=[b_staV])
            for j in range(NT):
                p.emit("tensor", (lambda j_=j: lambda e: e.matmul(
                    self.PS[0:65, j_ * 16:(j_ + 1) * 16], staKall[:, j_, 0:65],
                    self.ppool[:], start=True, stop=False))(),
                    reads=[b_staK, self.b_const], writes=[self.bPS])
                p.emit("tensor", (lambda j_=j: lambda e: e.matmul(
                    self.PS[0:65, j_ * 16:(j_ + 1) * 16], staKall[:, j_, 65:130],
                    self.ppool[:], start=False, stop=True))(),
                    reads=[b_staK, self.b_const], writes=[self.bPS])
                p.emit("tensor", (lambda j_=j: lambda e: e.matmul(
                    self.P2[0:64, 512 + j_ * 16: 512 + (j_ + 1) * 16], vwall[:, j_, :],
                    self.ppool[:], start=True, stop=True))(),
                    reads=[b_staV, self.b_const], writes=[self.bP2])
            p.emit("vector", (lambda h_=h: lambda e: e.tensor_copy(
                kcT[h_], self.PS[0:64, :]))(),
                reads=[self.bPS], writes=[b_kcT[h]])
            gsr = self.pool3[:, 7040:7552]
            b_gsr = Buf()
            p.emit("vector", lambda e: e.tensor_copy(gsr[64:65, :], self.PS[64:65, :]),
                   reads=[self.bPS], writes=[b_gsr])
            p.emit("vector", lambda e: e.tensor_scalar(
                gsr[64:65, :], gsr[64:65, :], zp[64:65, 36:37], None, ALU.add),
                reads=[self.b_gw, b_gsr], writes=[b_gsr])
            p.emit("vector", lambda e: e.reciprocal(gsr[64:65, :], gsr[64:65, :]),
                   reads=[b_gsr], writes=[b_gsr])
            p.emit("sync", lambda e: e.dma_start(
                out=self.dram["bnc_scr"][0:1, 0:512], in_=gsr[64:65, :]),
                reads=[b_gsr], writes=[self.dbuf["bnc_scr"]], dma=True)
            p.emit("sync", lambda e: e.dma_start(
                out=gsr[0:4, 0:128],
                in_=self.dram["bnc_scr"][0:1, 0:512].rearrange("o (a b) -> (o a) b", a=4)),
                reads=[self.dbuf["bnc_scr"]], writes=[b_gsr], dma=True)
            p.emit("tensor", lambda e: e.transpose(
                self.PT[:].bitcast(F32)[:, 0:4], gsr[0:4, 0:128], self.identf[0:4, 0:4]),
                reads=[b_gsr, self.b_const], writes=[self.bPT])
            p.emit("vector", (lambda h_=h: lambda e: e.tensor_copy(
                rgcol[:, h_, :], self.PT[:].bitcast(F32)[:, 0:4]))(),
                reads=[self.bPT], writes=[b_rg])
            vct = ering[0:64, 2, 0:512]
            p.emit("vector", lambda e: e.tensor_copy(vct, self.P2[0:64, 512:1024]),
                   reads=[self.bP2], writes=[b_er[2]])
            for cc in range(4):
                p.emit("tensor", (lambda cc_=cc: lambda e: e.transpose(
                    self.PT[:, cc_ * 64:(cc_ + 1) * 64], vct[:, cc_ * 128:(cc_ + 1) * 128],
                    self.ident[0:64, 0:64]))(),
                    reads=[b_er[2], self.b_const], writes=[self.bPT])
                p.emit("vector", (lambda cc_=cc, h_=h: lambda e: e.tensor_scalar(
                    vcn[:, h_, cc_ * 65: cc_ * 65 + 64], self.PT[:, cc_ * 64:(cc_ + 1) * 64],
                    rgcol[:, h_, cc_:cc_ + 1], None, ALU.mult))(),
                    reads=[self.bPT, b_rg], writes=[b_vcn])

        imp = eacc
        p.emit("vector", lambda e: e.tensor_scalar(imp, eacc, -0.4 / 16, None, ALU.mult),
               reads=[b_st], writes=[b_st])
        p.emit("vector", lambda e: e.scalar_tensor_tensor(imp, macc, 0.3 / 16, imp, ALU.mult, ALU.add),
               reads=[b_st], writes=[b_st])
        p.emit("vector", lambda e: e.scalar_tensor_tensor(imp, iacc, 0.3 / 16, imp, ALU.mult, ALU.add),
               reads=[b_st], writes=[b_st])
        g = self.gwork
        sV = self.sb("srch", [128, 8], F32)
        sVb = self.sb("srchb", [128, 2], BF16)
        b_sv = Buf()
        LO, HI, MID, CNT, PRD, T1, T2, MIDC = (sV[:, i:i + 1] for i in range(8))
        p.emit("vector", lambda e: e.reduce_max(g[:, 0:1], imp, AX.X), reads=[b_st], writes=[self.b_gw])
        p.emit("vector", lambda e: e.tensor_reduce(g[:, 1:2], imp, AX.X, ALU.min), reads=[b_st], writes=[self.b_gw])
        p.emit("vector", lambda e: e.tensor_copy(sVb[:, 0:2], g[:, 0:2]), reads=[self.b_gw], writes=[b_sv])
        p.emit("tensor", lambda e: e.transpose(self.PT[0:2, 0:128], sVb[:, 0:2], self.ident[:]),
               reads=[b_sv, self.b_const], writes=[self.bPT])
        p.emit("vector", lambda e: e.tensor_copy(ering[0:2, 0, 0:128], self.PT[0:2, 0:128]),
               reads=[self.bPT], writes=[b_er[0]])
        p.emit("vector", lambda e: e.reduce_max(g[0:1, 2:3], ering[0:1, 0, 0:128], AX.X),
               reads=[b_er[0]], writes=[self.b_gw])
        p.emit("vector", lambda e: e.tensor_copy(sVb[:, 0:1], g[:, 1:2]), reads=[self.b_gw], writes=[b_sv])
        p.emit("tensor", lambda e: e.transpose(self.PT[0:1, 128:256], sVb[:, 0:1], self.ident[:]),
               reads=[b_sv, self.b_const], writes=[self.bPT])
        p.emit("vector", lambda e: e.tensor_copy(ering[0:1, 0, 128:256], self.PT[0:1, 128:256]),
               reads=[self.bPT], writes=[b_er[0]])
        p.emit("vector", lambda e: e.tensor_reduce(g[0:1, 5:6], ering[0:1, 0, 128:256], AX.X, ALU.min),
               reads=[b_er[0]], writes=[self.b_gw])
        p.emit("vector", lambda e: e.tensor_scalar(LO[0:1, :], g[0:1, 5:6], -1.0, None, ALU.add),
               reads=[self.b_gw], writes=[b_sv])
        p.emit("vector", lambda e: e.tensor_scalar(HI[0:1, :], g[0:1, 2:3], 1.0, None, ALU.add),
               reads=[self.b_gw], writes=[b_sv])
        for it in range(40):
            p.emit("vector", lambda e: e.tensor_tensor(MID[0:1, :], LO[0:1, :], HI[0:1, :], ALU.add),
                   reads=[b_sv], writes=[b_sv])
            p.emit("vector", lambda e: e.tensor_scalar(MID[0:1, :], MID[0:1, :], 0.5, None, ALU.mult),
                   reads=[b_sv], writes=[b_sv])
            p.emit("tensor", lambda e: e.matmul(self.P2[:, 3:4], self.onesf[0:1, :], MID[0:1, :],
                                                start=True, stop=True),
                   reads=[b_sv, self.b_const], writes=[self.bP2])
            p.emit("vector", lambda e: e.tensor_copy(MIDC[:], self.P2[:, 3:4]),
                   reads=[self.bP2], writes=[b_sv])
            p.emit("vector", lambda e: e.tensor_scalar(
                ering[:, 3, 0:32], imp, MIDC[:], None, ALU.is_ge, ALU.add, accum_out=CNT[:]),
                reads=[b_st, b_sv], writes=[b_er[3], b_sv])
            p.emit("vector", lambda e: e.tensor_copy(sVb[:, 1:2], CNT[:]), reads=[b_sv], writes=[b_sv])
            p.emit("tensor", lambda e: e.matmul(self.P2[0:1, 4:5], self.ones_col[:], sVb[:, 1:2],
                                                start=True, stop=True),
                   reads=[b_sv, self.b_const], writes=[self.bP2])
            p.emit("vector", lambda e: e.tensor_copy(T1[0:1, :], self.P2[0:1, 4:5]),
                   reads=[self.bP2], writes=[b_sv])
            p.emit("vector", lambda e: e.tensor_scalar(PRD[0:1, :], T1[0:1, :], float(NSEL), None, ALU.is_ge),
                   reads=[b_sv], writes=[b_sv])
            p.emit("vector", lambda e: e.tensor_sub(T1[0:1, :], MID[0:1, :], LO[0:1, :]),
                   reads=[b_sv], writes=[b_sv])
            p.emit("vector", lambda e: e.tensor_tensor(T1[0:1, :], T1[0:1, :], PRD[0:1, :], ALU.mult),
                   reads=[b_sv], writes=[b_sv])
            p.emit("vector", lambda e: e.tensor_add(LO[0:1, :], LO[0:1, :], T1[0:1, :]),
                   reads=[b_sv], writes=[b_sv])
            p.emit("vector", lambda e: e.tensor_sub(T2[0:1, :], MID[0:1, :], HI[0:1, :]),
                   reads=[b_sv], writes=[b_sv])
            p.emit("vector", lambda e: e.tensor_scalar(T1[0:1, :], PRD[0:1, :], 1.0, -1.0, ALU.subtract, ALU.mult),
                   reads=[b_sv], writes=[b_sv])
            p.emit("vector", lambda e: e.tensor_tensor(T2[0:1, :], T2[0:1, :], T1[0:1, :], ALU.mult),
                   reads=[b_sv], writes=[b_sv])
            p.emit("vector", lambda e: e.tensor_add(HI[0:1, :], HI[0:1, :], T2[0:1, :]),
                   reads=[b_sv], writes=[b_sv])
        p.emit("tensor", lambda e: e.matmul(self.P2[:, 5:6], self.onesf[0:1, :], LO[0:1, :],
                                            start=True, stop=True),
               reads=[b_sv, self.b_const], writes=[self.bP2])
        p.emit("vector", lambda e: e.tensor_copy(MIDC[:], self.P2[:, 5:6]), reads=[self.bP2], writes=[b_sv])
        mask = macc
        p.emit("vector", lambda e: e.tensor_scalar(mask, imp, MIDC[:], None, ALU.is_ge),
               reads=[b_st, b_sv], writes=[b_st])
        maskb = ering[:, 3, 0:32]
        p.emit("vector", lambda e: e.tensor_copy(maskb, mask), reads=[b_st], writes=[b_er[3]])
        p.emit("tensor", lambda e: e.matmul(self.P2[:, 32:64], self.uinc[:], maskb,
                                            start=True, stop=False),
               reads=[b_er[3], self.b_const], writes=[self.bP2])
        p.emit("tensor", lambda e: e.matmul(self.P2[0:1, 576:608], self.ones_col[:], maskb,
                                            start=True, stop=True),
               reads=[b_er[3], self.b_const], writes=[self.bP2])
        p.emit("vector", lambda e: e.tensor_copy(ering[0:1, 3, 32:64], self.P2[0:1, 576:608]),
               reads=[self.bP2], writes=[b_er[3]])
        p.emit("tensor", lambda e: e.transpose(self.PT[0:32, 0:1], ering[0:1, 3, 32:64],
                                               self.ident[0:1, 0:1]),
               reads=[b_er[3], self.b_const], writes=[self.bPT])
        p.emit("vector", lambda e: e.tensor_copy(ering[0:32, 3, 64:65], self.PT[0:32, 0:1]),
               reads=[self.bPT], writes=[b_er[3]])
        p.emit("tensor", lambda e: e.matmul(self.P2[0:1, 608:640], ering[0:32, 3, 64:65], self.l32[:],
                                            start=True, stop=True),
               reads=[b_er[3], self.b_const], writes=[self.bP2])
        p.emit("vector", lambda e: e.tensor_copy(self.pool3[0:1, 7552:7584], self.P2[0:1, 608:640]),
               reads=[self.bP2], writes=[b_er[3]])
        p.emit("tensor", lambda e: e.matmul(self.P2[:, 32:64], self.onesf[0:1, :],
                                            self.pool3[0:1, 7552:7584], start=False, stop=True),
               reads=[b_er[3], self.b_const], writes=[self.bP2])
        posm1 = iacc
        p.emit("vector", lambda e: e.tensor_scalar(posm1, self.P2[:, 32:64], -1.0, None, ALU.add),
               reads=[self.bP2], writes=[b_st])

        self.drain_precast()
        p.barrier()
        outN = self.pool2[:, 0:32768].rearrange("p (c n) -> p c n", c=8)
        b_outN = Buf()
        iota512 = self.pool3[:, 7584:8096]
        b_iota = Buf()
        selj = p3[:, 16192:16704]
        b_selj = Buf()
        xnselT = p3[:, 17216:21312].rearrange("p (c n) -> p c n", c=8)
        b_xsel = Buf()
        qT2 = p3[:, 21312:25408]
        b_qT2 = Buf()
        kselT = p3[0:64, 25408:25920]
        b_ksel = Buf()
        vsel = p3[:, 25920:26180].rearrange("p (s n) -> p s n", s=4)
        b_vsel = Buf()
        esel = p3[:, 26180:28228]
        b_esel = Buf()
        ec = p3[:, 28228:30276]
        b_ec = Buf()
        zrow = self.pool3[:, 15168:15680]
        b_zrow = Buf()
        replS = self.pool3[:, 15680:16192]
        b_repl = Buf()
        padcol = self.sb("padcol", [128, 4], F32)
        b_pad = Buf()
        p.emit("sync", lambda e: e.dma_start(out=padcol[:], in_=self.dram["c_padbias"]),
               writes=[b_pad], dma=True)
        p.emit("gpsimd", lambda e: e.iota(iota512, pattern=[[1, 512]], base=0,
                                          channel_multiplier=0,
                                          allow_small_or_imprecise_dtypes=True),
               writes=[b_iota])
        p.emit("vector", lambda e: e.memset(
            vsel[:].rearrange("p s n -> p (s n)").rearrange("p (g o) -> p g o", o=65)[:, :, 64:65], 1.0),
            writes=[b_vsel])
        for pas in range(2):
            for j in range(NT):
                xt = self.xtile[:, j % 2, :]
                bxt = self.b_xt[j % 2]
                p.emit("sync", (lambda j_=j, xt_=xt: lambda e: e.dma_start(
                    out=xt_, in_=src[j_ * 128:(j_ + 1) * 128, :]))(),
                    reads=[bsrc], writes=[bxt], dma=True)
                xn = self.xntile[:, j % 2, :]
                bxn = self.b_xn[j % 2]
                self.ln_tile(xt, bxt, xn, bxn, self.lnsm, self.b_lnsm,
                             dummy=self.P2, bdummy=self.bP2)
                p.emit("vector", (lambda j_=j: lambda e: e.tensor_scalar(
                    selj, iota512, posm1[:, j_:j_ + 1], None, ALU.is_equal))(),
                    reads=[b_iota, b_st], writes=[b_selj])
                p.emit("vector", (lambda j_=j: lambda e: e.tensor_scalar(
                    selj, selj, mask[:, j_:j_ + 1], None, ALU.mult))(),
                    reads=[b_st, b_selj], writes=[b_selj])
                for ds in range(4):
                    dslice = pas * 4 + ds
                    pp = [self.P0, self.P1][ds // 2]
                    bpp = [self.bP0, self.bP1][ds // 2]
                    p.emit("tensor", (lambda j_=j, ds_=ds, dsl_=dslice, pp_=pp, bxn_=bxn, xn_=xn: lambda e: e.matmul(
                        pp_[:, (ds_ % 2) * 512:(ds_ % 2 + 1) * 512],
                        xn_[:, dsl_ * 128:(dsl_ + 1) * 128], selj,
                        start=(j_ == 0), stop=(j_ == NT - 1)))(),
                        reads=[bxn, b_selj], writes=[bpp])
            for ds in range(4):
                dslice = pas * 4 + ds
                pp = [self.P0, self.P1][ds // 2]
                bpp = [self.bP0, self.bP1][ds // 2]
                p.emit("vector", (lambda ds_=ds, dsl_=dslice, pp_=pp: lambda e: e.tensor_copy(
                    xnselT[:, dsl_, :], pp_[:, (ds_ % 2) * 512:(ds_ % 2 + 1) * 512]))(),
                    reads=[bpp], writes=[b_xsel])
        for h in range(NH):
            load_wslice("Wq", h, wsl[:, 0, :], b_wsl[0])
            wq8 = wsl[:, 0, :].rearrange("p (c n) -> p c n", c=8)
            for tb8 in range(8):
                for c in range(8):
                    p.emit("tensor", (lambda tb_=tb8, c_=c: lambda e: e.matmul(
                        self.PS[0:64, :], wq8[:, c_, :],
                        xnT[:, c_, tb_ * 512:(tb_ + 1) * 512],
                        start=(c_ == 0), stop=(c_ == 7)))(),
                        reads=[b_wsl[0], b_xnT], writes=[self.bPS])
                p.emit("scalar", (lambda tb_=tb8: lambda e: e.activation(
                    qT2[0:64, tb_ * 512:(tb_ + 1) * 512], self.PS[0:64, :],
                    AF.Copy, scale=0.125))(),
                    reads=[self.bPS], writes=[b_qT2])
            load_wslice("Wk", h, wsl[:, 1, :], b_wsl[1])
            wk8 = wsl[:, 1, :].rearrange("p (c n) -> p c n", c=8)
            for c in range(8):
                p.emit("tensor", (lambda c_=c: lambda e: e.matmul(
                    self.PS[0:64, :], wk8[:, c_, :], xnselT[:, c_, :],
                    start=(c_ == 0), stop=(c_ == 7)))(),
                    reads=[b_wsl[1], b_xsel], writes=[self.bPS])
            p.emit("vector", lambda e: e.tensor_copy(kselT, self.PS[0:64, :]),
                   reads=[self.bPS], writes=[b_ksel])
            load_wslice("Wv", h, wsl[:, 0, :], b_wsl[0])
            wv8 = wsl[:, 0, :].rearrange("p (c n) -> p c n", c=8)
            for s4 in range(4):
                for c in range(8):
                    p.emit("tensor", (lambda s_=s4, c_=c: lambda e: e.matmul(
                        self.P2[:, s_ * 64:(s_ + 1) * 64],
                        xnselT[:, c_, s_ * 128:(s_ + 1) * 128], wv8[:, c_, :],
                        start=(c_ == 0), stop=(c_ == 7)))(),
                        reads=[b_wsl[0], b_xsel], writes=[self.bP2])
                p.emit("vector", (lambda s_=s4: lambda e: e.tensor_copy(
                    vsel[:, s_, 0:64], self.P2[:, s_ * 64:(s_ + 1) * 64]))(),
                    reads=[self.bP2], writes=[b_vsel])
            for qb in range(8):
                for s4 in range(4):
                    pp = [self.P0, self.P1][s4 // 2]
                    bpp = [self.bP0, self.bP1][s4 // 2]
                    p.emit("tensor", (lambda s_=s4, qb_=qb, pp_=pp: lambda e: e.matmul(
                        pp_[:, (s_ % 2) * 512:(s_ % 2 + 1) * 512],
                        kselT[:, s_ * 128:(s_ + 1) * 128],
                        qT2[0:64, qb_ * 512:(qb_ + 1) * 512],
                        start=True, stop=True))(),
                        reads=[b_ksel, b_qT2], writes=[bpp])
                for s4 in range(4):
                    pp = [self.P0, self.P1][s4 // 2]
                    bpp = [self.bP0, self.bP1][s4 // 2]
                    p.emit("scalar", (lambda s_=s4, pp_=pp: lambda e: e.activation(
                        esel[:, s_ * 512:(s_ + 1) * 512], pp_[:, (s_ % 2) * 512:(s_ % 2 + 1) * 512],
                        AF.Exp, bias=padcol[:, s_:s_ + 1]))(),
                        reads=[bpp, b_pad], writes=[b_esel])
                for s4 in range(4):
                    pp = [self.P0, self.P1][s4 // 2]
                    bpp = [self.bP0, self.bP1][s4 // 2]
                    p.emit("tensor", (lambda s_=s4, qb_=qb, h_=h, pp_=pp: lambda e: e.matmul(
                        pp_[:, (s_ % 2) * 512:(s_ % 2 + 1) * 512],
                        kcT[h_][:, s_ * 128:(s_ + 1) * 128],
                        qT2[0:64, qb_ * 512:(qb_ + 1) * 512],
                        start=True, stop=True))(),
                        reads=[b_kcT[h], b_qT2], writes=[bpp])
                for s4 in range(4):
                    pp = [self.P0, self.P1][s4 // 2]
                    bpp = [self.bP0, self.bP1][s4 // 2]
                    p.emit("scalar", (lambda s_=s4, h_=h, pp_=pp: lambda e: e.activation(
                        ec[:, s_ * 512:(s_ + 1) * 512], pp_[:, (s_ % 2) * 512:(s_ % 2 + 1) * 512],
                        AF.Exp, scale=rgcol[:, h_, s_:s_ + 1]))(),
                        reads=[bpp, b_rg], writes=[b_ec])
                for s4 in range(4):
                    p.emit("tensor", (lambda s_=s4: lambda e: e.matmul(
                        self.PS[0:65, :], vsel[:, s_, :], esel[:, s_ * 512:(s_ + 1) * 512],
                        start=(s_ == 0), stop=False))(),
                        reads=[b_vsel, b_esel], writes=[self.bPS])
                for s4 in range(4):
                    p.emit("tensor", (lambda s_=s4, h_=h: lambda e: e.matmul(
                        self.PS[0:65, :], vcn[:, h_, s_ * 65:(s_ + 1) * 65],
                        ec[:, s_ * 512:(s_ + 1) * 512],
                        start=False, stop=(s_ == 3)))(),
                        reads=[b_vcn, b_ec], writes=[self.bPS])
                p.emit("vector", lambda e: e.tensor_copy(zrow[64:65, :], self.PS[64:65, :]),
                       reads=[self.bPS], writes=[b_zrow])
                p.emit("vector", lambda e: e.reciprocal(zrow[64:65, :], zrow[64:65, :]),
                       reads=[b_zrow], writes=[b_zrow])
                p.emit("tensor", lambda e: e.matmul(
                    self.P2[0:64, 0:512], self.onesf[64:65, 0:64], zrow[64:65, :],
                    start=True, stop=True),
                    reads=[b_zrow, self.b_const], writes=[self.bP2])
                p.emit("vector", lambda e: e.tensor_copy(replS[0:64, :], self.P2[0:64, 0:512]),
                       reads=[self.bP2], writes=[b_repl])
                p.emit("vector", (lambda h_=h, qb_=qb: lambda e: e.tensor_tensor(
                    outN[(h_ % 2) * 64:(h_ % 2) * 64 + 64, h_ // 2, qb_ * 512:(qb_ + 1) * 512],
                    self.PS[0:64, :], replS[0:64, :], ALU.mult))(),
                    reads=[self.bPS, b_repl], writes=[b_outN])
        p.barrier()
        Wo = p3[:, 17216:25408].rearrange("p (c n) -> p c n", c=8)
        b_Wo = Buf()
        for c in range(8):
            sl = c % 2
            p.emit("sync", (lambda c_=c, sl_=sl: lambda e: e.dma_start(
                out=self.wstg[:, sl_, :], in_=w["Wo"][c_ * 128:(c_ + 1) * 128, :]))(),
                writes=[self.b_wstg[sl]], dma=True)
            p.emit("vector", (lambda c_=c, sl_=sl: lambda e: e.tensor_copy(
                Wo[:, c_, :], self.wstg[:, sl_, :]))(),
                reads=[self.b_wstg[sl]], writes=[b_Wo])
        outtile = self.pool3[:, 13090:15138].rearrange("p (r n) -> p r n", r=2)
        b_ot = [Buf(), Buf()]
        outs = []
        for j in range(NT):
            pp = [self.P2, self.P0][j % 2]
            bpp = [self.bP2, self.bP0][j % 2]
            for mv in range(2):
                for c in range(8):
                    p.emit("tensor", (lambda j_=j, c_=c, mv_=mv, pp_=pp: lambda e: e.matmul(
                        pp_[:, mv_ * 512:(mv_ + 1) * 512],
                        outN[:, c_, j_ * 128:(j_ + 1) * 128],
                        Wo[:, c_, mv_ * 512:(mv_ + 1) * 512],
                        start=(c_ == 0), stop=(c_ == 7)))(),
                        reads=[b_outN, b_Wo], writes=[bpp])
            xt = self.xtile[:, j % 2, :]
            bxt = self.b_xt[j % 2]
            p.emit("sync", (lambda j_=j, xt_=xt: lambda e: e.dma_start(
                out=xt_, in_=src[j_ * 128:(j_ + 1) * 128, :]))(),
                reads=[bsrc], writes=[bxt], dma=True)
            p.emit("vector", (lambda j_=j, pp_=pp, xt_=xt: lambda e: e.scalar_tensor_tensor(
                outtile[:, j_ % 2, :], pp_[:], 1.0, xt_, ALU.mult, ALU.add))(),
                reads=[bpp, bxt], writes=[b_ot[j % 2]])
            outs.append(p.emit("sync", (lambda j_=j: lambda e: e.dma_start(
                out=dst[j_ * 128:(j_ + 1) * 128, :], in_=outtile[:, j_ % 2, :]))(),
                reads=[b_ot[j % 2]], writes=[bdst], dma=True))
        return outs

    def stage_attn_b(self, src, bsrc, dst, bdst, w):
        p = self.p
        xn2T = self.pool1[:, 0:32768].rearrange("p (c n) -> p c n", c=8)
        b_xn2T = self.b_pool1
        xo_all = self.pool2[:, 0:32768].rearrange("p (j n) -> p j n", j=32)
        b_xo = self.b_pool2
        p3 = self.pool3[:].bitcast(BF16)
        qkT = p3[:, 0:4096]
        b_qkT = Buf()
        vpair = p3[:, 4096:8192].rearrange("p (j n) -> p j n", j=32)
        b_vp = Buf()
        KEp = p3[:, 8192:9216].rearrange("p (j n) -> p j n", j=32)
        b_KE = Buf()
        qhp = p3[:, 9216:10240].rearrange("p (j n) -> p j n", j=32)
        b_qh = Buf()
        OFF = [self.pool3[0:1, 5120 + i * 1040: 5120 + (i + 1) * 1040] for i in range(2)]
        b_OFF = [Buf(), Buf()]
        wring = p3[:, 14592:20736].rearrange("p (r n) -> p r n", r=2)
        b_wr = [Buf(), Buf()]
        ktil = p3[0:16, 20736:24832]
        b_ktil = Buf()
        ktex = p3[0:16, 24832:28928]
        b_ktex = Buf()
        ab = p3[:, 28928:29968]
        b_ab = Buf()
        xp = p3[:, 29968:30992]
        b_xp = Buf()
        kvsb = p3[:, 31040:32080]
        b_kvsb = Buf()
        OFFB = [p3[0:1, 10240 + i * 1040: 10240 + (i + 1) * 1040] for i in range(2)]
        xored = self.pool3[:, 16064:16128]
        b_xor = Buf()
        codeT = p3[:, 30992:31024].rearrange("p (h n) -> p h n", h=2)
        b_code = Buf()
        kmx = self.gwork[0:16, 40:42]
        for j in range(NT):
            xt = self.xtile[:, j % 2, :]
            bxt = self.b_xt[j % 2]
            p.emit("sync", (lambda j_=j, xt_=xt: lambda e: e.dma_start(
                out=xt_, in_=src[j_ * 128:(j_ + 1) * 128, :]))(),
                reads=[bsrc], writes=[bxt], dma=True)
            xn = self.xntile[:, j % 2, :]
            bxn = self.b_xn[j % 2]
            self.ln_tile(xt, bxt, xn, bxn, self.lnsm, self.b_lnsm)
            self.transpose8(xn, bxn, xn2T[:, :, j * 128:(j + 1) * 128], b_xn2T)
        for pr in range(NH // 2):
            wr = wring[:, pr % 2, :].rearrange("p (b n) -> p b n", b=3)
            bwr = b_wr[pr % 2]
            for b3 in range(3):
                sl = b3 % 2
                p.emit("sync", (lambda b3_=b3, sl_=sl: lambda e: e.dma_start(
                    out=self.wstg[:, sl_, :],
                    in_=w["w_qkv"][b3_ * 1024 + pr * 128: b3_ * 1024 + (pr + 1) * 128, :]))(),
                    writes=[self.b_wstg[sl]], dma=True)
                p.emit("vector", (lambda sl_=sl: lambda e: e.tensor_copy(
                    self.w1b[:, sl_, :], self.wstg[:, sl_, :]))(),
                    reads=[self.b_wstg[sl]], writes=[self.b_w1[sl]])
                for c in range(8):
                    p.emit("tensor", (lambda c_=c, sl_=sl: lambda e: e.transpose(
                        self.PT[:, c_ * 128:(c_ + 1) * 128],
                        self.w1b[:, sl_, c_ * 128:(c_ + 1) * 128], self.ident[:]))(),
                        reads=[self.b_w1[sl], self.b_const], writes=[self.bPT])
                p.emit("vector", (lambda b3_=b3: lambda e: e.tensor_copy(
                    wr[:, b3_, :].rearrange("p (c n) -> p c n", c=8).transpose([0, 2, 1]).rearrange("p n c -> p (n c)") if False else
                    wr[:, b3_, :],
                    self.PT[:]))(),
                    reads=[self.bPT], writes=[bwr])
            for hip in range(2):
                h = pr * 2 + hip
                p.emit("sync", (lambda h_=h: lambda e: e.dma_start(
                    out=self.wstg[0:16, 0, 0:64], in_=w["fc_code"][0, h_]))(),
                    writes=[self.b_wstg[0]], dma=True)
                p.emit("vector", lambda e: e.tensor_copy(
                    self.w1b[0:16, 0, 0:64], self.wstg[0:16, 0, 0:64]),
                    reads=[self.b_wstg[0]], writes=[self.b_w1[0]])
                p.emit("tensor", lambda e: e.transpose(
                    self.PT[0:64, 0:16], self.w1b[0:16, 0, 0:64], self.ident[0:16, 0:16]),
                    reads=[self.b_w1[0], self.b_const], writes=[self.bPT])
                p.emit("tensor", lambda e: e.transpose(
                    self.PT[64:128, 16:32], self.w1b[0:16, 0, 0:64], self.ident[0:16, 0:16]),
                    reads=[self.b_w1[0], self.b_const], writes=[self.bPT])
                p.emit("scalar", (lambda hip_=hip: lambda e: e.activation(
                    codeT[0:64, hip_, :], self.PT[0:64, 0:16], AF.Copy, scale=0.125))(),
                    reads=[self.bPT], writes=[b_code])
                p.emit("scalar", (lambda hip_=hip: lambda e: e.activation(
                    codeT[64:128, hip_, :], self.PT[64:128, 16:32], AF.Copy, scale=0.125))(),
                    reads=[self.bPT], writes=[b_code])
                for half64 in range(2):
                    for tb8 in range(8):
                        for c in range(8):
                            p.emit("tensor", (lambda half_=half64, tb_=tb8, c_=c, hip_=hip: lambda e: e.matmul(
                                self.PS[0:64, :],
                                wr[:, half_, c_ * 128 + hip_ * 64: c_ * 128 + hip_ * 64 + 64],
                                xn2T[:, c_, tb_ * 512:(tb_ + 1) * 512],
                                start=(c_ == 0), stop=(c_ == 7)))(),
                                reads=[bwr, b_xn2T], writes=[self.bPS])
                        p.emit("vector", (lambda half_=half64, tb_=tb8: lambda e: e.tensor_copy(
                            qkT[half_ * 64:(half_ + 1) * 64, tb_ * 512:(tb_ + 1) * 512],
                            self.PS[0:64, :]))(),
                            reads=[self.bPS], writes=[b_qkT])
                for j in range(NT):
                    for c in range(8):
                        p.emit("tensor", (lambda j_=j, c_=c, hip_=hip: lambda e: e.matmul(
                            self.PS[:, 0:64],
                            xn2T[:, c_, j_ * 128:(j_ + 1) * 128],
                            wr[:, 2, c_ * 128 + hip_ * 64: c_ * 128 + hip_ * 64 + 64],
                            start=(c_ == 0), stop=(c_ == 7)))(),
                            reads=[bwr, b_xn2T], writes=[self.bPS])
                    p.emit("vector", (lambda j_=j, hip_=hip: lambda e: e.tensor_copy(
                        vpair[:, j_, hip_ * 64:(hip_ + 1) * 64], self.PS[:, 0:64]))(),
                        reads=[self.bPS], writes=[b_vp])
                for tb8 in range(8):
                    p.emit("tensor", (lambda tb_=tb8, hip_=hip: lambda e: e.matmul(
                        self.PS[0:16, :], codeT[64:128, hip_, :],
                        qkT[64:128, tb_ * 512:(tb_ + 1) * 512],
                        start=True, stop=True))(),
                        reads=[b_code, b_qkT], writes=[self.bPS])
                    p.emit("vector", (lambda tb_=tb8: lambda e: e.tensor_copy(
                        ktil[:, tb_ * 512:(tb_ + 1) * 512], self.PS[0:16, :]))(),
                        reads=[self.bPS], writes=[b_ktil])
                p.emit("vector", lambda e: e.reduce_max(kmx[:, 0:1], ktil, AX.X),
                       reads=[b_ktil], writes=[self.b_gw])
                p.emit("vector", lambda e: e.tensor_scalar(kmx[:, 1:2], kmx[:, 0:1], -1.0, None, ALU.mult),
                       reads=[self.b_gw], writes=[self.b_gw])
                p.emit("scalar", lambda e: e.activation(ktex, ktil, AF.Exp, bias=kmx[0:16, 1:2]),
                       reads=[b_ktil, self.b_gw], writes=[b_ktex])
                for half16 in range(2):
                    for jj in range(16):
                        j = half16 * 16 + jj
                        p.emit("tensor", (lambda j_=j, jj_=jj: lambda e: e.transpose(
                            self.PT[:, jj_ * 16:(jj_ + 1) * 16],
                            ktex[:, j_ * 128:(j_ + 1) * 128], self.ident[0:16, 0:16]))(),
                            reads=[b_ktex, self.b_const], writes=[self.bPT])
                    p.emit("vector", (lambda half_=half16, hip_=hip: lambda e: e.tensor_copy(
                        KEp[:, half_ * 16:(half_ + 1) * 16, hip_ * 16:(hip_ + 1) * 16],
                        self.PT[:, 0:256].rearrange("p (j n) -> p j n", j=16)))(),
                        reads=[self.bPT], writes=[b_KE])
                for j in range(NT):
                    p.emit("tensor", (lambda j_=j, hip_=hip: lambda e: e.matmul(
                        self.PS[:, j_ * 16:(j_ + 1) * 16], qkT[0:64, j_ * 128:(j_ + 1) * 128],
                        codeT[0:64, hip_, :], start=True, stop=True))(),
                        reads=[b_qkT, b_code], writes=[self.bPS])
                qsm = self.pool3[:, 8096:8608]
                qmx = self.pool3[:, 8608:8640]
                b_qs = Buf()
                p.emit("vector", lambda e: e.tensor_reduce(
                    qmx, self.PS[:].rearrange("p (j n) -> p j n", n=16), AX.X, ALU.max),
                    reads=[self.bPS], writes=[b_qs])
                p.emit("vector", lambda e: e.tensor_tensor(
                    qsm.rearrange("p (j n) -> p j n", n=16),
                    self.PS[:].rearrange("p (j n) -> p j n", n=16),
                    qmx.unsqueeze(2).broadcast_to([128, 32, 16]), ALU.subtract),
                    reads=[self.bPS, b_qs], writes=[b_qs])
                p.emit("scalar", lambda e: e.activation(qsm, qsm, AF.Exp),
                       reads=[b_qs], writes=[b_qs])
                p.emit("vector", lambda e: e.tensor_reduce(
                    qmx, qsm.rearrange("p (j n) -> p j n", n=16), AX.X, ALU.add),
                    reads=[b_qs], writes=[b_qs])
                p.emit("vector", lambda e: e.reciprocal(qmx, qmx), reads=[b_qs], writes=[b_qs])
                p.emit("vector", (lambda hip_=hip: lambda e: e.tensor_tensor(
                    qhp[:, :, hip_ * 16:(hip_ + 1) * 16],
                    qsm.rearrange("p (j n) -> p j n", n=16),
                    qmx.unsqueeze(2).broadcast_to([128, 32, 16]), ALU.mult))(),
                    reads=[b_qs], writes=[b_qh])
            for hip in range(2):
                p.emit("vector", (lambda hip_=hip: lambda e: e.memset(OFFB[hip_][:], 0.0))(),
                       writes=[b_OFF[hip]])
            for tb in range(NT):
                for hip in range(2):
                    h = pr * 2 + hip
                    PKV = [self.P0, self.P1][(tb * 2 + hip) % 2]
                    bPKV = [self.bP0, self.bP1][(tb * 2 + hip) % 2]
                    p.emit("vector", (lambda tb_=tb, hip_=hip: lambda e: e.tensor_tensor(
                        ab[:, 0:1024].rearrange("p (c d) -> p c d", c=16),
                        KEp[:, tb_, hip_ * 16:(hip_ + 1) * 16].unsqueeze(2).broadcast_to([128, 16, 64]),
                        vpair[:, tb_, hip_ * 64:(hip_ + 1) * 64].unsqueeze(1).broadcast_to([128, 16, 64]),
                        ALU.mult))(),
                        reads=[b_KE, b_vp], writes=[b_ab])
                    p.emit("vector", (lambda tb_=tb, hip_=hip: lambda e: e.tensor_copy(
                        ab[:, 1024:1040], KEp[:, tb_, hip_ * 16:(hip_ + 1) * 16]))(),
                        reads=[b_KE], writes=[b_ab])
                    for mv in range(2):
                        p.emit("tensor", (lambda mv_=mv, PKV_=PKV: lambda e: e.matmul(
                            PKV_[:, mv_ * 512:(mv_ + 1) * 512], self.uinc[:],
                            ab[:, mv_ * 512:(mv_ + 1) * 512], start=True, stop=False))(),
                            reads=[self.b_const, b_ab], writes=[bPKV])
                    p.emit("tensor", lambda e: e.matmul(
                        self.PS[:, 0:16], self.uinc[:], ab[:, 1024:1040],
                        start=True, stop=False),
                        reads=[self.b_const, b_ab], writes=[self.bPS])
                    for mv in range(2):
                        p.emit("tensor", (lambda mv_=mv, hip_=hip, PKV_=PKV: lambda e: e.matmul(
                            PKV_[:, mv_ * 512:(mv_ + 1) * 512], self.ones_row_b[0:1, :],
                            OFFB[hip_][:, mv_ * 512:(mv_ + 1) * 512],
                            start=False, stop=True))(),
                            reads=[self.b_const, b_OFF[hip]], writes=[bPKV])
                    p.emit("tensor", (lambda hip_=hip: lambda e: e.matmul(
                        self.PS[:, 0:16], self.ones_row_b[0:1, :],
                        OFFB[hip_][:, 1024:1040], start=False, stop=True))(),
                        reads=[self.b_const, b_OFF[hip]], writes=[self.bPS])
                    g = self.gwork
                    p.emit("vector", lambda e: e.tensor_scalar(
                        g[:, 20:36], self.PS[:, 0:16], self.eps9[:], None, ALU.add),
                        reads=[self.bPS, self.b_const], writes=[self.b_gw])
                    p.emit("vector", lambda e: e.reciprocal(g[:, 20:36], g[:, 20:36]),
                           reads=[self.b_gw], writes=[self.b_gw])
                    p.emit("vector", (lambda tb_=tb, hip_=hip: lambda e: e.scalar_tensor_tensor(
                        g[:, 20:36], qhp[:, tb_, hip_ * 16:(hip_ + 1) * 16], 0.125,
                        g[:, 20:36], ALU.mult, ALU.mult))(),
                        reads=[b_qh, self.b_gw], writes=[self.b_gw])
                    p.emit("scalar", (lambda PKV_=PKV: lambda e: e.activation(
                        kvsb[:, 0:1024], PKV_[:], AF.Copy))(),
                        reads=[bPKV], writes=[b_kvsb])
                    p.emit("scalar", lambda e: e.activation(
                        kvsb[:, 1024:1040], self.PS[:, 0:16], AF.Copy),
                        reads=[self.bPS], writes=[b_kvsb])
                    if tb < NT - 1:
                        p.emit("sync", (lambda hip_=hip: lambda e: e.dma_start(
                            out=OFFB[hip_][:], in_=kvsb[127:128, :]))(),
                            reads=[b_kvsb], writes=[b_OFF[hip]], dma=True)
                    p.emit("vector", lambda e: e.tensor_tensor(
                        xp[:].rearrange("p (c d) -> p c d", c=16),
                        g[:, 20:36].unsqueeze(2).broadcast_to([128, 16, 64]),
                        kvsb[:, 0:1024].rearrange("p (c d) -> p c d", c=16), ALU.mult),
                        reads=[self.b_gw, b_kvsb], writes=[b_xp])
                    p.emit("vector", lambda e: e.tensor_reduce(
                        xored, xp[:].rearrange("p (c d) -> p d c", c=16), AX.X, ALU.add),
                        reads=[b_xp], writes=[b_xor])
                    p.emit("vector", (lambda tb_=tb, h_=h: lambda e: e.tensor_copy(
                        xo_all[:, tb_, h_ * 64:(h_ + 1) * 64], xored))(),
                        reads=[b_xor], writes=[b_xo])
        p.barrier()
        woT = p3[:, 0:8192].rearrange("p (c n) -> p c n", c=8)
        b_woT = Buf()
        for r in range(8):
            sl = r % 2
            p.emit("sync", (lambda r_=r, sl_=sl: lambda e: e.dma_start(
                out=self.wstg[:, sl_, :], in_=w["w_out"][r_ * 128:(r_ + 1) * 128, :]))(),
                writes=[self.b_wstg[sl]], dma=True)
            p.emit("vector", (lambda sl_=sl: lambda e: e.tensor_copy(
                self.w1b[:, sl_, :], self.wstg[:, sl_, :]))(),
                reads=[self.b_wstg[sl]], writes=[self.b_w1[sl]])
            for c in range(8):
                p.emit("tensor", (lambda c_=c, sl_=sl: lambda e: e.transpose(
                    self.PT[:, c_ * 128:(c_ + 1) * 128],
                    self.w1b[:, sl_, c_ * 128:(c_ + 1) * 128], self.ident[:]))(),
                    reads=[self.b_w1[sl], self.b_const], writes=[self.bPT])
            p.emit("vector", (lambda r_=r: lambda e: e.tensor_copy(
                woT[:, :, r_ * 128:(r_ + 1) * 128],
                self.PT[:].rearrange("p (c n) -> p c n", c=8)))(),
                reads=[self.bPT], writes=[b_woT])
        xoT = xn2T
        b_xoT = b_xn2T
        for j in range(NT):
            for c in range(8):
                p.emit("tensor", (lambda j_=j, c_=c: lambda e: e.transpose(
                    self.PT[:, c_ * 128:(c_ + 1) * 128],
                    xo_all[:, j_, c_ * 128:(c_ + 1) * 128], self.ident[:]))(),
                    reads=[b_xo, self.b_const], writes=[self.bPT])
            p.emit("vector", (lambda j_=j: lambda e: e.tensor_copy(
                xoT[:, :, j_ * 128:(j_ + 1) * 128],
                self.PT[:].rearrange("p (c n) -> p c n", c=8)))(),
                reads=[self.bPT], writes=[b_xoT])
        outtile = self.pool3[:, 10240:12288].rearrange("p (r n) -> p r n", r=2)
        b_ot = [Buf(), Buf()]
        outs = []
        for j in range(NT):
            pp = [self.P2, self.P0][j % 2]
            bpp = [self.bP2, self.bP0][j % 2]
            for mv in range(2):
                for c in range(8):
                    p.emit("tensor", (lambda j_=j, c_=c, mv_=mv, pp_=pp: lambda e: e.matmul(
                        pp_[:, mv_ * 512:(mv_ + 1) * 512],
                        xoT[:, c_, j_ * 128:(j_ + 1) * 128],
                        woT[:, c_, mv_ * 512:(mv_ + 1) * 512],
                        start=(c_ == 0), stop=(c_ == 7)))(),
                        reads=[b_xoT, b_woT], writes=[bpp])
            xt = self.xtile[:, j % 2, :]
            bxt = self.b_xt[j % 2]
            p.emit("sync", (lambda j_=j, xt_=xt: lambda e: e.dma_start(
                out=xt_, in_=src[j_ * 128:(j_ + 1) * 128, :]))(),
                reads=[bsrc], writes=[bxt], dma=True)
            p.emit("vector", (lambda j_=j, pp_=pp, xt_=xt: lambda e: e.scalar_tensor_tensor(
                outtile[:, j_ % 2, :], pp_[:], 1.0, xt_, ALU.mult, ALU.add))(),
                reads=[bpp, bxt], writes=[b_ot[j % 2]])
            outs.append(p.emit("sync", (lambda j_=j: lambda e: e.dma_start(
                out=dst[j_ * 128:(j_ + 1) * 128, :], in_=outtile[:, j_ % 2, :]))(),
                reads=[b_ot[j % 2]], writes=[bdst], dma=True))
        return outs


def build_attn_b_only():
    nc = bass.Bass("TRN2", target_bir_lowering=False, dynamic_dma_scratch_size=8192)
    p = Prog()
    kb = KB(nc, p)
    src = kb.din("xsrc", [S, D])
    dst = kb.dout("out", [S, D])
    w = {
        "w_qkv": kb.din("w_qkv", [3 * D, D]),
        "w_out": kb.din("w_out", [D, D]),
        "fc_code": kb.din("fc_code", [1, NH, 16, DH]),
    }
    for nm, arr in host_consts().items():
        kb.din(nm, arr.shape, BF16 if arr.dtype == bf else F32)
    with nc.Block() as block:
        kb.setup(host_consts())
        outs = kb.stage_attn_b(src, kb.dbuf["xsrc"], dst, kb.dbuf["out"], w)
        p.finalize(nc, block, final_extra=outs)
    return nc


def build_ffn_only():
    nc = bass.Bass("TRN2", target_bir_lowering=False, dynamic_dma_scratch_size=8192)
    p = Prog()
    kb = KB(nc, p)
    src = kb.din("xsrc", [S, D])
    dst = kb.dout("out", [S, D])
    w = {
        "router_w": kb.din("router_w", [D, NE]),
        "router_b": kb.din("router_b", [NE]),
        "w1": kb.din("w1", [NE, D, HME]),
        "b1": kb.din("b1", [NE, HME]),
        "w2": kb.din("w2", [NE, HME, D]),
        "b2": kb.din("b2", [NE, D]),
        "sw1": kb.din("sw1", [NSH, D, HME]),
        "sb1": kb.din("sb1", [NSH, HME]),
        "sw2": kb.din("sw2", [NSH, HME, D]),
        "sb2": kb.din("sb2", [NSH, D]),
    }
    for nm, arr in host_consts().items():
        kb.din(nm, arr.shape, BF16 if arr.dtype == bf else F32)
    with nc.Block() as block:
        kb.setup(host_consts())
        outs = kb.stage_ffn(src, kb.dbuf["xsrc"], dst, kb.dbuf["out"], w)
        p.finalize(nc, block, final_extra=outs)
    return nc


def build_attn_a_only():
    nc = bass.Bass("TRN2", target_bir_lowering=False, dynamic_dma_scratch_size=8192)
    p = Prog()
    kb = KB(nc, p)
    src_ = kb.din("xsrc", [S, D])
    dst = kb.dout("out", [S, D])
    w = {"Wq": kb.din("Wq", [D, D]), "Wk": kb.din("Wk", [D, D]),
         "Wv": kb.din("Wv", [D, D]), "Wo": kb.din("Wo", [D, D])}
    for nm, arr in host_consts().items():
        kb.din(nm, arr.shape, BF16 if arr.dtype == bf else F32)
    with nc.Block() as block:
        kb.setup(host_consts())
        outs = kb.stage_attn_a(src_, kb.dbuf["xsrc"], dst, kb.dbuf["out"], w)
        p.finalize(nc, block, final_extra=outs)
    return nc


def build_full():
    nc = bass.Bass("TRN2", target_bir_lowering=False, dynamic_dma_scratch_size=8192)
    p = Prog()
    kb = KB(nc, p)
    x0 = kb.din("x0", [S, D])
    out = kb.dout("out", [S, D])
    wa = {"Wq": kb.din("L0_attn_a_Wq", [D, D]), "Wk": kb.din("L0_attn_a_Wk", [D, D]),
          "Wv": kb.din("L0_attn_a_Wv", [D, D]), "Wo": kb.din("L0_attn_a_Wo", [D, D])}
    wb = {"w_qkv": kb.din("L0_attn_b_w_qkv", [3 * D, D]),
          "w_out": kb.din("L0_attn_b_w_out", [D, D]),
          "fc_code": kb.din("L0_attn_b_fc_code", [1, NH, 16, DH])}
    wf = {"router_w": kb.din("L0_ffn_router_w", [D, NE]),
          "router_b": kb.din("L0_ffn_router_b", [NE]),
          "w1": kb.din("L0_ffn_w1", [NE, D, HME]), "b1": kb.din("L0_ffn_b1", [NE, HME]),
          "w2": kb.din("L0_ffn_w2", [NE, HME, D]), "b2": kb.din("L0_ffn_b2", [NE, D]),
          "sw1": kb.din("L0_ffn_sw1", [NSH, D, HME]), "sb1": kb.din("L0_ffn_sb1", [NSH, HME]),
          "sw2": kb.din("L0_ffn_sw2", [NSH, HME, D]), "sb2": kb.din("L0_ffn_sb2", [NSH, D])}
    for nm, arr in host_consts().items():
        kb.din(nm, arr.shape, BF16 if arr.dtype == bf else F32)
    X1 = kb.dint("X1", [S, D])
    X2 = kb.dint("X2", [S, D])
    X3 = kb.dint("X3", [S, D])
    with nc.Block() as block:
        kb.setup(host_consts())
        kb.init_precast(wf)
        kb.stage_attn_a(x0, kb.dbuf["x0"], X1, kb.dbuf["X1"], wa)
        p.barrier()
        kb.stage_ffn(X1, kb.dbuf["X1"], X2, kb.dbuf["X2"], wf)
        p.barrier()
        kb.stage_attn_b(X2, kb.dbuf["X2"], X3, kb.dbuf["X3"], wb)
        p.barrier()
        outs = kb.stage_ffn(X3, kb.dbuf["X3"], out, kb.dbuf["out"], wf)
        p.finalize(nc, block, final_extra=outs)
    return nc


_NC_CACHE = {}


def kernel(**inputs):
    import numpy as np
    if "nc" not in _NC_CACHE:
        _NC_CACHE["nc"] = build_full()
    nc = _NC_CACHE["nc"]
    from concourse.bass_utils import run_bass_kernel_spmd
    consts = host_consts()
    x = np.ascontiguousarray(inputs["L0_attn_a_x"])
    base = {k: np.ascontiguousarray(v) for k, v in inputs.items() if k != "L0_attn_a_x"}
    base.update(consts)
    in_maps = []
    for i in range(8):
        m = dict(base)
        m["x0"] = np.ascontiguousarray(x[i])
        in_maps.append(m)
    res = run_bass_kernel_spmd(nc, in_maps, core_ids=list(range(8)))
    return np.stack([r["out"] for r in res.results], axis=0)


# version 10
# speedup vs baseline: 1.1056x; 1.0008x over previous
import numpy as np
import ml_dtypes
import concourse.bass as bass
import concourse.mybir as mybir

F32 = mybir.dt.float32
BF16 = mybir.dt.bfloat16
AF = mybir.ActivationFunctionType
ALU = mybir.AluOpType
AX = mybir.AxisListType
bf = ml_dtypes.bfloat16

ENGINES = ("tensor", "vector", "scalar", "gpsimd", "sync")

S = 4096
D = 1024
NT = 32
NH = 16
DH = 64
NSEL = 409
NCMP = 512
HME = 2048
NE = 8
NSH = 2


class Buf:
    __slots__ = ("last_writer", "readers")

    def __init__(self):
        self.last_writer = None
        self.readers = []


class Prog:

    NSTREAMS = {"sync": 8, "gpsimd": 8, "scalar": 4, "vector": 2, "tensor": 2}

    def __init__(self):
        self.ops = []
        self._bar = []
        self._stream_counts = {}

    def emit(self, engine, fn, reads=(), writes=(), extra=(), dma=False):
        tok = len(self.ops)
        waits = set(self._bar)
        for b in reads:
            if b.last_writer is not None:
                waits.add(b.last_writer)
        for b in writes:
            if b.last_writer is not None:
                waits.add(b.last_writer)
            for r in b.readers:
                waits.add(r)
        for t in extra:
            if t is not None:
                waits.add(t)
        waits.discard(tok)
        skey = seq = None
        if dma:
            n = self._stream_counts.get(engine, 0)
            self._stream_counts[engine] = n + 1
            skey = (engine, n % self.NSTREAMS[engine])
            seq = n // self.NSTREAMS[engine] + 1
        self.ops.append([engine, fn, dma, waits, skey, seq])
        for b in reads:
            b.readers.append(tok)
        for b in writes:
            b.last_writer = tok
            b.readers = []
        return tok

    def barrier(self):
        last = {}
        for i, op in enumerate(self.ops):
            last[op[0]] = i
        self._bar = list(last.values())

    def finalize(self, nc, block, final_extra=()):
        ops = self.ops
        n = len(ops)
        esem = {e: nc.alloc_semaphore(f"sem_{e}") for e in ENGINES}
        ssem = {}
        for (eng, k) in {op[4] for op in ops if op[2]}:
            ssem[(eng, k)] = nc.alloc_semaphore(f"dma_{eng}_{k}")
        needed = [False] * n
        for _, _, _, waits, _, _ in ops:
            for t in waits:
                needed[t] = True
        for t in final_extra:
            needed[t] = True
        cum = [None] * n
        run = {e: 0 for e in ENGINES}
        for i, (eng, _, dma, _, skey, seq) in enumerate(ops):
            if dma:
                cum[i] = (ssem[skey], 16 * seq)
            else:
                if needed[i]:
                    run[eng] += 1
                cum[i] = (esem[eng], run[eng])

        by_engine = {e: [] for e in ENGINES}
        for i, op in enumerate(ops):
            by_engine[op[0]].append(i)

        def body(eng_name):
            def runner(eng):
                water = {}
                def do_wait(sem, v):
                    if water.get(id(sem), 0) < v:
                        eng.wait_ge(sem, v)
                        water[id(sem)] = v
                for i in by_engine[eng_name]:
                    _, fn, dma, waits, skey, seq = ops[i]
                    per = {}
                    for t in waits:
                        if ops[t][0] == "tensor" and eng_name == "tensor" and not ops[t][2]:
                            continue
                        sem, v = cum[t]
                        if v > per.get(id(sem), (None, 0))[1]:
                            per[id(sem)] = (sem, v)
                    for sem, v in per.values():
                        do_wait(sem, v)
                    if dma and seq > 1:
                        do_wait(ssem[skey], 16 * (seq - 1))
                    inst = fn(eng)
                    if dma:
                        inst.then_inc(ssem[skey], 16)
                    elif needed[i]:
                        inst.then_inc(esem[eng_name], 1)
                if eng_name == "sync":
                    per = {}
                    for t in final_extra:
                        sem, v = cum[t]
                        if v > per.get(id(sem), (None, 0))[1]:
                            per[id(sem)] = (sem, v)
                    for sem, v in per.values():
                        do_wait(sem, v)
            return runner

        for e in ENGINES:
            getattr(block, e)(body(e))


def host_consts():
    c = {}
    c["c_ident"] = np.eye(128).astype(bf)
    c["c_identf"] = np.eye(128).astype(np.float32)
    c["c_uinc"] = np.triu(np.ones((128, 128))).astype(bf)
    c["c_ppool"] = (np.arange(128)[:, None] // 8 == np.arange(16)[None, :]).astype(bf)
    c["c_l32"] = (np.arange(32)[:, None] < np.arange(32)[None, :]).astype(bf)
    c["c_ones_col"] = np.ones((128, 1)).astype(bf)
    c["c_ones_row_f"] = np.ones((1, 128)).astype(np.float32)
    c["c_onesf"] = np.ones((65, 128)).astype(np.float32)
    pb = np.zeros((128, 4), np.float32)
    pb[409 - 3 * 128:, 3] = -60.0
    c["c_padbias"] = pb
    c["c_ones_row_b"] = np.ones((1, 128)).astype(bf)
    return c


class KB:
    def __init__(self, nc, prog):
        self.nc = nc
        self.p = prog
        self.dram = {}
        self.dbuf = {}
        self.uid = 0

    def din(self, name, shape, dtype=F32):
        self.dram[name] = self.nc.dram_tensor(name, list(shape), dtype, kind="ExternalInput").ap()
        self.dbuf[name] = Buf()
        return self.dram[name]

    def dout(self, name, shape, dtype=F32):
        self.dram[name] = self.nc.dram_tensor(name, list(shape), dtype, kind="ExternalOutput").ap()
        self.dbuf[name] = Buf()
        return self.dram[name]

    def dint(self, name, shape, dtype=F32):
        self.dram[name] = self.nc.dram_tensor(name, list(shape), dtype, kind="Internal").ap()
        self.dbuf[name] = Buf()
        return self.dram[name]

    def sb(self, name, shape, dtype):
        return self.nc.alloc_sbuf_tensor(name, list(shape), dtype)

    def setup(self, consts):
        nc = self.nc
        self.pool1 = self.sb("pool1", [128, 32768], BF16)
        self.pool2 = self.sb("pool2", [128, 32768], BF16)
        self.pool3 = self.sb("pool3", [128, 16384], F32)
        self.b_pool1 = Buf()
        self.b_pool2 = Buf()
        self.b_pool3 = Buf()
        self.P0 = nc.alloc_psum_tensor("P0", [128, 1024], F32)
        self.P1 = nc.alloc_psum_tensor("P1", [128, 1024], F32)
        self.P2 = nc.alloc_psum_tensor("P2", [128, 1024], F32)
        self.PS = nc.alloc_psum_tensor("PS", [128, 512], F32)
        self.PT = nc.alloc_psum_tensor("PT", [128, 1024], BF16)
        self.bP0, self.bP1, self.bP2, self.bPS, self.bPT = Buf(), Buf(), Buf(), Buf(), Buf()
        self.ident = self.sb("ident", [128, 128], BF16)
        self.identf = self.sb("identf", [128, 128], F32)
        self.uinc = self.sb("uinc", [128, 128], BF16)
        self.ppool = self.sb("ppool", [128, 16], BF16)
        self.l32 = self.sb("l32", [32, 32], BF16)
        self.ones_col = self.sb("ones_col", [128, 1], BF16)
        self.ones_row_f = self.sb("ones_row_f", [1, 128], F32)
        self.onesf = self.sb("onesf", [65, 128], F32)
        self.ones_row_b = self.sb("ones_row_b", [1, 128], BF16)
        self.b_const = Buf()
        p = self.p
        for nm in ("c_ident", "c_uinc", "c_ppool", "c_l32", "c_ones_col",
                   "c_ones_row_f", "c_ones_row_b", "c_identf", "c_onesf"):
            dst = {"c_ident": self.ident, "c_uinc": self.uinc, "c_ppool": self.ppool,
                   "c_l32": self.l32, "c_ones_col": self.ones_col,
                   "c_ones_row_f": self.ones_row_f, "c_ones_row_b": self.ones_row_b,
                   "c_identf": self.identf, "c_onesf": self.onesf}[nm]
            src = self.dram[nm]
            p.emit("sync", (lambda d, s: lambda e: e.dma_start(out=d[:], in_=s))(dst, src),
                   writes=[self.b_const], dma=True)
        self.bnc = self.dint("bnc_scr", [128, 1024], F32)
        self.epsln = self.sb("epsln", [128, 1], F32)
        self.p.emit("vector", lambda e: e.memset(self.epsln[:], 1e-5), writes=[self.b_const])
        self.dummy_f = self.P0
        self.dummy_f2 = self.P0
        self.wstg = self.sb("wstg", [128, 2, 1024], F32)
        self.b_wstg = [Buf(), Buf()]
        self.w1b = self.sb("w1b", [128, 2, 1024], BF16)
        self.b_w1 = [Buf(), Buf()]
        self.gwork = self.sb("gwork", [128, 48], F32)
        self.b_gw = Buf()
        self.lnsm = self.sb("lnsm", [128, 8], F32)
        self.b_lnsm = Buf()
        self.xtile = self.wstg
        self.b_xt = self.b_wstg
        self.xntile = self.sb("xntile", [128, 2, 1024], BF16)
        self.b_xn = [Buf(), Buf()]
        self.eps9 = self.sb("eps9", [128, 1], F32)
        self.p.emit("vector", lambda e: e.memset(self.eps9[:], 1e-9), writes=[self.b_const])
        self.pcstg = self.wstg
        self.b_pcstg = self.b_wstg
        self.pcb = self.pool2[:, 24576:25600]
        self.b_pcb = Buf()
        self._pc_q = None
        self._pc_drained = False

    def init_precast(self, w):
        if getattr(self, "_precast_init", False):
            return
        self._precast_init = True
        self.W1C = self.dint("W1C", [NE + NSH, 16, 128, 1024], BF16)
        self.W2C = self.dint("W2C", [NE + NSH, 16, 128, 1024], BF16)
        self.b_wc = [Buf() for _ in range(NE + NSH)]
        q = []
        for ei in range(NE + NSH):
            shared = ei >= NE
            w1src = w["sw1"][ei - NE] if shared else w["w1"][ei]
            w2src = w["sw2"][ei - NE] if shared else w["w2"][ei]
            for hs in range(16):
                q.append((ei, hs, 0, w1src))
                q.append((ei, hs, 1, w2src))
        self._pc_q = q
        self._pc_i = 0

    def pump_precast(self, n):
        if self._pc_q is None:
            return
        p = self.p
        for _ in range(n):
            if self._pc_i >= len(self._pc_q):
                self._pc_q = None
                return
            ei, hs, kind, wsrc = self._pc_q[self._pc_i]
            self._pc_i += 1
            sl = self._pc_i % 2
            if kind == 0:
                p.emit("sync", (lambda hs_=hs, sl_=sl, ws=wsrc: lambda e: e.dma_start(
                    out=self.pcstg[:, sl_, :].rearrange("p (c n) -> p c n", c=8),
                    in_=ws[:, hs_ * 128:(hs_ + 1) * 128].rearrange("(c p) n -> p c n", p=128)))(),
                    writes=[self.b_pcstg[sl]], dma=True)
            else:
                p.emit("sync", (lambda hs_=hs, sl_=sl, ws=wsrc: lambda e: e.dma_start(
                    out=self.pcstg[:, sl_, :],
                    in_=ws[hs_ * 128:(hs_ + 1) * 128, :]))(),
                    writes=[self.b_pcstg[sl]], dma=True)
            p.emit("scalar", (lambda sl_=sl: lambda e: e.activation(
                self.pcb[:], self.pcstg[:, sl_, :], AF.Copy))(),
                reads=[self.b_pcstg[sl]], writes=[self.b_pcb])
            dstT = self.W1C if kind == 0 else self.W2C
            p.emit("scalar", (lambda ei_=ei, hs_=hs, d_=dstT: lambda e: e.dma_start(
                out=d_[ei_][hs_], in_=self.pcb[:]))(),
                reads=[self.b_pcb], writes=[self.b_wc[ei]], dma=True)

    def flush_precast(self):
        while self._pc_q is not None:
            self.pump_precast(8)
        self.drain_precast()

    def drain_precast(self):
        if self._pc_drained or self._pc_q is not None or not getattr(self, "_precast_init", False):
            return
        self._pc_drained = True
        self.p.emit("scalar", lambda e: e.copy(self.pcb[0:1, 0:8], self.pcb[0:1, 0:8]),
                    reads=[self.b_pcb], writes=[self.b_pcb])


    def u(self):
        self.uid += 1
        return self.uid

    def ln_tile(self, xt, bxt, out_ap, bout, smalls, bsm, dummy=None, bdummy=None):
        p = self.p
        if dummy is None:
            dummy, bdummy = self.P0, self.bP0
        ss, s_, m, a, var, rs = (smalls[:, i:i + 1] for i in range(6))
        p.emit("vector", (lambda d_=dummy: lambda e: e.scalar_tensor_tensor(
            d_[:], xt, 1.0, xt, ALU.mult, ALU.mult, accum_out=ss))(),
            reads=[bxt], writes=[bdummy, bsm])
        p.emit("vector", (lambda d_=dummy: lambda e: e.tensor_scalar(
            d_[:], xt, 0.0, None, ALU.add, ALU.add, accum_out=s_))(),
            reads=[bxt], writes=[bdummy, bsm])
        p.emit("vector", lambda e: e.tensor_scalar(m, s_, 1.0 / D, None, ALU.mult),
               reads=[bsm], writes=[bsm])
        p.emit("vector", lambda e: e.tensor_scalar(a, ss, 1.0 / D, None, ALU.mult),
               reads=[bsm], writes=[bsm])
        p.emit("vector", lambda e: e.scalar_tensor_tensor(
            var, m, 1.0, m, ALU.mult, ALU.mult), reads=[bsm], writes=[bsm])
        p.emit("vector", lambda e: e.tensor_sub(var, a, var), reads=[bsm], writes=[bsm])
        p.emit("scalar", lambda e: e.activation(a, var, AF.Ln, bias=self.epsln[:]),
               reads=[bsm], writes=[bsm])
        p.emit("scalar", lambda e: e.activation(rs, a, AF.Exp, scale=-0.5),
               reads=[bsm], writes=[bsm])
        p.emit("vector", lambda e: e.tensor_scalar(
            out_ap, xt, m, rs, ALU.subtract, ALU.mult),
            reads=[bxt, bsm], writes=[bout])

    def transpose8(self, src_ap, bsrc, dst_view, bdst):
        p = self.p
        for c in range(8):
            p.emit("tensor", (lambda c_: lambda e: e.transpose(
                self.PT[:, c_ * 128:(c_ + 1) * 128], src_ap[:, c_ * 128:(c_ + 1) * 128],
                self.ident[:]))(c),
                reads=[bsrc, self.b_const], writes=[self.bPT])
        p.emit("vector", lambda e: e.tensor_copy(
            dst_view, self.PT[:].rearrange("p (c n) -> p c n", c=8)),
            reads=[self.bPT], writes=[bdst])

    def stage_ffn(self, src, bsrc, dst, bdst, w):
        p = self.p
        nc = self.nc
        self.init_precast(w)
        self.flush_precast()
        p3 = self.pool3[:].bitcast(BF16)
        out_acc = self.pool3[:, 0:8192].rearrange("p (j n) -> p j n", j=8)
        b_acc = self.b_pool3
        xnT = p3[:, 16384:24576].rearrange("p (c n) -> p c n", c=8)
        b_xnT = Buf()
        hT = self.pool2[:, 0:32768].rearrange("p (r n) -> p r n", r=2)
        hTv = [hT[:, 0, :].rearrange("p (h n) -> p h n", h=16),
               hT[:, 1, :].rearrange("p (h n) -> p h n", h=16)]
        b_hT = [Buf(), Buf()]
        w2b = self.pool1[:, 0:32768].rearrange("p (r n) -> p r n", r=2)
        w2v = [w2b[:, 0, :].rearrange("p (h n) -> p h n", h=16),
               w2b[:, 1, :].rearrange("p (h n) -> p h n", h=16)]
        b_w2 = [Buf(), Buf()]
        if not hasattr(self, "ffn_bufs"):
            self.rwb = self.sb("rwb", [128, 8, 8], BF16)
            self.rbrow = self.sb("rbrow", [1, 8], F32)
            self.bext = self.sb("bext", [16, 1024], BF16)
            self.b1col = self.sb("b1col", [128, 2, 16], F32)
            self.b1stg = self.sb("b1stg", [16, 2, 128], F32)
            self.combf = self.sb("combf", [128, 8, 8], F32)
            self.combx = self.sb("combx", [128, 16], BF16)
            self.combxT = self.sb("combxT", [16, 2, 128], BF16)
            self.ffn_bufs = True
            self.b_rw = Buf()
            self.b_bext = Buf()
            self.b_b1 = [Buf(), Buf()]
            self.b_comb = Buf()
            p.emit("vector", lambda e: e.memset(self.combx[:], 0.0), writes=[self.b_comb])
            p.emit("vector", lambda e: e.memset(self.combx[:, 8:10], 0.5), writes=[self.b_comb])
        p.emit("sync", lambda e: e.dma_start(
            out=self.wstg[:, 0, 0:64].rearrange("p (c e) -> p c e", c=8),
            in_=w["router_w"].rearrange("(c p) e -> p c e", p=128)),
               writes=[self.b_wstg[0]], dma=True)
        p.emit("vector", lambda e: e.tensor_copy(
            self.rwb[:].rearrange("p a b -> p (a b)"), self.wstg[:, 0, 0:64]),
               reads=[self.b_wstg[0]], writes=[self.b_rw])
        p.emit("sync", lambda e: e.dma_start(out=self.rbrow[:], in_=w["router_b"].unsqueeze(0)),
               writes=[self.b_rw], dma=True)
        p.emit("vector", lambda e: e.memset(self.bext[:], 0.0), writes=[self.b_bext])
        p.emit("sync", lambda e: e.dma_start(out=self.wstg[0:8, 1, :], in_=w["b2"]),
               writes=[self.b_wstg[1]], dma=True)
        p.emit("sync", lambda e: e.dma_start(out=self.wstg[8:10, 1, :], in_=w["sb2"]),
               writes=[self.b_wstg[1]], dma=True)
        p.emit("vector", lambda e: e.tensor_copy(self.bext[0:10, :], self.wstg[0:10, 1, :]),
               reads=[self.b_wstg[1]], writes=[self.b_bext])

        outs = []
        for qt in range(4):
            r0 = qt * 1024
            for j in range(8):
                xt = self.xtile[:, j % 2, :]
                bxt = self.b_xt[j % 2]
                p.emit("sync", (lambda j_=j, xt_=xt, r0_=r0: lambda e: e.dma_start(
                    out=xt_, in_=src[r0_ + j_ * 128: r0_ + (j_ + 1) * 128, :]))(),
                    reads=[bsrc], writes=[bxt], dma=True)
                xn = self.xntile[:, j % 2, :]
                bxn = self.b_xn[j % 2]
                self.ln_tile(xt, bxt, xn, bxn, self.lnsm, self.b_lnsm)
                self.transpose8(xn, bxn, xnT[:, :, j * 128:(j + 1) * 128], b_xnT)
                for c in range(8):
                    p.emit("tensor", (lambda c_=c, j_=j: lambda e: e.matmul(
                        self.PS[:, 0:8], xnT[:, c_, j_ * 128:(j_ + 1) * 128],
                        self.rwb[:, c_, :], start=(c_ == 0), stop=False))(),
                        reads=[b_xnT, self.b_rw], writes=[self.bPS])
                p.emit("tensor", lambda e: e.matmul(
                    self.PS[:, 0:8], self.ones_row_f[:], self.rbrow[:],
                    start=False, stop=True),
                    reads=[self.b_rw, self.b_const], writes=[self.bPS])
                g = self.gwork
                bg = self.b_gw
                gs_, m1, e1, g2, m2, e2, d12, p1, p2, t1 = (
                    g[:, 0:8], g[:, 8:9], g[:, 9:17], g[:, 17:25], g[:, 25:26],
                    g[:, 26:34], g[:, 34:35], g[:, 35:36], g[:, 36:37], g[:, 37:45])
                comb = self.combf[:, j, :]
                p.emit("vector", lambda e: e.tensor_copy(gs_, self.PS[:, 0:8]),
                       reads=[self.bPS], writes=[bg])
                p.emit("vector", lambda e: e.reduce_max(m1, gs_, AX.X), reads=[bg], writes=[bg])
                p.emit("vector", lambda e: e.tensor_scalar(e1, gs_, m1, None, ALU.is_ge),
                       reads=[bg], writes=[bg])
                p.emit("vector", lambda e: e.scalar_tensor_tensor(
                    g2, e1, -1e30, gs_, ALU.mult, ALU.add), reads=[bg], writes=[bg])
                p.emit("vector", lambda e: e.reduce_max(m2, g2, AX.X), reads=[bg], writes=[bg])
                p.emit("vector", lambda e: e.tensor_scalar(e2, g2, m2, None, ALU.is_ge),
                       reads=[bg], writes=[bg])
                p.emit("vector", lambda e: e.tensor_tensor(d12, m2, m1, ALU.subtract),
                       reads=[bg], writes=[bg])
                p.emit("scalar", lambda e: e.activation(p2, d12, AF.Exp),
                       reads=[bg], writes=[bg])
                p.emit("vector", lambda e: e.tensor_scalar(p1, p2, 1.0, None, ALU.add),
                       reads=[bg], writes=[bg])
                p.emit("vector", lambda e: e.reciprocal(p1, p1), reads=[bg], writes=[bg])
                p.emit("vector", lambda e: e.tensor_tensor(p2, p2, p1, ALU.mult),
                       reads=[bg], writes=[bg])
                p.emit("vector", (lambda e1_=e1, p1_=p1, t1_=t1: lambda e: e.tensor_scalar(
                    t1_, e1_, p1_, None, ALU.mult))(), reads=[bg], writes=[bg])
                p.emit("vector", (lambda comb_=comb: lambda e: e.scalar_tensor_tensor(
                    comb_, e2, p2, t1, ALU.mult, ALU.add))(),
                    reads=[bg], writes=[self.b_comb])
                p.emit("vector", (lambda comb_=comb: lambda e: e.tensor_copy(
                    self.combx[:, 0:8], comb_))(), reads=[self.b_comb], writes=[self.b_comb])
                p.emit("tensor", lambda e: e.transpose(
                    self.PT[0:16, 0:128], self.combx[:], self.ident[:]),
                    reads=[self.b_comb, self.b_const], writes=[self.bPT])
                p.emit("vector", (lambda j_=j: lambda e: e.tensor_copy(
                    self.combxT[:, j_ % 2, :], self.PT[0:16, 0:128]))(),
                    reads=[self.bPT], writes=[self.b_comb])
                for mv in range(2):
                    p.emit("tensor", (lambda j_=j, mv_=mv: lambda e: e.matmul(
                        self.P2[:, mv_ * 512:(mv_ + 1) * 512], self.combxT[:, j_ % 2, :],
                        self.bext[:, mv_ * 512:(mv_ + 1) * 512],
                        start=True, stop=True))(),
                        reads=[self.b_comb, self.b_bext], writes=[self.bP2])
                p.emit("vector", (lambda j_=j, xt_=xt: lambda e: e.scalar_tensor_tensor(
                    out_acc[:, j_, :], self.P2[:], 1.0, xt_, ALU.mult, ALU.add))(),
                    reads=[self.bP2, bxt], writes=[b_acc])
            for ei in range(NE + NSH):
                par = ei % 2
                shared = ei >= NE
                w1src = w["sw1"][ei - NE] if shared else w["w1"][ei]
                w2src = w["sw2"][ei - NE] if shared else w["w2"][ei]
                b1src = w["sb1"][ei - NE] if shared else w["b1"][ei]
                p.emit("sync", (lambda b1s=b1src, par_=par: lambda e: e.dma_start(
                    out=self.b1stg[:, par_, :], in_=b1s.rearrange("(s p) -> s p", p=128)))(),
                    writes=[self.b_b1[par]], dma=True)
                ptf = self.PT[:].bitcast(F32)
                p.emit("tensor", (lambda par_=par: lambda e: e.transpose(
                    ptf[:, 0:16], self.b1stg[:, par_, :], self.identf[0:16, 0:16]))(),
                    reads=[self.b_b1[par], self.b_const], writes=[self.bPT])
                p.emit("vector", (lambda par_=par: lambda e: e.tensor_copy(
                    self.b1col[:, par_, :], ptf[:, 0:16]))(),
                    reads=[self.bPT], writes=[self.b_b1[par]])
                for hs in range(16):
                    sl = hs % 2
                    p.emit("sync",
                           (lambda hs_=hs, sl_=sl, ei_=ei: lambda e: e.dma_start(
                               out=self.w1b[:, sl_, :], in_=self.W1C[ei_][hs_]))(),
                           reads=[self.b_wc[ei]], writes=[self.b_w1[sl]], dma=True)
                    pp = [self.P0, self.P1][hs % 2]
                    bpp = [self.bP0, self.bP1][hs % 2]
                    for mv in range(2):
                        for c in range(8):
                            p.emit("tensor", (lambda sl_=sl, c_=c, mv_=mv, pp_=pp: lambda e: e.matmul(
                                pp_[:, mv_ * 512:(mv_ + 1) * 512],
                                self.w1b[:, sl_, c_ * 128:(c_ + 1) * 128],
                                xnT[:, c_, mv_ * 512:(mv_ + 1) * 512],
                                start=(c_ == 0), stop=(c_ == 7)))(),
                                reads=[self.b_w1[sl], b_xnT], writes=[bpp])
                    p.emit("scalar", (lambda hs_=hs, pp_=pp, par_=par: lambda e: e.activation(
                        hTv[par_][:, hs_, :], pp_[:],
                        AF.Gelu, bias=self.b1col[:, par_, hs_:hs_ + 1]))(),
                        reads=[bpp, self.b_b1[par]], writes=[b_hT[par]])
                    hc = hs
                    p.emit("scalar",
                           (lambda hc_=hc, ei_=ei, par_=par: lambda e: e.dma_start(
                               out=w2v[par_][:, hc_, :], in_=self.W2C[ei_][hc_]))(),
                           reads=[self.b_wc[ei]], writes=[b_w2[par]], dma=True)
                for j in range(8):
                    yp = [self.P0, self.P1, self.P2][j % 3]
                    byp = [self.bP0, self.bP1, self.bP2][j % 3]
                    for mv in range(2):
                        for hc in range(16):
                            p.emit("tensor", (lambda j_=j, hc_=hc, mv_=mv, par_=par, yp_=yp: lambda e: e.matmul(
                                yp_[:, mv_ * 512:(mv_ + 1) * 512],
                                hTv[par_][:, hc_, j_ * 128:(j_ + 1) * 128],
                                w2v[par_][:, hc_, mv_ * 512:(mv_ + 1) * 512],
                                start=(hc_ == 0), stop=(hc_ == 15)))(),
                                reads=[b_hT[par], b_w2[par]], writes=[byp])
                    if shared:
                        p.emit("vector", (lambda j_=j, yp_=yp: lambda e: e.scalar_tensor_tensor(
                            out_acc[:, j_, :], yp_[:], 0.5, out_acc[:, j_, :],
                            ALU.mult, ALU.add))(),
                            reads=[byp, b_acc], writes=[b_acc])
                    else:
                        p.emit("vector", (lambda j_=j, ei_=ei, yp_=yp: lambda e: e.scalar_tensor_tensor(
                            out_acc[:, j_, :], yp_[:], self.combf[:, j_, ei_:ei_ + 1],
                            out_acc[:, j_, :], ALU.mult, ALU.add))(),
                            reads=[byp, b_acc, self.b_comb], writes=[b_acc])
            for j in range(8):
                outs.append(p.emit("sync", (lambda j_=j, r0_=r0: lambda e: e.dma_start(
                    out=dst[r0_ + j_ * 128: r0_ + (j_ + 1) * 128, :], in_=out_acc[:, j_, :]))(),
                    reads=[b_acc], writes=[bdst], dma=True))
        return outs

    def stage_attn_a(self, src, bsrc, dst, bdst, w):
        p = self.p
        NSEL_PAD = 512
        xnT = self.pool1[:, 0:32768].rearrange("p (c n) -> p c n", c=8)
        b_xnT = self.b_pool1
        p2b = self.pool2
        qTaug = [p2b[:, 0:4096], p2b[:, 4096:8192]]
        b_qTa = [Buf(), Buf()]
        kTaug = [p2b[:, 8192:12288], p2b[:, 12288:16384]]
        b_kTa = [Buf(), Buf()]
        kv_s = p2b[:, 16384:20480].rearrange("p (j n) -> p j n", j=32)
        b_kv = Buf()
        ering = p2b[:, 20480:24576].rearrange("p (r n) -> p r n", r=4)
        b_er = [Buf(), Buf(), Buf(), Buf()]
        p3 = self.pool3[:].bitcast(BF16)
        wsl = p3[:, 0:1024].rearrange("p (r n) -> p r n", r=2)
        b_wsl = [Buf(), Buf()]
        kcT = [p3[0:64, 1024 + i * 512: 1024 + (i + 1) * 512] for i in range(16)]
        b_kcT = [Buf() for _ in range(16)]
        vcn = p3[:, 9216:13376].rearrange("p (h n) -> p h n", h=16)
        b_vcn = Buf()
        rgcol = self.pool3[:, 6688:6752].rearrange("p (h n) -> p h n", h=16)
        b_rg = Buf()
        st = self.pool3[:, 6752:7008].rearrange("p (k n) -> p k n", k=8)
        eacc, macc, iacc, lnzt, ut, uhit, ulot, n2t = (st[:, i, :] for i in range(8))
        b_st = Buf()
        b_lnz = Buf()
        b_u = Buf()
        zp = self.gwork
        p.emit("vector", lambda e: e.memset(eacc, 0.0), writes=[b_st])
        p.emit("vector", lambda e: e.memset(macc, 0.0), writes=[b_st])
        p.emit("vector", lambda e: e.memset(iacc, 0.0), writes=[b_st])
        for sl in range(2):
            p.emit("vector", (lambda sl_=sl: lambda e: e.memset(kTaug[sl_][64:67, :], 0.0))(),
                   writes=[b_kTa[sl]])
            p.emit("vector", (lambda sl_=sl: lambda e: e.tensor_scalar(
                kTaug[sl_][64:67, :], kTaug[sl_][64:67, :], 1.0, None, ALU.add))(),
                reads=[b_kTa[sl]], writes=[b_kTa[sl]])
        p.emit("vector", lambda e: e.memset(
            vcn[:].rearrange("p h n -> p (h n)").rearrange("p (g o) -> p g o", o=65)[:, :, 64:65], 1.0),
            writes=[b_vcn])
        bncb = self.dint("bncb_scr", [96, 128], BF16)
        b_bncb = Buf()
        staKC = self.sb("staKC", [128, 132], BF16)
        staV = self.sb("staV", [128, 64], BF16)
        b_staK, b_staV = Buf(), Buf()
        p.emit("vector", lambda e: e.memset(staKC[:, 65:129], 0.0), writes=[b_staK])

        def load_wslice(name, h, slot, bslot):
            p.emit("sync", (lambda h_=h: lambda e: e.dma_start(
                out=self.wstg[:, 0, 0:512].rearrange("p (c n) -> p c n", c=8),
                in_=w[name][:, h_ * 64:(h_ + 1) * 64].rearrange("(c p) n -> p c n", p=128)))(),
                writes=[self.b_wstg[0]], dma=True)
            p.emit("vector", (lambda slot_=slot: lambda e: e.tensor_copy(
                slot_, self.wstg[:, 0, 0:512]))(),
                reads=[self.b_wstg[0]], writes=[bslot])

        for j in range(NT):
            xt = self.xtile[:, j % 2, :]
            bxt = self.b_xt[j % 2]
            p.emit("sync", (lambda j_=j, xt_=xt: lambda e: e.dma_start(
                out=xt_, in_=src[j_ * 128:(j_ + 1) * 128, :]))(),
                reads=[bsrc], writes=[bxt], dma=True)
            xn = self.xntile[:, j % 2, :]
            bxn = self.b_xn[j % 2]
            self.ln_tile(xt, bxt, xn, bxn, self.lnsm, self.b_lnsm)
            self.transpose8(xn, bxn, xnT[:, :, j * 128:(j + 1) * 128], b_xnT)

        for h in range(NH):
            sl = h % 2
            qT, bqT = qTaug[sl], b_qTa[sl]
            kT, bkT = kTaug[sl], b_kTa[sl]
            load_wslice("Wq", h, wsl[:, 0, :], b_wsl[0])
            wq8 = wsl[:, 0, :].rearrange("p (c n) -> p c n", c=8)
            for tb8 in range(8):
                ppj = [self.P0, self.P1][tb8 % 2]
                bppj = [self.bP0, self.bP1][tb8 % 2]
                for c in range(8):
                    p.emit("tensor", (lambda tb_=tb8, c_=c, ppj_=ppj: lambda e: e.matmul(
                        ppj_[0:64, 0:512], wq8[:, c_, :],
                        xnT[:, c_, tb_ * 512:(tb_ + 1) * 512],
                        start=(c_ == 0), stop=(c_ == 7)))(),
                        reads=[b_wsl[0], b_xnT], writes=[bppj])
                p.emit("scalar", (lambda tb_=tb8, qT_=qT, ppj_=ppj: lambda e: e.activation(
                    qT_[0:64, tb_ * 512:(tb_ + 1) * 512], ppj_[0:64, 0:512],
                    AF.Copy, scale=0.125))(),
                    reads=[bppj], writes=[bqT])
                if tb8 % 2 == 1:
                    self.pump_precast(2)
            kvw = self.w1b[:, 0, :].rearrange("p (c n) -> p c n", c=8)
            p.emit("sync", (lambda h_=h: lambda e: e.dma_start(
                out=self.wstg[:, 0, 0:512].rearrange("p (c n) -> p c n", c=8),
                in_=w["Wk"][:, h_ * 64:(h_ + 1) * 64].rearrange("(c p) n -> p c n", p=128)))(),
                writes=[self.b_wstg[0]], dma=True)
            p.emit("sync", (lambda h_=h: lambda e: e.dma_start(
                out=self.wstg[:, 1, 0:512].rearrange("p (c n) -> p c n", c=8),
                in_=w["Wv"][:, h_ * 64:(h_ + 1) * 64].rearrange("(c p) n -> p c n", p=128)))(),
                writes=[self.b_wstg[1]], dma=True)
            p.emit("vector", lambda e: e.tensor_copy(
                kvw[:, :, 0:64], self.wstg[:, 0, 0:512].rearrange("p (c n) -> p c n", c=8)),
                reads=[self.b_wstg[0]], writes=[self.b_w1[0]])
            p.emit("vector", lambda e: e.tensor_copy(
                kvw[:, :, 64:128], self.wstg[:, 1, 0:512].rearrange("p (c n) -> p c n", c=8)),
                reads=[self.b_wstg[1]], writes=[self.b_w1[0]])
            for tb8 in range(8):
                ppj = [self.P0, self.P1][tb8 % 2]
                bppj = [self.bP0, self.bP1][tb8 % 2]
                for c in range(8):
                    p.emit("tensor", (lambda tb_=tb8, c_=c, ppj_=ppj: lambda e: e.matmul(
                        ppj_[0:64, 0:512], kvw[:, c_, 0:64],
                        xnT[:, c_, tb_ * 512:(tb_ + 1) * 512],
                        start=(c_ == 0), stop=(c_ == 7)))(),
                        reads=[self.b_w1[0], b_xnT], writes=[bppj])
                p.emit("scalar", (lambda tb_=tb8, kT_=kT, ppj_=ppj: lambda e: e.activation(
                    kT_[0:64, tb_ * 512:(tb_ + 1) * 512], ppj_[0:64, 0:512], AF.Copy))(),
                    reads=[bppj], writes=[bkT])
                if tb8 % 2 == 1:
                    self.pump_precast(2)
            for j in range(NT):
                off = (j % 2) * 128
                for c in range(8):
                    p.emit("tensor", (lambda j_=j, c_=c, off_=off: lambda e: e.matmul(
                        self.P2[:, off_:off_ + 128], xnT[:, c_, j_ * 128:(j_ + 1) * 128],
                        kvw[:, c_, :], start=(c_ == 0), stop=(c_ == 7)))(),
                        reads=[self.b_w1[0], b_xnT], writes=[self.bP2])
                p.emit("vector", (lambda j_=j, off_=off: lambda e: e.tensor_copy(
                    kv_s[:, j_, :], self.P2[:, off_:off_ + 128]))(),
                    reads=[self.bP2], writes=[b_kv])
                if j % 4 == 3:
                    self.pump_precast(1)
            zarr = self.pool3[:, 7040:7168]
            zarr = self.pool3[:, 8096:8224]
            warr = self.pool3[:, 8224:8352]
            marr = self.pool3[:, 8352:8480]
            b_zarr, b_warr, b_marr = Buf(), Buf(), Buf()
            for j in range(NT):
                for cb in range(4):
                    pp = [self.P0, self.P1][cb % 2]
                    bpp = [self.bP0, self.bP1][cb % 2]
                    for mv in range(2):
                        p.emit("tensor", (lambda j_=j, cb_=cb, mv_=mv, pp_=pp, qT_=qT, kT_=kT: lambda e: e.matmul(
                            pp_[:, mv_ * 512:(mv_ + 1) * 512],
                            qT_[0:64, j_ * 128:(j_ + 1) * 128],
                            kT_[0:64, (cb_ * 2 + mv_) * 512:(cb_ * 2 + mv_ + 1) * 512],
                            start=True, stop=True))(),
                            reads=[bqT, bkT], writes=[bpp])
                    er = ering[:, cb, :]
                    ber = b_er[cb]
                    p.emit("scalar", (lambda j_=j, cb_=cb, pp_=pp, er_=er: lambda e: e.activation(
                        er_, pp_[:], AF.Exp, accum_out=zarr[:, j_ * 4 + cb_:j_ * 4 + cb_ + 1]))(),
                        reads=[bpp], writes=[ber, b_zarr])
                    p.emit("vector", (lambda j_=j, cb_=cb, er_=er: lambda e: e.tensor_scalar(
                        er_, er_, 0.0, None, ALU.add, ALU.max,
                        accum_out=marr[:, j_ * 4 + cb_:j_ * 4 + cb_ + 1]))(),
                        reads=[ber], writes=[ber, b_marr])
                    p.emit("vector", (lambda j_=j, cb_=cb, pp_=pp, er_=er: lambda e: e.scalar_tensor_tensor(
                        er_, er_, 1.0, pp_[:], ALU.mult, ALU.mult,
                        accum_out=warr[:, j_ * 4 + cb_:j_ * 4 + cb_ + 1]))(),
                        reads=[ber, bpp], writes=[ber, b_warr])
            zt = self.pool3[:, 8480:8512]
            wt = self.pool3[:, 8512:8544]
            rzt = self.pool3[:, 8544:8576]
            b_zc = Buf()
            p.emit("vector", lambda e: e.tensor_reduce(
                zt, zarr.rearrange("p (j c) -> p j c", c=4), AX.X, ALU.add),
                reads=[b_zarr], writes=[b_zc])
            p.emit("vector", lambda e: e.tensor_reduce(
                wt, warr.rearrange("p (j c) -> p j c", c=4), AX.X, ALU.add),
                reads=[b_warr], writes=[b_zc])
            p.emit("vector", lambda e: e.tensor_reduce(
                rzt, marr.rearrange("p (j c) -> p j c", c=4), AX.X, ALU.max),
                reads=[b_marr], writes=[b_zc])
            p.emit("scalar", lambda e: e.activation(lnzt, zt, AF.Ln),
                   reads=[b_zc], writes=[b_lnz])
            p.emit("vector", lambda e: e.reciprocal(zt, zt), reads=[b_zc], writes=[b_zc])
            p.emit("vector", lambda e: e.tensor_tensor(rzt, rzt, zt, ALU.mult),
                   reads=[b_zc], writes=[b_zc])
            p.emit("vector", lambda e: e.tensor_add(macc, macc, rzt), reads=[b_zc], writes=[b_st])
            p.emit("vector", lambda e: e.tensor_tensor(wt, wt, zt, ALU.mult),
                   reads=[b_zc], writes=[b_zc])
            p.emit("vector", lambda e: e.tensor_sub(wt, lnzt, wt), reads=[b_zc, b_lnz], writes=[b_zc])
            p.emit("vector", lambda e: e.tensor_add(eacc, eacc, wt), reads=[b_zc], writes=[b_st])
            g = zp
            spl = ering[:, 0, 0:96].rearrange("p (v n) -> p v n", v=3)
            p.emit("vector", lambda e: e.tensor_scalar(g[:, 20:52] if False else lnzt, lnzt, -1.0, None, ALU.mult),
                   reads=[b_lnz], writes=[b_lnz])
            p.emit("vector", lambda e: e.tensor_copy(spl[:, 0, :], lnzt),
                   reads=[b_lnz], writes=[b_er[0]])
            tmp32 = self.pool3[:, 7008:7040]
            b_t32 = Buf()
            p.emit("vector", lambda e: e.tensor_sub(tmp32, lnzt, spl[:, 0, :]),
                   reads=[b_lnz, b_er[0]], writes=[b_t32])
            p.emit("vector", lambda e: e.tensor_copy(spl[:, 1, :], tmp32),
                   reads=[b_t32], writes=[b_er[0]])
            p.emit("vector", lambda e: e.tensor_sub(tmp32, tmp32, spl[:, 1, :]),
                   reads=[b_t32, b_er[0]], writes=[b_t32])
            p.emit("vector", lambda e: e.tensor_copy(spl[:, 2, :], tmp32),
                   reads=[b_t32], writes=[b_er[0]])
            p.emit("tensor", lambda e: e.transpose(
                self.PT[0:96, 0:128], spl.rearrange("p v n -> p (v n)"), self.ident[:]),
                reads=[b_er[0], self.b_const], writes=[self.bPT])
            p.emit("vector", lambda e: e.tensor_copy(ering[0:96, 1, 0:128], self.PT[0:96, 0:128]),
                   reads=[self.bPT], writes=[b_er[1]])
            p.emit("sync", lambda e: e.dma_start(out=self.dram["bncb_scr"], in_=ering[0:96, 1, 0:128]),
                   reads=[b_er[1]], writes=[b_bncb], dma=True)
            for v3 in range(3):
                p.emit("sync", (lambda v_=v3, qT_=qT: lambda e: e.dma_start(
                    out=qT_[64 + v_:65 + v_, :],
                    in_=self.dram["bncb_scr"].rearrange("a b -> (a b)").unsqueeze(0)[:, v_ * 4096:(v_ + 1) * 4096]))(),
                    reads=[b_bncb], writes=[bqT], dma=True)
            iar = self.pool3[:, 8576:8704]
            b_iar = Buf()
            swdum = p3[:, 23744:25792].rearrange("p (r n) -> p r n", r=2)
            b_swd = [Buf(), Buf()]
            for c32 in range(NT):
                for qb2 in range(4):
                    pp = [self.P0, self.P1][qb2 % 2]
                    bpp = [self.bP0, self.bP1][qb2 % 2]
                    for mv in range(2):
                        p.emit("tensor", (lambda c_=c32, q_=qb2, mv_=mv, pp_=pp, qT_=qT, kT_=kT: lambda e: e.matmul(
                            pp_[:, mv_ * 512:(mv_ + 1) * 512],
                            kT_[0:67, c_ * 128:(c_ + 1) * 128],
                            qT_[0:67, (q_ * 2 + mv_) * 512:(q_ * 2 + mv_ + 1) * 512],
                            start=True, stop=True))(),
                            reads=[bqT, bkT], writes=[bpp])
                    p.emit("scalar", (lambda c_=c32, q_=qb2, pp_=pp: lambda e: e.activation(
                        swdum[:, q_ % 2, :], pp_[:], AF.Exp,
                        accum_out=iar[:, c_ * 4 + q_:c_ * 4 + q_ + 1]))(),
                        reads=[bpp], writes=[b_swd[qb2 % 2], b_iar])
            p.emit("vector", lambda e: e.tensor_reduce(
                self.pool3[:, 8704:8736], iar.rearrange("p (j c) -> p j c", c=4), AX.X, ALU.add),
                reads=[b_iar], writes=[b_iar])
            p.emit("vector", lambda e: e.tensor_add(iacc, iacc, self.pool3[:, 8704:8736]),
                   reads=[b_iar], writes=[b_st])
            p.emit("vector", lambda e: e.tensor_tensor(
                ering[:, 0:2, :].rearrange("p r n -> p (r n)").rearrange("p (j n) -> p j n", n=64),
                kv_s[:][:, :, 0:64], kv_s[:][:, :, 0:64], ALU.mult),
                reads=[b_kv], writes=[b_er[0], b_er[1]])
            p.emit("vector", lambda e: e.tensor_reduce(
                n2t, ering[:, 0:2, :].rearrange("p r n -> p (r n)").rearrange("p (j n) -> p j n", n=64),
                AX.X, ALU.add),
                reads=[b_er[0], b_er[1]], writes=[b_st])
            p.emit("scalar", lambda e: e.activation(ut, n2t, AF.Ln), reads=[b_st], writes=[b_u])
            p.emit("scalar", lambda e: e.activation(ut, ut, AF.Exp, scale=0.5), reads=[b_u], writes=[b_u])
            p.emit("vector", lambda e: e.reduce_max(zp[:, 30:31], ut, AX.X),
                   reads=[b_u], writes=[self.b_gw])
            p.emit("vector", lambda e: e.tensor_copy(staV[:, 0:1], zp[:, 30:31]),
                   reads=[self.b_gw], writes=[b_staV])
            p.emit("tensor", lambda e: e.transpose(
                self.PT[0:1, 0:128], staV[:, 0:1], self.ident[:]),
                reads=[b_staV, self.b_const], writes=[self.bPT])
            p.emit("vector", lambda e: e.tensor_copy(ering[0:1, 0, 0:128], self.PT[0:1, 0:128]),
                   reads=[self.bPT], writes=[b_er[0]])
            p.emit("vector", lambda e: e.reduce_max(zp[0:1, 31:32], ering[0:1, 0, 0:128], AX.X),
                   reads=[b_er[0]], writes=[self.b_gw])
            p.emit("vector", lambda e: e.tensor_scalar(zp[0:1, 32:33], zp[0:1, 31:32], -1.0, None, ALU.mult),
                   reads=[self.b_gw], writes=[self.b_gw])
            p.emit("tensor", lambda e: e.matmul(
                self.P2[:, 0:1], self.onesf[0:1, :], zp[0:1, 32:33], start=True, stop=True),
                reads=[self.b_gw, self.b_const], writes=[self.bP2])
            p.emit("vector", lambda e: e.tensor_copy(zp[:, 33:34], self.P2[:, 0:1]),
                   reads=[self.bP2], writes=[self.b_gw])
            p.emit("vector", lambda e: e.tensor_scalar(ut, ut, zp[:, 33:34], None, ALU.add),
                   reads=[self.b_gw, b_u], writes=[b_u])
            p.emit("scalar", lambda e: e.activation(ut, ut, AF.Exp, accum_out=zp[:, 34:35]),
                   reads=[b_u], writes=[b_u, self.b_gw])
            p.emit("vector", lambda e: e.tensor_copy(staV[:, 1:2], zp[:, 34:35]),
                   reads=[self.b_gw], writes=[b_staV])
            p.emit("tensor", lambda e: e.matmul(
                self.P2[0:1, 1:2], self.ones_col[:], staV[:, 1:2], start=True, stop=True),
                reads=[b_staV, self.b_const], writes=[self.bP2])
            p.emit("vector", lambda e: e.tensor_copy(zp[0:1, 35:36], self.P2[0:1, 1:2]),
                   reads=[self.bP2], writes=[self.b_gw])
            p.emit("vector", lambda e: e.tensor_scalar(zp[0:1, 35:36], zp[0:1, 35:36], 1e-8, None, ALU.mult),
                   reads=[self.b_gw], writes=[self.b_gw])
            p.emit("tensor", lambda e: e.matmul(
                self.P2[:, 2:3], self.onesf[0:1, :], zp[0:1, 35:36], start=True, stop=True),
                reads=[self.b_gw, self.b_const], writes=[self.bP2])
            p.emit("vector", lambda e: e.tensor_copy(zp[:, 36:37], self.P2[:, 2:3]),
                   reads=[self.bP2], writes=[self.b_gw])
            p.emit("vector", lambda e: e.tensor_copy(uhit, ut), reads=[b_u], writes=[b_u])
            p.emit("vector", lambda e: e.tensor_sub(ulot, ut, uhit), reads=[b_u], writes=[b_u])
            staKall = p3[:, 17472:21696].rearrange("p (j n) -> p j n", j=32)
            vwall = p3[:, 21696:23744].rearrange("p (j n) -> p j n", j=32)
            if h == 0:
                p.emit("vector", lambda e: e.memset(
                    staKall[:].rearrange("p j n -> p (j n)").rearrange("p (j o) -> p j o", o=132)[:, :, 65:129], 0.0),
                    writes=[b_staK])
            p.emit("vector", lambda e: e.tensor_tensor(
                staKall[:, :, 0:64],
                kv_s[:].rearrange("p j n -> p j n")[:, :, 0:64],
                ut.unsqueeze(2).broadcast_to([128, 32, 64]), ALU.mult),
                reads=[b_kv, b_u], writes=[b_staK])
            p.emit("vector", lambda e: e.tensor_copy(staKall[:, :, 64:65], uhit.unsqueeze(2)),
                   reads=[b_u], writes=[b_staK])
            p.emit("vector", lambda e: e.tensor_copy(staKall[:, :, 129:130], ulot.unsqueeze(2)),
                   reads=[b_u], writes=[b_staK])
            p.emit("vector", lambda e: e.tensor_tensor(
                vwall[:], kv_s[:][:, :, 64:128],
                ut.unsqueeze(2).broadcast_to([128, 32, 64]), ALU.mult),
                reads=[b_kv, b_u], writes=[b_staV])
            for j in range(NT):
                p.emit("tensor", (lambda j_=j: lambda e: e.matmul(
                    self.PS[0:65, j_ * 16:(j_ + 1) * 16], staKall[:, j_, 0:65],
                    self.ppool[:], start=True, stop=False))(),
                    reads=[b_staK, self.b_const], writes=[self.bPS])
                p.emit("tensor", (lambda j_=j: lambda e: e.matmul(
                    self.PS[0:65, j_ * 16:(j_ + 1) * 16], staKall[:, j_, 65:130],
                    self.ppool[:], start=False, stop=True))(),
                    reads=[b_staK, self.b_const], writes=[self.bPS])
                p.emit("tensor", (lambda j_=j: lambda e: e.matmul(
                    self.P2[0:64, 512 + j_ * 16: 512 + (j_ + 1) * 16], vwall[:, j_, :],
                    self.ppool[:], start=True, stop=True))(),
                    reads=[b_staV, self.b_const], writes=[self.bP2])
            p.emit("vector", (lambda h_=h: lambda e: e.tensor_copy(
                kcT[h_], self.PS[0:64, :]))(),
                reads=[self.bPS], writes=[b_kcT[h]])
            gsr = self.pool3[:, 7040:7552]
            b_gsr = Buf()
            p.emit("vector", lambda e: e.tensor_copy(gsr[64:65, :], self.PS[64:65, :]),
                   reads=[self.bPS], writes=[b_gsr])
            p.emit("vector", lambda e: e.tensor_scalar(
                gsr[64:65, :], gsr[64:65, :], zp[64:65, 36:37], None, ALU.add),
                reads=[self.b_gw, b_gsr], writes=[b_gsr])
            p.emit("vector", lambda e: e.reciprocal(gsr[64:65, :], gsr[64:65, :]),
                   reads=[b_gsr], writes=[b_gsr])
            p.emit("sync", lambda e: e.dma_start(
                out=self.dram["bnc_scr"][0:1, 0:512], in_=gsr[64:65, :]),
                reads=[b_gsr], writes=[self.dbuf["bnc_scr"]], dma=True)
            p.emit("sync", lambda e: e.dma_start(
                out=gsr[0:4, 0:128],
                in_=self.dram["bnc_scr"][0:1, 0:512].rearrange("o (a b) -> (o a) b", a=4)),
                reads=[self.dbuf["bnc_scr"]], writes=[b_gsr], dma=True)
            p.emit("tensor", lambda e: e.transpose(
                self.PT[:].bitcast(F32)[:, 0:4], gsr[0:4, 0:128], self.identf[0:4, 0:4]),
                reads=[b_gsr, self.b_const], writes=[self.bPT])
            p.emit("vector", (lambda h_=h: lambda e: e.tensor_copy(
                rgcol[:, h_, :], self.PT[:].bitcast(F32)[:, 0:4]))(),
                reads=[self.bPT], writes=[b_rg])
            vct = ering[0:64, 2, 0:512]
            p.emit("vector", lambda e: e.tensor_copy(vct, self.P2[0:64, 512:1024]),
                   reads=[self.bP2], writes=[b_er[2]])
            for cc in range(4):
                p.emit("tensor", (lambda cc_=cc: lambda e: e.transpose(
                    self.PT[:, cc_ * 64:(cc_ + 1) * 64], vct[:, cc_ * 128:(cc_ + 1) * 128],
                    self.ident[0:64, 0:64]))(),
                    reads=[b_er[2], self.b_const], writes=[self.bPT])
                p.emit("vector", (lambda cc_=cc, h_=h: lambda e: e.tensor_scalar(
                    vcn[:, h_, cc_ * 65: cc_ * 65 + 64], self.PT[:, cc_ * 64:(cc_ + 1) * 64],
                    rgcol[:, h_, cc_:cc_ + 1], None, ALU.mult))(),
                    reads=[self.bPT, b_rg], writes=[b_vcn])

        imp = eacc
        p.emit("vector", lambda e: e.tensor_scalar(imp, eacc, -0.4 / 16, None, ALU.mult),
               reads=[b_st], writes=[b_st])
        p.emit("vector", lambda e: e.scalar_tensor_tensor(imp, macc, 0.3 / 16, imp, ALU.mult, ALU.add),
               reads=[b_st], writes=[b_st])
        p.emit("vector", lambda e: e.scalar_tensor_tensor(imp, iacc, 0.3 / 16, imp, ALU.mult, ALU.add),
               reads=[b_st], writes=[b_st])
        g = self.gwork
        sV = self.sb("srch", [128, 8], F32)
        sVb = self.sb("srchb", [128, 2], BF16)
        b_sv = Buf()
        LO, HI, MID, CNT, PRD, T1, T2, MIDC = (sV[:, i:i + 1] for i in range(8))
        p.emit("vector", lambda e: e.reduce_max(g[:, 0:1], imp, AX.X), reads=[b_st], writes=[self.b_gw])
        p.emit("vector", lambda e: e.tensor_reduce(g[:, 1:2], imp, AX.X, ALU.min), reads=[b_st], writes=[self.b_gw])
        p.emit("vector", lambda e: e.tensor_copy(sVb[:, 0:2], g[:, 0:2]), reads=[self.b_gw], writes=[b_sv])
        p.emit("tensor", lambda e: e.transpose(self.PT[0:2, 0:128], sVb[:, 0:2], self.ident[:]),
               reads=[b_sv, self.b_const], writes=[self.bPT])
        p.emit("vector", lambda e: e.tensor_copy(ering[0:2, 0, 0:128], self.PT[0:2, 0:128]),
               reads=[self.bPT], writes=[b_er[0]])
        p.emit("vector", lambda e: e.reduce_max(g[0:1, 2:3], ering[0:1, 0, 0:128], AX.X),
               reads=[b_er[0]], writes=[self.b_gw])
        p.emit("vector", lambda e: e.tensor_copy(sVb[:, 0:1], g[:, 1:2]), reads=[self.b_gw], writes=[b_sv])
        p.emit("tensor", lambda e: e.transpose(self.PT[0:1, 128:256], sVb[:, 0:1], self.ident[:]),
               reads=[b_sv, self.b_const], writes=[self.bPT])
        p.emit("vector", lambda e: e.tensor_copy(ering[0:1, 0, 128:256], self.PT[0:1, 128:256]),
               reads=[self.bPT], writes=[b_er[0]])
        p.emit("vector", lambda e: e.tensor_reduce(g[0:1, 5:6], ering[0:1, 0, 128:256], AX.X, ALU.min),
               reads=[b_er[0]], writes=[self.b_gw])
        p.emit("vector", lambda e: e.tensor_scalar(LO[0:1, :], g[0:1, 5:6], -1.0, None, ALU.add),
               reads=[self.b_gw], writes=[b_sv])
        p.emit("vector", lambda e: e.tensor_scalar(HI[0:1, :], g[0:1, 2:3], 1.0, None, ALU.add),
               reads=[self.b_gw], writes=[b_sv])
        for it in range(40):
            p.emit("vector", lambda e: e.tensor_tensor(MID[0:1, :], LO[0:1, :], HI[0:1, :], ALU.add),
                   reads=[b_sv], writes=[b_sv])
            p.emit("vector", lambda e: e.tensor_scalar(MID[0:1, :], MID[0:1, :], 0.5, None, ALU.mult),
                   reads=[b_sv], writes=[b_sv])
            p.emit("tensor", lambda e: e.matmul(self.P2[:, 3:4], self.onesf[0:1, :], MID[0:1, :],
                                                start=True, stop=True),
                   reads=[b_sv, self.b_const], writes=[self.bP2])
            p.emit("vector", lambda e: e.tensor_copy(MIDC[:], self.P2[:, 3:4]),
                   reads=[self.bP2], writes=[b_sv])
            p.emit("vector", lambda e: e.tensor_scalar(
                ering[:, 3, 0:32], imp, MIDC[:], None, ALU.is_ge, ALU.add, accum_out=CNT[:]),
                reads=[b_st, b_sv], writes=[b_er[3], b_sv])
            p.emit("vector", lambda e: e.tensor_copy(sVb[:, 1:2], CNT[:]), reads=[b_sv], writes=[b_sv])
            p.emit("tensor", lambda e: e.matmul(self.P2[0:1, 4:5], self.ones_col[:], sVb[:, 1:2],
                                                start=True, stop=True),
                   reads=[b_sv, self.b_const], writes=[self.bP2])
            p.emit("vector", lambda e: e.tensor_copy(T1[0:1, :], self.P2[0:1, 4:5]),
                   reads=[self.bP2], writes=[b_sv])
            p.emit("vector", lambda e: e.tensor_scalar(PRD[0:1, :], T1[0:1, :], float(NSEL), None, ALU.is_ge),
                   reads=[b_sv], writes=[b_sv])
            p.emit("vector", lambda e: e.tensor_sub(T1[0:1, :], MID[0:1, :], LO[0:1, :]),
                   reads=[b_sv], writes=[b_sv])
            p.emit("vector", lambda e: e.tensor_tensor(T1[0:1, :], T1[0:1, :], PRD[0:1, :], ALU.mult),
                   reads=[b_sv], writes=[b_sv])
            p.emit("vector", lambda e: e.tensor_add(LO[0:1, :], LO[0:1, :], T1[0:1, :]),
                   reads=[b_sv], writes=[b_sv])
            p.emit("vector", lambda e: e.tensor_sub(T2[0:1, :], MID[0:1, :], HI[0:1, :]),
                   reads=[b_sv], writes=[b_sv])
            p.emit("vector", lambda e: e.tensor_scalar(T1[0:1, :], PRD[0:1, :], 1.0, -1.0, ALU.subtract, ALU.mult),
                   reads=[b_sv], writes=[b_sv])
            p.emit("vector", lambda e: e.tensor_tensor(T2[0:1, :], T2[0:1, :], T1[0:1, :], ALU.mult),
                   reads=[b_sv], writes=[b_sv])
            p.emit("vector", lambda e: e.tensor_add(HI[0:1, :], HI[0:1, :], T2[0:1, :]),
                   reads=[b_sv], writes=[b_sv])
        p.emit("tensor", lambda e: e.matmul(self.P2[:, 5:6], self.onesf[0:1, :], LO[0:1, :],
                                            start=True, stop=True),
               reads=[b_sv, self.b_const], writes=[self.bP2])
        p.emit("vector", lambda e: e.tensor_copy(MIDC[:], self.P2[:, 5:6]), reads=[self.bP2], writes=[b_sv])
        mask = macc
        p.emit("vector", lambda e: e.tensor_scalar(mask, imp, MIDC[:], None, ALU.is_ge),
               reads=[b_st, b_sv], writes=[b_st])
        maskb = ering[:, 3, 0:32]
        p.emit("vector", lambda e: e.tensor_copy(maskb, mask), reads=[b_st], writes=[b_er[3]])
        p.emit("tensor", lambda e: e.matmul(self.P2[:, 32:64], self.uinc[:], maskb,
                                            start=True, stop=False),
               reads=[b_er[3], self.b_const], writes=[self.bP2])
        p.emit("tensor", lambda e: e.matmul(self.P2[0:1, 576:608], self.ones_col[:], maskb,
                                            start=True, stop=True),
               reads=[b_er[3], self.b_const], writes=[self.bP2])
        p.emit("vector", lambda e: e.tensor_copy(ering[0:1, 3, 32:64], self.P2[0:1, 576:608]),
               reads=[self.bP2], writes=[b_er[3]])
        p.emit("tensor", lambda e: e.transpose(self.PT[0:32, 0:1], ering[0:1, 3, 32:64],
                                               self.ident[0:1, 0:1]),
               reads=[b_er[3], self.b_const], writes=[self.bPT])
        p.emit("vector", lambda e: e.tensor_copy(ering[0:32, 3, 64:65], self.PT[0:32, 0:1]),
               reads=[self.bPT], writes=[b_er[3]])
        p.emit("tensor", lambda e: e.matmul(self.P2[0:1, 608:640], ering[0:32, 3, 64:65], self.l32[:],
                                            start=True, stop=True),
               reads=[b_er[3], self.b_const], writes=[self.bP2])
        p.emit("vector", lambda e: e.tensor_copy(self.pool3[0:1, 7552:7584], self.P2[0:1, 608:640]),
               reads=[self.bP2], writes=[b_er[3]])
        p.emit("tensor", lambda e: e.matmul(self.P2[:, 32:64], self.onesf[0:1, :],
                                            self.pool3[0:1, 7552:7584], start=False, stop=True),
               reads=[b_er[3], self.b_const], writes=[self.bP2])
        posm1 = iacc
        p.emit("vector", lambda e: e.tensor_scalar(posm1, self.P2[:, 32:64], -1.0, None, ALU.add),
               reads=[self.bP2], writes=[b_st])

        self.drain_precast()
        p.barrier()
        outN = self.pool2[:, 0:32768].rearrange("p (c n) -> p c n", c=8)
        b_outN = Buf()
        iota512 = self.pool3[:, 7584:8096]
        b_iota = Buf()
        selj = p3[:, 16192:16704]
        b_selj = Buf()
        xnselT = p3[:, 17216:21312].rearrange("p (c n) -> p c n", c=8)
        b_xsel = Buf()
        qT2 = p3[:, 21312:25408]
        b_qT2 = Buf()
        kselT = p3[0:64, 25408:25920]
        b_ksel = Buf()
        vsel = p3[:, 25920:26180].rearrange("p (s n) -> p s n", s=4)
        b_vsel = Buf()
        esel = p3[:, 26180:28228]
        b_esel = Buf()
        ec = p3[:, 28228:30276]
        b_ec = Buf()
        zrow = self.pool3[:, 15168:15680]
        b_zrow = Buf()
        replS = self.pool3[:, 15680:16192]
        b_repl = Buf()
        padcol = self.sb("padcol", [128, 4], F32)
        b_pad = Buf()
        p.emit("sync", lambda e: e.dma_start(out=padcol[:], in_=self.dram["c_padbias"]),
               writes=[b_pad], dma=True)
        p.emit("gpsimd", lambda e: e.iota(iota512, pattern=[[1, 512]], base=0,
                                          channel_multiplier=0,
                                          allow_small_or_imprecise_dtypes=True),
               writes=[b_iota])
        p.emit("vector", lambda e: e.memset(
            vsel[:].rearrange("p s n -> p (s n)").rearrange("p (g o) -> p g o", o=65)[:, :, 64:65], 1.0),
            writes=[b_vsel])
        for pas in range(2):
            for j in range(NT):
                xt = self.xtile[:, j % 2, :]
                bxt = self.b_xt[j % 2]
                p.emit("sync", (lambda j_=j, xt_=xt: lambda e: e.dma_start(
                    out=xt_, in_=src[j_ * 128:(j_ + 1) * 128, :]))(),
                    reads=[bsrc], writes=[bxt], dma=True)
                xn = self.xntile[:, j % 2, :]
                bxn = self.b_xn[j % 2]
                self.ln_tile(xt, bxt, xn, bxn, self.lnsm, self.b_lnsm,
                             dummy=self.P2, bdummy=self.bP2)
                p.emit("vector", (lambda j_=j: lambda e: e.tensor_scalar(
                    selj, iota512, posm1[:, j_:j_ + 1], None, ALU.is_equal))(),
                    reads=[b_iota, b_st], writes=[b_selj])
                p.emit("vector", (lambda j_=j: lambda e: e.tensor_scalar(
                    selj, selj, mask[:, j_:j_ + 1], None, ALU.mult))(),
                    reads=[b_st, b_selj], writes=[b_selj])
                for ds in range(4):
                    dslice = pas * 4 + ds
                    pp = [self.P0, self.P1][ds // 2]
                    bpp = [self.bP0, self.bP1][ds // 2]
                    p.emit("tensor", (lambda j_=j, ds_=ds, dsl_=dslice, pp_=pp, bxn_=bxn, xn_=xn: lambda e: e.matmul(
                        pp_[:, (ds_ % 2) * 512:(ds_ % 2 + 1) * 512],
                        xn_[:, dsl_ * 128:(dsl_ + 1) * 128], selj,
                        start=(j_ == 0), stop=(j_ == NT - 1)))(),
                        reads=[bxn, b_selj], writes=[bpp])
            for ds in range(4):
                dslice = pas * 4 + ds
                pp = [self.P0, self.P1][ds // 2]
                bpp = [self.bP0, self.bP1][ds // 2]
                p.emit("vector", (lambda ds_=ds, dsl_=dslice, pp_=pp: lambda e: e.tensor_copy(
                    xnselT[:, dsl_, :], pp_[:, (ds_ % 2) * 512:(ds_ % 2 + 1) * 512]))(),
                    reads=[bpp], writes=[b_xsel])
        for h in range(NH):
            load_wslice("Wq", h, wsl[:, 0, :], b_wsl[0])
            wq8 = wsl[:, 0, :].rearrange("p (c n) -> p c n", c=8)
            for tb8 in range(8):
                for c in range(8):
                    p.emit("tensor", (lambda tb_=tb8, c_=c: lambda e: e.matmul(
                        self.PS[0:64, :], wq8[:, c_, :],
                        xnT[:, c_, tb_ * 512:(tb_ + 1) * 512],
                        start=(c_ == 0), stop=(c_ == 7)))(),
                        reads=[b_wsl[0], b_xnT], writes=[self.bPS])
                p.emit("scalar", (lambda tb_=tb8: lambda e: e.activation(
                    qT2[0:64, tb_ * 512:(tb_ + 1) * 512], self.PS[0:64, :],
                    AF.Copy, scale=0.125))(),
                    reads=[self.bPS], writes=[b_qT2])
            load_wslice("Wk", h, wsl[:, 1, :], b_wsl[1])
            wk8 = wsl[:, 1, :].rearrange("p (c n) -> p c n", c=8)
            for c in range(8):
                p.emit("tensor", (lambda c_=c: lambda e: e.matmul(
                    self.PS[0:64, :], wk8[:, c_, :], xnselT[:, c_, :],
                    start=(c_ == 0), stop=(c_ == 7)))(),
                    reads=[b_wsl[1], b_xsel], writes=[self.bPS])
            p.emit("vector", lambda e: e.tensor_copy(kselT, self.PS[0:64, :]),
                   reads=[self.bPS], writes=[b_ksel])
            load_wslice("Wv", h, wsl[:, 0, :], b_wsl[0])
            wv8 = wsl[:, 0, :].rearrange("p (c n) -> p c n", c=8)
            for s4 in range(4):
                for c in range(8):
                    p.emit("tensor", (lambda s_=s4, c_=c: lambda e: e.matmul(
                        self.P2[:, s_ * 64:(s_ + 1) * 64],
                        xnselT[:, c_, s_ * 128:(s_ + 1) * 128], wv8[:, c_, :],
                        start=(c_ == 0), stop=(c_ == 7)))(),
                        reads=[b_wsl[0], b_xsel], writes=[self.bP2])
                p.emit("vector", (lambda s_=s4: lambda e: e.tensor_copy(
                    vsel[:, s_, 0:64], self.P2[:, s_ * 64:(s_ + 1) * 64]))(),
                    reads=[self.bP2], writes=[b_vsel])
            for qb in range(8):
                for s4 in range(4):
                    pp = [self.P0, self.P1][s4 // 2]
                    bpp = [self.bP0, self.bP1][s4 // 2]
                    p.emit("tensor", (lambda s_=s4, qb_=qb, pp_=pp: lambda e: e.matmul(
                        pp_[:, (s_ % 2) * 512:(s_ % 2 + 1) * 512],
                        kselT[:, s_ * 128:(s_ + 1) * 128],
                        qT2[0:64, qb_ * 512:(qb_ + 1) * 512],
                        start=True, stop=True))(),
                        reads=[b_ksel, b_qT2], writes=[bpp])
                for s4 in range(4):
                    pp = [self.P0, self.P1][s4 // 2]
                    bpp = [self.bP0, self.bP1][s4 // 2]
                    p.emit("scalar", (lambda s_=s4, pp_=pp: lambda e: e.activation(
                        esel[:, s_ * 512:(s_ + 1) * 512], pp_[:, (s_ % 2) * 512:(s_ % 2 + 1) * 512],
                        AF.Exp, bias=padcol[:, s_:s_ + 1]))(),
                        reads=[bpp, b_pad], writes=[b_esel])
                for s4 in range(4):
                    pp = [self.P0, self.P1][s4 // 2]
                    bpp = [self.bP0, self.bP1][s4 // 2]
                    p.emit("tensor", (lambda s_=s4, qb_=qb, h_=h, pp_=pp: lambda e: e.matmul(
                        pp_[:, (s_ % 2) * 512:(s_ % 2 + 1) * 512],
                        kcT[h_][:, s_ * 128:(s_ + 1) * 128],
                        qT2[0:64, qb_ * 512:(qb_ + 1) * 512],
                        start=True, stop=True))(),
                        reads=[b_kcT[h], b_qT2], writes=[bpp])
                for s4 in range(4):
                    pp = [self.P0, self.P1][s4 // 2]
                    bpp = [self.bP0, self.bP1][s4 // 2]
                    p.emit("scalar", (lambda s_=s4, h_=h, pp_=pp: lambda e: e.activation(
                        ec[:, s_ * 512:(s_ + 1) * 512], pp_[:, (s_ % 2) * 512:(s_ % 2 + 1) * 512],
                        AF.Exp, scale=rgcol[:, h_, s_:s_ + 1]))(),
                        reads=[bpp, b_rg], writes=[b_ec])
                for s4 in range(4):
                    p.emit("tensor", (lambda s_=s4: lambda e: e.matmul(
                        self.PS[0:65, :], vsel[:, s_, :], esel[:, s_ * 512:(s_ + 1) * 512],
                        start=(s_ == 0), stop=False))(),
                        reads=[b_vsel, b_esel], writes=[self.bPS])
                for s4 in range(4):
                    p.emit("tensor", (lambda s_=s4, h_=h: lambda e: e.matmul(
                        self.PS[0:65, :], vcn[:, h_, s_ * 65:(s_ + 1) * 65],
                        ec[:, s_ * 512:(s_ + 1) * 512],
                        start=False, stop=(s_ == 3)))(),
                        reads=[b_vcn, b_ec], writes=[self.bPS])
                p.emit("vector", lambda e: e.tensor_copy(zrow[64:65, :], self.PS[64:65, :]),
                       reads=[self.bPS], writes=[b_zrow])
                p.emit("vector", lambda e: e.reciprocal(zrow[64:65, :], zrow[64:65, :]),
                       reads=[b_zrow], writes=[b_zrow])
                p.emit("tensor", lambda e: e.matmul(
                    self.P2[0:64, 0:512], self.onesf[64:65, 0:64], zrow[64:65, :],
                    start=True, stop=True),
                    reads=[b_zrow, self.b_const], writes=[self.bP2])
                p.emit("vector", lambda e: e.tensor_copy(replS[0:64, :], self.P2[0:64, 0:512]),
                       reads=[self.bP2], writes=[b_repl])
                p.emit("vector", (lambda h_=h, qb_=qb: lambda e: e.tensor_tensor(
                    outN[(h_ % 2) * 64:(h_ % 2) * 64 + 64, h_ // 2, qb_ * 512:(qb_ + 1) * 512],
                    self.PS[0:64, :], replS[0:64, :], ALU.mult))(),
                    reads=[self.bPS, b_repl], writes=[b_outN])
        p.barrier()
        Wo = p3[:, 17216:25408].rearrange("p (c n) -> p c n", c=8)
        b_Wo = Buf()
        for c in range(8):
            sl = c % 2
            p.emit("sync", (lambda c_=c, sl_=sl: lambda e: e.dma_start(
                out=self.wstg[:, sl_, :], in_=w["Wo"][c_ * 128:(c_ + 1) * 128, :]))(),
                writes=[self.b_wstg[sl]], dma=True)
            p.emit("vector", (lambda c_=c, sl_=sl: lambda e: e.tensor_copy(
                Wo[:, c_, :], self.wstg[:, sl_, :]))(),
                reads=[self.b_wstg[sl]], writes=[b_Wo])
        outtile = self.pool3[:, 13090:15138].rearrange("p (r n) -> p r n", r=2)
        b_ot = [Buf(), Buf()]
        outs = []
        for j in range(NT):
            pp = [self.P2, self.P0][j % 2]
            bpp = [self.bP2, self.bP0][j % 2]
            for mv in range(2):
                for c in range(8):
                    p.emit("tensor", (lambda j_=j, c_=c, mv_=mv, pp_=pp: lambda e: e.matmul(
                        pp_[:, mv_ * 512:(mv_ + 1) * 512],
                        outN[:, c_, j_ * 128:(j_ + 1) * 128],
                        Wo[:, c_, mv_ * 512:(mv_ + 1) * 512],
                        start=(c_ == 0), stop=(c_ == 7)))(),
                        reads=[b_outN, b_Wo], writes=[bpp])
            xt = self.xtile[:, j % 2, :]
            bxt = self.b_xt[j % 2]
            p.emit("sync", (lambda j_=j, xt_=xt: lambda e: e.dma_start(
                out=xt_, in_=src[j_ * 128:(j_ + 1) * 128, :]))(),
                reads=[bsrc], writes=[bxt], dma=True)
            p.emit("vector", (lambda j_=j, pp_=pp, xt_=xt: lambda e: e.scalar_tensor_tensor(
                outtile[:, j_ % 2, :], pp_[:], 1.0, xt_, ALU.mult, ALU.add))(),
                reads=[bpp, bxt], writes=[b_ot[j % 2]])
            outs.append(p.emit("sync", (lambda j_=j: lambda e: e.dma_start(
                out=dst[j_ * 128:(j_ + 1) * 128, :], in_=outtile[:, j_ % 2, :]))(),
                reads=[b_ot[j % 2]], writes=[bdst], dma=True))
        return outs

    def stage_attn_b(self, src, bsrc, dst, bdst, w):
        p = self.p
        xn2T = self.pool1[:, 0:32768].rearrange("p (c n) -> p c n", c=8)
        b_xn2T = self.b_pool1
        xo_all = self.pool2[:, 0:32768].rearrange("p (j n) -> p j n", j=32)
        b_xo = self.b_pool2
        p3 = self.pool3[:].bitcast(BF16)
        qkT = p3[:, 0:4096]
        b_qkT = Buf()
        vpair = p3[:, 4096:8192].rearrange("p (j n) -> p j n", j=32)
        b_vp = Buf()
        KEp = p3[:, 8192:9216].rearrange("p (j n) -> p j n", j=32)
        b_KE = Buf()
        qhp = p3[:, 9216:10240].rearrange("p (j n) -> p j n", j=32)
        b_qh = Buf()
        OFF = [self.pool3[0:1, 5120 + i * 1040: 5120 + (i + 1) * 1040] for i in range(2)]
        b_OFF = [Buf(), Buf()]
        wring = p3[:, 14592:20736].rearrange("p (r n) -> p r n", r=2)
        b_wr = [Buf(), Buf()]
        ktil = p3[0:16, 20736:24832]
        b_ktil = Buf()
        ktex = p3[0:16, 24832:28928]
        b_ktex = Buf()
        abR = [p3[:, 28928:29968], p3[:, 13360:14400]]
        b_abR = [Buf(), Buf()]
        xp = p3[:, 29968:30992]
        b_xp = Buf()
        kvsbR = [p3[:, 31040:32080], p3[:, 12320:13360]]
        b_kvsbR = [Buf(), Buf()]
        OFFB = [p3[0:1, 10240 + i * 1040: 10240 + (i + 1) * 1040] for i in range(2)]
        xored = self.pool3[:, 16064:16128]
        b_xor = Buf()
        codeT = p3[:, 30992:31024].rearrange("p (h n) -> p h n", h=2)
        b_code = Buf()
        kmx = self.gwork[0:16, 40:42]
        for j in range(NT):
            xt = self.xtile[:, j % 2, :]
            bxt = self.b_xt[j % 2]
            p.emit("sync", (lambda j_=j, xt_=xt: lambda e: e.dma_start(
                out=xt_, in_=src[j_ * 128:(j_ + 1) * 128, :]))(),
                reads=[bsrc], writes=[bxt], dma=True)
            xn = self.xntile[:, j % 2, :]
            bxn = self.b_xn[j % 2]
            self.ln_tile(xt, bxt, xn, bxn, self.lnsm, self.b_lnsm)
            self.transpose8(xn, bxn, xn2T[:, :, j * 128:(j + 1) * 128], b_xn2T)
        for pr in range(NH // 2):
            wr = wring[:, pr % 2, :].rearrange("p (b n) -> p b n", b=3)
            bwr = b_wr[pr % 2]
            for b3 in range(3):
                sl = b3 % 2
                p.emit("sync", (lambda b3_=b3, sl_=sl: lambda e: e.dma_start(
                    out=self.wstg[:, sl_, :],
                    in_=w["w_qkv"][b3_ * 1024 + pr * 128: b3_ * 1024 + (pr + 1) * 128, :]))(),
                    writes=[self.b_wstg[sl]], dma=True)
                p.emit("vector", (lambda sl_=sl: lambda e: e.tensor_copy(
                    self.w1b[:, sl_, :], self.wstg[:, sl_, :]))(),
                    reads=[self.b_wstg[sl]], writes=[self.b_w1[sl]])
                for c in range(8):
                    p.emit("tensor", (lambda c_=c, sl_=sl: lambda e: e.transpose(
                        self.PT[:, c_ * 128:(c_ + 1) * 128],
                        self.w1b[:, sl_, c_ * 128:(c_ + 1) * 128], self.ident[:]))(),
                        reads=[self.b_w1[sl], self.b_const], writes=[self.bPT])
                p.emit("vector", (lambda b3_=b3: lambda e: e.tensor_copy(
                    wr[:, b3_, :].rearrange("p (c n) -> p c n", c=8).transpose([0, 2, 1]).rearrange("p n c -> p (n c)") if False else
                    wr[:, b3_, :],
                    self.PT[:]))(),
                    reads=[self.bPT], writes=[bwr])
            for hip in range(2):
                h = pr * 2 + hip
                p.emit("sync", (lambda h_=h: lambda e: e.dma_start(
                    out=self.wstg[0:16, 0, 0:64], in_=w["fc_code"][0, h_]))(),
                    writes=[self.b_wstg[0]], dma=True)
                p.emit("vector", lambda e: e.tensor_copy(
                    self.w1b[0:16, 0, 0:64], self.wstg[0:16, 0, 0:64]),
                    reads=[self.b_wstg[0]], writes=[self.b_w1[0]])
                p.emit("tensor", lambda e: e.transpose(
                    self.PT[0:64, 0:16], self.w1b[0:16, 0, 0:64], self.ident[0:16, 0:16]),
                    reads=[self.b_w1[0], self.b_const], writes=[self.bPT])
                p.emit("tensor", lambda e: e.transpose(
                    self.PT[64:128, 16:32], self.w1b[0:16, 0, 0:64], self.ident[0:16, 0:16]),
                    reads=[self.b_w1[0], self.b_const], writes=[self.bPT])
                p.emit("scalar", (lambda hip_=hip: lambda e: e.activation(
                    codeT[0:64, hip_, :], self.PT[0:64, 0:16], AF.Copy, scale=0.125))(),
                    reads=[self.bPT], writes=[b_code])
                p.emit("scalar", (lambda hip_=hip: lambda e: e.activation(
                    codeT[64:128, hip_, :], self.PT[64:128, 16:32], AF.Copy, scale=0.125))(),
                    reads=[self.bPT], writes=[b_code])
                for half64 in range(2):
                    for tb8 in range(8):
                        for c in range(8):
                            p.emit("tensor", (lambda half_=half64, tb_=tb8, c_=c, hip_=hip: lambda e: e.matmul(
                                self.PS[0:64, :],
                                wr[:, half_, c_ * 128 + hip_ * 64: c_ * 128 + hip_ * 64 + 64],
                                xn2T[:, c_, tb_ * 512:(tb_ + 1) * 512],
                                start=(c_ == 0), stop=(c_ == 7)))(),
                                reads=[bwr, b_xn2T], writes=[self.bPS])
                        p.emit("vector", (lambda half_=half64, tb_=tb8: lambda e: e.tensor_copy(
                            qkT[half_ * 64:(half_ + 1) * 64, tb_ * 512:(tb_ + 1) * 512],
                            self.PS[0:64, :]))(),
                            reads=[self.bPS], writes=[b_qkT])
                for j in range(NT):
                    for c in range(8):
                        p.emit("tensor", (lambda j_=j, c_=c, hip_=hip: lambda e: e.matmul(
                            self.PS[:, 0:64],
                            xn2T[:, c_, j_ * 128:(j_ + 1) * 128],
                            wr[:, 2, c_ * 128 + hip_ * 64: c_ * 128 + hip_ * 64 + 64],
                            start=(c_ == 0), stop=(c_ == 7)))(),
                            reads=[bwr, b_xn2T], writes=[self.bPS])
                    p.emit("vector", (lambda j_=j, hip_=hip: lambda e: e.tensor_copy(
                        vpair[:, j_, hip_ * 64:(hip_ + 1) * 64], self.PS[:, 0:64]))(),
                        reads=[self.bPS], writes=[b_vp])
                for tb8 in range(8):
                    p.emit("tensor", (lambda tb_=tb8, hip_=hip: lambda e: e.matmul(
                        self.PS[0:16, :], codeT[64:128, hip_, :],
                        qkT[64:128, tb_ * 512:(tb_ + 1) * 512],
                        start=True, stop=True))(),
                        reads=[b_code, b_qkT], writes=[self.bPS])
                    p.emit("vector", (lambda tb_=tb8: lambda e: e.tensor_copy(
                        ktil[:, tb_ * 512:(tb_ + 1) * 512], self.PS[0:16, :]))(),
                        reads=[self.bPS], writes=[b_ktil])
                p.emit("vector", lambda e: e.reduce_max(kmx[:, 0:1], ktil, AX.X),
                       reads=[b_ktil], writes=[self.b_gw])
                p.emit("vector", lambda e: e.tensor_scalar(kmx[:, 1:2], kmx[:, 0:1], -1.0, None, ALU.mult),
                       reads=[self.b_gw], writes=[self.b_gw])
                p.emit("scalar", lambda e: e.activation(ktex, ktil, AF.Exp, bias=kmx[0:16, 1:2]),
                       reads=[b_ktil, self.b_gw], writes=[b_ktex])
                for half16 in range(2):
                    for jj in range(16):
                        j = half16 * 16 + jj
                        p.emit("tensor", (lambda j_=j, jj_=jj: lambda e: e.transpose(
                            self.PT[:, jj_ * 16:(jj_ + 1) * 16],
                            ktex[:, j_ * 128:(j_ + 1) * 128], self.ident[0:16, 0:16]))(),
                            reads=[b_ktex, self.b_const], writes=[self.bPT])
                    p.emit("vector", (lambda half_=half16, hip_=hip: lambda e: e.tensor_copy(
                        KEp[:, half_ * 16:(half_ + 1) * 16, hip_ * 16:(hip_ + 1) * 16],
                        self.PT[:, 0:256].rearrange("p (j n) -> p j n", j=16)))(),
                        reads=[self.bPT], writes=[b_KE])
                for j in range(NT):
                    p.emit("tensor", (lambda j_=j, hip_=hip: lambda e: e.matmul(
                        self.PS[:, j_ * 16:(j_ + 1) * 16], qkT[0:64, j_ * 128:(j_ + 1) * 128],
                        codeT[0:64, hip_, :], start=True, stop=True))(),
                        reads=[b_qkT, b_code], writes=[self.bPS])
                qsm = self.pool3[:, 8096:8608]
                qmx = self.pool3[:, 8608:8640]
                b_qs = Buf()
                p.emit("vector", lambda e: e.tensor_reduce(
                    qmx, self.PS[:].rearrange("p (j n) -> p j n", n=16), AX.X, ALU.max),
                    reads=[self.bPS], writes=[b_qs])
                p.emit("vector", lambda e: e.tensor_tensor(
                    qsm.rearrange("p (j n) -> p j n", n=16),
                    self.PS[:].rearrange("p (j n) -> p j n", n=16),
                    qmx.unsqueeze(2).broadcast_to([128, 32, 16]), ALU.subtract),
                    reads=[self.bPS, b_qs], writes=[b_qs])
                p.emit("scalar", lambda e: e.activation(qsm, qsm, AF.Exp),
                       reads=[b_qs], writes=[b_qs])
                p.emit("vector", lambda e: e.tensor_reduce(
                    qmx, qsm.rearrange("p (j n) -> p j n", n=16), AX.X, ALU.add),
                    reads=[b_qs], writes=[b_qs])
                p.emit("vector", lambda e: e.reciprocal(qmx, qmx), reads=[b_qs], writes=[b_qs])
                p.emit("vector", (lambda hip_=hip: lambda e: e.tensor_tensor(
                    qhp[:, :, hip_ * 16:(hip_ + 1) * 16],
                    qsm.rearrange("p (j n) -> p j n", n=16),
                    qmx.unsqueeze(2).broadcast_to([128, 32, 16]), ALU.mult))(),
                    reads=[b_qs], writes=[b_qh])
            for hip in range(2):
                p.emit("vector", (lambda hip_=hip: lambda e: e.memset(OFFB[hip_][:], 0.0))(),
                       writes=[b_OFF[hip]])
            for tb in range(NT):
                for hip in range(2):
                    h = pr * 2 + hip
                    u = tb * 2 + hip
                    PKV = [self.P0, self.P1, self.P2][u % 3]
                    bPKV = [self.bP0, self.bP1, self.bP2][u % 3]
                    ab = abR[u % 2]
                    b_ab = b_abR[u % 2]
                    kvsb = kvsbR[u % 2]
                    b_kvsb = b_kvsbR[u % 2]
                    p.emit("vector", (lambda tb_=tb, hip_=hip, ab_=ab: lambda e: e.tensor_tensor(
                        ab_[:, 0:1024].rearrange("p (c d) -> p c d", c=16),
                        KEp[:, tb_, hip_ * 16:(hip_ + 1) * 16].unsqueeze(2).broadcast_to([128, 16, 64]),
                        vpair[:, tb_, hip_ * 64:(hip_ + 1) * 64].unsqueeze(1).broadcast_to([128, 16, 64]),
                        ALU.mult))(),
                        reads=[b_KE, b_vp], writes=[b_ab])
                    p.emit("vector", (lambda tb_=tb, hip_=hip, ab_=ab: lambda e: e.tensor_copy(
                        ab_[:, 1024:1040], KEp[:, tb_, hip_ * 16:(hip_ + 1) * 16]))(),
                        reads=[b_KE], writes=[b_ab])
                    for mv in range(2):
                        p.emit("tensor", (lambda mv_=mv, PKV_=PKV, ab_=ab: lambda e: e.matmul(
                            PKV_[:, mv_ * 512:(mv_ + 1) * 512], self.uinc[:],
                            ab_[:, mv_ * 512:(mv_ + 1) * 512], start=True, stop=False))(),
                            reads=[self.b_const, b_ab], writes=[bPKV])
                    p.emit("tensor", (lambda ab_=ab: lambda e: e.matmul(
                        self.PS[:, 0:16], self.uinc[:], ab_[:, 1024:1040],
                        start=True, stop=False))(),
                        reads=[self.b_const, b_ab], writes=[self.bPS])
                    for mv in range(2):
                        p.emit("tensor", (lambda mv_=mv, hip_=hip, PKV_=PKV: lambda e: e.matmul(
                            PKV_[:, mv_ * 512:(mv_ + 1) * 512], self.ones_row_b[0:1, :],
                            OFFB[hip_][:, mv_ * 512:(mv_ + 1) * 512],
                            start=False, stop=True))(),
                            reads=[self.b_const, b_OFF[hip]], writes=[bPKV])
                    p.emit("tensor", (lambda hip_=hip: lambda e: e.matmul(
                        self.PS[:, 0:16], self.ones_row_b[0:1, :],
                        OFFB[hip_][:, 1024:1040], start=False, stop=True))(),
                        reads=[self.b_const, b_OFF[hip]], writes=[self.bPS])
                    g = self.gwork
                    p.emit("vector", lambda e: e.tensor_scalar(
                        g[:, 20:36], self.PS[:, 0:16], self.eps9[:], None, ALU.add),
                        reads=[self.bPS, self.b_const], writes=[self.b_gw])
                    p.emit("vector", lambda e: e.reciprocal(g[:, 20:36], g[:, 20:36]),
                           reads=[self.b_gw], writes=[self.b_gw])
                    p.emit("vector", (lambda tb_=tb, hip_=hip: lambda e: e.scalar_tensor_tensor(
                        g[:, 20:36], qhp[:, tb_, hip_ * 16:(hip_ + 1) * 16], 0.125,
                        g[:, 20:36], ALU.mult, ALU.mult))(),
                        reads=[b_qh, self.b_gw], writes=[self.b_gw])
                    p.emit("scalar", (lambda PKV_=PKV, kvsb_=kvsb: lambda e: e.activation(
                        kvsb_[:, 0:1024], PKV_[:], AF.Copy))(),
                        reads=[bPKV], writes=[b_kvsb])
                    p.emit("scalar", (lambda kvsb_=kvsb: lambda e: e.activation(
                        kvsb_[:, 1024:1040], self.PS[:, 0:16], AF.Copy))(),
                        reads=[self.bPS], writes=[b_kvsb])
                    if tb < NT - 1:
                        p.emit("sync", (lambda hip_=hip, kvsb_=kvsb: lambda e: e.dma_start(
                            out=OFFB[hip_][:], in_=kvsb_[127:128, :]))(),
                            reads=[b_kvsb], writes=[b_OFF[hip]], dma=True)
                    p.emit("vector", (lambda kvsb_=kvsb: lambda e: e.tensor_tensor(
                        xp[:].rearrange("p (c d) -> p c d", c=16),
                        g[:, 20:36].unsqueeze(2).broadcast_to([128, 16, 64]),
                        kvsb_[:, 0:1024].rearrange("p (c d) -> p c d", c=16), ALU.mult))(),
                        reads=[self.b_gw, b_kvsb], writes=[b_xp])
                    p.emit("vector", lambda e: e.tensor_reduce(
                        xored, xp[:].rearrange("p (c d) -> p d c", c=16), AX.X, ALU.add),
                        reads=[b_xp], writes=[b_xor])
                    p.emit("vector", (lambda tb_=tb, h_=h: lambda e: e.tensor_copy(
                        xo_all[:, tb_, h_ * 64:(h_ + 1) * 64], xored))(),
                        reads=[b_xor], writes=[b_xo])
        p.barrier()
        woT = p3[:, 0:8192].rearrange("p (c n) -> p c n", c=8)
        b_woT = Buf()
        for r in range(8):
            sl = r % 2
            p.emit("sync", (lambda r_=r, sl_=sl: lambda e: e.dma_start(
                out=self.wstg[:, sl_, :], in_=w["w_out"][r_ * 128:(r_ + 1) * 128, :]))(),
                writes=[self.b_wstg[sl]], dma=True)
            p.emit("vector", (lambda sl_=sl: lambda e: e.tensor_copy(
                self.w1b[:, sl_, :], self.wstg[:, sl_, :]))(),
                reads=[self.b_wstg[sl]], writes=[self.b_w1[sl]])
            for c in range(8):
                p.emit("tensor", (lambda c_=c, sl_=sl: lambda e: e.transpose(
                    self.PT[:, c_ * 128:(c_ + 1) * 128],
                    self.w1b[:, sl_, c_ * 128:(c_ + 1) * 128], self.ident[:]))(),
                    reads=[self.b_w1[sl], self.b_const], writes=[self.bPT])
            p.emit("vector", (lambda r_=r: lambda e: e.tensor_copy(
                woT[:, :, r_ * 128:(r_ + 1) * 128],
                self.PT[:].rearrange("p (c n) -> p c n", c=8)))(),
                reads=[self.bPT], writes=[b_woT])
        xoT = xn2T
        b_xoT = b_xn2T
        for j in range(NT):
            for c in range(8):
                p.emit("tensor", (lambda j_=j, c_=c: lambda e: e.transpose(
                    self.PT[:, c_ * 128:(c_ + 1) * 128],
                    xo_all[:, j_, c_ * 128:(c_ + 1) * 128], self.ident[:]))(),
                    reads=[b_xo, self.b_const], writes=[self.bPT])
            p.emit("vector", (lambda j_=j: lambda e: e.tensor_copy(
                xoT[:, :, j_ * 128:(j_ + 1) * 128],
                self.PT[:].rearrange("p (c n) -> p c n", c=8)))(),
                reads=[self.bPT], writes=[b_xoT])
        outtile = self.pool3[:, 10240:12288].rearrange("p (r n) -> p r n", r=2)
        b_ot = [Buf(), Buf()]
        outs = []
        for j in range(NT):
            pp = [self.P2, self.P0][j % 2]
            bpp = [self.bP2, self.bP0][j % 2]
            for mv in range(2):
                for c in range(8):
                    p.emit("tensor", (lambda j_=j, c_=c, mv_=mv, pp_=pp: lambda e: e.matmul(
                        pp_[:, mv_ * 512:(mv_ + 1) * 512],
                        xoT[:, c_, j_ * 128:(j_ + 1) * 128],
                        woT[:, c_, mv_ * 512:(mv_ + 1) * 512],
                        start=(c_ == 0), stop=(c_ == 7)))(),
                        reads=[b_xoT, b_woT], writes=[bpp])
            xt = self.xtile[:, j % 2, :]
            bxt = self.b_xt[j % 2]
            p.emit("sync", (lambda j_=j, xt_=xt: lambda e: e.dma_start(
                out=xt_, in_=src[j_ * 128:(j_ + 1) * 128, :]))(),
                reads=[bsrc], writes=[bxt], dma=True)
            p.emit("vector", (lambda j_=j, pp_=pp, xt_=xt: lambda e: e.scalar_tensor_tensor(
                outtile[:, j_ % 2, :], pp_[:], 1.0, xt_, ALU.mult, ALU.add))(),
                reads=[bpp, bxt], writes=[b_ot[j % 2]])
            outs.append(p.emit("sync", (lambda j_=j: lambda e: e.dma_start(
                out=dst[j_ * 128:(j_ + 1) * 128, :], in_=outtile[:, j_ % 2, :]))(),
                reads=[b_ot[j % 2]], writes=[bdst], dma=True))
        return outs


def build_attn_b_only():
    nc = bass.Bass("TRN2", target_bir_lowering=False, dynamic_dma_scratch_size=8192)
    p = Prog()
    kb = KB(nc, p)
    src = kb.din("xsrc", [S, D])
    dst = kb.dout("out", [S, D])
    w = {
        "w_qkv": kb.din("w_qkv", [3 * D, D]),
        "w_out": kb.din("w_out", [D, D]),
        "fc_code": kb.din("fc_code", [1, NH, 16, DH]),
    }
    for nm, arr in host_consts().items():
        kb.din(nm, arr.shape, BF16 if arr.dtype == bf else F32)
    with nc.Block() as block:
        kb.setup(host_consts())
        outs = kb.stage_attn_b(src, kb.dbuf["xsrc"], dst, kb.dbuf["out"], w)
        p.finalize(nc, block, final_extra=outs)
    return nc


def build_ffn_only():
    nc = bass.Bass("TRN2", target_bir_lowering=False, dynamic_dma_scratch_size=8192)
    p = Prog()
    kb = KB(nc, p)
    src = kb.din("xsrc", [S, D])
    dst = kb.dout("out", [S, D])
    w = {
        "router_w": kb.din("router_w", [D, NE]),
        "router_b": kb.din("router_b", [NE]),
        "w1": kb.din("w1", [NE, D, HME]),
        "b1": kb.din("b1", [NE, HME]),
        "w2": kb.din("w2", [NE, HME, D]),
        "b2": kb.din("b2", [NE, D]),
        "sw1": kb.din("sw1", [NSH, D, HME]),
        "sb1": kb.din("sb1", [NSH, HME]),
        "sw2": kb.din("sw2", [NSH, HME, D]),
        "sb2": kb.din("sb2", [NSH, D]),
    }
    for nm, arr in host_consts().items():
        kb.din(nm, arr.shape, BF16 if arr.dtype == bf else F32)
    with nc.Block() as block:
        kb.setup(host_consts())
        outs = kb.stage_ffn(src, kb.dbuf["xsrc"], dst, kb.dbuf["out"], w)
        p.finalize(nc, block, final_extra=outs)
    return nc


def build_attn_a_only():
    nc = bass.Bass("TRN2", target_bir_lowering=False, dynamic_dma_scratch_size=8192)
    p = Prog()
    kb = KB(nc, p)
    src_ = kb.din("xsrc", [S, D])
    dst = kb.dout("out", [S, D])
    w = {"Wq": kb.din("Wq", [D, D]), "Wk": kb.din("Wk", [D, D]),
         "Wv": kb.din("Wv", [D, D]), "Wo": kb.din("Wo", [D, D])}
    for nm, arr in host_consts().items():
        kb.din(nm, arr.shape, BF16 if arr.dtype == bf else F32)
    with nc.Block() as block:
        kb.setup(host_consts())
        outs = kb.stage_attn_a(src_, kb.dbuf["xsrc"], dst, kb.dbuf["out"], w)
        p.finalize(nc, block, final_extra=outs)
    return nc


def build_full():
    nc = bass.Bass("TRN2", target_bir_lowering=False, dynamic_dma_scratch_size=8192)
    p = Prog()
    kb = KB(nc, p)
    x0 = kb.din("x0", [S, D])
    out = kb.dout("out", [S, D])
    wa = {"Wq": kb.din("L0_attn_a_Wq", [D, D]), "Wk": kb.din("L0_attn_a_Wk", [D, D]),
          "Wv": kb.din("L0_attn_a_Wv", [D, D]), "Wo": kb.din("L0_attn_a_Wo", [D, D])}
    wb = {"w_qkv": kb.din("L0_attn_b_w_qkv", [3 * D, D]),
          "w_out": kb.din("L0_attn_b_w_out", [D, D]),
          "fc_code": kb.din("L0_attn_b_fc_code", [1, NH, 16, DH])}
    wf = {"router_w": kb.din("L0_ffn_router_w", [D, NE]),
          "router_b": kb.din("L0_ffn_router_b", [NE]),
          "w1": kb.din("L0_ffn_w1", [NE, D, HME]), "b1": kb.din("L0_ffn_b1", [NE, HME]),
          "w2": kb.din("L0_ffn_w2", [NE, HME, D]), "b2": kb.din("L0_ffn_b2", [NE, D]),
          "sw1": kb.din("L0_ffn_sw1", [NSH, D, HME]), "sb1": kb.din("L0_ffn_sb1", [NSH, HME]),
          "sw2": kb.din("L0_ffn_sw2", [NSH, HME, D]), "sb2": kb.din("L0_ffn_sb2", [NSH, D])}
    for nm, arr in host_consts().items():
        kb.din(nm, arr.shape, BF16 if arr.dtype == bf else F32)
    X1 = kb.dint("X1", [S, D])
    X2 = kb.dint("X2", [S, D])
    X3 = kb.dint("X3", [S, D])
    with nc.Block() as block:
        kb.setup(host_consts())
        kb.init_precast(wf)
        kb.stage_attn_a(x0, kb.dbuf["x0"], X1, kb.dbuf["X1"], wa)
        p.barrier()
        kb.stage_ffn(X1, kb.dbuf["X1"], X2, kb.dbuf["X2"], wf)
        p.barrier()
        kb.stage_attn_b(X2, kb.dbuf["X2"], X3, kb.dbuf["X3"], wb)
        p.barrier()
        outs = kb.stage_ffn(X3, kb.dbuf["X3"], out, kb.dbuf["out"], wf)
        p.finalize(nc, block, final_extra=outs)
    return nc


_NC_CACHE = {}


def kernel(**inputs):
    import numpy as np
    if "nc" not in _NC_CACHE:
        _NC_CACHE["nc"] = build_full()
    nc = _NC_CACHE["nc"]
    from concourse.bass_utils import run_bass_kernel_spmd
    consts = host_consts()
    x = np.ascontiguousarray(inputs["L0_attn_a_x"])
    base = {k: np.ascontiguousarray(v) for k, v in inputs.items() if k != "L0_attn_a_x"}
    base.update(consts)
    in_maps = []
    for i in range(8):
        m = dict(base)
        m["x0"] = np.ascontiguousarray(x[i])
        in_maps.append(m)
    res = run_bass_kernel_spmd(nc, in_maps, core_ids=list(range(8)))
    return np.stack([r["out"] for r in res.results], axis=0)


# version 11
# speedup vs baseline: 1.1088x; 1.0029x over previous
import numpy as np
import ml_dtypes
import concourse.bass as bass
import concourse.mybir as mybir

F32 = mybir.dt.float32
BF16 = mybir.dt.bfloat16
AF = mybir.ActivationFunctionType
ALU = mybir.AluOpType
AX = mybir.AxisListType
bf = ml_dtypes.bfloat16

ENGINES = ("tensor", "vector", "scalar", "gpsimd", "sync")

S = 4096
D = 1024
NT = 32
NH = 16
DH = 64
NSEL = 409
NCMP = 512
HME = 2048
NE = 8
NSH = 2


class Buf:
    __slots__ = ("last_writer", "readers")

    def __init__(self):
        self.last_writer = None
        self.readers = []


class Prog:

    NSTREAMS = {"sync": 8, "gpsimd": 8, "scalar": 4, "vector": 2, "tensor": 2}

    def __init__(self):
        self.ops = []
        self._bar = []
        self._stream_counts = {}

    def emit(self, engine, fn, reads=(), writes=(), extra=(), dma=False):
        tok = len(self.ops)
        waits = set(self._bar)
        for b in reads:
            if b.last_writer is not None:
                waits.add(b.last_writer)
        for b in writes:
            if b.last_writer is not None:
                waits.add(b.last_writer)
            for r in b.readers:
                waits.add(r)
        for t in extra:
            if t is not None:
                waits.add(t)
        waits.discard(tok)
        skey = seq = None
        if dma:
            n = self._stream_counts.get(engine, 0)
            self._stream_counts[engine] = n + 1
            skey = (engine, n % self.NSTREAMS[engine])
            seq = n // self.NSTREAMS[engine] + 1
        self.ops.append([engine, fn, dma, waits, skey, seq])
        for b in reads:
            b.readers.append(tok)
        for b in writes:
            b.last_writer = tok
            b.readers = []
        return tok

    def barrier(self):
        last = {}
        for i, op in enumerate(self.ops):
            last[op[0]] = i
        self._bar = list(last.values())

    def finalize(self, nc, block, final_extra=()):
        ops = self.ops
        n = len(ops)
        esem = {e: nc.alloc_semaphore(f"sem_{e}") for e in ENGINES}
        ssem = {}
        for (eng, k) in {op[4] for op in ops if op[2]}:
            ssem[(eng, k)] = nc.alloc_semaphore(f"dma_{eng}_{k}")
        needed = [False] * n
        for _, _, _, waits, _, _ in ops:
            for t in waits:
                needed[t] = True
        for t in final_extra:
            needed[t] = True
        cum = [None] * n
        run = {e: 0 for e in ENGINES}
        for i, (eng, _, dma, _, skey, seq) in enumerate(ops):
            if dma:
                cum[i] = (ssem[skey], 16 * seq)
            else:
                if needed[i]:
                    run[eng] += 1
                cum[i] = (esem[eng], run[eng])

        by_engine = {e: [] for e in ENGINES}
        for i, op in enumerate(ops):
            by_engine[op[0]].append(i)

        def body(eng_name):
            def runner(eng):
                water = {}
                def do_wait(sem, v):
                    if water.get(id(sem), 0) < v:
                        eng.wait_ge(sem, v)
                        water[id(sem)] = v
                for i in by_engine[eng_name]:
                    _, fn, dma, waits, skey, seq = ops[i]
                    per = {}
                    for t in waits:
                        if ops[t][0] == "tensor" and eng_name == "tensor" and not ops[t][2]:
                            continue
                        sem, v = cum[t]
                        if v > per.get(id(sem), (None, 0))[1]:
                            per[id(sem)] = (sem, v)
                    for sem, v in per.values():
                        do_wait(sem, v)
                    if dma and seq > 1:
                        do_wait(ssem[skey], 16 * (seq - 1))
                    inst = fn(eng)
                    if dma:
                        inst.then_inc(ssem[skey], 16)
                    elif needed[i]:
                        inst.then_inc(esem[eng_name], 1)
                if eng_name == "sync":
                    per = {}
                    for t in final_extra:
                        sem, v = cum[t]
                        if v > per.get(id(sem), (None, 0))[1]:
                            per[id(sem)] = (sem, v)
                    for sem, v in per.values():
                        do_wait(sem, v)
            return runner

        for e in ENGINES:
            getattr(block, e)(body(e))


def host_consts():
    c = {}
    c["c_ident"] = np.eye(128).astype(bf)
    c["c_identf"] = np.eye(128).astype(np.float32)
    c["c_uinc"] = np.triu(np.ones((128, 128))).astype(bf)
    c["c_ppool"] = (np.arange(128)[:, None] // 8 == np.arange(16)[None, :]).astype(bf)
    c["c_l32"] = (np.arange(32)[:, None] < np.arange(32)[None, :]).astype(bf)
    c["c_ones_col"] = np.ones((128, 1)).astype(bf)
    c["c_ones_row_f"] = np.ones((1, 128)).astype(np.float32)
    c["c_onesf"] = np.ones((65, 128)).astype(np.float32)
    pb = np.zeros((128, 4), np.float32)
    pb[409 - 3 * 128:, 3] = -60.0
    c["c_padbias"] = pb
    c["c_ones_row_b"] = np.ones((1, 128)).astype(bf)
    return c


class KB:
    def __init__(self, nc, prog):
        self.nc = nc
        self.p = prog
        self.dram = {}
        self.dbuf = {}
        self.uid = 0

    def din(self, name, shape, dtype=F32):
        self.dram[name] = self.nc.dram_tensor(name, list(shape), dtype, kind="ExternalInput").ap()
        self.dbuf[name] = Buf()
        return self.dram[name]

    def dout(self, name, shape, dtype=F32):
        self.dram[name] = self.nc.dram_tensor(name, list(shape), dtype, kind="ExternalOutput").ap()
        self.dbuf[name] = Buf()
        return self.dram[name]

    def dint(self, name, shape, dtype=F32):
        self.dram[name] = self.nc.dram_tensor(name, list(shape), dtype, kind="Internal").ap()
        self.dbuf[name] = Buf()
        return self.dram[name]

    def sb(self, name, shape, dtype):
        return self.nc.alloc_sbuf_tensor(name, list(shape), dtype)

    def setup(self, consts):
        nc = self.nc
        self.pool1 = self.sb("pool1", [128, 32768], BF16)
        self.pool2 = self.sb("pool2", [128, 32768], BF16)
        self.pool3 = self.sb("pool3", [128, 16384], F32)
        self.b_pool1 = Buf()
        self.b_pool2 = Buf()
        self.b_pool3 = Buf()
        self.P0 = nc.alloc_psum_tensor("P0", [128, 1024], F32)
        self.P1 = nc.alloc_psum_tensor("P1", [128, 1024], F32)
        self.P2 = nc.alloc_psum_tensor("P2", [128, 1024], F32)
        self.PS = nc.alloc_psum_tensor("PS", [128, 512], F32)
        self.PT = nc.alloc_psum_tensor("PT", [128, 1024], BF16)
        self.bP0, self.bP1, self.bP2, self.bPS, self.bPT = Buf(), Buf(), Buf(), Buf(), Buf()
        self.ident = self.sb("ident", [128, 128], BF16)
        self.identf = self.sb("identf", [128, 128], F32)
        self.uinc = self.sb("uinc", [128, 128], BF16)
        self.ppool = self.sb("ppool", [128, 16], BF16)
        self.l32 = self.sb("l32", [32, 32], BF16)
        self.ones_col = self.sb("ones_col", [128, 1], BF16)
        self.ones_row_f = self.sb("ones_row_f", [1, 128], F32)
        self.onesf = self.sb("onesf", [65, 128], F32)
        self.ones_row_b = self.sb("ones_row_b", [1, 128], BF16)
        self.b_const = Buf()
        p = self.p
        for nm in ("c_ident", "c_uinc", "c_ppool", "c_l32", "c_ones_col",
                   "c_ones_row_f", "c_ones_row_b", "c_identf", "c_onesf"):
            dst = {"c_ident": self.ident, "c_uinc": self.uinc, "c_ppool": self.ppool,
                   "c_l32": self.l32, "c_ones_col": self.ones_col,
                   "c_ones_row_f": self.ones_row_f, "c_ones_row_b": self.ones_row_b,
                   "c_identf": self.identf, "c_onesf": self.onesf}[nm]
            src = self.dram[nm]
            p.emit("sync", (lambda d, s: lambda e: e.dma_start(out=d[:], in_=s))(dst, src),
                   writes=[self.b_const], dma=True)
        self.bnc = self.dint("bnc_scr", [128, 1024], F32)
        self.epsln = self.sb("epsln", [128, 1], F32)
        self.p.emit("vector", lambda e: e.memset(self.epsln[:], 1e-5), writes=[self.b_const])
        self.dummy_f = self.P0
        self.dummy_f2 = self.P0
        self.wstg = self.sb("wstg", [128, 2, 1024], F32)
        self.b_wstg = [Buf(), Buf()]
        self.w1b = self.sb("w1b", [128, 2, 1024], BF16)
        self.b_w1 = [Buf(), Buf()]
        self.gwork = self.sb("gwork", [128, 48], F32)
        self.b_gw = Buf()
        self.lnsm = self.sb("lnsm", [128, 8], F32)
        self.b_lnsm = Buf()
        self.xtile = self.wstg
        self.b_xt = self.b_wstg
        self.xntile = self.sb("xntile", [128, 2, 1024], BF16)
        self.b_xn = [Buf(), Buf()]
        self.eps9 = self.sb("eps9", [128, 1], F32)
        self.p.emit("vector", lambda e: e.memset(self.eps9[:], 1e-9), writes=[self.b_const])
        self.pcstg = self.wstg
        self.b_pcstg = self.b_wstg
        self.pcb = self.pool2[:, 24576:25600]
        self.b_pcb = Buf()
        self._pc_q = None
        self._pc_drained = False

    def init_precast(self, w):
        if getattr(self, "_precast_init", False):
            return
        self._precast_init = True
        self.W1C = self.dint("W1C", [NE + NSH, 16, 128, 1024], BF16)
        self.W2C = self.dint("W2C", [NE + NSH, 16, 128, 1024], BF16)
        self.b_wc = [Buf() for _ in range(NE + NSH)]
        q = []
        for ei in range(NE + NSH):
            shared = ei >= NE
            w1src = w["sw1"][ei - NE] if shared else w["w1"][ei]
            w2src = w["sw2"][ei - NE] if shared else w["w2"][ei]
            for hs in range(16):
                q.append((ei, hs, 0, w1src))
                q.append((ei, hs, 1, w2src))
        self._pc_q = q
        self._pc_i = 0

    def pump_precast(self, n):
        if self._pc_q is None:
            return
        p = self.p
        for _ in range(n):
            if self._pc_i >= len(self._pc_q):
                self._pc_q = None
                return
            ei, hs, kind, wsrc = self._pc_q[self._pc_i]
            self._pc_i += 1
            sl = self._pc_i % 2
            if kind == 0:
                p.emit("sync", (lambda hs_=hs, sl_=sl, ws=wsrc: lambda e: e.dma_start(
                    out=self.pcstg[:, sl_, :].rearrange("p (c n) -> p c n", c=8),
                    in_=ws[:, hs_ * 128:(hs_ + 1) * 128].rearrange("(c p) n -> p c n", p=128)))(),
                    writes=[self.b_pcstg[sl]], dma=True)
            else:
                p.emit("sync", (lambda hs_=hs, sl_=sl, ws=wsrc: lambda e: e.dma_start(
                    out=self.pcstg[:, sl_, :],
                    in_=ws[hs_ * 128:(hs_ + 1) * 128, :]))(),
                    writes=[self.b_pcstg[sl]], dma=True)
            p.emit("scalar", (lambda sl_=sl: lambda e: e.activation(
                self.pcb[:], self.pcstg[:, sl_, :], AF.Copy))(),
                reads=[self.b_pcstg[sl]], writes=[self.b_pcb])
            dstT = self.W1C if kind == 0 else self.W2C
            p.emit("scalar", (lambda ei_=ei, hs_=hs, d_=dstT: lambda e: e.dma_start(
                out=d_[ei_][hs_], in_=self.pcb[:]))(),
                reads=[self.b_pcb], writes=[self.b_wc[ei]], dma=True)

    def flush_precast(self):
        while self._pc_q is not None:
            self.pump_precast(8)
        self.drain_precast()

    def drain_precast(self):
        if self._pc_drained or self._pc_q is not None or not getattr(self, "_precast_init", False):
            return
        self._pc_drained = True
        self.p.emit("scalar", lambda e: e.copy(self.pcb[0:1, 0:8], self.pcb[0:1, 0:8]),
                    reads=[self.b_pcb], writes=[self.b_pcb])


    def u(self):
        self.uid += 1
        return self.uid

    def ln_tile(self, xt, bxt, out_ap, bout, smalls, bsm, dummy=None, bdummy=None):
        p = self.p
        if dummy is None:
            dummy, bdummy = self.P0, self.bP0
        ss, s_, m, a, var, rs = (smalls[:, i:i + 1] for i in range(6))
        p.emit("vector", (lambda d_=dummy: lambda e: e.scalar_tensor_tensor(
            d_[:], xt, 1.0, xt, ALU.mult, ALU.mult, accum_out=ss))(),
            reads=[bxt], writes=[bdummy, bsm])
        p.emit("vector", (lambda d_=dummy: lambda e: e.tensor_scalar(
            d_[:], xt, 0.0, None, ALU.add, ALU.add, accum_out=s_))(),
            reads=[bxt], writes=[bdummy, bsm])
        p.emit("vector", lambda e: e.tensor_scalar(m, s_, 1.0 / D, None, ALU.mult),
               reads=[bsm], writes=[bsm])
        p.emit("vector", lambda e: e.tensor_scalar(a, ss, 1.0 / D, None, ALU.mult),
               reads=[bsm], writes=[bsm])
        p.emit("vector", lambda e: e.scalar_tensor_tensor(
            var, m, 1.0, m, ALU.mult, ALU.mult), reads=[bsm], writes=[bsm])
        p.emit("vector", lambda e: e.tensor_sub(var, a, var), reads=[bsm], writes=[bsm])
        p.emit("scalar", lambda e: e.activation(a, var, AF.Ln, bias=self.epsln[:]),
               reads=[bsm], writes=[bsm])
        p.emit("scalar", lambda e: e.activation(rs, a, AF.Exp, scale=-0.5),
               reads=[bsm], writes=[bsm])
        p.emit("vector", lambda e: e.tensor_scalar(
            out_ap, xt, m, rs, ALU.subtract, ALU.mult),
            reads=[bxt, bsm], writes=[bout])

    def transpose8(self, src_ap, bsrc, dst_view, bdst):
        p = self.p
        for c in range(8):
            p.emit("tensor", (lambda c_: lambda e: e.transpose(
                self.PT[:, c_ * 128:(c_ + 1) * 128], src_ap[:, c_ * 128:(c_ + 1) * 128],
                self.ident[:]))(c),
                reads=[bsrc, self.b_const], writes=[self.bPT])
        p.emit("vector", lambda e: e.tensor_copy(
            dst_view, self.PT[:].rearrange("p (c n) -> p c n", c=8)),
            reads=[self.bPT], writes=[bdst])

    def stage_ffn(self, src, bsrc, dst, bdst, w):
        p = self.p
        nc = self.nc
        self.init_precast(w)
        self.flush_precast()
        p3 = self.pool3[:].bitcast(BF16)
        out_acc = self.pool3[:, 0:8192].rearrange("p (j n) -> p j n", j=8)
        b_acc = self.b_pool3
        xnT = p3[:, 16384:24576].rearrange("p (c n) -> p c n", c=8)
        b_xnT = Buf()
        hT = self.pool2[:, 0:32768].rearrange("p (r n) -> p r n", r=2)
        hTv = [hT[:, 0, :].rearrange("p (h n) -> p h n", h=16),
               hT[:, 1, :].rearrange("p (h n) -> p h n", h=16)]
        b_hT = [Buf(), Buf()]
        w2b = self.pool1[:, 0:32768].rearrange("p (r n) -> p r n", r=2)
        w2v = [w2b[:, 0, :].rearrange("p (h n) -> p h n", h=16),
               w2b[:, 1, :].rearrange("p (h n) -> p h n", h=16)]
        b_w2 = [Buf(), Buf()]
        if not hasattr(self, "ffn_bufs"):
            self.rwb = self.sb("rwb", [128, 8, 8], BF16)
            self.rbrow = self.sb("rbrow", [1, 8], F32)
            self.bext = self.sb("bext", [16, 1024], BF16)
            self.b1col = self.sb("b1col", [128, 2, 16], F32)
            self.b1stg = self.sb("b1stg", [16, 2, 128], F32)
            self.combf = self.sb("combf", [128, 8, 8], F32)
            self.combx = self.sb("combx", [128, 16], BF16)
            self.combxT = self.sb("combxT", [16, 2, 128], BF16)
            self.ffn_bufs = True
            self.b_rw = Buf()
            self.b_bext = Buf()
            self.b_b1 = [Buf(), Buf()]
            self.b_comb = Buf()
            p.emit("vector", lambda e: e.memset(self.combx[:], 0.0), writes=[self.b_comb])
            p.emit("vector", lambda e: e.memset(self.combx[:, 8:10], 0.5), writes=[self.b_comb])
        p.emit("sync", lambda e: e.dma_start(
            out=self.wstg[:, 0, 0:64].rearrange("p (c e) -> p c e", c=8),
            in_=w["router_w"].rearrange("(c p) e -> p c e", p=128)),
               writes=[self.b_wstg[0]], dma=True)
        p.emit("vector", lambda e: e.tensor_copy(
            self.rwb[:].rearrange("p a b -> p (a b)"), self.wstg[:, 0, 0:64]),
               reads=[self.b_wstg[0]], writes=[self.b_rw])
        p.emit("sync", lambda e: e.dma_start(out=self.rbrow[:], in_=w["router_b"].unsqueeze(0)),
               writes=[self.b_rw], dma=True)
        p.emit("vector", lambda e: e.memset(self.bext[:], 0.0), writes=[self.b_bext])
        p.emit("sync", lambda e: e.dma_start(out=self.wstg[0:8, 1, :], in_=w["b2"]),
               writes=[self.b_wstg[1]], dma=True)
        p.emit("sync", lambda e: e.dma_start(out=self.wstg[8:10, 1, :], in_=w["sb2"]),
               writes=[self.b_wstg[1]], dma=True)
        p.emit("vector", lambda e: e.tensor_copy(self.bext[0:10, :], self.wstg[0:10, 1, :]),
               reads=[self.b_wstg[1]], writes=[self.b_bext])

        outs = []
        for qt in range(4):
            r0 = qt * 1024
            for j in range(8):
                xt = self.xtile[:, j % 2, :]
                bxt = self.b_xt[j % 2]
                p.emit("sync", (lambda j_=j, xt_=xt, r0_=r0: lambda e: e.dma_start(
                    out=xt_, in_=src[r0_ + j_ * 128: r0_ + (j_ + 1) * 128, :]))(),
                    reads=[bsrc], writes=[bxt], dma=True)
                xn = self.xntile[:, j % 2, :]
                bxn = self.b_xn[j % 2]
                self.ln_tile(xt, bxt, xn, bxn, self.lnsm, self.b_lnsm)
                self.transpose8(xn, bxn, xnT[:, :, j * 128:(j + 1) * 128], b_xnT)
                for c in range(8):
                    p.emit("tensor", (lambda c_=c, j_=j: lambda e: e.matmul(
                        self.PS[:, 0:8], xnT[:, c_, j_ * 128:(j_ + 1) * 128],
                        self.rwb[:, c_, :], start=(c_ == 0), stop=False))(),
                        reads=[b_xnT, self.b_rw], writes=[self.bPS])
                p.emit("tensor", lambda e: e.matmul(
                    self.PS[:, 0:8], self.ones_row_f[:], self.rbrow[:],
                    start=False, stop=True),
                    reads=[self.b_rw, self.b_const], writes=[self.bPS])
                g = self.gwork
                bg = self.b_gw
                gs_, m1, e1, g2, m2, e2, d12, p1, p2, t1 = (
                    g[:, 0:8], g[:, 8:9], g[:, 9:17], g[:, 17:25], g[:, 25:26],
                    g[:, 26:34], g[:, 34:35], g[:, 35:36], g[:, 36:37], g[:, 37:45])
                comb = self.combf[:, j, :]
                p.emit("vector", lambda e: e.tensor_copy(gs_, self.PS[:, 0:8]),
                       reads=[self.bPS], writes=[bg])
                p.emit("vector", lambda e: e.reduce_max(m1, gs_, AX.X), reads=[bg], writes=[bg])
                p.emit("vector", lambda e: e.tensor_scalar(e1, gs_, m1, None, ALU.is_ge),
                       reads=[bg], writes=[bg])
                p.emit("vector", lambda e: e.scalar_tensor_tensor(
                    g2, e1, -1e30, gs_, ALU.mult, ALU.add), reads=[bg], writes=[bg])
                p.emit("vector", lambda e: e.reduce_max(m2, g2, AX.X), reads=[bg], writes=[bg])
                p.emit("vector", lambda e: e.tensor_scalar(e2, g2, m2, None, ALU.is_ge),
                       reads=[bg], writes=[bg])
                p.emit("vector", lambda e: e.tensor_tensor(d12, m2, m1, ALU.subtract),
                       reads=[bg], writes=[bg])
                p.emit("scalar", lambda e: e.activation(p2, d12, AF.Exp),
                       reads=[bg], writes=[bg])
                p.emit("vector", lambda e: e.tensor_scalar(p1, p2, 1.0, None, ALU.add),
                       reads=[bg], writes=[bg])
                p.emit("vector", lambda e: e.reciprocal(p1, p1), reads=[bg], writes=[bg])
                p.emit("vector", lambda e: e.tensor_tensor(p2, p2, p1, ALU.mult),
                       reads=[bg], writes=[bg])
                p.emit("vector", (lambda e1_=e1, p1_=p1, t1_=t1: lambda e: e.tensor_scalar(
                    t1_, e1_, p1_, None, ALU.mult))(), reads=[bg], writes=[bg])
                p.emit("vector", (lambda comb_=comb: lambda e: e.scalar_tensor_tensor(
                    comb_, e2, p2, t1, ALU.mult, ALU.add))(),
                    reads=[bg], writes=[self.b_comb])
                p.emit("vector", (lambda comb_=comb: lambda e: e.tensor_copy(
                    self.combx[:, 0:8], comb_))(), reads=[self.b_comb], writes=[self.b_comb])
                p.emit("tensor", lambda e: e.transpose(
                    self.PT[0:16, 0:128], self.combx[:], self.ident[:]),
                    reads=[self.b_comb, self.b_const], writes=[self.bPT])
                p.emit("vector", (lambda j_=j: lambda e: e.tensor_copy(
                    self.combxT[:, j_ % 2, :], self.PT[0:16, 0:128]))(),
                    reads=[self.bPT], writes=[self.b_comb])
                for mv in range(2):
                    p.emit("tensor", (lambda j_=j, mv_=mv: lambda e: e.matmul(
                        self.P2[:, mv_ * 512:(mv_ + 1) * 512], self.combxT[:, j_ % 2, :],
                        self.bext[:, mv_ * 512:(mv_ + 1) * 512],
                        start=True, stop=True))(),
                        reads=[self.b_comb, self.b_bext], writes=[self.bP2])
                p.emit("vector", (lambda j_=j, xt_=xt: lambda e: e.scalar_tensor_tensor(
                    out_acc[:, j_, :], self.P2[:], 1.0, xt_, ALU.mult, ALU.add))(),
                    reads=[self.bP2, bxt], writes=[b_acc])
            for ei in range(NE + NSH):
                par = ei % 2
                shared = ei >= NE
                w1src = w["sw1"][ei - NE] if shared else w["w1"][ei]
                w2src = w["sw2"][ei - NE] if shared else w["w2"][ei]
                b1src = w["sb1"][ei - NE] if shared else w["b1"][ei]
                p.emit("sync", (lambda b1s=b1src, par_=par: lambda e: e.dma_start(
                    out=self.b1stg[:, par_, :], in_=b1s.rearrange("(s p) -> s p", p=128)))(),
                    writes=[self.b_b1[par]], dma=True)
                ptf = self.PT[:].bitcast(F32)
                p.emit("tensor", (lambda par_=par: lambda e: e.transpose(
                    ptf[:, 0:16], self.b1stg[:, par_, :], self.identf[0:16, 0:16]))(),
                    reads=[self.b_b1[par], self.b_const], writes=[self.bPT])
                p.emit("vector", (lambda par_=par: lambda e: e.tensor_copy(
                    self.b1col[:, par_, :], ptf[:, 0:16]))(),
                    reads=[self.bPT], writes=[self.b_b1[par]])
                for hs in range(16):
                    sl = hs % 2
                    p.emit("sync",
                           (lambda hs_=hs, sl_=sl, ei_=ei: lambda e: e.dma_start(
                               out=self.w1b[:, sl_, :], in_=self.W1C[ei_][hs_]))(),
                           reads=[self.b_wc[ei]], writes=[self.b_w1[sl]], dma=True)
                    pp = [self.P0, self.P1][hs % 2]
                    bpp = [self.bP0, self.bP1][hs % 2]
                    for mv in range(2):
                        for c in range(8):
                            p.emit("tensor", (lambda sl_=sl, c_=c, mv_=mv, pp_=pp: lambda e: e.matmul(
                                pp_[:, mv_ * 512:(mv_ + 1) * 512],
                                self.w1b[:, sl_, c_ * 128:(c_ + 1) * 128],
                                xnT[:, c_, mv_ * 512:(mv_ + 1) * 512],
                                start=(c_ == 0), stop=(c_ == 7)))(),
                                reads=[self.b_w1[sl], b_xnT], writes=[bpp])
                    p.emit("scalar", (lambda hs_=hs, pp_=pp, par_=par: lambda e: e.activation(
                        hTv[par_][:, hs_, :], pp_[:],
                        AF.Gelu, bias=self.b1col[:, par_, hs_:hs_ + 1]))(),
                        reads=[bpp, self.b_b1[par]], writes=[b_hT[par]])
                    hc = hs
                    p.emit("scalar",
                           (lambda hc_=hc, ei_=ei, par_=par: lambda e: e.dma_start(
                               out=w2v[par_][:, hc_, :], in_=self.W2C[ei_][hc_]))(),
                           reads=[self.b_wc[ei]], writes=[b_w2[par]], dma=True)
                for j in range(8):
                    yp = [self.P0, self.P1, self.P2][j % 3]
                    byp = [self.bP0, self.bP1, self.bP2][j % 3]
                    for mv in range(2):
                        for hc in range(16):
                            p.emit("tensor", (lambda j_=j, hc_=hc, mv_=mv, par_=par, yp_=yp: lambda e: e.matmul(
                                yp_[:, mv_ * 512:(mv_ + 1) * 512],
                                hTv[par_][:, hc_, j_ * 128:(j_ + 1) * 128],
                                w2v[par_][:, hc_, mv_ * 512:(mv_ + 1) * 512],
                                start=(hc_ == 0), stop=(hc_ == 15)))(),
                                reads=[b_hT[par], b_w2[par]], writes=[byp])
                    if shared:
                        p.emit("vector", (lambda j_=j, yp_=yp: lambda e: e.scalar_tensor_tensor(
                            out_acc[:, j_, :], yp_[:], 0.5, out_acc[:, j_, :],
                            ALU.mult, ALU.add))(),
                            reads=[byp, b_acc], writes=[b_acc])
                    else:
                        p.emit("vector", (lambda j_=j, ei_=ei, yp_=yp: lambda e: e.scalar_tensor_tensor(
                            out_acc[:, j_, :], yp_[:], self.combf[:, j_, ei_:ei_ + 1],
                            out_acc[:, j_, :], ALU.mult, ALU.add))(),
                            reads=[byp, b_acc, self.b_comb], writes=[b_acc])
            for j in range(8):
                outs.append(p.emit("sync", (lambda j_=j, r0_=r0: lambda e: e.dma_start(
                    out=dst[r0_ + j_ * 128: r0_ + (j_ + 1) * 128, :], in_=out_acc[:, j_, :]))(),
                    reads=[b_acc], writes=[bdst], dma=True))
        return outs

    def stage_attn_a(self, src, bsrc, dst, bdst, w):
        p = self.p
        NSEL_PAD = 512
        xnT = self.pool1[:, 0:32768].rearrange("p (c n) -> p c n", c=8)
        b_xnT = self.b_pool1
        p2b = self.pool2
        qTaug = [p2b[:, 0:4096], p2b[:, 4096:8192]]
        b_qTa = [Buf(), Buf()]
        kTaug = [p2b[:, 8192:12288], p2b[:, 12288:16384]]
        b_kTa = [Buf(), Buf()]
        kv_s = p2b[:, 16384:20480].rearrange("p (j n) -> p j n", j=32)
        b_kv = Buf()
        ering = p2b[:, 20480:24576].rearrange("p (r n) -> p r n", r=4)
        b_er = [Buf(), Buf(), Buf(), Buf()]
        p3 = self.pool3[:].bitcast(BF16)
        wsl = p3[:, 0:1024].rearrange("p (r n) -> p r n", r=2)
        b_wsl = [Buf(), Buf()]
        kcT = [p3[0:64, 1024 + i * 512: 1024 + (i + 1) * 512] for i in range(16)]
        b_kcT = [Buf() for _ in range(16)]
        vcn = p3[:, 9216:13376].rearrange("p (h n) -> p h n", h=16)
        b_vcn = Buf()
        rgcol = self.pool3[:, 6688:6752].rearrange("p (h n) -> p h n", h=16)
        b_rg = Buf()
        st = self.pool3[:, 6752:7008].rearrange("p (k n) -> p k n", k=8)
        eacc, macc, iacc, lnzt, ut, uhit, ulot, n2t = (st[:, i, :] for i in range(8))
        b_st = Buf()
        b_lnz = Buf()
        b_u = Buf()
        zp = self.gwork
        p.emit("vector", lambda e: e.memset(eacc, 0.0), writes=[b_st])
        p.emit("vector", lambda e: e.memset(macc, 0.0), writes=[b_st])
        p.emit("vector", lambda e: e.memset(iacc, 0.0), writes=[b_st])
        for sl in range(2):
            p.emit("vector", (lambda sl_=sl: lambda e: e.memset(kTaug[sl_][64:67, :], 0.0))(),
                   writes=[b_kTa[sl]])
            p.emit("vector", (lambda sl_=sl: lambda e: e.tensor_scalar(
                kTaug[sl_][64:67, :], kTaug[sl_][64:67, :], 1.0, None, ALU.add))(),
                reads=[b_kTa[sl]], writes=[b_kTa[sl]])
        p.emit("vector", lambda e: e.memset(
            vcn[:].rearrange("p h n -> p (h n)").rearrange("p (g o) -> p g o", o=65)[:, :, 64:65], 1.0),
            writes=[b_vcn])
        bncb = self.dint("bncb_scr", [96, 128], BF16)
        b_bncb = Buf()
        staKC = self.sb("staKC", [128, 132], BF16)
        staV = self.sb("staV", [128, 64], BF16)
        b_staK, b_staV = Buf(), Buf()
        p.emit("vector", lambda e: e.memset(staKC[:, 65:129], 0.0), writes=[b_staK])

        def load_wslice(name, h, slot, bslot):
            p.emit("sync", (lambda h_=h: lambda e: e.dma_start(
                out=self.wstg[:, 0, 0:512].rearrange("p (c n) -> p c n", c=8),
                in_=w[name][:, h_ * 64:(h_ + 1) * 64].rearrange("(c p) n -> p c n", p=128)))(),
                writes=[self.b_wstg[0]], dma=True)
            p.emit("vector", (lambda slot_=slot: lambda e: e.tensor_copy(
                slot_, self.wstg[:, 0, 0:512]))(),
                reads=[self.b_wstg[0]], writes=[bslot])

        for j in range(NT):
            xt = self.xtile[:, j % 2, :]
            bxt = self.b_xt[j % 2]
            p.emit("sync", (lambda j_=j, xt_=xt: lambda e: e.dma_start(
                out=xt_, in_=src[j_ * 128:(j_ + 1) * 128, :]))(),
                reads=[bsrc], writes=[bxt], dma=True)
            xn = self.xntile[:, j % 2, :]
            bxn = self.b_xn[j % 2]
            self.ln_tile(xt, bxt, xn, bxn, self.lnsm, self.b_lnsm)
            self.transpose8(xn, bxn, xnT[:, :, j * 128:(j + 1) * 128], b_xnT)

        for h in range(NH):
            sl = h % 2
            qT, bqT = qTaug[sl], b_qTa[sl]
            kT, bkT = kTaug[sl], b_kTa[sl]
            load_wslice("Wq", h, wsl[:, 0, :], b_wsl[0])
            wq8 = wsl[:, 0, :].rearrange("p (c n) -> p c n", c=8)
            for tb8 in range(8):
                ppj = [self.P0, self.P1][tb8 % 2]
                bppj = [self.bP0, self.bP1][tb8 % 2]
                for c in range(8):
                    p.emit("tensor", (lambda tb_=tb8, c_=c, ppj_=ppj: lambda e: e.matmul(
                        ppj_[0:64, 0:512], wq8[:, c_, :],
                        xnT[:, c_, tb_ * 512:(tb_ + 1) * 512],
                        start=(c_ == 0), stop=(c_ == 7)))(),
                        reads=[b_wsl[0], b_xnT], writes=[bppj])
                p.emit("scalar", (lambda tb_=tb8, qT_=qT, ppj_=ppj: lambda e: e.activation(
                    qT_[0:64, tb_ * 512:(tb_ + 1) * 512], ppj_[0:64, 0:512],
                    AF.Copy, scale=0.125))(),
                    reads=[bppj], writes=[bqT])
                if tb8 % 2 == 1:
                    self.pump_precast(2)
            kvw = self.w1b[:, 0, :].rearrange("p (c n) -> p c n", c=8)
            p.emit("sync", (lambda h_=h: lambda e: e.dma_start(
                out=self.wstg[:, 0, 0:512].rearrange("p (c n) -> p c n", c=8),
                in_=w["Wk"][:, h_ * 64:(h_ + 1) * 64].rearrange("(c p) n -> p c n", p=128)))(),
                writes=[self.b_wstg[0]], dma=True)
            p.emit("sync", (lambda h_=h: lambda e: e.dma_start(
                out=self.wstg[:, 1, 0:512].rearrange("p (c n) -> p c n", c=8),
                in_=w["Wv"][:, h_ * 64:(h_ + 1) * 64].rearrange("(c p) n -> p c n", p=128)))(),
                writes=[self.b_wstg[1]], dma=True)
            p.emit("vector", lambda e: e.tensor_copy(
                kvw[:, :, 0:64], self.wstg[:, 0, 0:512].rearrange("p (c n) -> p c n", c=8)),
                reads=[self.b_wstg[0]], writes=[self.b_w1[0]])
            p.emit("vector", lambda e: e.tensor_copy(
                kvw[:, :, 64:128], self.wstg[:, 1, 0:512].rearrange("p (c n) -> p c n", c=8)),
                reads=[self.b_wstg[1]], writes=[self.b_w1[0]])
            for tb8 in range(8):
                ppj = [self.P0, self.P1][tb8 % 2]
                bppj = [self.bP0, self.bP1][tb8 % 2]
                for c in range(8):
                    p.emit("tensor", (lambda tb_=tb8, c_=c, ppj_=ppj: lambda e: e.matmul(
                        ppj_[0:64, 0:512], kvw[:, c_, 0:64],
                        xnT[:, c_, tb_ * 512:(tb_ + 1) * 512],
                        start=(c_ == 0), stop=(c_ == 7)))(),
                        reads=[self.b_w1[0], b_xnT], writes=[bppj])
                p.emit("scalar", (lambda tb_=tb8, kT_=kT, ppj_=ppj: lambda e: e.activation(
                    kT_[0:64, tb_ * 512:(tb_ + 1) * 512], ppj_[0:64, 0:512], AF.Copy))(),
                    reads=[bppj], writes=[bkT])
                if tb8 % 2 == 1:
                    self.pump_precast(2)
            for j in range(NT):
                off = (j % 2) * 128
                for c in range(8):
                    p.emit("tensor", (lambda j_=j, c_=c, off_=off: lambda e: e.matmul(
                        self.P2[:, off_:off_ + 128], xnT[:, c_, j_ * 128:(j_ + 1) * 128],
                        kvw[:, c_, :], start=(c_ == 0), stop=(c_ == 7)))(),
                        reads=[self.b_w1[0], b_xnT], writes=[self.bP2])
                p.emit("vector", (lambda j_=j, off_=off: lambda e: e.tensor_copy(
                    kv_s[:, j_, :], self.P2[:, off_:off_ + 128]))(),
                    reads=[self.bP2], writes=[b_kv])
                if j % 4 == 3:
                    self.pump_precast(1)
            zarr = self.pool3[:, 7040:7168]
            zarr = self.pool3[:, 8096:8224]
            warr = self.pool3[:, 8224:8352]
            marr = self.pool3[:, 8352:8480]
            b_zarr, b_warr, b_marr = Buf(), Buf(), Buf()
            for j in range(NT):
                for cb in range(4):
                    pp = [self.P0, self.P1][cb % 2]
                    bpp = [self.bP0, self.bP1][cb % 2]
                    for mv in range(2):
                        p.emit("tensor", (lambda j_=j, cb_=cb, mv_=mv, pp_=pp, qT_=qT, kT_=kT: lambda e: e.matmul(
                            pp_[:, mv_ * 512:(mv_ + 1) * 512],
                            qT_[0:64, j_ * 128:(j_ + 1) * 128],
                            kT_[0:64, (cb_ * 2 + mv_) * 512:(cb_ * 2 + mv_ + 1) * 512],
                            start=True, stop=True))(),
                            reads=[bqT, bkT], writes=[bpp])
                    er = ering[:, cb, :]
                    ber = b_er[cb]
                    p.emit("scalar", (lambda j_=j, cb_=cb, pp_=pp, er_=er: lambda e: e.activation(
                        er_, pp_[:], AF.Exp, accum_out=zarr[:, j_ * 4 + cb_:j_ * 4 + cb_ + 1]))(),
                        reads=[bpp], writes=[ber, b_zarr])
                    p.emit("vector", (lambda j_=j, cb_=cb, er_=er: lambda e: e.tensor_scalar(
                        er_, er_, 0.0, None, ALU.add, ALU.max,
                        accum_out=marr[:, j_ * 4 + cb_:j_ * 4 + cb_ + 1]))(),
                        reads=[ber], writes=[ber, b_marr])
                    p.emit("vector", (lambda j_=j, cb_=cb, pp_=pp, er_=er: lambda e: e.scalar_tensor_tensor(
                        er_, er_, 1.0, pp_[:], ALU.mult, ALU.mult,
                        accum_out=warr[:, j_ * 4 + cb_:j_ * 4 + cb_ + 1]))(),
                        reads=[ber, bpp], writes=[ber, b_warr])
            zt = self.pool3[:, 8480:8512]
            wt = self.pool3[:, 8512:8544]
            rzt = self.pool3[:, 8544:8576]
            b_zc = Buf()
            p.emit("vector", lambda e: e.tensor_reduce(
                zt, zarr.rearrange("p (j c) -> p j c", c=4), AX.X, ALU.add),
                reads=[b_zarr], writes=[b_zc])
            p.emit("vector", lambda e: e.tensor_reduce(
                wt, warr.rearrange("p (j c) -> p j c", c=4), AX.X, ALU.add),
                reads=[b_warr], writes=[b_zc])
            p.emit("vector", lambda e: e.tensor_reduce(
                rzt, marr.rearrange("p (j c) -> p j c", c=4), AX.X, ALU.max),
                reads=[b_marr], writes=[b_zc])
            p.emit("scalar", lambda e: e.activation(lnzt, zt, AF.Ln),
                   reads=[b_zc], writes=[b_lnz])
            p.emit("vector", lambda e: e.reciprocal(zt, zt), reads=[b_zc], writes=[b_zc])
            p.emit("vector", lambda e: e.tensor_tensor(rzt, rzt, zt, ALU.mult),
                   reads=[b_zc], writes=[b_zc])
            p.emit("vector", lambda e: e.tensor_add(macc, macc, rzt), reads=[b_zc], writes=[b_st])
            p.emit("vector", lambda e: e.tensor_tensor(wt, wt, zt, ALU.mult),
                   reads=[b_zc], writes=[b_zc])
            p.emit("vector", lambda e: e.tensor_sub(wt, lnzt, wt), reads=[b_zc, b_lnz], writes=[b_zc])
            p.emit("vector", lambda e: e.tensor_add(eacc, eacc, wt), reads=[b_zc], writes=[b_st])
            g = zp
            spl = ering[:, 0, 0:96].rearrange("p (v n) -> p v n", v=3)
            p.emit("vector", lambda e: e.tensor_scalar(g[:, 20:52] if False else lnzt, lnzt, -1.0, None, ALU.mult),
                   reads=[b_lnz], writes=[b_lnz])
            p.emit("vector", lambda e: e.tensor_copy(spl[:, 0, :], lnzt),
                   reads=[b_lnz], writes=[b_er[0]])
            tmp32 = self.pool3[:, 7008:7040]
            b_t32 = Buf()
            p.emit("vector", lambda e: e.tensor_sub(tmp32, lnzt, spl[:, 0, :]),
                   reads=[b_lnz, b_er[0]], writes=[b_t32])
            p.emit("vector", lambda e: e.tensor_copy(spl[:, 1, :], tmp32),
                   reads=[b_t32], writes=[b_er[0]])
            p.emit("vector", lambda e: e.tensor_sub(tmp32, tmp32, spl[:, 1, :]),
                   reads=[b_t32, b_er[0]], writes=[b_t32])
            p.emit("vector", lambda e: e.tensor_copy(spl[:, 2, :], tmp32),
                   reads=[b_t32], writes=[b_er[0]])
            p.emit("tensor", lambda e: e.transpose(
                self.PT[0:96, 0:128], spl.rearrange("p v n -> p (v n)"), self.ident[:]),
                reads=[b_er[0], self.b_const], writes=[self.bPT])
            p.emit("vector", lambda e: e.tensor_copy(ering[0:96, 1, 0:128], self.PT[0:96, 0:128]),
                   reads=[self.bPT], writes=[b_er[1]])
            p.emit("sync", lambda e: e.dma_start(out=self.dram["bncb_scr"], in_=ering[0:96, 1, 0:128]),
                   reads=[b_er[1]], writes=[b_bncb], dma=True)
            for v3 in range(3):
                p.emit("sync", (lambda v_=v3, qT_=qT: lambda e: e.dma_start(
                    out=qT_[64 + v_:65 + v_, :],
                    in_=self.dram["bncb_scr"].rearrange("a b -> (a b)").unsqueeze(0)[:, v_ * 4096:(v_ + 1) * 4096]))(),
                    reads=[b_bncb], writes=[bqT], dma=True)
            iar = self.pool3[:, 8576:8704]
            b_iar = Buf()
            swdum = p3[:, 23744:25792].rearrange("p (r n) -> p r n", r=2)
            b_swd = [Buf(), Buf()]
            for c32 in range(NT):
                for qb2 in range(4):
                    pp = [self.P0, self.P1][qb2 % 2]
                    bpp = [self.bP0, self.bP1][qb2 % 2]
                    for mv in range(2):
                        p.emit("tensor", (lambda c_=c32, q_=qb2, mv_=mv, pp_=pp, qT_=qT, kT_=kT: lambda e: e.matmul(
                            pp_[:, mv_ * 512:(mv_ + 1) * 512],
                            kT_[0:67, c_ * 128:(c_ + 1) * 128],
                            qT_[0:67, (q_ * 2 + mv_) * 512:(q_ * 2 + mv_ + 1) * 512],
                            start=True, stop=True))(),
                            reads=[bqT, bkT], writes=[bpp])
                    p.emit("scalar", (lambda c_=c32, q_=qb2, pp_=pp: lambda e: e.activation(
                        swdum[:, q_ % 2, :], pp_[:], AF.Exp,
                        accum_out=iar[:, c_ * 4 + q_:c_ * 4 + q_ + 1]))(),
                        reads=[bpp], writes=[b_swd[qb2 % 2], b_iar])
            p.emit("vector", lambda e: e.tensor_reduce(
                self.pool3[:, 8704:8736], iar.rearrange("p (j c) -> p j c", c=4), AX.X, ALU.add),
                reads=[b_iar], writes=[b_iar])
            p.emit("vector", lambda e: e.tensor_add(iacc, iacc, self.pool3[:, 8704:8736]),
                   reads=[b_iar], writes=[b_st])
            p.emit("vector", lambda e: e.tensor_tensor(
                ering[:, 0:2, :].rearrange("p r n -> p (r n)").rearrange("p (j n) -> p j n", n=64),
                kv_s[:][:, :, 0:64], kv_s[:][:, :, 0:64], ALU.mult),
                reads=[b_kv], writes=[b_er[0], b_er[1]])
            p.emit("vector", lambda e: e.tensor_reduce(
                n2t, ering[:, 0:2, :].rearrange("p r n -> p (r n)").rearrange("p (j n) -> p j n", n=64),
                AX.X, ALU.add),
                reads=[b_er[0], b_er[1]], writes=[b_st])
            p.emit("scalar", lambda e: e.activation(ut, n2t, AF.Ln), reads=[b_st], writes=[b_u])
            p.emit("scalar", lambda e: e.activation(ut, ut, AF.Exp, scale=0.5), reads=[b_u], writes=[b_u])
            p.emit("vector", lambda e: e.reduce_max(zp[:, 30:31], ut, AX.X),
                   reads=[b_u], writes=[self.b_gw])
            p.emit("vector", lambda e: e.tensor_copy(staV[:, 0:1], zp[:, 30:31]),
                   reads=[self.b_gw], writes=[b_staV])
            p.emit("tensor", lambda e: e.transpose(
                self.PT[0:1, 0:128], staV[:, 0:1], self.ident[:]),
                reads=[b_staV, self.b_const], writes=[self.bPT])
            p.emit("vector", lambda e: e.tensor_copy(ering[0:1, 0, 0:128], self.PT[0:1, 0:128]),
                   reads=[self.bPT], writes=[b_er[0]])
            p.emit("vector", lambda e: e.reduce_max(zp[0:1, 31:32], ering[0:1, 0, 0:128], AX.X),
                   reads=[b_er[0]], writes=[self.b_gw])
            p.emit("vector", lambda e: e.tensor_scalar(zp[0:1, 32:33], zp[0:1, 31:32], -1.0, None, ALU.mult),
                   reads=[self.b_gw], writes=[self.b_gw])
            p.emit("tensor", lambda e: e.matmul(
                self.P2[:, 0:1], self.onesf[0:1, :], zp[0:1, 32:33], start=True, stop=True),
                reads=[self.b_gw, self.b_const], writes=[self.bP2])
            p.emit("vector", lambda e: e.tensor_copy(zp[:, 33:34], self.P2[:, 0:1]),
                   reads=[self.bP2], writes=[self.b_gw])
            p.emit("vector", lambda e: e.tensor_scalar(ut, ut, zp[:, 33:34], None, ALU.add),
                   reads=[self.b_gw, b_u], writes=[b_u])
            p.emit("scalar", lambda e: e.activation(ut, ut, AF.Exp, accum_out=zp[:, 34:35]),
                   reads=[b_u], writes=[b_u, self.b_gw])
            p.emit("vector", lambda e: e.tensor_copy(staV[:, 1:2], zp[:, 34:35]),
                   reads=[self.b_gw], writes=[b_staV])
            p.emit("tensor", lambda e: e.matmul(
                self.P2[0:1, 1:2], self.ones_col[:], staV[:, 1:2], start=True, stop=True),
                reads=[b_staV, self.b_const], writes=[self.bP2])
            p.emit("vector", lambda e: e.tensor_copy(zp[0:1, 35:36], self.P2[0:1, 1:2]),
                   reads=[self.bP2], writes=[self.b_gw])
            p.emit("vector", lambda e: e.tensor_scalar(zp[0:1, 35:36], zp[0:1, 35:36], 1e-8, None, ALU.mult),
                   reads=[self.b_gw], writes=[self.b_gw])
            p.emit("tensor", lambda e: e.matmul(
                self.P2[:, 2:3], self.onesf[0:1, :], zp[0:1, 35:36], start=True, stop=True),
                reads=[self.b_gw, self.b_const], writes=[self.bP2])
            p.emit("vector", lambda e: e.tensor_copy(zp[:, 36:37], self.P2[:, 2:3]),
                   reads=[self.bP2], writes=[self.b_gw])
            p.emit("vector", lambda e: e.tensor_copy(uhit, ut), reads=[b_u], writes=[b_u])
            p.emit("vector", lambda e: e.tensor_sub(ulot, ut, uhit), reads=[b_u], writes=[b_u])
            staKall = p3[:, 17472:21696].rearrange("p (j n) -> p j n", j=32)
            vwall = p3[:, 21696:23744].rearrange("p (j n) -> p j n", j=32)
            if h == 0:
                p.emit("vector", lambda e: e.memset(
                    staKall[:].rearrange("p j n -> p (j n)").rearrange("p (j o) -> p j o", o=132)[:, :, 65:129], 0.0),
                    writes=[b_staK])
            p.emit("vector", lambda e: e.tensor_tensor(
                staKall[:, :, 0:64],
                kv_s[:].rearrange("p j n -> p j n")[:, :, 0:64],
                ut.unsqueeze(2).broadcast_to([128, 32, 64]), ALU.mult),
                reads=[b_kv, b_u], writes=[b_staK])
            p.emit("vector", lambda e: e.tensor_copy(staKall[:, :, 64:65], uhit.unsqueeze(2)),
                   reads=[b_u], writes=[b_staK])
            p.emit("vector", lambda e: e.tensor_copy(staKall[:, :, 129:130], ulot.unsqueeze(2)),
                   reads=[b_u], writes=[b_staK])
            p.emit("vector", lambda e: e.tensor_tensor(
                vwall[:], kv_s[:][:, :, 64:128],
                ut.unsqueeze(2).broadcast_to([128, 32, 64]), ALU.mult),
                reads=[b_kv, b_u], writes=[b_staV])
            for j in range(NT):
                p.emit("tensor", (lambda j_=j: lambda e: e.matmul(
                    self.PS[0:65, j_ * 16:(j_ + 1) * 16], staKall[:, j_, 0:65],
                    self.ppool[:], start=True, stop=False))(),
                    reads=[b_staK, self.b_const], writes=[self.bPS])
                p.emit("tensor", (lambda j_=j: lambda e: e.matmul(
                    self.PS[0:65, j_ * 16:(j_ + 1) * 16], staKall[:, j_, 65:130],
                    self.ppool[:], start=False, stop=True))(),
                    reads=[b_staK, self.b_const], writes=[self.bPS])
                p.emit("tensor", (lambda j_=j: lambda e: e.matmul(
                    self.P2[0:64, 512 + j_ * 16: 512 + (j_ + 1) * 16], vwall[:, j_, :],
                    self.ppool[:], start=True, stop=True))(),
                    reads=[b_staV, self.b_const], writes=[self.bP2])
            p.emit("vector", (lambda h_=h: lambda e: e.tensor_copy(
                kcT[h_], self.PS[0:64, :]))(),
                reads=[self.bPS], writes=[b_kcT[h]])
            gsr = self.pool3[:, 7040:7552]
            b_gsr = Buf()
            p.emit("vector", lambda e: e.tensor_copy(gsr[64:65, :], self.PS[64:65, :]),
                   reads=[self.bPS], writes=[b_gsr])
            p.emit("vector", lambda e: e.tensor_scalar(
                gsr[64:65, :], gsr[64:65, :], zp[64:65, 36:37], None, ALU.add),
                reads=[self.b_gw, b_gsr], writes=[b_gsr])
            p.emit("vector", lambda e: e.reciprocal(gsr[64:65, :], gsr[64:65, :]),
                   reads=[b_gsr], writes=[b_gsr])
            p.emit("sync", lambda e: e.dma_start(
                out=self.dram["bnc_scr"][0:1, 0:512], in_=gsr[64:65, :]),
                reads=[b_gsr], writes=[self.dbuf["bnc_scr"]], dma=True)
            p.emit("sync", lambda e: e.dma_start(
                out=gsr[0:4, 0:128],
                in_=self.dram["bnc_scr"][0:1, 0:512].rearrange("o (a b) -> (o a) b", a=4)),
                reads=[self.dbuf["bnc_scr"]], writes=[b_gsr], dma=True)
            p.emit("tensor", lambda e: e.transpose(
                self.PT[:].bitcast(F32)[:, 0:4], gsr[0:4, 0:128], self.identf[0:4, 0:4]),
                reads=[b_gsr, self.b_const], writes=[self.bPT])
            p.emit("vector", (lambda h_=h: lambda e: e.tensor_copy(
                rgcol[:, h_, :], self.PT[:].bitcast(F32)[:, 0:4]))(),
                reads=[self.bPT], writes=[b_rg])
            vct = ering[0:64, 2, 0:512]
            p.emit("vector", lambda e: e.tensor_copy(vct, self.P2[0:64, 512:1024]),
                   reads=[self.bP2], writes=[b_er[2]])
            for cc in range(4):
                p.emit("tensor", (lambda cc_=cc: lambda e: e.transpose(
                    self.PT[:, cc_ * 64:(cc_ + 1) * 64], vct[:, cc_ * 128:(cc_ + 1) * 128],
                    self.ident[0:64, 0:64]))(),
                    reads=[b_er[2], self.b_const], writes=[self.bPT])
                p.emit("vector", (lambda cc_=cc, h_=h: lambda e: e.tensor_scalar(
                    vcn[:, h_, cc_ * 65: cc_ * 65 + 64], self.PT[:, cc_ * 64:(cc_ + 1) * 64],
                    rgcol[:, h_, cc_:cc_ + 1], None, ALU.mult))(),
                    reads=[self.bPT, b_rg], writes=[b_vcn])

        imp = eacc
        p.emit("vector", lambda e: e.tensor_scalar(imp, eacc, -0.4 / 16, None, ALU.mult),
               reads=[b_st], writes=[b_st])
        p.emit("vector", lambda e: e.scalar_tensor_tensor(imp, macc, 0.3 / 16, imp, ALU.mult, ALU.add),
               reads=[b_st], writes=[b_st])
        p.emit("vector", lambda e: e.scalar_tensor_tensor(imp, iacc, 0.3 / 16, imp, ALU.mult, ALU.add),
               reads=[b_st], writes=[b_st])
        g = self.gwork
        sV = self.sb("srch", [128, 8], F32)
        sVb = self.sb("srchb", [128, 2], BF16)
        b_sv = Buf()
        LO, HI, MID, CNT, PRD, T1, T2, MIDC = (sV[:, i:i + 1] for i in range(8))
        p.emit("vector", lambda e: e.reduce_max(g[:, 0:1], imp, AX.X), reads=[b_st], writes=[self.b_gw])
        p.emit("vector", lambda e: e.tensor_reduce(g[:, 1:2], imp, AX.X, ALU.min), reads=[b_st], writes=[self.b_gw])
        p.emit("vector", lambda e: e.tensor_copy(sVb[:, 0:2], g[:, 0:2]), reads=[self.b_gw], writes=[b_sv])
        p.emit("tensor", lambda e: e.transpose(self.PT[0:2, 0:128], sVb[:, 0:2], self.ident[:]),
               reads=[b_sv, self.b_const], writes=[self.bPT])
        p.emit("vector", lambda e: e.tensor_copy(ering[0:2, 0, 0:128], self.PT[0:2, 0:128]),
               reads=[self.bPT], writes=[b_er[0]])
        p.emit("vector", lambda e: e.reduce_max(g[0:1, 2:3], ering[0:1, 0, 0:128], AX.X),
               reads=[b_er[0]], writes=[self.b_gw])
        p.emit("vector", lambda e: e.tensor_copy(sVb[:, 0:1], g[:, 1:2]), reads=[self.b_gw], writes=[b_sv])
        p.emit("tensor", lambda e: e.transpose(self.PT[0:1, 128:256], sVb[:, 0:1], self.ident[:]),
               reads=[b_sv, self.b_const], writes=[self.bPT])
        p.emit("vector", lambda e: e.tensor_copy(ering[0:1, 0, 128:256], self.PT[0:1, 128:256]),
               reads=[self.bPT], writes=[b_er[0]])
        p.emit("vector", lambda e: e.tensor_reduce(g[0:1, 5:6], ering[0:1, 0, 128:256], AX.X, ALU.min),
               reads=[b_er[0]], writes=[self.b_gw])
        p.emit("vector", lambda e: e.tensor_scalar(LO[0:1, :], g[0:1, 5:6], -1.0, None, ALU.add),
               reads=[self.b_gw], writes=[b_sv])
        p.emit("vector", lambda e: e.tensor_scalar(HI[0:1, :], g[0:1, 2:3], 1.0, None, ALU.add),
               reads=[self.b_gw], writes=[b_sv])
        for it in range(40):
            p.emit("vector", lambda e: e.tensor_tensor(MID[0:1, :], LO[0:1, :], HI[0:1, :], ALU.add),
                   reads=[b_sv], writes=[b_sv])
            p.emit("vector", lambda e: e.tensor_scalar(MID[0:1, :], MID[0:1, :], 0.5, None, ALU.mult),
                   reads=[b_sv], writes=[b_sv])
            p.emit("tensor", lambda e: e.matmul(self.P2[:, 3:4], self.onesf[0:1, :], MID[0:1, :],
                                                start=True, stop=True),
                   reads=[b_sv, self.b_const], writes=[self.bP2])
            p.emit("vector", lambda e: e.tensor_copy(MIDC[:], self.P2[:, 3:4]),
                   reads=[self.bP2], writes=[b_sv])
            p.emit("vector", lambda e: e.tensor_scalar(
                ering[:, 3, 0:32], imp, MIDC[:], None, ALU.is_ge, ALU.add, accum_out=CNT[:]),
                reads=[b_st, b_sv], writes=[b_er[3], b_sv])
            p.emit("vector", lambda e: e.tensor_copy(sVb[:, 1:2], CNT[:]), reads=[b_sv], writes=[b_sv])
            p.emit("tensor", lambda e: e.matmul(self.P2[0:1, 4:5], self.ones_col[:], sVb[:, 1:2],
                                                start=True, stop=True),
                   reads=[b_sv, self.b_const], writes=[self.bP2])
            p.emit("vector", lambda e: e.tensor_copy(T1[0:1, :], self.P2[0:1, 4:5]),
                   reads=[self.bP2], writes=[b_sv])
            p.emit("vector", lambda e: e.tensor_scalar(PRD[0:1, :], T1[0:1, :], float(NSEL), None, ALU.is_ge),
                   reads=[b_sv], writes=[b_sv])
            p.emit("vector", lambda e: e.tensor_sub(T1[0:1, :], MID[0:1, :], LO[0:1, :]),
                   reads=[b_sv], writes=[b_sv])
            p.emit("vector", lambda e: e.tensor_tensor(T1[0:1, :], T1[0:1, :], PRD[0:1, :], ALU.mult),
                   reads=[b_sv], writes=[b_sv])
            p.emit("vector", lambda e: e.tensor_add(LO[0:1, :], LO[0:1, :], T1[0:1, :]),
                   reads=[b_sv], writes=[b_sv])
            p.emit("vector", lambda e: e.tensor_sub(T2[0:1, :], MID[0:1, :], HI[0:1, :]),
                   reads=[b_sv], writes=[b_sv])
            p.emit("vector", lambda e: e.tensor_scalar(T1[0:1, :], PRD[0:1, :], 1.0, -1.0, ALU.subtract, ALU.mult),
                   reads=[b_sv], writes=[b_sv])
            p.emit("vector", lambda e: e.tensor_tensor(T2[0:1, :], T2[0:1, :], T1[0:1, :], ALU.mult),
                   reads=[b_sv], writes=[b_sv])
            p.emit("vector", lambda e: e.tensor_add(HI[0:1, :], HI[0:1, :], T2[0:1, :]),
                   reads=[b_sv], writes=[b_sv])
        p.emit("tensor", lambda e: e.matmul(self.P2[:, 5:6], self.onesf[0:1, :], LO[0:1, :],
                                            start=True, stop=True),
               reads=[b_sv, self.b_const], writes=[self.bP2])
        p.emit("vector", lambda e: e.tensor_copy(MIDC[:], self.P2[:, 5:6]), reads=[self.bP2], writes=[b_sv])
        mask = macc
        p.emit("vector", lambda e: e.tensor_scalar(mask, imp, MIDC[:], None, ALU.is_ge),
               reads=[b_st, b_sv], writes=[b_st])
        maskb = ering[:, 3, 0:32]
        p.emit("vector", lambda e: e.tensor_copy(maskb, mask), reads=[b_st], writes=[b_er[3]])
        p.emit("tensor", lambda e: e.matmul(self.P2[:, 32:64], self.uinc[:], maskb,
                                            start=True, stop=False),
               reads=[b_er[3], self.b_const], writes=[self.bP2])
        p.emit("tensor", lambda e: e.matmul(self.P2[0:1, 576:608], self.ones_col[:], maskb,
                                            start=True, stop=True),
               reads=[b_er[3], self.b_const], writes=[self.bP2])
        p.emit("vector", lambda e: e.tensor_copy(ering[0:1, 3, 32:64], self.P2[0:1, 576:608]),
               reads=[self.bP2], writes=[b_er[3]])
        p.emit("tensor", lambda e: e.transpose(self.PT[0:32, 0:1], ering[0:1, 3, 32:64],
                                               self.ident[0:1, 0:1]),
               reads=[b_er[3], self.b_const], writes=[self.bPT])
        p.emit("vector", lambda e: e.tensor_copy(ering[0:32, 3, 64:65], self.PT[0:32, 0:1]),
               reads=[self.bPT], writes=[b_er[3]])
        p.emit("tensor", lambda e: e.matmul(self.P2[0:1, 608:640], ering[0:32, 3, 64:65], self.l32[:],
                                            start=True, stop=True),
               reads=[b_er[3], self.b_const], writes=[self.bP2])
        p.emit("vector", lambda e: e.tensor_copy(self.pool3[0:1, 7552:7584], self.P2[0:1, 608:640]),
               reads=[self.bP2], writes=[b_er[3]])
        p.emit("tensor", lambda e: e.matmul(self.P2[:, 32:64], self.onesf[0:1, :],
                                            self.pool3[0:1, 7552:7584], start=False, stop=True),
               reads=[b_er[3], self.b_const], writes=[self.bP2])
        posm1 = iacc
        p.emit("vector", lambda e: e.tensor_scalar(posm1, self.P2[:, 32:64], -1.0, None, ALU.add),
               reads=[self.bP2], writes=[b_st])

        self.drain_precast()
        p.barrier()
        outN = self.pool2[:, 0:32768].rearrange("p (c n) -> p c n", c=8)
        b_outN = Buf()
        iota512 = self.pool3[:, 7584:8096]
        b_iota = Buf()
        selj = p3[:, 16192:16704]
        b_selj = Buf()
        xnselT = p3[:, 17216:21312].rearrange("p (c n) -> p c n", c=8)
        b_xsel = Buf()
        qT2 = p3[:, 21312:25408]
        b_qT2 = Buf()
        kselT = p3[0:64, 25408:25920]
        b_ksel = Buf()
        vsel = p3[:, 25920:26180].rearrange("p (s n) -> p s n", s=4)
        b_vsel = Buf()
        esel = p3[:, 26180:28228]
        b_esel = Buf()
        ec = p3[:, 28228:30276]
        b_ec = Buf()
        zrow = self.pool3[:, 15168:15680]
        b_zrow = Buf()
        replS = self.pool3[:, 15680:16192]
        b_repl = Buf()
        padcol = self.sb("padcol", [128, 4], F32)
        b_pad = Buf()
        p.emit("sync", lambda e: e.dma_start(out=padcol[:], in_=self.dram["c_padbias"]),
               writes=[b_pad], dma=True)
        p.emit("gpsimd", lambda e: e.iota(iota512, pattern=[[1, 512]], base=0,
                                          channel_multiplier=0,
                                          allow_small_or_imprecise_dtypes=True),
               writes=[b_iota])
        p.emit("vector", lambda e: e.memset(
            vsel[:].rearrange("p s n -> p (s n)").rearrange("p (g o) -> p g o", o=65)[:, :, 64:65], 1.0),
            writes=[b_vsel])
        for pas in range(2):
            for j in range(NT):
                xt = self.xtile[:, j % 2, :]
                bxt = self.b_xt[j % 2]
                p.emit("sync", (lambda j_=j, xt_=xt: lambda e: e.dma_start(
                    out=xt_, in_=src[j_ * 128:(j_ + 1) * 128, :]))(),
                    reads=[bsrc], writes=[bxt], dma=True)
                xn = self.xntile[:, j % 2, :]
                bxn = self.b_xn[j % 2]
                self.ln_tile(xt, bxt, xn, bxn, self.lnsm, self.b_lnsm,
                             dummy=self.P2, bdummy=self.bP2)
                p.emit("vector", (lambda j_=j: lambda e: e.tensor_scalar(
                    selj, iota512, posm1[:, j_:j_ + 1], None, ALU.is_equal))(),
                    reads=[b_iota, b_st], writes=[b_selj])
                p.emit("vector", (lambda j_=j: lambda e: e.tensor_scalar(
                    selj, selj, mask[:, j_:j_ + 1], None, ALU.mult))(),
                    reads=[b_st, b_selj], writes=[b_selj])
                for ds in range(4):
                    dslice = pas * 4 + ds
                    pp = [self.P0, self.P1][ds // 2]
                    bpp = [self.bP0, self.bP1][ds // 2]
                    p.emit("tensor", (lambda j_=j, ds_=ds, dsl_=dslice, pp_=pp, bxn_=bxn, xn_=xn: lambda e: e.matmul(
                        pp_[:, (ds_ % 2) * 512:(ds_ % 2 + 1) * 512],
                        xn_[:, dsl_ * 128:(dsl_ + 1) * 128], selj,
                        start=(j_ == 0), stop=(j_ == NT - 1)))(),
                        reads=[bxn, b_selj], writes=[bpp])
            for ds in range(4):
                dslice = pas * 4 + ds
                pp = [self.P0, self.P1][ds // 2]
                bpp = [self.bP0, self.bP1][ds // 2]
                p.emit("vector", (lambda ds_=ds, dsl_=dslice, pp_=pp: lambda e: e.tensor_copy(
                    xnselT[:, dsl_, :], pp_[:, (ds_ % 2) * 512:(ds_ % 2 + 1) * 512]))(),
                    reads=[bpp], writes=[b_xsel])
        for h in range(NH):
            load_wslice("Wq", h, wsl[:, 0, :], b_wsl[0])
            wq8 = wsl[:, 0, :].rearrange("p (c n) -> p c n", c=8)
            for tb8 in range(8):
                for c in range(8):
                    p.emit("tensor", (lambda tb_=tb8, c_=c: lambda e: e.matmul(
                        self.PS[0:64, :], wq8[:, c_, :],
                        xnT[:, c_, tb_ * 512:(tb_ + 1) * 512],
                        start=(c_ == 0), stop=(c_ == 7)))(),
                        reads=[b_wsl[0], b_xnT], writes=[self.bPS])
                p.emit("scalar", (lambda tb_=tb8: lambda e: e.activation(
                    qT2[0:64, tb_ * 512:(tb_ + 1) * 512], self.PS[0:64, :],
                    AF.Copy, scale=0.125))(),
                    reads=[self.bPS], writes=[b_qT2])
            load_wslice("Wk", h, wsl[:, 1, :], b_wsl[1])
            wk8 = wsl[:, 1, :].rearrange("p (c n) -> p c n", c=8)
            for c in range(8):
                p.emit("tensor", (lambda c_=c: lambda e: e.matmul(
                    self.PS[0:64, :], wk8[:, c_, :], xnselT[:, c_, :],
                    start=(c_ == 0), stop=(c_ == 7)))(),
                    reads=[b_wsl[1], b_xsel], writes=[self.bPS])
            p.emit("vector", lambda e: e.tensor_copy(kselT, self.PS[0:64, :]),
                   reads=[self.bPS], writes=[b_ksel])
            load_wslice("Wv", h, wsl[:, 0, :], b_wsl[0])
            wv8 = wsl[:, 0, :].rearrange("p (c n) -> p c n", c=8)
            for s4 in range(4):
                for c in range(8):
                    p.emit("tensor", (lambda s_=s4, c_=c: lambda e: e.matmul(
                        self.P2[:, s_ * 64:(s_ + 1) * 64],
                        xnselT[:, c_, s_ * 128:(s_ + 1) * 128], wv8[:, c_, :],
                        start=(c_ == 0), stop=(c_ == 7)))(),
                        reads=[b_wsl[0], b_xsel], writes=[self.bP2])
                p.emit("vector", (lambda s_=s4: lambda e: e.tensor_copy(
                    vsel[:, s_, 0:64], self.P2[:, s_ * 64:(s_ + 1) * 64]))(),
                    reads=[self.bP2], writes=[b_vsel])
            for qb in range(8):
                for s4 in range(4):
                    pp = [self.P0, self.P1][s4 // 2]
                    bpp = [self.bP0, self.bP1][s4 // 2]
                    p.emit("tensor", (lambda s_=s4, qb_=qb, pp_=pp: lambda e: e.matmul(
                        pp_[:, (s_ % 2) * 512:(s_ % 2 + 1) * 512],
                        kselT[:, s_ * 128:(s_ + 1) * 128],
                        qT2[0:64, qb_ * 512:(qb_ + 1) * 512],
                        start=True, stop=True))(),
                        reads=[b_ksel, b_qT2], writes=[bpp])
                for s4 in range(4):
                    pp = [self.P0, self.P1][s4 // 2]
                    bpp = [self.bP0, self.bP1][s4 // 2]
                    p.emit("scalar", (lambda s_=s4, pp_=pp: lambda e: e.activation(
                        esel[:, s_ * 512:(s_ + 1) * 512], pp_[:, (s_ % 2) * 512:(s_ % 2 + 1) * 512],
                        AF.Exp, bias=padcol[:, s_:s_ + 1]))(),
                        reads=[bpp, b_pad], writes=[b_esel])
                for s4 in range(4):
                    pp = [self.P0, self.P1][s4 // 2]
                    bpp = [self.bP0, self.bP1][s4 // 2]
                    p.emit("tensor", (lambda s_=s4, qb_=qb, h_=h, pp_=pp: lambda e: e.matmul(
                        pp_[:, (s_ % 2) * 512:(s_ % 2 + 1) * 512],
                        kcT[h_][:, s_ * 128:(s_ + 1) * 128],
                        qT2[0:64, qb_ * 512:(qb_ + 1) * 512],
                        start=True, stop=True))(),
                        reads=[b_kcT[h], b_qT2], writes=[bpp])
                for s4 in range(4):
                    pp = [self.P0, self.P1][s4 // 2]
                    bpp = [self.bP0, self.bP1][s4 // 2]
                    p.emit("scalar", (lambda s_=s4, h_=h, pp_=pp: lambda e: e.activation(
                        ec[:, s_ * 512:(s_ + 1) * 512], pp_[:, (s_ % 2) * 512:(s_ % 2 + 1) * 512],
                        AF.Exp, scale=rgcol[:, h_, s_:s_ + 1]))(),
                        reads=[bpp, b_rg], writes=[b_ec])
                for s4 in range(4):
                    p.emit("tensor", (lambda s_=s4: lambda e: e.matmul(
                        self.PS[0:65, :], vsel[:, s_, :], esel[:, s_ * 512:(s_ + 1) * 512],
                        start=(s_ == 0), stop=False))(),
                        reads=[b_vsel, b_esel], writes=[self.bPS])
                for s4 in range(4):
                    p.emit("tensor", (lambda s_=s4, h_=h: lambda e: e.matmul(
                        self.PS[0:65, :], vcn[:, h_, s_ * 65:(s_ + 1) * 65],
                        ec[:, s_ * 512:(s_ + 1) * 512],
                        start=False, stop=(s_ == 3)))(),
                        reads=[b_vcn, b_ec], writes=[self.bPS])
                p.emit("vector", lambda e: e.tensor_copy(zrow[64:65, :], self.PS[64:65, :]),
                       reads=[self.bPS], writes=[b_zrow])
                p.emit("vector", lambda e: e.reciprocal(zrow[64:65, :], zrow[64:65, :]),
                       reads=[b_zrow], writes=[b_zrow])
                p.emit("tensor", lambda e: e.matmul(
                    self.P2[0:64, 0:512], self.onesf[64:65, 0:64], zrow[64:65, :],
                    start=True, stop=True),
                    reads=[b_zrow, self.b_const], writes=[self.bP2])
                p.emit("vector", lambda e: e.tensor_copy(replS[0:64, :], self.P2[0:64, 0:512]),
                       reads=[self.bP2], writes=[b_repl])
                p.emit("vector", (lambda h_=h, qb_=qb: lambda e: e.tensor_tensor(
                    outN[(h_ % 2) * 64:(h_ % 2) * 64 + 64, h_ // 2, qb_ * 512:(qb_ + 1) * 512],
                    self.PS[0:64, :], replS[0:64, :], ALU.mult))(),
                    reads=[self.bPS, b_repl], writes=[b_outN])
        p.barrier()
        Wo = p3[:, 17216:25408].rearrange("p (c n) -> p c n", c=8)
        b_Wo = Buf()
        for c in range(8):
            sl = c % 2
            p.emit("sync", (lambda c_=c, sl_=sl: lambda e: e.dma_start(
                out=self.wstg[:, sl_, :], in_=w["Wo"][c_ * 128:(c_ + 1) * 128, :]))(),
                writes=[self.b_wstg[sl]], dma=True)
            p.emit("vector", (lambda c_=c, sl_=sl: lambda e: e.tensor_copy(
                Wo[:, c_, :], self.wstg[:, sl_, :]))(),
                reads=[self.b_wstg[sl]], writes=[b_Wo])
        outtile = self.pool3[:, 13090:15138].rearrange("p (r n) -> p r n", r=2)
        b_ot = [Buf(), Buf()]
        outs = []
        for j in range(NT):
            pp = [self.P2, self.P0][j % 2]
            bpp = [self.bP2, self.bP0][j % 2]
            for mv in range(2):
                for c in range(8):
                    p.emit("tensor", (lambda j_=j, c_=c, mv_=mv, pp_=pp: lambda e: e.matmul(
                        pp_[:, mv_ * 512:(mv_ + 1) * 512],
                        outN[:, c_, j_ * 128:(j_ + 1) * 128],
                        Wo[:, c_, mv_ * 512:(mv_ + 1) * 512],
                        start=(c_ == 0), stop=(c_ == 7)))(),
                        reads=[b_outN, b_Wo], writes=[bpp])
            xt = self.xtile[:, j % 2, :]
            bxt = self.b_xt[j % 2]
            p.emit("sync", (lambda j_=j, xt_=xt: lambda e: e.dma_start(
                out=xt_, in_=src[j_ * 128:(j_ + 1) * 128, :]))(),
                reads=[bsrc], writes=[bxt], dma=True)
            p.emit("vector", (lambda j_=j, pp_=pp, xt_=xt: lambda e: e.scalar_tensor_tensor(
                outtile[:, j_ % 2, :], pp_[:], 1.0, xt_, ALU.mult, ALU.add))(),
                reads=[bpp, bxt], writes=[b_ot[j % 2]])
            outs.append(p.emit("sync", (lambda j_=j: lambda e: e.dma_start(
                out=dst[j_ * 128:(j_ + 1) * 128, :], in_=outtile[:, j_ % 2, :]))(),
                reads=[b_ot[j % 2]], writes=[bdst], dma=True))
        return outs

    def stage_attn_b(self, src, bsrc, dst, bdst, w):
        p = self.p
        xn2T = self.pool1[:, 0:32768].rearrange("p (c n) -> p c n", c=8)
        b_xn2T = self.b_pool1
        xo_all = self.pool2[:, 0:32768].rearrange("p (j n) -> p j n", j=32)
        b_xo = self.b_pool2
        p3 = self.pool3[:].bitcast(BF16)
        qkT = p3[:, 0:4096]
        b_qkT = Buf()
        vpair = p3[:, 4096:8192].rearrange("p (j n) -> p j n", j=32)
        b_vp = Buf()
        KEp = p3[:, 8192:9216].rearrange("p (j n) -> p j n", j=32)
        b_KE = Buf()
        qhp = p3[:, 9216:10240].rearrange("p (j n) -> p j n", j=32)
        b_qh = Buf()
        OFF = [self.pool3[0:1, 5120 + i * 1040: 5120 + (i + 1) * 1040] for i in range(2)]
        b_OFF = [Buf(), Buf()]
        wring = p3[:, 14592:20736].rearrange("p (r n) -> p r n", r=2)
        b_wr = [Buf(), Buf()]
        ktil = p3[0:16, 20736:24832]
        b_ktil = Buf()
        ktex = p3[0:16, 24832:28928]
        b_ktex = Buf()
        abR = [p3[:, 28928:29968], p3[:, 13360:14400]]
        b_abR = [Buf(), Buf()]
        xp = p3[:, 29968:30992]
        b_xp = Buf()
        kvsbR = [p3[:, 31040:32080], p3[:, 12320:13360]]
        b_kvsbR = [Buf(), Buf()]
        OFFB = [p3[0:1, 10240 + i * 1040: 10240 + (i + 1) * 1040] for i in range(2)]
        xored = self.pool3[:, 16064:16128]
        b_xor = Buf()
        codeT = p3[:, 30992:31024].rearrange("p (h n) -> p h n", h=2)
        b_code = Buf()
        kmx = self.gwork[0:16, 40:42]
        for j in range(NT):
            xt = self.xtile[:, j % 2, :]
            bxt = self.b_xt[j % 2]
            p.emit("sync", (lambda j_=j, xt_=xt: lambda e: e.dma_start(
                out=xt_, in_=src[j_ * 128:(j_ + 1) * 128, :]))(),
                reads=[bsrc], writes=[bxt], dma=True)
            xn = self.xntile[:, j % 2, :]
            bxn = self.b_xn[j % 2]
            self.ln_tile(xt, bxt, xn, bxn, self.lnsm, self.b_lnsm)
            self.transpose8(xn, bxn, xn2T[:, :, j * 128:(j + 1) * 128], b_xn2T)
        for pr in range(NH // 2):
            wr = wring[:, pr % 2, :].rearrange("p (b n) -> p b n", b=3)
            bwr = b_wr[pr % 2]
            for b3 in range(3):
                sl = b3 % 2
                p.emit("sync", (lambda b3_=b3, sl_=sl: lambda e: e.dma_start(
                    out=self.wstg[:, sl_, :],
                    in_=w["w_qkv"][b3_ * 1024 + pr * 128: b3_ * 1024 + (pr + 1) * 128, :]))(),
                    writes=[self.b_wstg[sl]], dma=True)
                p.emit("vector", (lambda sl_=sl: lambda e: e.tensor_copy(
                    self.w1b[:, sl_, :], self.wstg[:, sl_, :]))(),
                    reads=[self.b_wstg[sl]], writes=[self.b_w1[sl]])
                for c in range(8):
                    p.emit("tensor", (lambda c_=c, sl_=sl: lambda e: e.transpose(
                        self.PT[:, c_ * 128:(c_ + 1) * 128],
                        self.w1b[:, sl_, c_ * 128:(c_ + 1) * 128], self.ident[:]))(),
                        reads=[self.b_w1[sl], self.b_const], writes=[self.bPT])
                p.emit("vector", (lambda b3_=b3: lambda e: e.tensor_copy(
                    wr[:, b3_, :].rearrange("p (c n) -> p c n", c=8).transpose([0, 2, 1]).rearrange("p n c -> p (n c)") if False else
                    wr[:, b3_, :],
                    self.PT[:]))(),
                    reads=[self.bPT], writes=[bwr])
            for hip in range(2):
                h = pr * 2 + hip
                p.emit("sync", (lambda h_=h: lambda e: e.dma_start(
                    out=self.wstg[0:16, 0, 0:64], in_=w["fc_code"][0, h_]))(),
                    writes=[self.b_wstg[0]], dma=True)
                p.emit("vector", lambda e: e.tensor_copy(
                    self.w1b[0:16, 0, 0:64], self.wstg[0:16, 0, 0:64]),
                    reads=[self.b_wstg[0]], writes=[self.b_w1[0]])
                p.emit("tensor", lambda e: e.transpose(
                    self.PT[0:64, 0:16], self.w1b[0:16, 0, 0:64], self.ident[0:16, 0:16]),
                    reads=[self.b_w1[0], self.b_const], writes=[self.bPT])
                p.emit("tensor", lambda e: e.transpose(
                    self.PT[64:128, 16:32], self.w1b[0:16, 0, 0:64], self.ident[0:16, 0:16]),
                    reads=[self.b_w1[0], self.b_const], writes=[self.bPT])
                p.emit("scalar", (lambda hip_=hip: lambda e: e.activation(
                    codeT[0:64, hip_, :], self.PT[0:64, 0:16], AF.Copy, scale=0.125))(),
                    reads=[self.bPT], writes=[b_code])
                p.emit("scalar", (lambda hip_=hip: lambda e: e.activation(
                    codeT[64:128, hip_, :], self.PT[64:128, 16:32], AF.Copy, scale=0.125))(),
                    reads=[self.bPT], writes=[b_code])
                for half64 in range(2):
                    for tb8 in range(8):
                        for c in range(8):
                            p.emit("tensor", (lambda half_=half64, tb_=tb8, c_=c, hip_=hip: lambda e: e.matmul(
                                self.PS[0:64, :],
                                wr[:, half_, c_ * 128 + hip_ * 64: c_ * 128 + hip_ * 64 + 64],
                                xn2T[:, c_, tb_ * 512:(tb_ + 1) * 512],
                                start=(c_ == 0), stop=(c_ == 7)))(),
                                reads=[bwr, b_xn2T], writes=[self.bPS])
                        p.emit("vector", (lambda half_=half64, tb_=tb8: lambda e: e.tensor_copy(
                            qkT[half_ * 64:(half_ + 1) * 64, tb_ * 512:(tb_ + 1) * 512],
                            self.PS[0:64, :]))(),
                            reads=[self.bPS], writes=[b_qkT])
                for j in range(NT):
                    for c in range(8):
                        p.emit("tensor", (lambda j_=j, c_=c, hip_=hip: lambda e: e.matmul(
                            self.PS[:, 0:64],
                            xn2T[:, c_, j_ * 128:(j_ + 1) * 128],
                            wr[:, 2, c_ * 128 + hip_ * 64: c_ * 128 + hip_ * 64 + 64],
                            start=(c_ == 0), stop=(c_ == 7)))(),
                            reads=[bwr, b_xn2T], writes=[self.bPS])
                    p.emit("vector", (lambda j_=j, hip_=hip: lambda e: e.tensor_copy(
                        vpair[:, j_, hip_ * 64:(hip_ + 1) * 64], self.PS[:, 0:64]))(),
                        reads=[self.bPS], writes=[b_vp])
                for tb8 in range(8):
                    p.emit("tensor", (lambda tb_=tb8, hip_=hip: lambda e: e.matmul(
                        self.PS[0:16, :], codeT[64:128, hip_, :],
                        qkT[64:128, tb_ * 512:(tb_ + 1) * 512],
                        start=True, stop=True))(),
                        reads=[b_code, b_qkT], writes=[self.bPS])
                    p.emit("vector", (lambda tb_=tb8: lambda e: e.tensor_copy(
                        ktil[:, tb_ * 512:(tb_ + 1) * 512], self.PS[0:16, :]))(),
                        reads=[self.bPS], writes=[b_ktil])
                p.emit("vector", lambda e: e.reduce_max(kmx[:, 0:1], ktil, AX.X),
                       reads=[b_ktil], writes=[self.b_gw])
                p.emit("vector", lambda e: e.tensor_scalar(kmx[:, 1:2], kmx[:, 0:1], -1.0, None, ALU.mult),
                       reads=[self.b_gw], writes=[self.b_gw])
                p.emit("scalar", lambda e: e.activation(ktex, ktil, AF.Exp, bias=kmx[0:16, 1:2]),
                       reads=[b_ktil, self.b_gw], writes=[b_ktex])
                for half16 in range(2):
                    for jj in range(16):
                        j = half16 * 16 + jj
                        p.emit("tensor", (lambda j_=j, jj_=jj: lambda e: e.transpose(
                            self.PT[:, jj_ * 16:(jj_ + 1) * 16],
                            ktex[:, j_ * 128:(j_ + 1) * 128], self.ident[0:16, 0:16]))(),
                            reads=[b_ktex, self.b_const], writes=[self.bPT])
                    p.emit("vector", (lambda half_=half16, hip_=hip: lambda e: e.tensor_copy(
                        KEp[:, half_ * 16:(half_ + 1) * 16, hip_ * 16:(hip_ + 1) * 16],
                        self.PT[:, 0:256].rearrange("p (j n) -> p j n", j=16)))(),
                        reads=[self.bPT], writes=[b_KE])
                for j in range(NT):
                    p.emit("tensor", (lambda j_=j, hip_=hip: lambda e: e.matmul(
                        self.PS[:, j_ * 16:(j_ + 1) * 16], qkT[0:64, j_ * 128:(j_ + 1) * 128],
                        codeT[0:64, hip_, :], start=True, stop=True))(),
                        reads=[b_qkT, b_code], writes=[self.bPS])
                qsm = self.pool3[:, 8096:8608]
                qmx = self.pool3[:, 8608:8640]
                b_qs = Buf()
                p.emit("vector", lambda e: e.tensor_reduce(
                    qmx, self.PS[:].rearrange("p (j n) -> p j n", n=16), AX.X, ALU.max),
                    reads=[self.bPS], writes=[b_qs])
                p.emit("vector", lambda e: e.tensor_tensor(
                    qsm.rearrange("p (j n) -> p j n", n=16),
                    self.PS[:].rearrange("p (j n) -> p j n", n=16),
                    qmx.unsqueeze(2).broadcast_to([128, 32, 16]), ALU.subtract),
                    reads=[self.bPS, b_qs], writes=[b_qs])
                p.emit("scalar", lambda e: e.activation(qsm, qsm, AF.Exp),
                       reads=[b_qs], writes=[b_qs])
                p.emit("vector", lambda e: e.tensor_reduce(
                    qmx, qsm.rearrange("p (j n) -> p j n", n=16), AX.X, ALU.add),
                    reads=[b_qs], writes=[b_qs])
                p.emit("vector", lambda e: e.reciprocal(qmx, qmx), reads=[b_qs], writes=[b_qs])
                p.emit("vector", (lambda hip_=hip: lambda e: e.tensor_tensor(
                    qhp[:, :, hip_ * 16:(hip_ + 1) * 16],
                    qsm.rearrange("p (j n) -> p j n", n=16),
                    qmx.unsqueeze(2).broadcast_to([128, 32, 16]), ALU.mult))(),
                    reads=[b_qs], writes=[b_qh])
            for hip in range(2):
                p.emit("vector", (lambda hip_=hip: lambda e: e.memset(OFFB[hip_][:], 0.0))(),
                       writes=[b_OFF[hip]])
            for tb in range(NT):
                for hip in range(2):
                    h = pr * 2 + hip
                    u = tb * 2 + hip
                    PKV = [self.P0, self.P1, self.P2][u % 3]
                    bPKV = [self.bP0, self.bP1, self.bP2][u % 3]
                    ab = abR[u % 2]
                    b_ab = b_abR[u % 2]
                    kvsb = kvsbR[u % 2]
                    b_kvsb = b_kvsbR[u % 2]
                    p.emit("vector", (lambda tb_=tb, hip_=hip, ab_=ab: lambda e: e.tensor_tensor(
                        ab_[:, 0:1024].rearrange("p (c d) -> p c d", c=16),
                        KEp[:, tb_, hip_ * 16:(hip_ + 1) * 16].unsqueeze(2).broadcast_to([128, 16, 64]),
                        vpair[:, tb_, hip_ * 64:(hip_ + 1) * 64].unsqueeze(1).broadcast_to([128, 16, 64]),
                        ALU.mult))(),
                        reads=[b_KE, b_vp], writes=[b_ab])
                    p.emit("scalar", (lambda tb_=tb, hip_=hip, ab_=ab: lambda e: e.activation(
                        ab_[:, 1024:1040], KEp[:, tb_, hip_ * 16:(hip_ + 1) * 16], AF.Copy))(),
                        reads=[b_KE], writes=[b_ab])
                    for mv in range(2):
                        p.emit("tensor", (lambda mv_=mv, PKV_=PKV, ab_=ab: lambda e: e.matmul(
                            PKV_[:, mv_ * 512:(mv_ + 1) * 512], self.uinc[:],
                            ab_[:, mv_ * 512:(mv_ + 1) * 512], start=True, stop=False))(),
                            reads=[self.b_const, b_ab], writes=[bPKV])
                    p.emit("tensor", (lambda ab_=ab: lambda e: e.matmul(
                        self.PS[:, 0:16], self.uinc[:], ab_[:, 1024:1040],
                        start=True, stop=False))(),
                        reads=[self.b_const, b_ab], writes=[self.bPS])
                    for mv in range(2):
                        p.emit("tensor", (lambda mv_=mv, hip_=hip, PKV_=PKV: lambda e: e.matmul(
                            PKV_[:, mv_ * 512:(mv_ + 1) * 512], self.ones_row_b[0:1, :],
                            OFFB[hip_][:, mv_ * 512:(mv_ + 1) * 512],
                            start=False, stop=True))(),
                            reads=[self.b_const, b_OFF[hip]], writes=[bPKV])
                    p.emit("tensor", (lambda hip_=hip: lambda e: e.matmul(
                        self.PS[:, 0:16], self.ones_row_b[0:1, :],
                        OFFB[hip_][:, 1024:1040], start=False, stop=True))(),
                        reads=[self.b_const, b_OFF[hip]], writes=[self.bPS])
                    g = self.gwork
                    p.emit("vector", lambda e: e.tensor_scalar(
                        g[:, 20:36], self.PS[:, 0:16], self.eps9[:], None, ALU.add),
                        reads=[self.bPS, self.b_const], writes=[self.b_gw])
                    p.emit("vector", lambda e: e.reciprocal(g[:, 20:36], g[:, 20:36]),
                           reads=[self.b_gw], writes=[self.b_gw])
                    p.emit("vector", (lambda tb_=tb, hip_=hip: lambda e: e.scalar_tensor_tensor(
                        g[:, 20:36], qhp[:, tb_, hip_ * 16:(hip_ + 1) * 16], 0.125,
                        g[:, 20:36], ALU.mult, ALU.mult))(),
                        reads=[b_qh, self.b_gw], writes=[self.b_gw])
                    p.emit("scalar", (lambda PKV_=PKV, kvsb_=kvsb: lambda e: e.activation(
                        kvsb_[:, 0:1024], PKV_[:], AF.Copy))(),
                        reads=[bPKV], writes=[b_kvsb])
                    p.emit("scalar", (lambda kvsb_=kvsb: lambda e: e.activation(
                        kvsb_[:, 1024:1040], self.PS[:, 0:16], AF.Copy))(),
                        reads=[self.bPS], writes=[b_kvsb])
                    if tb < NT - 1:
                        p.emit("sync", (lambda hip_=hip, kvsb_=kvsb: lambda e: e.dma_start(
                            out=OFFB[hip_][:], in_=kvsb_[127:128, :]))(),
                            reads=[b_kvsb], writes=[b_OFF[hip]], dma=True)
                    p.emit("vector", (lambda kvsb_=kvsb: lambda e: e.tensor_tensor(
                        xp[:].rearrange("p (c d) -> p c d", c=16),
                        g[:, 20:36].unsqueeze(2).broadcast_to([128, 16, 64]),
                        kvsb_[:, 0:1024].rearrange("p (c d) -> p c d", c=16), ALU.mult))(),
                        reads=[self.b_gw, b_kvsb], writes=[b_xp])
                    p.emit("vector", lambda e: e.tensor_reduce(
                        xored, xp[:].rearrange("p (c d) -> p d c", c=16), AX.X, ALU.add),
                        reads=[b_xp], writes=[b_xor])
                    p.emit("scalar", (lambda tb_=tb, h_=h: lambda e: e.activation(
                        xo_all[:, tb_, h_ * 64:(h_ + 1) * 64], xored, AF.Copy))(),
                        reads=[b_xor], writes=[b_xo])
        p.barrier()
        woT = p3[:, 0:8192].rearrange("p (c n) -> p c n", c=8)
        b_woT = Buf()
        for r in range(8):
            sl = r % 2
            p.emit("sync", (lambda r_=r, sl_=sl: lambda e: e.dma_start(
                out=self.wstg[:, sl_, :], in_=w["w_out"][r_ * 128:(r_ + 1) * 128, :]))(),
                writes=[self.b_wstg[sl]], dma=True)
            p.emit("vector", (lambda sl_=sl: lambda e: e.tensor_copy(
                self.w1b[:, sl_, :], self.wstg[:, sl_, :]))(),
                reads=[self.b_wstg[sl]], writes=[self.b_w1[sl]])
            for c in range(8):
                p.emit("tensor", (lambda c_=c, sl_=sl: lambda e: e.transpose(
                    self.PT[:, c_ * 128:(c_ + 1) * 128],
                    self.w1b[:, sl_, c_ * 128:(c_ + 1) * 128], self.ident[:]))(),
                    reads=[self.b_w1[sl], self.b_const], writes=[self.bPT])
            p.emit("vector", (lambda r_=r: lambda e: e.tensor_copy(
                woT[:, :, r_ * 128:(r_ + 1) * 128],
                self.PT[:].rearrange("p (c n) -> p c n", c=8)))(),
                reads=[self.bPT], writes=[b_woT])
        xoT = xn2T
        b_xoT = b_xn2T
        for j in range(NT):
            for c in range(8):
                p.emit("tensor", (lambda j_=j, c_=c: lambda e: e.transpose(
                    self.PT[:, c_ * 128:(c_ + 1) * 128],
                    xo_all[:, j_, c_ * 128:(c_ + 1) * 128], self.ident[:]))(),
                    reads=[b_xo, self.b_const], writes=[self.bPT])
            p.emit("vector", (lambda j_=j: lambda e: e.tensor_copy(
                xoT[:, :, j_ * 128:(j_ + 1) * 128],
                self.PT[:].rearrange("p (c n) -> p c n", c=8)))(),
                reads=[self.bPT], writes=[b_xoT])
        outtile = self.pool3[:, 10240:12288].rearrange("p (r n) -> p r n", r=2)
        b_ot = [Buf(), Buf()]
        outs = []
        for j in range(NT):
            pp = [self.P2, self.P0][j % 2]
            bpp = [self.bP2, self.bP0][j % 2]
            for mv in range(2):
                for c in range(8):
                    p.emit("tensor", (lambda j_=j, c_=c, mv_=mv, pp_=pp: lambda e: e.matmul(
                        pp_[:, mv_ * 512:(mv_ + 1) * 512],
                        xoT[:, c_, j_ * 128:(j_ + 1) * 128],
                        woT[:, c_, mv_ * 512:(mv_ + 1) * 512],
                        start=(c_ == 0), stop=(c_ == 7)))(),
                        reads=[b_xoT, b_woT], writes=[bpp])
            xt = self.xtile[:, j % 2, :]
            bxt = self.b_xt[j % 2]
            p.emit("sync", (lambda j_=j, xt_=xt: lambda e: e.dma_start(
                out=xt_, in_=src[j_ * 128:(j_ + 1) * 128, :]))(),
                reads=[bsrc], writes=[bxt], dma=True)
            p.emit("vector", (lambda j_=j, pp_=pp, xt_=xt: lambda e: e.scalar_tensor_tensor(
                outtile[:, j_ % 2, :], pp_[:], 1.0, xt_, ALU.mult, ALU.add))(),
                reads=[bpp, bxt], writes=[b_ot[j % 2]])
            outs.append(p.emit("sync", (lambda j_=j: lambda e: e.dma_start(
                out=dst[j_ * 128:(j_ + 1) * 128, :], in_=outtile[:, j_ % 2, :]))(),
                reads=[b_ot[j % 2]], writes=[bdst], dma=True))
        return outs


def build_attn_b_only():
    nc = bass.Bass("TRN2", target_bir_lowering=False, dynamic_dma_scratch_size=8192)
    p = Prog()
    kb = KB(nc, p)
    src = kb.din("xsrc", [S, D])
    dst = kb.dout("out", [S, D])
    w = {
        "w_qkv": kb.din("w_qkv", [3 * D, D]),
        "w_out": kb.din("w_out", [D, D]),
        "fc_code": kb.din("fc_code", [1, NH, 16, DH]),
    }
    for nm, arr in host_consts().items():
        kb.din(nm, arr.shape, BF16 if arr.dtype == bf else F32)
    with nc.Block() as block:
        kb.setup(host_consts())
        outs = kb.stage_attn_b(src, kb.dbuf["xsrc"], dst, kb.dbuf["out"], w)
        p.finalize(nc, block, final_extra=outs)
    return nc


def build_ffn_only():
    nc = bass.Bass("TRN2", target_bir_lowering=False, dynamic_dma_scratch_size=8192)
    p = Prog()
    kb = KB(nc, p)
    src = kb.din("xsrc", [S, D])
    dst = kb.dout("out", [S, D])
    w = {
        "router_w": kb.din("router_w", [D, NE]),
        "router_b": kb.din("router_b", [NE]),
        "w1": kb.din("w1", [NE, D, HME]),
        "b1": kb.din("b1", [NE, HME]),
        "w2": kb.din("w2", [NE, HME, D]),
        "b2": kb.din("b2", [NE, D]),
        "sw1": kb.din("sw1", [NSH, D, HME]),
        "sb1": kb.din("sb1", [NSH, HME]),
        "sw2": kb.din("sw2", [NSH, HME, D]),
        "sb2": kb.din("sb2", [NSH, D]),
    }
    for nm, arr in host_consts().items():
        kb.din(nm, arr.shape, BF16 if arr.dtype == bf else F32)
    with nc.Block() as block:
        kb.setup(host_consts())
        outs = kb.stage_ffn(src, kb.dbuf["xsrc"], dst, kb.dbuf["out"], w)
        p.finalize(nc, block, final_extra=outs)
    return nc


def build_attn_a_only():
    nc = bass.Bass("TRN2", target_bir_lowering=False, dynamic_dma_scratch_size=8192)
    p = Prog()
    kb = KB(nc, p)
    src_ = kb.din("xsrc", [S, D])
    dst = kb.dout("out", [S, D])
    w = {"Wq": kb.din("Wq", [D, D]), "Wk": kb.din("Wk", [D, D]),
         "Wv": kb.din("Wv", [D, D]), "Wo": kb.din("Wo", [D, D])}
    for nm, arr in host_consts().items():
        kb.din(nm, arr.shape, BF16 if arr.dtype == bf else F32)
    with nc.Block() as block:
        kb.setup(host_consts())
        outs = kb.stage_attn_a(src_, kb.dbuf["xsrc"], dst, kb.dbuf["out"], w)
        p.finalize(nc, block, final_extra=outs)
    return nc


def build_full():
    nc = bass.Bass("TRN2", target_bir_lowering=False, dynamic_dma_scratch_size=8192)
    p = Prog()
    kb = KB(nc, p)
    x0 = kb.din("x0", [S, D])
    out = kb.dout("out", [S, D])
    wa = {"Wq": kb.din("L0_attn_a_Wq", [D, D]), "Wk": kb.din("L0_attn_a_Wk", [D, D]),
          "Wv": kb.din("L0_attn_a_Wv", [D, D]), "Wo": kb.din("L0_attn_a_Wo", [D, D])}
    wb = {"w_qkv": kb.din("L0_attn_b_w_qkv", [3 * D, D]),
          "w_out": kb.din("L0_attn_b_w_out", [D, D]),
          "fc_code": kb.din("L0_attn_b_fc_code", [1, NH, 16, DH])}
    wf = {"router_w": kb.din("L0_ffn_router_w", [D, NE]),
          "router_b": kb.din("L0_ffn_router_b", [NE]),
          "w1": kb.din("L0_ffn_w1", [NE, D, HME]), "b1": kb.din("L0_ffn_b1", [NE, HME]),
          "w2": kb.din("L0_ffn_w2", [NE, HME, D]), "b2": kb.din("L0_ffn_b2", [NE, D]),
          "sw1": kb.din("L0_ffn_sw1", [NSH, D, HME]), "sb1": kb.din("L0_ffn_sb1", [NSH, HME]),
          "sw2": kb.din("L0_ffn_sw2", [NSH, HME, D]), "sb2": kb.din("L0_ffn_sb2", [NSH, D])}
    for nm, arr in host_consts().items():
        kb.din(nm, arr.shape, BF16 if arr.dtype == bf else F32)
    X1 = kb.dint("X1", [S, D])
    X2 = kb.dint("X2", [S, D])
    X3 = kb.dint("X3", [S, D])
    with nc.Block() as block:
        kb.setup(host_consts())
        kb.init_precast(wf)
        kb.stage_attn_a(x0, kb.dbuf["x0"], X1, kb.dbuf["X1"], wa)
        p.barrier()
        kb.stage_ffn(X1, kb.dbuf["X1"], X2, kb.dbuf["X2"], wf)
        p.barrier()
        kb.stage_attn_b(X2, kb.dbuf["X2"], X3, kb.dbuf["X3"], wb)
        p.barrier()
        outs = kb.stage_ffn(X3, kb.dbuf["X3"], out, kb.dbuf["out"], wf)
        p.finalize(nc, block, final_extra=outs)
    return nc


_NC_CACHE = {}


def kernel(**inputs):
    import numpy as np
    if "nc" not in _NC_CACHE:
        _NC_CACHE["nc"] = build_full()
    nc = _NC_CACHE["nc"]
    from concourse.bass_utils import run_bass_kernel_spmd
    consts = host_consts()
    x = np.ascontiguousarray(inputs["L0_attn_a_x"])
    base = {k: np.ascontiguousarray(v) for k, v in inputs.items() if k != "L0_attn_a_x"}
    base.update(consts)
    in_maps = []
    for i in range(8):
        m = dict(base)
        m["x0"] = np.ascontiguousarray(x[i])
        in_maps.append(m)
    res = run_bass_kernel_spmd(nc, in_maps, core_ids=list(range(8)))
    return np.stack([r["out"] for r in res.results], axis=0)
